# version 09
# baseline (speedup 1.0000x reference)
import contextlib
import numpy as np
import concourse.bass as bass
import concourse.mybir as mybir
from concourse.bass_utils import run_bass_kernel_spmd
from concourse.alu_op_type import AluOpType

F32 = mybir.dt.float32
F16 = mybir.dt.float16
BF16 = mybir.dt.bfloat16
AF = mybir.ActivationFunctionType
AX = mybir.AxisListType

TOK = 2048
NT = 16
D = 1024
FC = 8
H = 16
DH = 64
M = 256
DHID = 4096
NJC = 32
LN_EPS = 1e-5
LNB_EPS = 1e-6
LN16 = float(np.log(np.float64(16.0)))

ENGS = ("sync", "tensor", "vector", "scalar", "gpsimd")


class PW:

    def __init__(self, nc, stack):
        self.nc = nc
        self.stack = stack
        self.ops = {e: [] for e in ENGS}
        self.cnt = {}
        self.sems = {}

    def sem(self, name):
        if name not in self.sems:
            self.sems[name] = self.stack.enter_context(self.nc.semaphore(name))
            self.cnt[name] = 0
        return self.sems[name]

    def emit(self, eng, fn):
        self.ops[eng].append(fn)

    def wait(self, eng, name, val=None):
        s, v = self.sem(name), (self.cnt[name] if val is None else val)
        self.emit(eng, lambda e: e.wait_ge(s, v))

    def dma(self, eng, out, in_, sem=None, slow=False):
        kw = {"allow_slow_non_contiguous": True} if slow else {}
        if sem is None:
            self.emit(eng, lambda e: e.dma_start(out=out, in_=in_, **kw))
            return None
        s = self.sem(sem)
        self.cnt[sem] += 16
        self.emit(eng, lambda e: e.dma_start(out=out, in_=in_, **kw).then_inc(s, 16))
        return self.cnt[sem]

    AUTO = {"vector": "dve", "scalar": "act", "gpsimd": "gp"}

    def _op(self, eng, fn, sem=None, n=1, dep=True):
        auto = self.AUTO.get(eng)
        if auto is not None:
            s = self.sem(auto)
            if dep and self.cnt[auto] > 0:
                v = self.cnt[auto]
                self.emit(eng, lambda e: e.wait_ge(s, v))
            self.cnt[auto] += 1
            self.emit(eng, lambda e: fn(e).then_inc(s, 1))
            return self.cnt[auto]
        if sem is None:
            self.emit(eng, fn)
            return None
        s = self.sem(sem)
        self.cnt[sem] += n
        self.emit(eng, lambda e: fn(e).then_inc(s, n))
        return self.cnt[sem]

    def mm(self, out, lhsT, rhs, start, stop, sem=None, skip=False):
        return self._op("tensor",
                        lambda e: e.matmul(out, lhsT, rhs, start=start, stop=stop,
                                           skip_group_check=skip),
                        sem)

    def trans(self, out, in_, ident, sem=None):
        return self._op("tensor", lambda e: e.transpose(out, in_, ident), sem)

    def mmT(self, out, data, ident, start, stop, sem=None):
        return self._op("tensor",
                        lambda e: e.matmul(out, data, ident, start=start, stop=stop,
                                           is_transpose=True), sem)

    def act(self, out, in_, func, bias=None, scale=None, accum_out=None, sem=None,
            dep=True):
        kw = {}
        if bias is not None:
            kw["bias"] = bias
        if scale is not None:
            kw["scale"] = scale
        if accum_out is not None:
            kw["accum_out"] = accum_out
        return self._op("scalar",
                        lambda e: e.activation(out=out, in_=in_, func=func, **kw),
                        sem, dep=dep)

    def ts(self, out, in0, s1, op0, s2=None, op1=None, sem=None, eng="vector",
           dep=True):
        return self._op(eng,
                        lambda e: e.tensor_scalar(out=out, in0=in0, scalar1=s1,
                                                  scalar2=s2, op0=op0,
                                                  **({"op1": op1} if op1 is not None else {})),
                        sem, dep=dep)

    def tt(self, out, in0, in1, op, sem=None, eng="vector", dep=True):
        return self._op(eng,
                        lambda e: e.tensor_tensor(out=out, in0=in0, in1=in1, op=op),
                        sem, dep=dep)

    def stt(self, out, in0, scalar, in1, op0, op1, sem=None, dep=True):
        return self._op("vector",
                        lambda e: e.scalar_tensor_tensor(out=out, in0=in0, scalar=scalar,
                                                         in1=in1, op0=op0, op1=op1),
                        sem, dep=dep)

    def copy(self, out, in_, sem=None, eng="vector", dep=True):
        return self._op(eng, lambda e: e.tensor_copy(out, in_), sem, dep=dep)

    def red(self, out, in_, op, absval=None, sem=None, negate=None, dep=True):
        return self._op("vector",
                        lambda e: e.tensor_reduce(out=out, in_=in_, axis=AX.X, op=op,
                                                  apply_absolute_value=absval,
                                                  negate=negate), sem, dep=dep)

    def recip(self, out, in_, sem=None, dep=True):
        return self._op("vector", lambda e: e.reciprocal(out=out, in_=in_), sem,
                        dep=dep)

    def memset(self, ap, c, sem=None, dep=True):
        return self._op("gpsimd", lambda e: e.memset(ap, c), sem, dep=dep)

    def coll(self, kind, op, groups, ins, outs, sem=None, dep=True):
        return self._op("gpsimd",
                        lambda e: e.collective_compute(kind, op, replica_groups=groups,
                                                       ins=ins, outs=outs), sem,
                        dep=dep)

    def barrier(self, tag):
        vals = dict(self.cnt)
        for eng in ENGS:
            for name, v in vals.items():
                if v > 0 and not name.startswith("_"):
                    self.wait(eng, name, v)

    def replay(self, block):
        for eng in ENGS:
            ops = self.ops[eng]
            dec = getattr(block, eng)

            def make(fn_list):
                def body(e):
                    for f in fn_list:
                        f(e)
                return body
            dec(make(ops))



def ph_setup(pw, T, x_in, VEC, b_proj, f_convw):
    nc = pw.nc
    pw.dma("sync", T["X"][:, :, :], x_in.rearrange("(t p) f -> p t f", p=128), sem="ld")
    for col, name in enumerate(["a_bproj", "a_bpre", "b_qb", "b_kb", "b_vb",
                                "b_ob", "b_gamma", "b_beta"]):
        pw.dma("sync", T["vecs"][:, 8 * col:8 * col + 8],
               VEC[name].rearrange("(f p) -> p f", p=128), sem="ld", slow=True)
    pw.dma("sync", T["scal"][:, 0:1], VEC["pi_bc"], sem="ld")
    pw.dma("sync", T["scal"][:, 1:2], VEC["isodd_bc"], sem="ld")
    pw.dma("sync", T["scal"][:, 2:3], VEC["mleft_bc"], sem="ld")
    pw.dma("sync", T["scal"][:, 3:4], VEC["mright_bc"], sem="ld")
    pw.dma("sync", T["convw"][:, :, :], f_convw.rearrange("(c p) k -> p c k", p=128), sem="ld", slow=True)
    pw.dma("sync", T["convb"][:], VEC["f_convb"].rearrange("(c p) -> p c", p=128), sem="ld", slow=True)
    pw.dma("sync", T["projx2"][0:64, :], b_proj.rearrange("m d -> d m"), sem="ld", slow=True)

    cvals = T["cvals"]
    for i, v in enumerate(cvals):
        pw.memset(T["consts"][:, i:i + 1], float(v))
    pw.memset(T["ident"][:], 1.0)
    pw._op("gpsimd", lambda e: e.affine_select(
        out=T["ident"][:], in_=T["ident"][:], compare_op=AluOpType.is_equal,
        fill=0.0, base=0, pattern=[[-1, 128]], channel_multiplier=1))
    pw.memset(T["identb"][:], 1.0)
    pw._op("gpsimd", lambda e: e.affine_select(
        out=T["identb"][:], in_=T["identb"][:], compare_op=AluOpType.is_equal,
        fill=0.0, base=0, pattern=[[-1, 128]], channel_multiplier=1))
    pw.memset(T["mask_ut"][:], 1.0)
    pw._op("gpsimd", lambda e: e.affine_select(
        out=T["mask_ut"][:], in_=T["mask_ut"][:], compare_op=AluOpType.is_ge,
        fill=0.0, base=0, pattern=[[1, 128]], channel_multiplier=-1))
    pw.memset(T["ones_col"][:], 1.0)
    pw.memset(T["ones_row"][:], 1.0)
    pw.memset(T["shift64"][:], 1.0)
    pw._op("gpsimd", lambda e: e.affine_select(
        out=T["shift64"][:], in_=T["shift64"][:], compare_op=AluOpType.is_equal,
        fill=0.0, base=-64, pattern=[[1, 128]], channel_multiplier=-1))
    pw.memset(T["projx2"][64:128, :], -0.5, sem="gp")
    pw.wait("vector", "ld")
    pw.wait("vector", "gp")
    pw.ts(T["scal"][:, 4:5], T["scal"][:, 0:1], 1.0 / 32.0, AluOpType.mult)
    pw.ts(T["vecs"][:, 96:104], T["vecs"][:, 16:24], 0.125, AluOpType.mult)
    pw.ts(T["vecs"][:, 104:112], T["vecs"][:, 24:32], 0.125, AluOpType.mult,
          sem="dve")


def emit_ln_tile(pw, x_tile, out_tile, tmp, stat, eps):
    pw.red(stat[:, 0:1], x_tile, AluOpType.add)
    pw.ts(stat[:, 1:2], stat[:, 0:1], 1.0 / D, AluOpType.mult)
    pw.ts(tmp, x_tile, stat[:, 1:2], AluOpType.subtract)
    pw.tt(out_tile, tmp, tmp, AluOpType.mult)
    pw.red(stat[:, 0:1], out_tile, AluOpType.add)
    v1 = pw.ts(stat[:, 2:3], stat[:, 0:1], 1.0 / D, AluOpType.mult, eps,
               AluOpType.add, sem="dve")
    pw.wait("scalar", "dve", v1)
    a1 = pw.act(stat[:, 2:3], stat[:, 2:3], AF.Sqrt, sem="act")
    pw.wait("vector", "act", a1)
    pw.recip(stat[:, 3:4], stat[:, 2:3])
    return pw.ts(out_tile, tmp, stat[:, 3:4], AluOpType.mult, sem="dve")


def emit_transpose_tm_to_fm(pw, src_tile_fn, dst_fn, nt_tiles, nfc, T, bankA, bankB,
                            dtype_ident="ident"):
    ident = T[dtype_ident][:]
    psum = T["psum"]
    slots = [bankA, bankB]
    pending = []
    mmc0 = pw.cnt.get("mm", 0)
    for t in range(nt_tiles):
        src = src_tile_fn(t)
        for fb in range(0, nfc, 4):
            nf = min(4, nfc - fb)
            slot = slots[(t * ((nfc + 3) // 4) + fb // 4) % 2]
            key = slot
            prev = pw._slot_state.get(key)
            if prev is not None:
                pw.wait("tensor", prev[0], prev[1])
            last = None
            for j in range(nf):
                last = pw.mm(psum[:, slot + 128 * j:slot + 128 * (j + 1)],
                             src[:, 128 * (fb + j):128 * (fb + j + 1)], ident,
                             start=(j == 0), stop=(j == nf - 1),
                             sem="mm" if j == nf - 1 else None)
            pw.wait("vector", "mm", last)
            for j in range(nf):
                s = pw.copy(dst_fn(t, fb + j),
                            psum[:, slot + 128 * j:slot + 128 * (j + 1)],
                            sem="dve" if j == nf - 1 else None)
            pw._slot_state[key] = ("dve", s)


def emit_proj_fm(pw, T, ld_done, evac, dma_dst, dma_sem, span_hook=None,
                 rhs_buf=None, prev_sems=()):
    psum = T["psum"]
    xlnT = T["ARN_A"] if rhs_buf is None else rhs_buf
    pw.wait("tensor", "ld", ld_done)
    for s_ in prev_sems:
        if s_ in pw.cnt:
            pw.wait("scalar", s_)
            pw.wait("vector", s_)
    dma_hist = {}
    for sp in range(4):
        for of in range(FC):
            it = sp * FC + of
            bk = it % 6
            prev = pw._slot_state.get(("pj", bk))
            if prev is not None:
                pw.wait("tensor", prev[0], prev[1])
            last = None
            for f in range(FC):
                last = pw.mm(psum[:, 512 * bk:512 * bk + 512],
                             T["wbuf3"][:, f, 128 * of:128 * of + 128],
                             xlnT[:, 2048 * f + 512 * sp:2048 * f + 512 * (sp + 1)],
                             start=(f == 0), stop=(f == FC - 1),
                             sem="mm" if f == FC - 1 else None)
            eng, cnt, region = evac(of, sp, psum[:, 512 * bk:512 * bk + 512],
                                    last, it >= 4)
            pw._slot_state[("pj", bk)] = (eng, cnt)
            pw.wait("sync", eng, cnt)
            dma_hist[it % 4] = pw.dma("sync", dma_dst(of, sp), region, sem=dma_sem)
        if span_hook is not None:
            span_hook(sp)


def ph_attn_a_proj(pw, T, W, G, dbg):
    X, ARN_A, ARN_B = T["X"], T["ARN_A"], T["ARN_B"]
    psum, stage = T["psum"], T["stage"]
    groups = T["groups"]

    xln = ARN_B
    tmp, stat = T["tmpA"], T["stat"]
    pw.wait("vector", "ld")
    for t in range(NT):
        emit_ln_tile(pw, X[:, t, :], xln[:, 1024 * t:1024 * (t + 1)], tmp[:, 0:1024],
                     stat, LN_EPS)
    pw.wait("tensor", "dve")
    pw.wait("tensor", "gp")
    xlnT = ARN_A

    def dst(t, f):
        return xlnT[:, 2048 * f + 128 * t:2048 * f + 128 * (t + 1)]
    emit_transpose_tm_to_fm(pw, lambda t: xln[:, 1024 * t:1024 * (t + 1)],
                            dst, NT, FC, T, 0, 512)
    if dbg and dbg.get("upto") == "ln1":
        pw.barrier("dbgln")
        pw.dma("sync", dbg["d_xln"], xln.rearrange("p (t f) -> p t f", f=1024),
               sem="st")
        return True
    pw.barrier("projstart")

    def load_W(name):
        return pw.dma("sync", T["wbuf3"], W[name].rearrange("(f p) o -> p f o", p=128),
                      sem="ld")

    ag_done = {}

    def mk_evac(kind, dsem):
        def evac(of, sp, ps, mmcnt, need_guard):
            it = sp * FC + of
            region = stage[:, 512 * (it % 4):512 * (it % 4) + 512]
            pw.wait("scalar", "mm", mmcnt)
            if need_guard:
                pw.wait("scalar", dsem)
            if kind == "silu_bpre":
                c = pw.act(region, ps, AF.Silu, bias=T["vecs"][:, 8 + of:9 + of])
            elif kind == "copy":
                c = pw.act(region, ps, AF.Copy)
            elif kind == "q8":
                c = pw.act(region, ps, AF.Copy, scale=0.125)
            return ("act", c, region)
        return evac

    ld = load_W("aWpre")

    def hook_pre(sp):
        pw.wait("gpsimd", "pst")
        ag_done[("pre", sp)] = pw.coll(
            "AllGather", AluOpType.bypass, groups,
            [G[f"g_pre_in_s{sp}"].opt()], [G[f"g_pre_s{sp}"].opt()])
    emit_proj_fm(pw, T, ld, mk_evac("silu_bpre", "pst"),
                 lambda of, sp: G[f"g_pre_in_s{sp}"][128 * of:128 * (of + 1), :],
                 "pst", hook_pre)

    ld = load_W("aWk")

    def hook_k(sp):
        pw.wait("gpsimd", "kst")
        ag_done[("k", sp)] = pw.coll(
            "AllGather", AluOpType.bypass, groups,
            [G[f"g_k_in_s{sp}"].opt()], [G[f"g_k_s{sp}"].opt()])
    emit_proj_fm(pw, T, ld, mk_evac("copy", "kst"),
                 lambda of, sp: G[f"g_k_in_s{sp}"][128 * of:128 * (of + 1), :],
                 "kst", hook_k, prev_sems=("pst",))

    ld = load_W("aWv")
    pw.memset(T["vstage"], 1.0)
    pw.wait("vector", "gp")
    pw.wait("tensor", "ld", ld)
    vdma = {}
    for mc in range(NT):
        for vh in range(2):
            bk = (mc * 2 + vh) % 6
            prev = pw._slot_state.get(("pj", bk))
            if prev is not None:
                pw.wait("tensor", prev[0], prev[1])
            last = None
            for f in range(FC):
                last = pw.mm(psum[:, 512 * bk:512 * bk + 512],
                             T["ARN_A"][:, 2048 * f + 128 * mc:2048 * f + 128 * (mc + 1)],
                             T["wbuf3"][:, f, 512 * vh:512 * (vh + 1)],
                             start=(f == 0), stop=(f == FC - 1),
                             sem="mm" if f == FC - 1 else None)
            pw.wait("vector", "mm", last)
            if vdma.get(mc % 2) is not None:
                pw.wait("vector", "vst")
            dstv = T["vstage"][:, 1040 * (mc % 2) + 520 * vh:1040 * (mc % 2) + 520 * (vh + 1)] \
                .rearrange("p (h c) -> p h c", c=65)[:, :, 0:64]
            c = pw.copy(dstv, psum[:, 512 * bk:512 * bk + 512]
                        .rearrange("p (h c) -> p h c", c=64), dep=False)
            pw._slot_state[("pj", bk)] = ("dve", c)
        pw.wait("sync", "dve")
        sp, r = mc // 4, mc % 4
        vdma[mc % 2] = pw.dma("sync", G[f"g_v_in_s{sp}"][:, 1040 * r:1040 * (r + 1)],
                              T["vstage"][:, 1040 * (mc % 2):1040 * (mc % 2) + 1040],
                              sem="vst")
        if r == 3:
            pw.wait("gpsimd", "vst")
            ag_done[("v", sp)] = pw.coll(
                "AllGather", AluOpType.bypass, groups,
                [G[f"g_v_in_s{sp}"].opt()], [G[f"g_v_s{sp}"].opt()])

    ld = load_W("aWq")
    emit_proj_fm(pw, T, ld, mk_evac("q8", "qst"),
                 lambda of, sp: G["g_qT"][:, of, 512 * sp:512 * (sp + 1)], "qst",
                 prev_sems=("kst",))

    pw.wait("sync", "pst")
    pw.wait("sync", "qst")
    for of in range(FC):
        l = None
        for sp in range(4):
            l = pw.dma("sync", stage[:, 512 * sp:512 * (sp + 1)],
                       G[f"g_pre_in_s{sp}"][128 * of:128 * (of + 1), :], sem="ld")
        pw.wait("vector", "ld", l)
        if of >= 1:
            pw.wait("vector", "qst")
        s = pw.ts(T["stage2"][:, 0:2048], stage[:, 0:2048], T["scal"][:, 4:5],
                  AluOpType.mult, dep=True)
        pw.wait("sync", "dve", s)
        pw.dma("sync", G["g_pqT"][:, of, :], T["stage2"][:, 0:2048], sem="qst")

    pw._ag_done = ag_done
    if dbg and dbg.get("upto") == "aproj":
        pw.barrier("dbgproj")
        for sp in range(4):
            pw.dma("sync", dbg["d_preT"][:, 512 * sp:512 * (sp + 1)],
                   G[f"g_pre_in_s{sp}"], sem="st")
            pw.dma("sync", dbg["d_kT"][:, 512 * sp:512 * (sp + 1)],
                   G[f"g_k_in_s{sp}"], sem="st")
            pw.dma("sync", dbg["d_vext"][:, 4160 * sp:4160 * (sp + 1)],
                   G[f"g_v_in_s{sp}"], sem="st")
            pw.dma("sync", dbg["d_gpre"][0:1024, 512 * sp:512 * (sp + 1)],
                   G[f"g_pre_s{sp}"][0], sem="st")
            pw.dma("sync", dbg["d_gpre"][1024:2048, 512 * sp:512 * (sp + 1)],
                   G[f"g_pre_s{sp}"][1], sem="st")
        pw.dma("sync", dbg["d_qT"].rearrange("(f p) t -> p f t", p=128), G["g_qT"],
               sem="st")
        pw.dma("sync", dbg["d_pqT"].rearrange("(f p) t -> p f t", p=128), G["g_pqT"],
               sem="st")
        return True
    return False


def ph_attn_a_logits(pw, T, G, dbg):
    A = T["ARN_A"]
    psum = T["psum"]
    PREQ, PM, LT = 0, 8192, 10240
    agv = max(pw._ag_done.values())
    pw.wait("sync", "gp", agv)
    pw.wait("sync", "qst")
    for qh in range(2):
        pw.wait("sync", "mm")
        pw.wait("sync", "act")
        lq = pw.dma("sync", A[:, PREQ:PREQ + 8192].rearrange("p (f t) -> p f t", t=1024),
                    G["g_pqT"][:, :, 1024 * qh:1024 * (qh + 1)], sem="alq")
        pw.wait("tensor", "alq", lq)
        mld = {}

        def issue(mc):
            if mc >= 32:
                return
            par = mc % 2
            slot, g = mc // 16, mc % 16
            sp, r = g // 4, g % 4
            mld[mc] = pw.dma("sync", A[:, PM + 1024 * par:PM + 1024 * (par + 1)]
                             .rearrange("p (f t) -> p f t", t=128),
                             G[f"g_pre_s{sp}"][slot].rearrange("(f p) t -> p f t", p=128)
                             [:, :, 128 * r:128 * (r + 1)], sem="alm" + str(par))
        issue(0)
        issue(1)
        ldma = {}
        for mc in range(32):
            par = mc % 2
            pw.wait("tensor", "alm" + str(par), mld[mc])
            pre = A[:, PM + 1024 * par:PM + 1024 * (par + 1)]
            for qs in range(2):
                bk = (mc * 2 + qs) % 6
                prev = pw._slot_state.get(512 * bk)
                if prev is not None:
                    pw.wait("tensor", prev[0], prev[1])
                last = None
                for f in range(FC):
                    last = pw.mm(psum[:, 512 * bk:512 * bk + 512],
                                 pre[:, 128 * f:128 * (f + 1)],
                                 A[:, PREQ + 1024 * f + 512 * qs:PREQ + 1024 * f + 512 * (qs + 1)],
                                 start=(f == 0), stop=(f == FC - 1),
                                 sem="mm" if f == FC - 1 else None)
                pw.wait("scalar", "mm", last)
                if ldma.get(par) is not None and qs == 0:
                    pw.wait("scalar", "alst" + str(par))
                lc = pw.act(A[:, LT + 1024 * par + 512 * qs:LT + 1024 * par + 512 * (qs + 1)],
                            psum[:, 512 * bk:512 * bk + 512], AF.Copy, dep=False)
                pw._slot_state[512 * bk] = ("act", lc)
            pw.wait("sync", "act")
            ldma[par] = pw.dma("sync", G["g_lg"][mc][:, 1024 * qh:1024 * (qh + 1)],
                               A[:, LT + 1024 * par:LT + 1024 * (par + 1)],
                               sem="alst" + str(par))
            pw.wait("sync", "mm")
            issue(mc + 2)
        pw.wait("sync", "alst0")
        pw.wait("sync", "alst1")


def ph_attn_a_attn(pw, T, W, G, dbg, whook=None):
    X, A, B = T["X"], T["ARN_A"], T["ARN_B"]
    psum = T["psum"]
    QS = 0
    KT2 = 4096
    VX2 = 4608
    LG2 = 5184
    TMPX = 6208
    EXPX = 7232
    ATT = 8448
    STG = 12544
    LNB = 13568

    ldw = pw.dma("sync", T["wbuf3"], W["aWproj"].rearrange("(f p) o -> p f o", p=128),
                 sem="ld")

    for span in range(4):
        pw.wait("sync", "mm")
        pw.wait("sync", "dve")
        lq = pw.dma("sync", A[:, QS:QS + 4096].rearrange("p (f t) -> p f t", t=512),
                    G["g_qT"][:, :, 512 * span:512 * (span + 1)], sem="aq")
        pw.wait("tensor", "aq", lq)
        for hg in range(4):
            mld = {}

            def issue(mc):
                if mc >= 32:
                    return
                par = mc % 2
                sm = "am" + str(par)
                slot, g = mc // 16, mc % 16
                sp, r = g // 4, g % 4
                pw.dma("sync", A[:, KT2 + 256 * par:KT2 + 256 * (par + 1)]
                       .rearrange("p (f t) -> p f t", t=128),
                       G[f"g_k_s{sp}"][slot].rearrange("(f p) t -> p f t", p=128)
                       [:, 2 * hg:2 * hg + 2, 128 * r:128 * (r + 1)], sem=sm)
                pw.dma("sync", A[:, VX2 + 260 * par:VX2 + 260 * (par + 1)],
                       G[f"g_v_s{sp}"][slot][:, 1040 * r + 260 * hg:
                                             1040 * r + 260 * (hg + 1)], sem=sm)
                mld[mc] = pw.dma("sync", A[:, LG2 + 512 * par:LG2 + 512 * (par + 1)],
                                 G["g_lg"][mc][:, 512 * span:512 * (span + 1)], sem=sm)
            pw.wait("sync", "mm")
            pw.wait("sync", "dve")
            issue(0)
            issue(1)
            pend = None
            ec_hist = {}
            for mc in range(32):
                if whook is not None and mc in (8, 16, 24):
                    whook()
                par = mc % 2
                pw.wait("tensor", "am" + str(par), mld[mc])
                kt = A[:, KT2 + 256 * par:KT2 + 256 * (par + 1)]
                vx = A[:, VX2 + 260 * par:VX2 + 260 * (par + 1)]
                lg = A[:, LG2 + 512 * par:LG2 + 512 * (par + 1)]
                for hh in range(4):
                    h = 4 * hg + hh
                    i = mc * 4 + hh
                    bk = 2560 if i % 2 == 0 else 3072
                    eb = i % 2
                    prev = pw._slot_state.get(bk)
                    if prev is not None:
                        pw.wait("tensor", prev[0], prev[1])
                    sm2 = pw.mm(psum[:, bk:bk + 512],
                                kt[64 * (hh % 2):64 * (hh % 2) + 64,
                                   128 * (hh // 2):128 * (hh // 2) + 128],
                                A[64 * (h % 2):64 * (h % 2) + 64,
                                  QS + 512 * (h // 2):QS + 512 * (h // 2) + 512],
                                start=True, stop=True, sem="mm")
                    if pend is not None:
                        phh, pmc, pec, peb, pvx = pend
                        pw.wait("tensor", "act", pec)
                        pw.mm(psum[0:65, 512 * phh:512 * phh + 512],
                              pvx[:, 65 * phh:65 * (phh + 1)],
                              A[:, EXPX + 512 * peb:EXPX + 512 * (peb + 1)],
                              start=(pmc == 0), stop=(pmc == 31),
                              sem="mm" if phh == 3 else None)
                        pend = None
                        if hh == 0:
                            pw.wait("sync", "mm")
                            issue(mc + 1)
                    pw.wait("vector", "mm", sm2)
                    if eb in ec_hist:
                        pw.wait("vector", "act", ec_hist[eb])
                    tv = pw.tt(A[:, TMPX + 512 * eb:TMPX + 512 * (eb + 1)],
                               psum[:, bk:bk + 512], lg, AluOpType.add, dep=False)
                    pw.wait("scalar", "dve", tv)
                    ec = pw.act(A[:, EXPX + 512 * eb:EXPX + 512 * (eb + 1)],
                                A[:, TMPX + 512 * eb:TMPX + 512 * (eb + 1)],
                                AF.Exp, dep=False)
                    ec_hist[eb] = ec
                    pw._slot_state[bk] = ("dve", tv)
                    pend = (hh, mc, ec, eb, vx)
            phh, pmc, pec, peb, pvx = pend
            pw.wait("tensor", "act", pec)
            pw.mm(psum[0:65, 512 * phh:512 * phh + 512],
                  pvx[:, 65 * phh:65 * (phh + 1)],
                  A[:, EXPX + 512 * peb:EXPX + 512 * (peb + 1)],
                  start=(pmc == 0), stop=(pmc == 31), sem="mm")
            pend = None
            pw.wait("vector", "mm")
            pw.wait("tensor", "dve")
            for hh in range(4):
                h = 4 * hg + hh
                rd = pw.recip(A[64:65, STG + 512:STG + 1024],
                              psum[64:65, 512 * hh:512 * hh + 512], dep=True)
                cv = pw.copy(A[0:64, STG:STG + 512],
                             psum[0:64, 512 * hh:512 * hh + 512], dep=True)
                pw.wait("tensor", "dve", cv)
                bcm = pw.mm(psum[:, 2560:2560 + 512], T["ones_row"][64:65, :],
                            A[64:65, STG + 512:STG + 1024], start=True, stop=True,
                            sem="mm")
                pw.wait("vector", "mm", bcm)
                if h % 2 == 0:
                    dv = pw.tt(A[0:64, ATT + 512 * (h // 2):ATT + 512 * (h // 2) + 512],
                               A[0:64, STG:STG + 512], psum[0:64, 2560:2560 + 512],
                               AluOpType.mult, dep=True)
                else:
                    sv = pw.tt(A[0:64, STG:STG + 512], A[0:64, STG:STG + 512],
                               psum[0:64, 2560:2560 + 512], AluOpType.mult, dep=True)
                    pw.wait("tensor", "dve", sv)
                    sh = pw.mm(psum[:, 3072:3072 + 512], T["shift64"][0:64, :],
                               A[0:64, STG:STG + 512], start=True, stop=True,
                               sem="mm")
                    pw.wait("vector", "mm", sh)
                    dv = pw.copy(A[64:128, ATT + 512 * (h // 2):ATT + 512 * (h // 2) + 512],
                                 psum[64:128, 3072:3072 + 512], dep=True)
                pw._slot_state[2560] = ("dve", dv)
                pw._slot_state[3072] = ("dve", dv)
            if whook is not None:
                whook()

        pw.wait("tensor", "dve")
        pw.wait("tensor", "ld", ldw)
        for of in range(FC):
            bk = 2560 if of % 2 == 0 else 3072
            prev = pw._slot_state.get(bk)
            if prev is not None:
                pw.wait("tensor", prev[0], prev[1])
            last = None
            for f in range(FC):
                last = pw.mm(psum[:, bk:bk + 512],
                             T["wbuf3"][:, f, 128 * of:128 * of + 128],
                             A[:, ATT + 512 * f:ATT + 512 * (f + 1)],
                             start=(f == 0), stop=(f == FC - 1),
                             sem="mm" if f == FC - 1 else None)
            pw.wait("scalar", "mm", last)
            oc = pw.act(A[:, STG + 512 * 0:STG + 512] if False else
                        B[:, 8192 + 512 * of:8192 + 512 * (of + 1)],
                        psum[:, bk:bk + 512], AF.Identity,
                        bias=T["vecs"][:, 0 + of:1 + of], dep=False)
            pw._slot_state[bk] = ("act", oc)
        pw.wait("tensor", "act")
        for tc in range(4):
            t_global = 4 * span + tc
            pw.wait("vector", "mm")
            emit_ln_tile(pw, X[:, t_global, :], A[:, LNB:LNB + 1024],
                         T["tmpA"][:, 0:1024], T["stat"], LN_EPS)
            for ob in range(0, FC, 4):
                bk = 2560 if (ob // 4) % 2 == 0 else 3072
                prev = pw._slot_state.get(bk)
                if prev is not None:
                    pw.wait("tensor", prev[0], prev[1])
                last = None
                for j in range(4):
                    last = pw.mm(psum[:, bk + 128 * j:bk + 128 * (j + 1)],
                                 B[:, 8192 + 512 * (ob + j) + 128 * tc:
                                   8192 + 512 * (ob + j) + 128 * (tc + 1)],
                                 T["ident"][:], start=(j == 0), stop=(j == 3),
                                 sem="mm" if j == 3 else None)
                pw.wait("vector", "mm", last)
                pw.tt(A[:, LNB + 128 * ob:LNB + 128 * (ob + 4)],
                      A[:, LNB + 128 * ob:LNB + 128 * (ob + 4)],
                      psum[:, bk:bk + 512], AluOpType.add, dep=True)
                dv = pw.tt(X[:, t_global, 128 * ob:128 * (ob + 4)],
                           X[:, t_global, 128 * ob:128 * (ob + 4)],
                           A[:, LNB + 128 * ob:LNB + 128 * (ob + 4)],
                           AluOpType.add, dep=True)
                pw._slot_state[bk] = ("dve", dv)

    if dbg and dbg.get("upto") == "attna":
        pw.barrier("dbgattna")
        pw.dma("sync", dbg["d_x1"], X[:, :, :], sem="st")
        return True
    return False


def row_from_cols(pw, G, cols_ap, row_ap, nprt, ncol, sem):
    n = pw.cnt.get("_rfc", 0)
    pw.cnt["_rfc"] = n + 1
    s = f"rfc{n}"
    if n > 0:
        pw.wait("sync", f"rfc{n - 1}")
    pw.dma("sync", G["g_fr"][0:nprt, 0:ncol], cols_ap, sem=s, slow=True)
    pw.wait("sync", s)
    pw.dma("sync", row_ap,
           G["g_fr"][0:nprt, 0:ncol].rearrange("i t -> t i"), sem=s, slow=True)
    pw.wait("tensor", s)


def ph_ffn_weights(pw, T, W, G):
    psum, psumb = T["psum"], T["psumb"]
    wbuf3 = T["wbuf3"]
    acc = T["stage"]

    for wi, (wname, nblk, gdst, mcol, scol) in enumerate(
            [("f_w1", 32, "g_t1T", 5, 7), ("f_w2", 8, "g_t2T", 6, 8)]):
        ncols = 1024 if wi == 0 else 4096
        per = 8 if wi == 0 else 2
        nld = nblk // per
        pw.red(acc[:, 64:65], T["fcols"][:, 32 * wi:32 * wi + 32], AluOpType.add)
        pw.wait("tensor", "dve")
        s = pw.mm(psum[0:1, 0:1], T["ones_col"][:], acc[:, 64:65], start=True,
                  stop=True, sem="mm")
        pw.wait("vector", "mm", s)
        pw.ts(acc[0:1, 65:66], psum[0:1, 0:1], 1.0 / (nblk * 128 * ncols),
              AluOpType.mult, 1e-5, AluOpType.max)
        v = pw.recip(acc[0:1, 66:67], acc[0:1, 65:66])
        pw.wait("tensor", "dve", v)
        s = pw.mm(psum[:, 4:5], T["ones_row"][0:1, :], acc[0:1, 65:66], start=True,
                  stop=True)
        s = pw.mm(psum[:, 5:6], T["ones_row"][0:1, :], acc[0:1, 66:67], start=True,
                  stop=True, sem="mm")
        pw.wait("vector", "mm", s)
        pw.copy(T["scal"][:, mcol:mcol + 1], psum[:, 4:5])
        pw.copy(T["scal"][:, scol:scol + 1], psum[:, 5:6])

        for g in range(nld):
            pw.wait("sync", "dve")
            l = pw.dma("sync", wbuf3 if wi == 0 else
                       wbuf3.rearrange("p a o -> p (a o)").rearrange(
                           "p (b c) -> p b c", b=2),
                       W[wname].rearrange("(a p) d -> p a d", p=128)
                       [:, per * g:per * (g + 1), :], sem="ld")
            pw.wait("vector", "ld", l)
            for a in range(per):
                blk = wbuf3[:, a, :] if wi == 0 else                     wbuf3.rearrange("p a o -> p (a o)")[:, 4096 * a:4096 * (a + 1)]
                j = g * per + a
                for c0 in range(0, ncols, 1024):
                    sl = blk[:, c0:c0 + 1024] if wi else blk
                    pw.ts(T["tmpA"][:, 0:1024], sl, T["scal"][:, scol:scol + 1],
                          AluOpType.mult)
                    pw.ts(T["stage2"][:, 0:1024], T["tmpA"][:, 0:1024], 0.5,
                          AluOpType.is_gt)
                    pw.ts(T["tmpA"][:, 0:1024], T["tmpA"][:, 0:1024], -0.5,
                          AluOpType.is_lt)
                    pw.tt(T["stage2"][:, 0:1024], T["stage2"][:, 0:1024],
                          T["tmpA"][:, 0:1024], AluOpType.subtract)
                    tv = pw.copy(T["tbf"][:, 0:1024], T["stage2"][:, 0:1024])
                    pw.wait("tensor", "dve", tv)
                    prev = pw._slot_state.get("psb")
                    if prev is not None:
                        pw.wait("tensor", prev[0], prev[1])
                    last = None
                    for dc in range(8):
                        last = pw.mmT(psumb[:, 128 * dc:128 * (dc + 1)],
                                      T["tbf"][:, 128 * dc:128 * (dc + 1)],
                                      T["identb"][:], start=(dc == 0), stop=(dc == 7),
                                      sem="mm" if dc == 7 else None)
                    pw.wait("vector", "mm", last)
                    if "wst" in pw.cnt:
                        pw.wait("vector", "wst")
                    dv = pw.copy(T["tbf"][:, 1024:2048]
                                 .rearrange("p (a b) -> p a b", a=8), psumb[:]
                                 .rearrange("p (a b) -> p a b", a=8), dep=False)
                    pw._slot_state["psb"] = ("dve", dv)
                    pw.wait("sync", "dve", dv)
                    if wi == 0:
                        dst = G["g_t1T"].rearrange("(dc p) j -> p dc j", p=128)[
                            :, :, 128 * j:128 * (j + 1)]
                    else:
                        s_ = c0 // 1024
                        j2 = 2 * g + a
                        dst = G["g_t2T"].rearrange("(hc p) j -> p hc j", p=128)[
                            :, 8 * s_:8 * (s_ + 1), 128 * j2:128 * (j2 + 1)]
                    pw.dma("sync", dst,
                           T["tbf"][:, 1024:2048].rearrange("p (a b) -> p a b", a=8),
                           sem="wst")
    pw.wait("sync", "wst")


def ph_ffn(pw, T, W, G, dbg, which):
    X, A, B = T["X"], T["ARN_A"], T["ARN_B"]
    psum, psumb = T["psum"], T["psumb"]
    groups = T["groups"]
    tag = f"f{which}"
    NTT = 0
    XL = 8192
    FB = 9216
    MAXA = 11264
    SB_ = 13312
    TMPC = 0
    hx = T["hx"]
    nbfT = T["nbfT"]
    tstat = T["tstat"]
    z16 = T["z16"]
    nbf = T["tbf"]

    pw.wait("sync", "dve")
    pw.dma("sync", G["g_halo_in"][0:1, :], X[0:1, 0, :], sem=tag + "hs")
    hd = pw.dma("sync", G["g_halo_in"][1:2, :], X[127:128, 15, :], sem=tag + "hs")
    pw.wait("gpsimd", tag + "hs", hd)
    agh = pw.coll("AllGather", AluOpType.bypass, groups, [G["g_halo_in"].opt()],
                  [G["g_halo"].opt()])

    tmp, stat = T["tmpA"], T["stat"]
    tr_done = None
    for t in range(NT):
        if tr_done is not None:
            pw.wait("vector", "mm", tr_done)
        emit_ln_tile(pw, X[:, t, :], A[:, XL:XL + 1024], tmp[:, 0:1024], stat, LN_EPS)
        pw.red(stat[:, 6:7], A[:, XL:XL + 1024], AluOpType.max, absval=True)
        pw.ts(stat[:, 6:7], stat[:, 6:7], 1e-5, AluOpType.max)
        pw.recip(stat[:, 7:8], stat[:, 6:7])
        pw.ts(stat[:, 7:8], stat[:, 7:8], 127.0, AluOpType.mult)
        pw.ts(T["fcols"][:, t:t + 1], stat[:, 6:7], T["scal"][:, 5:6],
              AluOpType.mult, 1.0 / 127.0, AluOpType.mult)
        pw.ts(z16[:], A[:, XL:XL + 1024], stat[:, 7:8], AluOpType.mult, 1536.0,
              AluOpType.add)
        qv = pw.ts(nbf[:, 0:1024], z16[:], -1536.0, AluOpType.add)
        pw.wait("tensor", "dve", qv)
        prev = pw._slot_state.get("psb")
        if prev is not None:
            pw.wait("tensor", prev[0], prev[1])
        last = None
        for dc in range(8):
            last = pw.mmT(psumb[:, 128 * dc:128 * (dc + 1)],
                          nbf[:, 128 * dc:128 * (dc + 1)], T["identb"][:],
                          start=(dc == 0), stop=(dc == 7),
                          sem="mm" if dc == 7 else None)
        pw.wait("vector", "mm", last)
        dv = pw.copy(nbfT[:, :, 128 * t:128 * (t + 1)],
                     psumb[:].rearrange("p (a b) -> p a b", a=8), dep=False)
        pw._slot_state["psb"] = ("dve", dv)
        tr_done = last

    fv = pw.cnt["dve"]
    pw.wait("sync", "dve", fv)
    row_from_cols(pw, G, T["fcols"][:, 0:16], T["frow"][0:1, 0:2048], 128, 16,
                  tag + "h")
    s = pw.mm(psum[:, 0:512], T["ones_row"][0:1, :], T["frow"][0:1, 0:512], start=True,
              stop=True)
    s = pw.mm(psum[:, 512:1024], T["ones_row"][0:1, :], T["frow"][0:1, 512:1024],
              start=True, stop=True)
    s = pw.mm(psum[:, 1024:1536], T["ones_row"][0:1, :], T["frow"][0:1, 1024:1536],
              start=True, stop=True)
    s = pw.mm(psum[:, 1536:2048], T["ones_row"][0:1, :], T["frow"][0:1, 1536:2048],
              start=True, stop=True, sem="mm")
    pw.wait("vector", "mm", s)
    pw.copy(A[:, FB:FB + 2048], psum[:, 0:2048], dep=False)

    pw.wait("sync", "gp", agh)
    hl = pw.dma("sync", T["halo2"][0:1, :], G["g_halo"][0, 1, :][None, :],
                sem=tag + "hl")
    hl = pw.dma("sync", T["halo2"][1:2, :], G["g_halo"][1, 0, :][None, :],
                sem=tag + "hl")
    pw.wait("vector", tag + "hl", hl)
    emit_ln_tile(pw, T["halo2"][0:2, :], T["halo2b"][0:2, :], tmp[0:2, 0:1024],
                 stat[0:2, :], LN_EPS)
    pw.red(stat[0:2, 6:7], T["halo2b"][0:2, :], AluOpType.max, absval=True)
    pw.ts(stat[0:2, 6:7], stat[0:2, 6:7], 1e-5, AluOpType.max)
    pw.recip(stat[0:2, 7:8], stat[0:2, 6:7])
    pw.ts(stat[0:2, 7:8], stat[0:2, 7:8], 127.0, AluOpType.mult)
    pw.ts(T["fcols"][0:2, 16:17], stat[0:2, 6:7], T["scal"][0:2, 5:6],
          AluOpType.mult, 1.0 / 127.0, AluOpType.mult)
    pw.ts(T["z16h"][0:2, :], T["halo2b"][0:2, :], stat[0:2, 7:8], AluOpType.mult,
          1536.0, AluOpType.add)
    qv = pw.ts(nbf[0:2, 0:1024], T["z16h"][0:2, :], -1536.0, AluOpType.add)
    pw.wait("tensor", "dve", qv)
    prev = pw._slot_state.get("psb")
    if prev is not None:
        pw.wait("tensor", prev[0], prev[1])
    last = None
    for dc in range(8):
        last = pw.mmT(psumb[:, 16 * dc:16 * dc + 2],
                      nbf[0:2, 128 * dc:128 * (dc + 1)], T["identb"][0:2, 0:2],
                      start=(dc == 0), stop=(dc == 7), sem="mm" if dc == 7 else None)
    pw.wait("vector", "mm", last)
    dv = pw.copy(T["nhT"][:, :, :], psumb[:, 0:128].rearrange("p (a b) -> p a b", b=16)
                 [:, :, 0:2], dep=False)
    pw._slot_state["psb"] = ("dve", dv)
    fh = pw.cnt["dve"]
    pw.wait("sync", "dve", fh)
    row_from_cols(pw, G, T["fcols"][0:2, 16:17], T["frow"][0:1, 2048:2050], 2, 1,
                  tag + "h")
    s = pw.mm(psum[:, 512 * 5:512 * 5 + 2], T["ones_row"][0:1, :],
              T["frow"][0:1, 2048:2050], start=True, stop=True, sem="mm")
    pw.wait("vector", "mm", s)
    pw.copy(T["haloh"][:, 0:2], psum[:, 512 * 5:512 * 5 + 2], dep=False)
    pw.ts(T["haloh"][:, 0:1], T["haloh"][:, 0:1], T["scal"][:, 2:3],
          AluOpType.mult)
    pw.ts(T["haloh"][:, 1:2], T["haloh"][:, 1:2], T["scal"][:, 3:4],
          AluOpType.mult)

    pw.memset(A[:, MAXA:MAXA + 2048], 0.0)
    pw.wait("vector", "gp")
    pw.wait("tensor", "dve")
    dv0 = pw.cnt["dve"]
    for b_ in range(5):
        pw._slot_state[("fx", b_)] = ("dve", dv0)
    pw._slot_state["hb6"] = ("dve", dv0)
    hdma = {}
    av_hist = {}
    tstat4 = T["tstat4"]
    mmh1 = {-4: pw.cnt["mm"], -3: pw.cnt["mm"],
            -2: pw.cnt["mm"], -1: pw.cnt["mm"]}
    for jc in range(NJC):
        pw.wait("sync", "mm", mmh1[jc - 4])
        l = pw.dma("sync", tstat4[:, jc % 4, :, :],
                   G["g_t1T"].rearrange("(dc p) j -> p dc j", p=128)
                   [:, :, 128 * jc:128 * (jc + 1)], sem=tag + "w" + str(jc % 4))
        pw.wait("tensor", tag + "w" + str(jc % 4), l)
        strip = hx[:, jc % 2, :]
        cbuf = (T["stage2"][:, 0:2048] if jc % 2 == 0
                else T["stage"][:, 0:2048])
        for sp in range(4):
            bk = (jc * 4 + sp) % 5
            prev = pw._slot_state.get(("fx", bk))
            if prev is not None:
                pw.wait("tensor", prev[0], prev[1])
            last = None
            for dc in range(8):
                last = pw.mm(psum[:, 512 * bk:512 * bk + 512],
                             tstat4[:, jc % 4, dc, :],
                             nbfT[:, dc, 512 * sp:512 * (sp + 1)],
                             start=(dc == 0), stop=(dc == 7),
                             sem="mm" if dc == 7 else None)
            pw.wait("vector", "mm", last)
            if sp == 0:
                if hdma.get(jc % 2) is not None:
                    pw.wait("vector", tag + "hd" + str(jc % 2), hdma[jc % 2])
                if av_hist.get(jc % 2) is not None:
                    pw.wait("vector", "act", av_hist[jc % 2])
            c = pw.tt(strip[:, 1 + 512 * sp:1 + 512 * (sp + 1)],
                      psum[:, 512 * bk:512 * bk + 512],
                      A[:, FB + 512 * sp:FB + 512 * (sp + 1)], AluOpType.mult,
                      dep=False)
            pw._slot_state[("fx", bk)] = ("dve", c)
        prev = pw._slot_state.get("hb6")
        if prev is not None:
            pw.wait("tensor", prev[0], prev[1])
        lasth = None
        for dc in range(8):
            lasth = pw.mm(psum[:, 512 * 6:512 * 6 + 2],
                          tstat4[:, jc % 4, dc, :], T["nhT"][:, dc, :],
                          start=(dc == 0), stop=(dc == 7),
                          sem="mm" if dc == 7 else None)
        mmh1[jc] = pw.cnt["mm"]
        pw.wait("vector", "mm", lasth)
        pw.tt(strip[:, 0:1], psum[:, 512 * 6:512 * 6 + 1], T["haloh"][:, 0:1],
              AluOpType.mult, dep=True)
        hc2 = pw.tt(strip[:, 2049:2050], psum[:, 512 * 6 + 1:512 * 6 + 2],
                    T["haloh"][:, 1:2], AluOpType.mult, dep=True)
        pw._slot_state["hb6"] = ("dve", hc2)
        pw.ts(cbuf, strip[:, 0:2048], T["convw"][:, jc, 0:1], AluOpType.mult)
        pw.stt(cbuf, strip[:, 1:2049], T["convw"][:, jc, 1:2],
               cbuf, AluOpType.mult, AluOpType.add)
        cv = pw.stt(cbuf, strip[:, 2:2050], T["convw"][:, jc, 2:3],
                    cbuf, AluOpType.mult, AluOpType.add)
        pw.wait("scalar", "dve", cv)
        av = pw.act(strip[:, 1:2049],
                    cbuf, AF.Silu, bias=T["convb"][:, jc:jc + 1],
                    dep=False)
        av_hist[jc % 2] = av
        pw.wait("vector", "act", av)
        pw.stt(A[:, MAXA:MAXA + 2048], strip[:, 1:2049], 1.0,
               A[:, MAXA:MAXA + 2048], AluOpType.mult, AluOpType.max)
        pw.stt(A[:, MAXA:MAXA + 2048], strip[:, 1:2049], -1.0,
               A[:, MAXA:MAXA + 2048], AluOpType.mult, AluOpType.max)
        pw.wait("gpsimd", "act", av)
        hdma[jc % 2] = pw.dma("gpsimd", G["g_hact"][128 * jc:128 * (jc + 1), :],
                              strip[:, 1:2049], sem=tag + "hd" + str(jc % 2))

    pw.wait("tensor", "dve")
    for tt_ in range(NT):
        prev = pw._slot_state.get(("fx", tt_ % 2))
        if prev is not None:
            pw.wait("tensor", prev[0], prev[1])
        s = pw.mm(psum[:, 512 * (tt_ % 2):512 * (tt_ % 2) + 128],
                  A[:, MAXA + 128 * tt_:MAXA + 128 * (tt_ + 1)], T["ident"][:],
                  start=True, stop=True, sem="mm")
        pw.wait("vector", "mm", s)
        rv = pw.red(T["fcols"][:, 32 + tt_:33 + tt_],
                    psum[:, 512 * (tt_ % 2):512 * (tt_ % 2) + 128],
                    AluOpType.max, dep=False)
        pw._slot_state[("fx", tt_ % 2)] = ("dve", rv)
    pw.ts(T["fcols"][:, 32:48], T["fcols"][:, 32:48], 1e-5, AluOpType.max)
    pw.ts(T["fcols"][:, 48:64], T["fcols"][:, 32:48], T["scal"][:, 6:7],
          AluOpType.mult, 1.0 / 127.0, AluOpType.mult)
    rv = pw.recip(T["fcols"][:, 32:48], T["fcols"][:, 32:48])
    sv = pw.ts(T["fcols"][:, 32:48], T["fcols"][:, 32:48], 127.0, AluOpType.mult)
    pw.wait("sync", "dve", sv)
    row_from_cols(pw, G, T["fcols"][:, 32:48], T["frow"][0:1, 0:2048], 128, 16,
                  tag + "h")
    row_from_cols(pw, G, T["fcols"][:, 48:64], T["f2row"][0:1, 0:2048], 128, 16,
                  tag + "h")
    pw.wait("tensor", "dve")
    for half in range(4):
        s = pw.mm(psum[:, 512 * half:512 * (half + 1)], T["ones_row"][0:1, :],
                  T["frow"][0:1, 512 * half:512 * (half + 1)], start=True, stop=True,
                  sem="mm" if half == 3 else None)
    pw.wait("vector", "mm", s)
    pw.copy(A[:, SB_:SB_ + 2048], psum[:, 0:2048], dep=False)
    pw.wait("tensor", "dve")
    for half in range(4):
        s = pw.mm(psum[:, 512 * half:512 * (half + 1)], T["ones_row"][0:1, :],
                  T["f2row"][0:1, 512 * half:512 * (half + 1)], start=True, stop=True,
                  sem="mm" if half == 3 else None)
    pw.wait("vector", "mm", s)
    f2v = pw.copy(A[:, FB:FB + 2048], psum[:, 0:2048], dep=False)

    pw.wait("sync", "mm", pw.cnt["mm"])
    lt2 = pw.dma("sync", T["t2s"][:, :, :],
                 G["g_t2T"].rearrange("(hc p) j -> p hc j", p=128)
                 [:, :, 0:512], sem=tag + "t2")

    pw.wait("sync", "dve", f2v)
    pw.wait("sync", tag + "hd0")
    pw.wait("sync", tag + "hd1")
    pw.wait("vector", "act")
    pw.wait("tensor", "dve", f2v)
    pw.wait("tensor", tag + "t2", lt2)
    ndma = {}
    qh_tt = {}
    nv_hist = {}
    fmm = {-3: pw.cnt["mm"], -2: pw.cnt["mm"], -1: pw.cnt["mm"]}

    def fused_mms(jc):
        nh_b = T["nh2"] if jc % 2 == 0 else T["tbf"][:, 0:2048]
        pw.wait("tensor", "dve", nv_hist[jc])
        for oc in range(4):
            pw.mm(psum[:, 512 * oc:512 * oc + 512],
                  T["t2s"][:, jc, 128 * oc:128 * (oc + 1)],
                  nh_b[:, 0:512],
                  start=(jc == 0), stop=(jc == NJC - 1),
                  sem="mm" if oc == 3 else None)
        fmm[jc] = pw.cnt["mm"]

    for jc in range(NJC):
        if qh_tt.get(jc % 2) is not None:
            pw.wait("sync", "dve", qh_tt[jc % 2])
        l = pw.dma("sync", hx[:, jc % 2, 0:2048],
                   G["g_hact"][128 * jc:128 * (jc + 1), :], sem=tag + "q" + str(jc % 2))
        pw.wait("vector", tag + "q" + str(jc % 2), l)
        if ndma.get(jc % 2) is not None:
            pw.wait("vector", tag + "nd" + str(jc % 2), ndma[jc % 2])
        nh_buf = T["nh2"] if jc % 2 == 0 else T["tbf"][:, 0:2048]
        pw.wait("vector", "mm", fmm[jc - 2])
        qh_tt[jc % 2] = pw.tt(T["stage2"][:, 0:2048], hx[:, jc % 2, 0:2048],
                              A[:, SB_:SB_ + 2048], AluOpType.mult, dep=False)
        pw.ts(T["z16w"][:, 0:2048], T["stage2"][:, 0:2048], 1536.0, AluOpType.add)
        nv = pw.ts(nh_buf, T["z16w"][:, 0:2048], -1536.0, AluOpType.add)
        nv_hist[jc] = nv
        pw.wait("gpsimd", "dve", nv)
        ndma[jc % 2] = pw.dma("gpsimd", G["g_nh"][128 * jc:128 * (jc + 1), :],
                              nh_buf, sem=tag + "nd" + str(jc % 2))
        if jc >= 1:
            fused_mms(jc - 1)
    fused_mms(NJC - 1)

    pw.wait("vector", "mm", fmm[NJC - 1])
    for oc in range(4):
        pw.tt(T["ofm2"][:, oc, :], psum[:, 512 * oc:512 * oc + 512],
              A[:, FB:FB + 512], AluOpType.mult, dep=False)
    pw.wait("tensor", "dve")
    for tt_ in range(4):
        bk = 4 + (tt_ % 2)
        prev = pw._slot_state.get(("fx2", bk))
        if prev is not None:
            pw.wait("tensor", prev[0], prev[1])
        last = None
        for oc in range(4):
            last = pw.mm(psum[:, 512 * bk + 128 * oc:512 * bk + 128 * (oc + 1)],
                         T["ofm2"][:, oc, 128 * tt_:128 * (tt_ + 1)],
                         T["ident"][:], start=(oc == 0), stop=(oc == 3),
                         sem="mm" if oc == 3 else None)
        pw.wait("vector", "mm", last)
        xv = pw.tt(X[:, tt_, 0:512], X[:, tt_, 0:512],
                   psum[:, 512 * bk:512 * bk + 512], AluOpType.add, dep=False)
        pw._slot_state[("fx2", bk)] = ("dve", xv)

    pw.wait("sync", tag + "nd0")
    pw.wait("sync", tag + "nd1")
    pw.wait("sync", "mm")
    pw.wait("sync", "dve", nv_hist[NJC - 1])
    pw.wait("tensor", "dve")
    for oh in range(2):
        if oh == 1:
            pw.wait("sync", "mm")
            l2 = pw.dma("sync", T["t2s"][:, :, :],
                        G["g_t2T"].rearrange("(hc p) j -> p hc j", p=128)
                        [:, :, 512 * oh:512 * (oh + 1)], sem=tag + "t2")
            pw.wait("tensor", tag + "t2", l2)
        for sp in range(4):
            if oh == 0 and sp == 0:
                continue
            mmh2 = {-4: pw.cnt["mm"], -3: pw.cnt["mm"],
                    -2: pw.cnt["mm"], -1: pw.cnt["mm"]}
            for jc in range(NJC):
                pw.wait("sync", "mm", mmh2[jc - 4])
                sm2 = tag + "x" + str(jc % 4)
                l1 = pw.dma("sync", T["nstat4"][:, jc % 4, :],
                            G["g_nh"][128 * jc:128 * (jc + 1),
                                      512 * sp:512 * (sp + 1)], sem=sm2)
                pw.wait("tensor", sm2, l1)
                for oc in range(4):
                    pw.mm(psum[:, 512 * oc:512 * oc + 512],
                          T["t2s"][:, jc, 128 * oc:128 * (oc + 1)],
                          T["nstat4"][:, jc % 4, :],
                          start=(jc == 0), stop=(jc == NJC - 1),
                          sem="mm" if oc == 3 else None)
                mmh2[jc] = pw.cnt["mm"]
            pw.wait("vector", "mm")
            for oc in range(4):
                ev = pw.tt(T["ofm2"][:, oc, :],
                           psum[:, 512 * oc:512 * oc + 512],
                           A[:, FB + 512 * sp:FB + 512 * (sp + 1)],
                           AluOpType.mult, dep=False)
            pw.wait("tensor", "dve")
            for tt_ in range(4):
                t_global = 4 * sp + tt_
                bk = 4 + (tt_ % 2)
                prev = pw._slot_state.get(("fx2", bk))
                if prev is not None:
                    pw.wait("tensor", prev[0], prev[1])
                last = None
                for oc in range(4):
                    last = pw.mm(psum[:, 512 * bk + 128 * oc:512 * bk + 128 * (oc + 1)],
                                 T["ofm2"][:, oc, 128 * tt_:128 * (tt_ + 1)],
                                 T["ident"][:], start=(oc == 0), stop=(oc == 3),
                                 sem="mm" if oc == 3 else None)
                pw.wait("vector", "mm", last)
                xv = pw.tt(X[:, t_global, 512 * oh:512 * (oh + 1)],
                           X[:, t_global, 512 * oh:512 * (oh + 1)],
                           psum[:, 512 * bk:512 * bk + 512], AluOpType.add,
                           dep=False)
                pw._slot_state[("fx2", bk)] = ("dve", xv)

    if dbg and dbg.get("upto") == "ffn" + str(which):
        pw.barrier("dbgffn")
        pw.dma("sync", dbg["d_x1"], X[:, :, :], sem="st")
        return True
    return False


def ph_attn_b(pw, T, W, G, dbg):
    X, A, B = T["X"], T["ARN_A"], T["ARN_B"]
    psum = T["psum"]
    groups = T["groups"]
    BW = T["BWK"]
    VXB = 0
    KVS = 1040
    STS = 1172
    DEN = 1304
    ASB = 1336
    QP0, QP1 = 0, 4096
    CATQ, CATK = 8192, 10240
    KPF = 12288
    KPT = 8192
    GBC = 1024

    tmp, stat = T["tmpA"], T["stat"]

    pw.wait("vector", "mm")
    xln = B
    for t in range(NT):
        emit_ln_tile(pw, X[:, t, :], xln[:, 1024 * t:1024 * (t + 1)], tmp[:, 0:1024],
                     stat, LN_EPS)
    pw.wait("tensor", "dve")

    def dst(t, f):
        return A[:, 2048 * f + 128 * t:2048 * f + 128 * (t + 1)]
    emit_transpose_tm_to_fm(pw, lambda t: xln[:, 1024 * t:1024 * (t + 1)],
                            dst, NT, FC, T, 0, 512)
    pw.barrier("battnb")

    def load_W(name):
        return pw.dma("sync", T["wbuf3"], W[name].rearrange("(f p) o -> p f o", p=128),
                      sem="ld")

    stage = T["stage"]

    def mk_evac(kind, dsem):
        def evac(of, sp, ps, mmcnt, need_guard):
            it = sp * FC + of
            region = stage[:, 512 * (it % 4):512 * (it % 4) + 512]
            pw.wait("scalar", "mm", mmcnt)
            if need_guard:
                pw.wait("scalar", dsem)
            if kind == "q":
                c = pw.act(region, ps, AF.Identity, scale=0.125,
                           bias=T["vecs"][:, 96 + of:97 + of])
            else:
                c = pw.act(region, ps, AF.Identity, scale=0.125,
                           bias=T["vecs"][:, 104 + of:105 + of])
            return ("act", c, region)
        return evac

    ld = load_W("b_qw")
    emit_proj_fm(pw, T, ld, mk_evac("q", "bqs"),
                 lambda of, sp: G["g_bqT"][:, of, 512 * sp:512 * (sp + 1)], "bqs")
    ld = load_W("b_kw")
    emit_proj_fm(pw, T, ld, mk_evac("k", "bks"),
                 lambda of, sp: G["g_bkT"][:, of, 512 * sp:512 * (sp + 1)], "bks",
                 prev_sems=("bqs",))

    row_from_cols(pw, G, T["vecs"][:, 32:40], T["frow"][0:1, 0:1024], 128, 8, "bvr")
    pw.wait("tensor", "dve")
    for half in range(2):
        s = pw.mm(psum[:, 512 * half:512 * (half + 1)], T["ones_row"][0:1, :],
                  T["frow"][0:1, 512 * half:512 * (half + 1)], start=True, stop=True,
                  sem="mm" if half == 1 else None)
    pw.wait("vector", "mm", s)
    pw.copy(T["stage2"][:, 0:1024], psum[:, 0:1024], dep=False)

    ld = load_W("b_vw")
    pw.wait("gpsimd", "dve")
    mz = pw.memset(T["vstage"], 1.0)
    pw.wait("vector", "gp", mz)
    pw.wait("sync", "gp", mz)
    pw.wait("tensor", "ld", ld)
    pw.wait("vector", "bks")
    pw.wait("tensor", "dve")
    vdma = {}
    for mc in range(NT):
        for vh in range(2):
            bk = (mc * 2 + vh) % 4
            prev = pw._slot_state.get(512 * bk)
            if prev is not None:
                pw.wait("tensor", prev[0], prev[1])
            last = None
            for f in range(FC):
                last = pw.mm(psum[:, 512 * bk:512 * bk + 512],
                             A[:, 2048 * f + 128 * mc:2048 * f + 128 * (mc + 1)],
                             T["wbuf3"][:, f, 512 * vh:512 * (vh + 1)],
                             start=(f == 0), stop=(f == FC - 1),
                             sem="mm" if f == FC - 1 else None)
            pw.wait("vector", "mm", last)
            if vdma.get(mc % 2) is not None:
                pw.wait("vector", "bvs")
            dstv = T["vstage"][:, 1040 * (mc % 2) + 520 * vh:1040 * (mc % 2) + 520 * (vh + 1)] \
                .rearrange("p (h c) -> p h c", c=65)[:, :, 0:64]
            c = pw.stt(dstv, psum[:, 512 * bk:512 * bk + 512]
                       .rearrange("p (h c) -> p h c", c=64), 1.0,
                       T["stage2"][:, 512 * vh:512 * (vh + 1)]
                       .rearrange("p (h c) -> p h c", c=64), AluOpType.mult,
                       AluOpType.add, dep=False)
            pw._slot_state[512 * bk] = ("dve", c)
        pw.wait("sync", "dve")
        vdma[mc % 2] = pw.dma("sync", G["g_bv"][:, 1040 * mc:1040 * (mc + 1)],
                              T["vstage"][:, 1040 * (mc % 2):1040 * (mc % 2) + 1040],
                              sem="bvs")
    pw.wait("sync", "bvs")

    if dbg and dbg.get("upto") == "bproj":
        pw.barrier("dbgbp")
        pw.dma("sync", dbg["d_qT"].rearrange("(f p) t -> p f t", p=128), G["g_bqT"],
               sem="st")
        pw.dma("sync", dbg["d_kT"].rearrange("(f p) t -> p f t", p=128), G["g_bkT"],
               sem="st")
        pw.dma("sync", dbg["d_vext"], G["g_bv"], sem="st")
        pw.wait("sync", "st")
        return True

    ag_h = {}
    ld_hist = {}

    def load_feats(h):
        sm = "bh" + str(h % 2)
        for (buf, g) in ((CATQ, G["g_bqT"]), (CATK, G["g_bkT"])):
            pw.dma("sync", A[0:64, buf:buf + 2048],
                   g[64 * (h % 2):64 * (h % 2) + 64, h // 2, :], sem=sm)
            pw.dma("sync", A[64:128, buf:buf + 2048],
                   g[64 * (h % 2):64 * (h % 2) + 64, h // 2, :], sem=sm)
        lv = pw.dma("sync", BW[:, VXB:VXB + 1040].rearrange("p (tc c) -> p tc c", c=65),
                    G["g_bv"].rearrange("p (tc c) -> p tc c", c=1040)
                    [:, :, 65 * h:65 * (h + 1)], sem=sm)
        pw.wait("scalar", sm, lv)
        pw.act(A[64:128, CATQ:CATQ + 2048], A[64:128, CATQ:CATQ + 2048],
               AF.Square, dep=False)
        a2 = pw.act(A[64:128, CATK:CATK + 2048], A[64:128, CATK:CATK + 2048],
                    AF.Square, dep=False)
        ld_hist[h] = a2

    def features(h):
        qpb = QP0 if h % 2 == 0 else QP1
        pw.wait("tensor", "act", ld_hist[h])
        for (src2, dstc) in ((CATQ, qpb), (CATK, KPF)):
            for mc in range(2):
                for sp in range(4):
                    bk = 3 + (mc * 4 + sp) % 4
                    prev = pw._slot_state.get(512 * bk)
                    if prev is not None:
                        pw.wait("tensor", prev[0], prev[1])
                    mmv = pw.mm(psum[:, 512 * bk:512 * bk + 512],
                                T["projx2"][:, 128 * mc:128 * (mc + 1)],
                                A[:, src2 + 512 * sp:src2 + 512 * (sp + 1)],
                                start=True, stop=True, sem="mm")
                    pw.wait("scalar", "mm", mmv)
                    dst_ap = A[:, dstc + 2048 * mc + 512 * sp:
                               dstc + 2048 * mc + 512 * (sp + 1)]
                    ec = pw.act(dst_ap, psum[:, 512 * bk:512 * bk + 512], AF.Exp,
                                bias=-LN16, dep=False)
                    pw._slot_state[512 * bk] = ("act", ec)
        for tc in range(NT):
            bk = 3 + tc % 4
            prev = pw._slot_state.get(512 * bk)
            if prev is not None:
                pw.wait("tensor", prev[0], prev[1])
            mmv = pw.mm(psum[:, 512 * bk:512 * bk + 256],
                        A[:, CATK + 128 * tc:CATK + 128 * (tc + 1)],
                        T["projx2"][:, 0:256], start=True, stop=True, sem="mm")
            pw.wait("scalar", "mm", mmv)
            ec = pw.act(A[:, KPT + 256 * tc:KPT + 256 * (tc + 1)],
                        psum[:, 512 * bk:512 * bk + 256], AF.Exp, bias=-LN16,
                        dep=False)
            pw._slot_state[512 * bk] = ("act", ec)

    def scan(h):
        qpb = QP0 if h % 2 == 0 else QP1
        pw.wait("gpsimd", "act")
        pw.wait("gpsimd", "mm")
        for p_ in ("bst0", "bst1"):
            if p_ in pw.cnt:
                pw.wait("gpsimd", p_)
        z = pw.memset(BW[:, KVS:KVS + 130], 0.0)
        pw.wait("vector", "gp", z)
        pw.wait("tensor", "gp", z)
        pw.wait("scalar", "act")
        am_prev = None
        kv_last = None
        for c in range(NT):
            if am_prev is not None:
                pw.wait("tensor", "dve", am_prev)
            last = None
            for mc in range(2):
                last = pw.mm(psum[:, 512:512 + 128],
                             A[:, KPF + 2048 * mc + 128 * c:KPF + 2048 * mc + 128 * (c + 1)],
                             A[:, qpb + 2048 * mc + 128 * c:qpb + 2048 * mc + 128 * (c + 1)],
                             start=(mc == 0), stop=(mc == 1),
                             sem="mm" if mc == 1 else None)
            if c > 0:
                pw.wait("vector", "mm", kv_last)
                pw.copy(BW[:, KVS:KVS + 130], psum[:, 0:130], dep=False)
            pw.wait("vector", "mm", last)
            am = pw.tt(BW[:, ASB:ASB + 128], psum[:, 512:512 + 128], T["mask_ut"][:],
                       AluOpType.mult, dep=False)
            am_prev = am
            pw.wait("tensor", "dve", am)
            for mc in range(2):
                pw.mm(psum[:, 1024:1024 + 65],
                      A[:, qpb + 2048 * mc + 128 * c:qpb + 2048 * mc + 128 * (c + 1)],
                      BW[:, KVS + 65 * mc:KVS + 65 * (mc + 1)],
                      start=(mc == 0), stop=False)
            nm = pw.mm(psum[:, 1024:1024 + 65], BW[:, ASB:ASB + 128],
                       BW[:, VXB + 65 * c:VXB + 65 * (c + 1)],
                       start=False, stop=True, sem="mm")
            for mc in range(2):
                pw.mm(psum[:, 65 * mc:65 * (mc + 1)],
                      A[:, KPT + 256 * c + 128 * mc:KPT + 256 * c + 128 * (mc + 1)],
                      BW[:, VXB + 65 * c:VXB + 65 * (c + 1)],
                      start=(c == 0 and mc == 0), stop=(mc == 1),
                      sem="mm" if mc == 1 else None, skip=True)
            kv_last = pw.cnt["mm"]
            pw.wait("vector", "mm", nm)
            pw.copy(B[:, 1024 * c + 64 * h:1024 * c + 64 * (h + 1)],
                    psum[:, 1024:1024 + 64], dep=False)
            pw.copy(BW[:, DEN + 16 * (h % 2) + c:DEN + 16 * (h % 2) + c + 1],
                    psum[:, 1024 + 64:1024 + 65], dep=False)
        pw.wait("vector", "mm", kv_last)
        sv = pw.copy(BW[:, KVS:KVS + 130], psum[:, 0:130], dep=False)
        pw.wait("sync", "dve", sv)
        sd = pw.dma("sync", G["g_st_in"][h].rearrange("(mc p) c -> p mc c", p=128),
                    BW[:, KVS:KVS + 130].rearrange("p (mc c) -> p mc c", c=65),
                    sem="bst" + str(h % 2))
        pw.wait("gpsimd", "bst" + str(h % 2), sd)
        ag_h[h] = pw.coll("AllGather", AluOpType.bypass, groups,
                          [G["g_st_in"][h].opt()], [G["g_st"][h].opt()])
        if h + 1 < H:
            pw.wait("sync", "mm", kv_last)
            load_feats(h + 1)

    def finalize(h):
        qpb = QP0 if h % 2 == 0 else QP1
        pw.wait("sync", "gp", ag_h[h])
        pw.wait("sync", "mm")
        lv = pw.dma("sync", BW[:, STS:STS + 130].rearrange("p (mc c) -> p mc c", c=65),
                    G["g_st"][h][0].rearrange("(mc p) c -> p mc c", p=128),
                    sem="bfs" + str(h % 2))
        pw.wait("vector", "bfs" + str(h % 2), lv)
        mk = pw.ts(BW[:, STS:STS + 130], BW[:, STS:STS + 130], T["scal"][:, 1:2],
                   AluOpType.mult, dep=True)
        pw.wait("tensor", "dve", mk)
        for c in range(NT):
            bk = 2 + c % 2
            prev = pw._slot_state.get(512 * bk)
            if prev is not None:
                pw.wait("tensor", prev[0], prev[1])
            last = None
            for mc in range(2):
                last = pw.mm(psum[:, 512 * bk:512 * bk + 65],
                             A[:, qpb + 2048 * mc + 128 * c:qpb + 2048 * mc + 128 * (c + 1)],
                             BW[:, STS + 65 * mc:STS + 65 * (mc + 1)],
                             start=(mc == 0), stop=(mc == 1),
                             sem="mm" if mc == 1 else None)
            pw.wait("vector", "mm", last)
            pw.tt(B[:, 1024 * c + 64 * h:1024 * c + 64 * (h + 1)],
                  B[:, 1024 * c + 64 * h:1024 * c + 64 * (h + 1)],
                  psum[:, 512 * bk:512 * bk + 64], AluOpType.add, dep=False)
            pw.tt(stat[:, 6:7], BW[:, DEN + 16 * (h % 2) + c:DEN + 16 * (h % 2) + c + 1],
                  psum[:, 512 * bk + 64:512 * bk + 65], AluOpType.add)
            rv = pw.recip(stat[:, 7:8], stat[:, 6:7])
            dv = pw.ts(B[:, 1024 * c + 64 * h:1024 * c + 64 * (h + 1)],
                       B[:, 1024 * c + 64 * h:1024 * c + 64 * (h + 1)],
                       stat[:, 7:8], AluOpType.mult)
            pw._slot_state[512 * bk] = ("dve", dv)

    pw.wait("sync", "mm")
    pw.wait("sync", "dve")
    load_feats(0)
    for h in range(H):
        features(h)
        scan(h)
        if h >= 1:
            finalize(h - 1)
    finalize(H - 1)

    if dbg and dbg.get("upto") == "bheads":
        pw.barrier("dbgbh")
        pw.dma("sync", dbg["d_x1"], B[:, 0:16384].rearrange("p (t f) -> p t f", f=1024),
               sem="st")
        pw.dma("sync", dbg["d_den"], BW[:, DEN:DEN + 32], sem="st")
        pw.wait("sync", "st")
        return True

    pw.wait("tensor", "dve")

    def dsta(t, f):
        return A[:, 2048 * f + 128 * t:2048 * f + 128 * (t + 1)]
    emit_transpose_tm_to_fm(pw, lambda t: B[:, 1024 * t:1024 * (t + 1)],
                            dsta, NT, FC, T, 512 * 5, 512 * 6)
    pw.wait("sync", "mm")
    pw.wait("sync", "dve")
    ldw = load_W("b_ow")
    GBC2 = 8192
    RES = 10240
    row_from_cols(pw, G, T["vecs"][:, 48:56], T["frow"][0:1, 0:1024], 128, 8, "bgr")
    row_from_cols(pw, G, T["vecs"][:, 56:64], T["frow"][0:1, 1024:2048], 128, 8, "bbr")
    pw.wait("tensor", "dve")
    for half in range(4):
        s = pw.mm(psum[:, 512 * half:512 * (half + 1)], T["ones_row"][0:1, :],
                  T["frow"][0:1, 512 * half:512 * (half + 1)], start=True, stop=True,
                  sem="mm" if half == 3 else None)
    pw.wait("vector", "mm", s)
    gv = pw.copy(B[:, GBC2:GBC2 + 2048], psum[:, 0:2048], dep=False)
    pw.wait("tensor", "dve", gv)
    pw.wait("tensor", "ld", ldw)

    for t in range(NT):
        emit_ln_tile(pw, X[:, t, :], B[:, RES:RES + 1024], tmp[:, 0:1024], stat,
                     LN_EPS)
        for of in range(FC):
            bk = of % 2
            prev = pw._slot_state.get(("eo", bk))
            if prev is not None:
                pw.wait("tensor", prev[0], prev[1])
            last = None
            for f in range(FC):
                last = pw.mm(psum[:, 512 * bk:512 * bk + 128],
                             T["wbuf3"][:, f, 128 * of:128 * of + 128],
                             A[:, 2048 * f + 128 * t:2048 * f + 128 * (t + 1)],
                             start=(f == 0), stop=(f == FC - 1),
                             sem="mm" if f == FC - 1 else None)
            pw.wait("vector", "mm", last)
            ec2 = pw.copy(B[:, 11264 + 128 * bk:11264 + 128 * (bk + 1)],
                          psum[:, 512 * bk:512 * bk + 128], dep=False)
            pw.wait("tensor", "dve", ec2)
            tb = pw.mm(psum[:, 512 * (2 + bk):512 * (2 + bk) + 128],
                       B[:, 11264 + 128 * bk:11264 + 128 * (bk + 1)], T["ident"][:],
                       start=True, stop=True, sem="mm")
            pw.wait("vector", "mm", tb)
            ev = pw.stt(B[:, RES + 128 * of:RES + 128 * (of + 1)],
                        psum[:, 512 * (2 + bk):512 * (2 + bk) + 128],
                        T["vecs"][:, 40 + of:41 + of],
                        B[:, RES + 128 * of:RES + 128 * (of + 1)],
                        AluOpType.add, AluOpType.add, dep=True)
            pw._slot_state[("eo", bk)] = ("dve", ev)
        pw.red(stat[:, 0:1], B[:, RES:RES + 1024], AluOpType.add)
        pw.ts(stat[:, 1:2], stat[:, 0:1], 1.0 / D, AluOpType.mult)
        v0 = pw.ts(tmp[:, 0:1024], B[:, RES:RES + 1024], stat[:, 1:2],
                   AluOpType.subtract)
        pw.wait("scalar", "dve", v0)
        aq = pw.act(B[:, RES:RES + 1024], tmp[:, 0:1024], AF.Square, dep=False)
        pw.wait("vector", "act", aq)
        pw.red(stat[:, 2:3], B[:, RES:RES + 1024], AluOpType.add)
        v1 = pw.ts(stat[:, 3:4], stat[:, 2:3], 1.0 / D, AluOpType.mult,
                   LNB_EPS, AluOpType.add)
        pw.wait("scalar", "dve", v1)
        a1 = pw.act(stat[:, 4:5], stat[:, 3:4], AF.Sqrt, dep=False)
        pw.wait("vector", "act", a1)
        pw.recip(stat[:, 5:6], stat[:, 4:5])
        pw.ts(tmp[:, 0:1024], tmp[:, 0:1024], stat[:, 5:6], AluOpType.mult)
        pw.tt(tmp[:, 0:1024], tmp[:, 0:1024], B[:, GBC2:GBC2 + 1024],
              AluOpType.mult)
        pw.tt(tmp[:, 0:1024], tmp[:, 0:1024], B[:, GBC2 + 1024:GBC2 + 2048],
              AluOpType.add)
        pw.tt(X[:, t, :], X[:, t, :], tmp[:, 0:1024], AluOpType.add)

    if dbg and dbg.get("upto") == "attnb":
        pw.barrier("dbgattnb")
        pw.dma("sync", dbg["d_x1"], X[:, :, :], sem="st")
        return True
    return False


def ph_store(pw, T, y_out):
    pw.barrier("final")
    v = pw.dma("sync", y_out.rearrange("(t p) f -> p t f", p=128), T["X"][:, :, :],
               sem="yst")
    pw.wait("sync", "yst", v)



def build_nc(debug_upto=None):
    nc = bass.Bass("TRN2", target_bir_lowering=False)
    x_in = nc.dram_tensor("x0", [TOK, D], F32, kind="ExternalInput").ap()
    W = {}
    for n, s in [("aWq", [D, D]), ("aWk", [D, D]), ("aWv", [D, D]),
                 ("aWproj", [D, D]), ("aWpre", [D, D]),
                 ("b_qw", [D, D]), ("b_kw", [D, D]), ("b_vw", [D, D]),
                 ("b_ow", [D, D]), ("f_w1", [DHID, D]), ("f_w2", [D, DHID])]:
        W[n] = nc.dram_tensor(n, s, F32, kind="ExternalInput").ap()
    VEC = {}
    for n, s in [("a_bproj", [D]), ("a_bpre", [D]), ("b_qb", [D]), ("b_kb", [D]),
                 ("b_vb", [D]), ("b_ob", [D]), ("b_gamma", [D]), ("b_beta", [D]),
                 ("f_convb", [DHID]),
                 ("pi_bc", [128, 1]), ("isodd_bc", [128, 1]),
                 ("mleft_bc", [128, 1]), ("mright_bc", [128, 1])]:
        VEC[n] = nc.dram_tensor(n, s, F32, kind="ExternalInput").ap()
    b_proj = nc.dram_tensor("b_proj", [M, DH], F32, kind="ExternalInput").ap()
    f_convw = nc.dram_tensor("f_convw", [DHID, 3], F32, kind="ExternalInput").ap()
    y_out = nc.dram_tensor("y", [TOK, D], F32, kind="ExternalOutput").ap()

    G = {}
    span_tensors = []
    for sp in range(4):
        span_tensors += [(f"g_pre_in_s{sp}", [D, 512]), (f"g_pre_s{sp}", [2, D, 512]),
                         (f"g_k_in_s{sp}", [D, 512]), (f"g_k_s{sp}", [2, D, 512]),
                         (f"g_v_in_s{sp}", [128, 4 * 1040]),
                         (f"g_v_s{sp}", [2, 128, 4 * 1040])]
    for n, s in span_tensors + [
                 ("g_qT", [128, FC, TOK]), ("g_pqT", [128, FC, TOK]),
                 ("g_halo_in", [2, D]), ("g_halo", [2, 2, D]),
                 ("g_bqT", [128, FC, TOK]), ("g_bkT", [128, FC, TOK]),
                 ("g_bv", [128, NT * 1040]),
                 ("g_st_in", [H, 256, 65]), ("g_st", [H, 2, 256, 65])]:
        G[n] = nc.dram_tensor(n, s, F32).ap()
    G["g_t1T"] = nc.dram_tensor("g_t1T", [D, DHID], BF16).ap()
    G["g_t2T"] = nc.dram_tensor("g_t2T", [DHID, D], BF16).ap()
    G["g_hact"] = nc.dram_tensor("g_hact", [DHID, TOK], F32).ap()
    G["g_lg"] = nc.dram_tensor("g_lg", [32, 128, TOK], F32).ap()
    G["g_fr"] = nc.dram_tensor("g_fr", [128, 64], F32).ap()
    G["g_nh"] = nc.dram_tensor("g_nh", [DHID, TOK], BF16).ap()

    dbg = None
    if debug_upto is not None:
        dbg = {"upto": debug_upto}
        for n, s in [("d_xln", [128, NT, D]), ("d_qT", [D, TOK]), ("d_kT", [D, TOK]),
                     ("d_preT", [D, TOK]), ("d_pqT", [D, TOK]),
                     ("d_vext", [128, NT * 1040]), ("d_gpre", [2 * D, TOK]),
                     ("d_x1", [128, NT, D]), ("d_attn", [8, 128, 2 * D]),
                     ("d_logit", [8, 128, 256]), ("d_den", [128, 32])]:
            dbg[n] = nc.dram_tensor(n, s, F32, kind="ExternalOutput").ap()

    with contextlib.ExitStack() as stack:
        block = stack.enter_context(nc.Block())
        pw = PW(nc, stack)
        pw._slot_state = {}

        T = {}
        T["groups"] = [[0, 1], [2, 3], [4, 5], [6, 7]]
        T["X"] = stack.enter_context(nc.sbuf_tensor("X", [128, NT, D], F32))
        T["ARN_A"] = stack.enter_context(nc.sbuf_tensor("ARN_A", [128, 16384], F32))
        T["ARN_B"] = stack.enter_context(nc.sbuf_tensor("ARN_B", [128, 16384], F32))
        T["tmpA"] = stack.enter_context(nc.sbuf_tensor("tmpA", [128, 1024], F32))
        T["stat"] = stack.enter_context(nc.sbuf_tensor("stat", [128, 8], F32))
        T["fcols"] = stack.enter_context(nc.sbuf_tensor("fcols", [128, 64], F32))
        T["ones_col"] = stack.enter_context(nc.sbuf_tensor("ones_col", [128, 1], F32))
        T["BWK"] = stack.enter_context(nc.sbuf_tensor("BWK", [128, 1472], F32))
        T["ones_row"] = stack.enter_context(nc.sbuf_tensor("ones_row", [128, 128], F32))
        T["shift64"] = stack.enter_context(nc.sbuf_tensor("shift64", [128, 128], F32))
        arnb = T["ARN_B"]
        T["wbuf3"] = arnb[:, 0:8192].rearrange("p (f o) -> p f o", o=1024)
        T["stage"] = arnb[:, 8192:10240]
        T["stage2"] = arnb[:, 10240:12288]
        T["vstage"] = arnb[:, 12288:14368]
        T["hx"] = arnb[:, 0:4100].rearrange("p (a b) -> p a b", a=2)
        T["tstat"] = arnb[:, 4104:5128].bitcast(BF16).rearrange(
            "p (a b c) -> p a b c", a=2, b=8)
        T["tstat4"] = arnb[:, 4104:6152].bitcast(BF16).rearrange(
            "p (a b c) -> p a b c", a=4, b=8)
        T["nstat"] = arnb[:, 5128:5640].bitcast(BF16).rearrange(
            "p (a b) -> p a b", a=2)
        T["nstat4"] = arnb[:, 6152:7176].bitcast(BF16).rearrange(
            "p (a b) -> p a b", a=4)
        T["nh2"] = arnb[:, 5640:6664].bitcast(BF16)
        T["z16w"] = arnb[:, 6664:7688].bitcast(F16)
        T["haloh"] = arnb[:, 7688:7752]
        T["tbf"] = arnb[:, 12288:13312].bitcast(BF16)
        T["frow"] = arnb[0:1, 13312:15372]
        T["halo2"] = T["stage"][0:2, 0:1024]
        T["halo2b"] = T["stage"][0:2, 1024:2048]
        arna = T["ARN_A"]
        T["nbfT"] = arna[:, 0:8192].bitcast(BF16).rearrange(
            "p (a b) -> p a b", a=8)
        T["ofm"] = arna[:, 0:2048].rearrange("p (a b) -> p a b", a=4)
        T["z16"] = arna[:, 15360:15872].bitcast(F16)
        T["nhT"] = arna[:, 15872:15880].bitcast(BF16).rearrange(
            "p (a b) -> p a b", a=8)
        T["t2s"] = arna[:, 0:8192].bitcast(BF16).rearrange(
            "p (a b) -> p a b", a=32)
        T["ofm2"] = arna[:, 11264:13312].rearrange("p (a b) -> p a b", a=4)
        T["z16h"] = T["z16"][0:2, :]
        T["f2row"] = arna[0:1, 2048:4096]
        T["ident"] = stack.enter_context(nc.sbuf_tensor("ident", [128, 128], F32))
        T["identb"] = stack.enter_context(nc.sbuf_tensor("identb", [128, 128], BF16))
        T["mask_ut"] = stack.enter_context(nc.sbuf_tensor("mask_ut", [128, 128], F32))
        T["consts"] = stack.enter_context(nc.sbuf_tensor("consts", [128, 8], F32))
        T["vecs"] = stack.enter_context(nc.sbuf_tensor("vecs", [128, 112], F32))
        T["scal"] = stack.enter_context(nc.sbuf_tensor("scal", [128, 16], F32))
        T["convw"] = stack.enter_context(nc.sbuf_tensor("convw", [128, NJC, 3], F32))
        T["convb"] = stack.enter_context(nc.sbuf_tensor("convb", [128, NJC], F32))
        T["projx2"] = stack.enter_context(nc.sbuf_tensor("projx2", [128, M], F32))
        T["psum"] = stack.enter_context(nc.psum_tensor("ps", [128, 3584], F32))
        T["psumb"] = stack.enter_context(nc.psum_tensor("psb", [128, 1024], BF16))

        T["cvals"] = [0.0, 1.0, 0.125, -LN16, 1536.0, -1536.0, 0.5, -128.0]
        for i, v in enumerate(T["cvals"]):
            nc.const_aps.register(F32, float(v), T["consts"][:, i:i + 1])

        ph_setup(pw, T, x_in, VEC, b_proj, f_convw)
        stop = ph_attn_a_proj(pw, T, W, G, dbg)
        if not stop:
            ph_attn_a_logits(pw, T, G, dbg)
            arnb_ = T["ARN_B"]
            wmbuf = [arnb_[:, 12288:14336].rearrange("p (a b) -> p a b", a=2),
                     arnb_[:, 14336:16384].rearrange("p (a b) -> p a b", a=2)]

            def wm_src(L):
                if L < 16:
                    return W["f_w1"].rearrange("(a p) d -> p a d", p=128)[
                        :, 2 * L:2 * L + 2, :]
                Lp = L - 16
                return W["f_w2"].rearrange("(a p) d -> p a d", p=128)[
                    :, Lp // 2:Lp // 2 + 1,
                    2048 * (Lp % 2):2048 * (Lp % 2) + 2048].rearrange(
                    "p a (b c) -> p (a b) c", c=1024)

            wm_state = {"k": 0}
            wm_pml = {}
            wm_red = {}

            def wm_load(L):
                if L >= 32:
                    return
                if L >= 2:
                    pw.wait("gpsimd", "dve", wm_red[2 * L - 3])
                wm_pml[L] = pw.dma("gpsimd", wmbuf[L % 2], wm_src(L),
                                   sem="pm" + str(L % 2))

            pw.wait("gpsimd", "qst")
            wm_load(0)
            wm_load(1)

            def whook():
                k = wm_state["k"]
                if k >= 64:
                    return
                wm_state["k"] = k + 1
                L = k // 2
                pw.wait("vector", "pm" + str(L % 2), wm_pml[L])
                wm_red[k] = pw.red(T["fcols"][:, k:k + 1],
                                   wmbuf[L % 2][:, k % 2, :],
                                   AluOpType.add, absval=True, dep=False)
                if k % 2 == 1:
                    wm_load(L + 2)

            stop = ph_attn_a_attn(pw, T, W, G, dbg, whook)
        if not stop:
            pw.barrier("b1")
            ph_ffn_weights(pw, T, W, G)
            pw.barrier("b2")
            stop = ph_ffn(pw, T, W, G, dbg, 1)
        if not stop:
            pw.barrier("b3")
            stop = ph_attn_b(pw, T, W, G, dbg)
        if not stop:
            pw.barrier("b4")
            stop = ph_ffn(pw, T, W, G, dbg, 2)
        if not stop:
            ph_store(pw, T, y_out)

        pw.wait("sync", "st") if "st" in pw.cnt else None
        pw.replay(block)
    return nc


def _shard_inputs(inputs):
    x = np.ascontiguousarray(np.asarray(inputs["L0_attn_a_x"], dtype=np.float32))
    maps = []
    for c in range(8):
        b, h = c // 2, c % 2
        m = {"x0": np.ascontiguousarray(x[b, 2048 * h:2048 * (h + 1), :])}
        m["aWq"] = inputs["L0_attn_a_Wq"]
        m["aWk"] = inputs["L0_attn_a_Wk"]
        m["aWv"] = inputs["L0_attn_a_Wv"]
        m["aWproj"] = inputs["L0_attn_a_Wproj"]
        m["aWpre"] = inputs["L0_attn_a_Wpre"]
        m["b_qw"] = inputs["L0_attn_b_q_w"]
        m["b_kw"] = inputs["L0_attn_b_k_w"]
        m["b_vw"] = inputs["L0_attn_b_v_w"]
        m["b_ow"] = inputs["L0_attn_b_out_w"]
        m["f_w1"] = inputs["L0_ffn_w1"]
        m["f_w2"] = inputs["L0_ffn_w2"]
        m["a_bproj"] = inputs["L0_attn_a_bproj"]
        m["a_bpre"] = inputs["L0_attn_a_bpre"]
        m["b_qb"] = inputs["L0_attn_b_q_b"]
        m["b_kb"] = inputs["L0_attn_b_k_b"]
        m["b_vb"] = inputs["L0_attn_b_v_b"]
        m["b_ob"] = inputs["L0_attn_b_out_b"]
        m["b_gamma"] = inputs["L0_attn_b_gamma"]
        m["b_beta"] = inputs["L0_attn_b_beta"]
        m["f_convb"] = inputs["L0_ffn_conv_b"]
        m["b_proj"] = inputs["L0_attn_b_proj"]
        m["f_convw"] = np.ascontiguousarray(
            np.asarray(inputs["L0_ffn_conv_w"], dtype=np.float32)[:, 0, :])
        m["pi_bc"] = np.ascontiguousarray(
            np.broadcast_to(np.asarray(inputs["L0_attn_a_pi"], np.float32), (128, 1)))
        m["isodd_bc"] = np.full((128, 1), float(h), np.float32)
        m["mleft_bc"] = np.full((128, 1), float(h), np.float32)
        m["mright_bc"] = np.full((128, 1), float(1 - h), np.float32)
        m = {k: np.ascontiguousarray(np.asarray(v, dtype=np.float32)) if k not in ()
             else v for k, v in m.items()}
        maps.append(m)
    return maps


def kernel(**inputs):
    nc = build_nc()
    maps = _shard_inputs(inputs)
    res = run_bass_kernel_spmd(nc, maps, core_ids=list(range(8)))
    out = np.empty((4, 4096, 1024), np.float32)
    for c in range(8):
        b, h = c // 2, c % 2
        out[b, 2048 * h:2048 * (h + 1), :] = res.results[c]["y"]
    return out


# version 10
# speedup vs baseline: 1.1795x; 1.1795x over previous
import contextlib
import numpy as np
import concourse.bass as bass
import concourse.mybir as mybir
from concourse.bass_utils import run_bass_kernel_spmd
from concourse.alu_op_type import AluOpType

F32 = mybir.dt.float32
F16 = mybir.dt.float16
BF16 = mybir.dt.bfloat16
AF = mybir.ActivationFunctionType
AX = mybir.AxisListType

TOK = 2048
NT = 16
D = 1024
FC = 8
H = 16
DH = 64
M = 256
DHID = 4096
NJC = 32
LN_EPS = 1e-5
LNB_EPS = 1e-6
LN16 = float(np.log(np.float64(16.0)))

ENGS = ("sync", "tensor", "vector", "scalar", "gpsimd")


class PW:

    def __init__(self, nc, stack):
        self.nc = nc
        self.stack = stack
        self.ops = {e: [] for e in ENGS}
        self.cnt = {}
        self.sems = {}

    def sem(self, name):
        if name not in self.sems:
            self.sems[name] = self.stack.enter_context(self.nc.semaphore(name))
            self.cnt[name] = 0
        return self.sems[name]

    def emit(self, eng, fn):
        self.ops[eng].append(fn)

    def wait(self, eng, name, val=None):
        s, v = self.sem(name), (self.cnt[name] if val is None else val)
        self.emit(eng, lambda e: e.wait_ge(s, v))

    def dma(self, eng, out, in_, sem=None, slow=False):
        kw = {"allow_slow_non_contiguous": True} if slow else {}
        if sem is None:
            self.emit(eng, lambda e: e.dma_start(out=out, in_=in_, **kw))
            return None
        s = self.sem(sem)
        self.cnt[sem] += 16
        self.emit(eng, lambda e: e.dma_start(out=out, in_=in_, **kw).then_inc(s, 16))
        return self.cnt[sem]

    AUTO = {"vector": "dve", "scalar": "act", "gpsimd": "gp"}

    def _op(self, eng, fn, sem=None, n=1, dep=True):
        auto = self.AUTO.get(eng)
        if auto is not None:
            s = self.sem(auto)
            if dep and self.cnt[auto] > 0:
                v = self.cnt[auto]
                self.emit(eng, lambda e: e.wait_ge(s, v))
            self.cnt[auto] += 1
            self.emit(eng, lambda e: fn(e).then_inc(s, 1))
            return self.cnt[auto]
        if sem is None:
            self.emit(eng, fn)
            return None
        s = self.sem(sem)
        self.cnt[sem] += n
        self.emit(eng, lambda e: fn(e).then_inc(s, n))
        return self.cnt[sem]

    def mm(self, out, lhsT, rhs, start, stop, sem=None, skip=False):
        return self._op("tensor",
                        lambda e: e.matmul(out, lhsT, rhs, start=start, stop=stop,
                                           skip_group_check=skip),
                        sem)

    def trans(self, out, in_, ident, sem=None):
        return self._op("tensor", lambda e: e.transpose(out, in_, ident), sem)

    def mmT(self, out, data, ident, start, stop, sem=None):
        return self._op("tensor",
                        lambda e: e.matmul(out, data, ident, start=start, stop=stop,
                                           is_transpose=True), sem)

    def act(self, out, in_, func, bias=None, scale=None, accum_out=None, sem=None,
            dep=True):
        kw = {}
        if bias is not None:
            kw["bias"] = bias
        if scale is not None:
            kw["scale"] = scale
        if accum_out is not None:
            kw["accum_out"] = accum_out
        return self._op("scalar",
                        lambda e: e.activation(out=out, in_=in_, func=func, **kw),
                        sem, dep=dep)

    def ts(self, out, in0, s1, op0, s2=None, op1=None, sem=None, eng="vector",
           dep=True):
        return self._op(eng,
                        lambda e: e.tensor_scalar(out=out, in0=in0, scalar1=s1,
                                                  scalar2=s2, op0=op0,
                                                  **({"op1": op1} if op1 is not None else {})),
                        sem, dep=dep)

    def tt(self, out, in0, in1, op, sem=None, eng="vector", dep=True):
        return self._op(eng,
                        lambda e: e.tensor_tensor(out=out, in0=in0, in1=in1, op=op),
                        sem, dep=dep)

    def stt(self, out, in0, scalar, in1, op0, op1, sem=None, dep=True):
        return self._op("vector",
                        lambda e: e.scalar_tensor_tensor(out=out, in0=in0, scalar=scalar,
                                                         in1=in1, op0=op0, op1=op1),
                        sem, dep=dep)

    def copy(self, out, in_, sem=None, eng="vector", dep=True):
        return self._op(eng, lambda e: e.tensor_copy(out, in_), sem, dep=dep)

    def red(self, out, in_, op, absval=None, sem=None, negate=None, dep=True):
        return self._op("vector",
                        lambda e: e.tensor_reduce(out=out, in_=in_, axis=AX.X, op=op,
                                                  apply_absolute_value=absval,
                                                  negate=negate), sem, dep=dep)

    def recip(self, out, in_, sem=None, dep=True):
        return self._op("vector", lambda e: e.reciprocal(out=out, in_=in_), sem,
                        dep=dep)

    def memset(self, ap, c, sem=None, dep=True):
        return self._op("gpsimd", lambda e: e.memset(ap, c), sem, dep=dep)

    def coll(self, kind, op, groups, ins, outs, sem=None, dep=True):
        return self._op("gpsimd",
                        lambda e: e.collective_compute(kind, op, replica_groups=groups,
                                                       ins=ins, outs=outs), sem,
                        dep=dep)

    def barrier(self, tag):
        vals = dict(self.cnt)
        for eng in ENGS:
            for name, v in vals.items():
                if v > 0 and not name.startswith("_"):
                    self.wait(eng, name, v)

    def replay(self, block):
        for eng in ENGS:
            ops = self.ops[eng]
            dec = getattr(block, eng)

            def make(fn_list):
                def body(e):
                    for f in fn_list:
                        f(e)
                return body
            dec(make(ops))



def ph_setup(pw, T, x_in, VEC, b_proj, f_convw):
    nc = pw.nc
    pw.dma("sync", T["X"][:, :, :], x_in.rearrange("(t p) f -> p t f", p=128), sem="ld")
    for col, name in enumerate(["a_bproj", "a_bpre", "b_qb", "b_kb", "b_vb",
                                "b_ob", "b_gamma", "b_beta"]):
        pw.dma("sync", T["vecs"][:, 8 * col:8 * col + 8],
               VEC[name].rearrange("(f p) -> p f", p=128), sem="ld", slow=True)
    pw.dma("sync", T["scal"][:, 0:1], VEC["pi_bc"], sem="ld")
    pw.dma("sync", T["scal"][:, 1:2], VEC["isodd_bc"], sem="ld")
    pw.dma("sync", T["scal"][:, 2:3], VEC["mleft_bc"], sem="ld")
    pw.dma("sync", T["scal"][:, 3:4], VEC["mright_bc"], sem="ld")
    pw.dma("sync", T["convw"][:, :, :], f_convw.rearrange("(c p) k -> p c k", p=128), sem="ld", slow=True)
    pw.dma("sync", T["convb"][:], VEC["f_convb"].rearrange("(c p) -> p c", p=128), sem="ld", slow=True)
    pw.dma("sync", T["projx2"][0:64, :], b_proj.rearrange("m d -> d m"), sem="ld", slow=True)

    cvals = T["cvals"]
    for i, v in enumerate(cvals):
        pw.memset(T["consts"][:, i:i + 1], float(v))
    pw.memset(T["ident"][:], 1.0)
    pw._op("gpsimd", lambda e: e.affine_select(
        out=T["ident"][:], in_=T["ident"][:], compare_op=AluOpType.is_equal,
        fill=0.0, base=0, pattern=[[-1, 128]], channel_multiplier=1))
    pw.memset(T["identb"][:], 1.0)
    pw._op("gpsimd", lambda e: e.affine_select(
        out=T["identb"][:], in_=T["identb"][:], compare_op=AluOpType.is_equal,
        fill=0.0, base=0, pattern=[[-1, 128]], channel_multiplier=1))
    pw.memset(T["mask_ut"][:], 1.0)
    pw._op("gpsimd", lambda e: e.affine_select(
        out=T["mask_ut"][:], in_=T["mask_ut"][:], compare_op=AluOpType.is_ge,
        fill=0.0, base=0, pattern=[[1, 128]], channel_multiplier=-1))
    pw.memset(T["ones_col"][:], 1.0)
    pw.memset(T["ones_row"][:], 1.0)
    pw.memset(T["shift64"][:], 1.0)
    pw._op("gpsimd", lambda e: e.affine_select(
        out=T["shift64"][:], in_=T["shift64"][:], compare_op=AluOpType.is_equal,
        fill=0.0, base=-64, pattern=[[1, 128]], channel_multiplier=-1))
    pw.memset(T["projx2"][64:128, :], -0.5, sem="gp")
    pw.wait("vector", "ld")
    pw.wait("vector", "gp")
    pw.ts(T["scal"][:, 4:5], T["scal"][:, 0:1], 1.0 / 32.0, AluOpType.mult)
    pw.ts(T["vecs"][:, 96:104], T["vecs"][:, 16:24], 0.125, AluOpType.mult)
    pw.ts(T["vecs"][:, 104:112], T["vecs"][:, 24:32], 0.125, AluOpType.mult,
          sem="dve")


def emit_ln_tile(pw, x_tile, out_tile, tmp, stat, eps):
    pw.red(stat[:, 0:1], x_tile, AluOpType.add)
    pw.ts(stat[:, 1:2], stat[:, 0:1], 1.0 / D, AluOpType.mult)
    pw.ts(tmp, x_tile, stat[:, 1:2], AluOpType.subtract)
    pw.tt(out_tile, tmp, tmp, AluOpType.mult)
    pw.red(stat[:, 0:1], out_tile, AluOpType.add)
    v1 = pw.ts(stat[:, 2:3], stat[:, 0:1], 1.0 / D, AluOpType.mult, eps,
               AluOpType.add, sem="dve")
    pw.wait("scalar", "dve", v1)
    a1 = pw.act(stat[:, 2:3], stat[:, 2:3], AF.Sqrt, sem="act")
    pw.wait("vector", "act", a1)
    pw.recip(stat[:, 3:4], stat[:, 2:3])
    return pw.ts(out_tile, tmp, stat[:, 3:4], AluOpType.mult, sem="dve")


def emit_transpose_tm_to_fm(pw, src_tile_fn, dst_fn, nt_tiles, nfc, T, bankA, bankB,
                            dtype_ident="ident"):
    ident = T[dtype_ident][:]
    psum = T["psum"]
    slots = [bankA, bankB]
    pending = []
    mmc0 = pw.cnt.get("mm", 0)
    for t in range(nt_tiles):
        src = src_tile_fn(t)
        for fb in range(0, nfc, 4):
            nf = min(4, nfc - fb)
            slot = slots[(t * ((nfc + 3) // 4) + fb // 4) % 2]
            key = slot
            prev = pw._slot_state.get(key)
            if prev is not None:
                pw.wait("tensor", prev[0], prev[1])
            last = None
            for j in range(nf):
                last = pw.mm(psum[:, slot + 128 * j:slot + 128 * (j + 1)],
                             src[:, 128 * (fb + j):128 * (fb + j + 1)], ident,
                             start=(j == 0), stop=(j == nf - 1),
                             sem="mm" if j == nf - 1 else None)
            pw.wait("vector", "mm", last)
            for j in range(nf):
                s = pw.copy(dst_fn(t, fb + j),
                            psum[:, slot + 128 * j:slot + 128 * (j + 1)],
                            sem="dve" if j == nf - 1 else None)
            pw._slot_state[key] = ("dve", s)


def emit_proj_fm(pw, T, ld_done, evac, dma_dst, dma_sem, span_hook=None,
                 rhs_buf=None, prev_sems=()):
    psum = T["psum"]
    xlnT = T["ARN_A"] if rhs_buf is None else rhs_buf
    pw.wait("tensor", "ld", ld_done)
    for s_ in prev_sems:
        if s_ in pw.cnt:
            pw.wait("scalar", s_)
            pw.wait("vector", s_)
    dma_hist = {}
    for sp in range(4):
        for of in range(FC):
            it = sp * FC + of
            bk = it % 6
            prev = pw._slot_state.get(("pj", bk))
            if prev is not None:
                pw.wait("tensor", prev[0], prev[1])
            last = None
            for f in range(FC):
                last = pw.mm(psum[:, 512 * bk:512 * bk + 512],
                             T["wbuf3"][:, f, 128 * of:128 * of + 128],
                             xlnT[:, 2048 * f + 512 * sp:2048 * f + 512 * (sp + 1)],
                             start=(f == 0), stop=(f == FC - 1),
                             sem="mm" if f == FC - 1 else None)
            eng, cnt, region = evac(of, sp, psum[:, 512 * bk:512 * bk + 512],
                                    last, it >= 4)
            pw._slot_state[("pj", bk)] = (eng, cnt)
            pw.wait("sync", eng, cnt)
            dma_hist[it % 4] = pw.dma("sync", dma_dst(of, sp), region, sem=dma_sem)
        if span_hook is not None:
            span_hook(sp)


def ph_attn_a_proj(pw, T, W, G, dbg):
    X, ARN_A, ARN_B = T["X"], T["ARN_A"], T["ARN_B"]
    psum, stage = T["psum"], T["stage"]
    groups = T["groups"]

    xln = ARN_B
    tmp, stat = T["tmpA"], T["stat"]
    pw.wait("vector", "ld")
    for t in range(NT):
        emit_ln_tile(pw, X[:, t, :], xln[:, 1024 * t:1024 * (t + 1)], tmp[:, 0:1024],
                     stat, LN_EPS)
    pw.wait("tensor", "dve")
    pw.wait("tensor", "gp")
    xlnT = ARN_A

    def dst(t, f):
        return xlnT[:, 2048 * f + 128 * t:2048 * f + 128 * (t + 1)]
    emit_transpose_tm_to_fm(pw, lambda t: xln[:, 1024 * t:1024 * (t + 1)],
                            dst, NT, FC, T, 0, 512)
    if dbg and dbg.get("upto") == "ln1":
        pw.barrier("dbgln")
        pw.dma("sync", dbg["d_xln"], xln.rearrange("p (t f) -> p t f", f=1024),
               sem="st")
        return True
    pw.barrier("projstart")

    def load_W(name):
        return pw.dma("sync", T["wbuf3"], W[name].rearrange("(f p) o -> p f o", p=128),
                      sem="ld")

    ag_done = {}

    def mk_evac(kind, dsem):
        def evac(of, sp, ps, mmcnt, need_guard):
            it = sp * FC + of
            region = stage[:, 512 * (it % 4):512 * (it % 4) + 512]
            pw.wait("scalar", "mm", mmcnt)
            if need_guard:
                pw.wait("scalar", dsem)
            if kind == "silu_bpre":
                c = pw.act(region, ps, AF.Silu, bias=T["vecs"][:, 8 + of:9 + of])
            elif kind == "copy":
                c = pw.act(region, ps, AF.Copy)
            elif kind == "q8":
                c = pw.act(region, ps, AF.Copy, scale=0.125)
            return ("act", c, region)
        return evac

    ld = load_W("aWpre")

    def hook_pre(sp):
        pw.wait("gpsimd", "pst")
        ag_done[("pre", sp)] = pw.coll(
            "AllGather", AluOpType.bypass, groups,
            [G[f"g_pre_in_s{sp}"].opt()], [G[f"g_pre_s{sp}"].opt()])
    emit_proj_fm(pw, T, ld, mk_evac("silu_bpre", "pst"),
                 lambda of, sp: G[f"g_pre_in_s{sp}"][128 * of:128 * (of + 1), :],
                 "pst", hook_pre)

    ld = load_W("aWk")

    def hook_k(sp):
        pw.wait("gpsimd", "kst")
        ag_done[("k", sp)] = pw.coll(
            "AllGather", AluOpType.bypass, groups,
            [G[f"g_k_in_s{sp}"].opt()], [G[f"g_k_s{sp}"].opt()])
    emit_proj_fm(pw, T, ld, mk_evac("copy", "kst"),
                 lambda of, sp: G[f"g_k_in_s{sp}"][128 * of:128 * (of + 1), :],
                 "kst", hook_k, prev_sems=("pst",))

    ld = load_W("aWv")
    pw.memset(T["vstage"], 1.0)
    pw.wait("vector", "gp")
    pw.wait("tensor", "ld", ld)
    vdma = {}
    for mc in range(NT):
        for vh in range(2):
            bk = (mc * 2 + vh) % 6
            prev = pw._slot_state.get(("pj", bk))
            if prev is not None:
                pw.wait("tensor", prev[0], prev[1])
            last = None
            for f in range(FC):
                last = pw.mm(psum[:, 512 * bk:512 * bk + 512],
                             T["ARN_A"][:, 2048 * f + 128 * mc:2048 * f + 128 * (mc + 1)],
                             T["wbuf3"][:, f, 512 * vh:512 * (vh + 1)],
                             start=(f == 0), stop=(f == FC - 1),
                             sem="mm" if f == FC - 1 else None)
            pw.wait("vector", "mm", last)
            if vdma.get(mc % 2) is not None:
                pw.wait("vector", "vst")
            dstv = T["vstage"][:, 1040 * (mc % 2) + 520 * vh:1040 * (mc % 2) + 520 * (vh + 1)] \
                .rearrange("p (h c) -> p h c", c=65)[:, :, 0:64]
            c = pw.copy(dstv, psum[:, 512 * bk:512 * bk + 512]
                        .rearrange("p (h c) -> p h c", c=64), dep=False)
            pw._slot_state[("pj", bk)] = ("dve", c)
        pw.wait("sync", "dve")
        sp, r = mc // 4, mc % 4
        vdma[mc % 2] = pw.dma("sync", G[f"g_v_in_s{sp}"][:, 1040 * r:1040 * (r + 1)],
                              T["vstage"][:, 1040 * (mc % 2):1040 * (mc % 2) + 1040],
                              sem="vst")
        if r == 3:
            pw.wait("gpsimd", "vst")
            ag_done[("v", sp)] = pw.coll(
                "AllGather", AluOpType.bypass, groups,
                [G[f"g_v_in_s{sp}"].opt()], [G[f"g_v_s{sp}"].opt()])

    ld = load_W("aWq")
    emit_proj_fm(pw, T, ld, mk_evac("q8", "qst"),
                 lambda of, sp: G["g_qT"][:, of, 512 * sp:512 * (sp + 1)], "qst",
                 prev_sems=("kst",))

    pw.wait("sync", "pst")
    pw.wait("sync", "qst")
    for of in range(FC):
        l = None
        for sp in range(4):
            l = pw.dma("sync", stage[:, 512 * sp:512 * (sp + 1)],
                       G[f"g_pre_in_s{sp}"][128 * of:128 * (of + 1), :], sem="ld")
        pw.wait("vector", "ld", l)
        if of >= 1:
            pw.wait("vector", "qst")
        s = pw.ts(T["stage2"][:, 0:2048], stage[:, 0:2048], T["scal"][:, 4:5],
                  AluOpType.mult, dep=True)
        pw.wait("sync", "dve", s)
        pw.dma("sync", G["g_pqT"][:, of, :], T["stage2"][:, 0:2048], sem="qst")

    pw._ag_done = ag_done
    if dbg and dbg.get("upto") == "aproj":
        pw.barrier("dbgproj")
        for sp in range(4):
            pw.dma("sync", dbg["d_preT"][:, 512 * sp:512 * (sp + 1)],
                   G[f"g_pre_in_s{sp}"], sem="st")
            pw.dma("sync", dbg["d_kT"][:, 512 * sp:512 * (sp + 1)],
                   G[f"g_k_in_s{sp}"], sem="st")
            pw.dma("sync", dbg["d_vext"][:, 4160 * sp:4160 * (sp + 1)],
                   G[f"g_v_in_s{sp}"], sem="st")
            pw.dma("sync", dbg["d_gpre"][0:1024, 512 * sp:512 * (sp + 1)],
                   G[f"g_pre_s{sp}"][0], sem="st")
            pw.dma("sync", dbg["d_gpre"][1024:2048, 512 * sp:512 * (sp + 1)],
                   G[f"g_pre_s{sp}"][1], sem="st")
        pw.dma("sync", dbg["d_qT"].rearrange("(f p) t -> p f t", p=128), G["g_qT"],
               sem="st")
        pw.dma("sync", dbg["d_pqT"].rearrange("(f p) t -> p f t", p=128), G["g_pqT"],
               sem="st")
        return True
    return False


def ph_attn_a_logits(pw, T, G, dbg):
    A = T["ARN_A"]
    psum = T["psum"]
    PREQ, PM, LT = 0, 8192, 10240
    agv = max(pw._ag_done.values())
    pw.wait("sync", "gp", agv)
    pw.wait("sync", "qst")
    for qh in range(2):
        pw.wait("sync", "mm")
        pw.wait("sync", "act")
        lq = pw.dma("sync", A[:, PREQ:PREQ + 8192].rearrange("p (f t) -> p f t", t=1024),
                    G["g_pqT"][:, :, 1024 * qh:1024 * (qh + 1)], sem="alq")
        pw.wait("tensor", "alq", lq)
        mld = {}

        def issue(mc):
            if mc >= 32:
                return
            par = mc % 2
            slot, g = mc // 16, mc % 16
            sp, r = g // 4, g % 4
            mld[mc] = pw.dma("sync", A[:, PM + 1024 * par:PM + 1024 * (par + 1)]
                             .rearrange("p (f t) -> p f t", t=128),
                             G[f"g_pre_s{sp}"][slot].rearrange("(f p) t -> p f t", p=128)
                             [:, :, 128 * r:128 * (r + 1)], sem="alm" + str(par))
        issue(0)
        issue(1)
        ldma = {}
        for mc in range(32):
            par = mc % 2
            pw.wait("tensor", "alm" + str(par), mld[mc])
            pre = A[:, PM + 1024 * par:PM + 1024 * (par + 1)]
            for qs in range(2):
                bk = (mc * 2 + qs) % 6
                prev = pw._slot_state.get(512 * bk)
                if prev is not None:
                    pw.wait("tensor", prev[0], prev[1])
                last = None
                for f in range(FC):
                    last = pw.mm(psum[:, 512 * bk:512 * bk + 512],
                                 pre[:, 128 * f:128 * (f + 1)],
                                 A[:, PREQ + 1024 * f + 512 * qs:PREQ + 1024 * f + 512 * (qs + 1)],
                                 start=(f == 0), stop=(f == FC - 1),
                                 sem="mm" if f == FC - 1 else None)
                pw.wait("scalar", "mm", last)
                if ldma.get(par) is not None and qs == 0:
                    pw.wait("scalar", "alst" + str(par))
                lc = pw.act(A[:, LT + 1024 * par + 512 * qs:LT + 1024 * par + 512 * (qs + 1)],
                            psum[:, 512 * bk:512 * bk + 512], AF.Copy, dep=False)
                pw._slot_state[512 * bk] = ("act", lc)
            pw.wait("sync", "act")
            ldma[par] = pw.dma("sync", G["g_lg"][mc][:, 1024 * qh:1024 * (qh + 1)],
                               A[:, LT + 1024 * par:LT + 1024 * (par + 1)],
                               sem="alst" + str(par))
            pw.wait("sync", "mm")
            issue(mc + 2)
        pw.wait("sync", "alst0")
        pw.wait("sync", "alst1")


def ph_attn_a_attn(pw, T, W, G, dbg, whook=None):
    X, A, B = T["X"], T["ARN_A"], T["ARN_B"]
    psum = T["psum"]
    QS = 0
    KT2 = 4096
    VX2 = 4608
    LG2 = 5184
    TMPX = 6208
    EXPX = 7232
    ATT = 8448
    STG = 12544
    LNB = 13568

    ldw = pw.dma("sync", T["wbuf3"], W["aWproj"].rearrange("(f p) o -> p f o", p=128),
                 sem="ld")

    for span in range(4):
        pw.wait("sync", "mm")
        pw.wait("sync", "dve")
        lq = pw.dma("sync", A[:, QS:QS + 4096].rearrange("p (f t) -> p f t", t=512),
                    G["g_qT"][:, :, 512 * span:512 * (span + 1)], sem="aq")
        pw.wait("tensor", "aq", lq)
        for hg in range(4):
            mld = {}

            def issue(mc):
                if mc >= 32:
                    return
                par = mc % 2
                sm = "am" + str(par)
                slot, g = mc // 16, mc % 16
                sp, r = g // 4, g % 4
                pw.dma("sync", A[:, KT2 + 256 * par:KT2 + 256 * (par + 1)]
                       .rearrange("p (f t) -> p f t", t=128),
                       G[f"g_k_s{sp}"][slot].rearrange("(f p) t -> p f t", p=128)
                       [:, 2 * hg:2 * hg + 2, 128 * r:128 * (r + 1)], sem=sm)
                pw.dma("sync", A[:, VX2 + 260 * par:VX2 + 260 * (par + 1)],
                       G[f"g_v_s{sp}"][slot][:, 1040 * r + 260 * hg:
                                             1040 * r + 260 * (hg + 1)], sem=sm)
                mld[mc] = pw.dma("sync", A[:, LG2 + 512 * par:LG2 + 512 * (par + 1)],
                                 G["g_lg"][mc][:, 512 * span:512 * (span + 1)], sem=sm)
            pw.wait("sync", "mm")
            pw.wait("sync", "dve")
            issue(0)
            issue(1)
            pend = None
            ec_hist = {}
            for mc in range(32):
                if whook is not None and mc in (8, 16, 24):
                    whook()
                par = mc % 2
                pw.wait("tensor", "am" + str(par), mld[mc])
                kt = A[:, KT2 + 256 * par:KT2 + 256 * (par + 1)]
                vx = A[:, VX2 + 260 * par:VX2 + 260 * (par + 1)]
                lg = A[:, LG2 + 512 * par:LG2 + 512 * (par + 1)]
                for hh in range(4):
                    h = 4 * hg + hh
                    i = mc * 4 + hh
                    bk = 2560 if i % 2 == 0 else 3072
                    eb = i % 2
                    prev = pw._slot_state.get(bk)
                    if prev is not None:
                        pw.wait("tensor", prev[0], prev[1])
                    sm2 = pw.mm(psum[:, bk:bk + 512],
                                kt[64 * (hh % 2):64 * (hh % 2) + 64,
                                   128 * (hh // 2):128 * (hh // 2) + 128],
                                A[64 * (h % 2):64 * (h % 2) + 64,
                                  QS + 512 * (h // 2):QS + 512 * (h // 2) + 512],
                                start=True, stop=True, sem="mm")
                    if pend is not None:
                        phh, pmc, pec, peb, pvx = pend
                        pw.wait("tensor", "act", pec)
                        pw.mm(psum[0:65, 512 * phh:512 * phh + 512],
                              pvx[:, 65 * phh:65 * (phh + 1)],
                              A[:, EXPX + 512 * peb:EXPX + 512 * (peb + 1)],
                              start=(pmc == 0), stop=(pmc == 31),
                              sem="mm" if phh == 3 else None)
                        pend = None
                        if hh == 0:
                            pw.wait("sync", "mm")
                            issue(mc + 1)
                    pw.wait("vector", "mm", sm2)
                    if eb in ec_hist:
                        pw.wait("vector", "act", ec_hist[eb])
                    tv = pw.tt(A[:, TMPX + 512 * eb:TMPX + 512 * (eb + 1)],
                               psum[:, bk:bk + 512], lg, AluOpType.add, dep=False)
                    pw.wait("scalar", "dve", tv)
                    ec = pw.act(A[:, EXPX + 512 * eb:EXPX + 512 * (eb + 1)],
                                A[:, TMPX + 512 * eb:TMPX + 512 * (eb + 1)],
                                AF.Exp, dep=False)
                    ec_hist[eb] = ec
                    pw._slot_state[bk] = ("dve", tv)
                    pend = (hh, mc, ec, eb, vx)
            phh, pmc, pec, peb, pvx = pend
            pw.wait("tensor", "act", pec)
            pw.mm(psum[0:65, 512 * phh:512 * phh + 512],
                  pvx[:, 65 * phh:65 * (phh + 1)],
                  A[:, EXPX + 512 * peb:EXPX + 512 * (peb + 1)],
                  start=(pmc == 0), stop=(pmc == 31), sem="mm")
            pend = None
            pw.wait("vector", "mm")
            pw.wait("tensor", "dve")
            for hh in range(4):
                h = 4 * hg + hh
                rd = pw.recip(A[64:65, STG + 512:STG + 1024],
                              psum[64:65, 512 * hh:512 * hh + 512], dep=True)
                cv = pw.copy(A[0:64, STG:STG + 512],
                             psum[0:64, 512 * hh:512 * hh + 512], dep=True)
                pw.wait("tensor", "dve", cv)
                bcm = pw.mm(psum[:, 2560:2560 + 512], T["ones_row"][64:65, :],
                            A[64:65, STG + 512:STG + 1024], start=True, stop=True,
                            sem="mm")
                pw.wait("vector", "mm", bcm)
                if h % 2 == 0:
                    dv = pw.tt(A[0:64, ATT + 512 * (h // 2):ATT + 512 * (h // 2) + 512],
                               A[0:64, STG:STG + 512], psum[0:64, 2560:2560 + 512],
                               AluOpType.mult, dep=True)
                else:
                    sv = pw.tt(A[0:64, STG:STG + 512], A[0:64, STG:STG + 512],
                               psum[0:64, 2560:2560 + 512], AluOpType.mult, dep=True)
                    pw.wait("tensor", "dve", sv)
                    sh = pw.mm(psum[:, 3072:3072 + 512], T["shift64"][0:64, :],
                               A[0:64, STG:STG + 512], start=True, stop=True,
                               sem="mm")
                    pw.wait("vector", "mm", sh)
                    dv = pw.copy(A[64:128, ATT + 512 * (h // 2):ATT + 512 * (h // 2) + 512],
                                 psum[64:128, 3072:3072 + 512], dep=True)
                pw._slot_state[2560] = ("dve", dv)
                pw._slot_state[3072] = ("dve", dv)
            if whook is not None:
                whook()

        pw.wait("tensor", "dve")
        pw.wait("tensor", "ld", ldw)
        for of in range(FC):
            bk = 2560 if of % 2 == 0 else 3072
            prev = pw._slot_state.get(bk)
            if prev is not None:
                pw.wait("tensor", prev[0], prev[1])
            last = None
            for f in range(FC):
                last = pw.mm(psum[:, bk:bk + 512],
                             T["wbuf3"][:, f, 128 * of:128 * of + 128],
                             A[:, ATT + 512 * f:ATT + 512 * (f + 1)],
                             start=(f == 0), stop=(f == FC - 1),
                             sem="mm" if f == FC - 1 else None)
            pw.wait("scalar", "mm", last)
            oc = pw.act(A[:, STG + 512 * 0:STG + 512] if False else
                        B[:, 8192 + 512 * of:8192 + 512 * (of + 1)],
                        psum[:, bk:bk + 512], AF.Identity,
                        bias=T["vecs"][:, 0 + of:1 + of], dep=False)
            pw._slot_state[bk] = ("act", oc)
        pw.wait("tensor", "act")
        for tc in range(4):
            t_global = 4 * span + tc
            pw.wait("vector", "mm")
            emit_ln_tile(pw, X[:, t_global, :], A[:, LNB:LNB + 1024],
                         T["tmpA"][:, 0:1024], T["stat"], LN_EPS)
            for ob in range(0, FC, 4):
                bk = 2560 if (ob // 4) % 2 == 0 else 3072
                prev = pw._slot_state.get(bk)
                if prev is not None:
                    pw.wait("tensor", prev[0], prev[1])
                last = None
                for j in range(4):
                    last = pw.mm(psum[:, bk + 128 * j:bk + 128 * (j + 1)],
                                 B[:, 8192 + 512 * (ob + j) + 128 * tc:
                                   8192 + 512 * (ob + j) + 128 * (tc + 1)],
                                 T["ident"][:], start=(j == 0), stop=(j == 3),
                                 sem="mm" if j == 3 else None)
                pw.wait("vector", "mm", last)
                pw.tt(A[:, LNB + 128 * ob:LNB + 128 * (ob + 4)],
                      A[:, LNB + 128 * ob:LNB + 128 * (ob + 4)],
                      psum[:, bk:bk + 512], AluOpType.add, dep=True)
                dv = pw.tt(X[:, t_global, 128 * ob:128 * (ob + 4)],
                           X[:, t_global, 128 * ob:128 * (ob + 4)],
                           A[:, LNB + 128 * ob:LNB + 128 * (ob + 4)],
                           AluOpType.add, dep=True)
                pw._slot_state[bk] = ("dve", dv)

    if dbg and dbg.get("upto") == "attna":
        pw.barrier("dbgattna")
        pw.dma("sync", dbg["d_x1"], X[:, :, :], sem="st")
        return True
    return False


def row_from_cols(pw, G, cols_ap, row_ap, nprt, ncol, sem):
    n = pw.cnt.get("_rfc", 0)
    pw.cnt["_rfc"] = n + 1
    s = f"rfc{n}"
    if n > 0:
        pw.wait("sync", f"rfc{n - 1}")
    pw.dma("sync", G["g_fr"][0:nprt, 0:ncol], cols_ap, sem=s, slow=True)
    pw.wait("sync", s)
    pw.dma("sync", row_ap,
           G["g_fr"][0:nprt, 0:ncol].rearrange("i t -> t i"), sem=s, slow=True)
    pw.wait("tensor", s)


def ph_ffn_weights(pw, T, W, G):
    psum, psumb = T["psum"], T["psumb"]
    wbuf3 = T["wbuf3"]
    acc = T["stage"]

    for wi, (wname, nblk, gdst, mcol, scol) in enumerate(
            [("f_w1", 32, "g_t1T", 5, 7), ("f_w2", 8, "g_t2T", 6, 8)]):
        ncols = 1024 if wi == 0 else 4096
        per = 8 if wi == 0 else 2
        nld = nblk // per
        pw.red(acc[:, 64:65], T["fcols"][:, 32 * wi:32 * wi + 32], AluOpType.add)
        pw.wait("tensor", "dve")
        s = pw.mm(psum[0:1, 0:1], T["ones_col"][:], acc[:, 64:65], start=True,
                  stop=True, sem="mm")
        pw.wait("vector", "mm", s)
        pw.ts(acc[0:1, 65:66], psum[0:1, 0:1], 1.0 / (nblk * 128 * ncols),
              AluOpType.mult, 1e-5, AluOpType.max)
        v = pw.recip(acc[0:1, 66:67], acc[0:1, 65:66])
        pw.wait("tensor", "dve", v)
        s = pw.mm(psum[:, 4:5], T["ones_row"][0:1, :], acc[0:1, 65:66], start=True,
                  stop=True)
        s = pw.mm(psum[:, 5:6], T["ones_row"][0:1, :], acc[0:1, 66:67], start=True,
                  stop=True, sem="mm")
        pw.wait("vector", "mm", s)
        pw.copy(T["scal"][:, mcol:mcol + 1], psum[:, 4:5])
        pw.copy(T["scal"][:, scol:scol + 1], psum[:, 5:6])

        for g in range(nld):
            pw.wait("sync", "dve")
            pw.wait("sync", "act")
            l = pw.dma("sync", wbuf3 if wi == 0 else
                       wbuf3.rearrange("p a o -> p (a o)").rearrange(
                           "p (b c) -> p b c", b=2),
                       W[wname].rearrange("(a p) d -> p a d", p=128)
                       [:, per * g:per * (g + 1), :], sem="ld")
            pw.wait("vector", "ld", l)
            pw.wait("scalar", "ld", l)
            for a in range(per):
                blk = wbuf3[:, a, :] if wi == 0 else                     wbuf3.rearrange("p a o -> p (a o)")[:, 4096 * a:4096 * (a + 1)]
                j = g * per + a
                for c0 in range(0, ncols, 1024):
                    sl = blk[:, c0:c0 + 1024] if wi else blk
                    av = pw.act(T["tmp16"][:, 0:1024], sl, AF.Identity,
                                bias=1536.0, scale=T["scal"][:, scol:scol + 1],
                                dep=False)
                    pw.wait("vector", "act", av)
                    ts1 = pw.ts(T["stage2"][:, 0:1024], T["tmp16"][:, 0:1024],
                                -1536.0, AluOpType.add, 1.0, AluOpType.min)
                    tv = pw.ts(T["tbf"][:, 0:1024], T["stage2"][:, 0:1024],
                               -1.0, AluOpType.max)
                    pw.wait("scalar", "dve", ts1)
                    pw.wait("tensor", "dve", tv)
                    prev = pw._slot_state.get("psb")
                    if prev is not None:
                        pw.wait("tensor", prev[0], prev[1])
                    last = None
                    for dc in range(8):
                        last = pw.mmT(psumb[:, 128 * dc:128 * (dc + 1)],
                                      T["tbf"][:, 128 * dc:128 * (dc + 1)],
                                      T["identb"][:], start=(dc == 0), stop=(dc == 7),
                                      sem="mm" if dc == 7 else None)
                    pw.wait("vector", "mm", last)
                    if "wst" in pw.cnt:
                        pw.wait("vector", "wst")
                    dv = pw.copy(T["tbf"][:, 1024:2048]
                                 .rearrange("p (a b) -> p a b", a=8), psumb[:]
                                 .rearrange("p (a b) -> p a b", a=8), dep=False)
                    pw._slot_state["psb"] = ("dve", dv)
                    pw.wait("sync", "dve", dv)
                    if wi == 0:
                        dst = G["g_t1T"].rearrange("(dc p) j -> p dc j", p=128)[
                            :, :, 128 * j:128 * (j + 1)]
                    else:
                        s_ = c0 // 1024
                        j2 = 2 * g + a
                        dst = G["g_t2T"].rearrange("(hc p) j -> p hc j", p=128)[
                            :, 8 * s_:8 * (s_ + 1), 128 * j2:128 * (j2 + 1)]
                    pw.dma("sync", dst,
                           T["tbf"][:, 1024:2048].rearrange("p (a b) -> p a b", a=8),
                           sem="wst")
    pw.wait("sync", "wst")


def ph_ffn(pw, T, W, G, dbg, which):
    X, A, B = T["X"], T["ARN_A"], T["ARN_B"]
    psum, psumb = T["psum"], T["psumb"]
    groups = T["groups"]
    tag = f"f{which}"
    NTT = 0
    XL = 8192
    FB = 9216
    MAXA = 11264
    SB_ = 13312
    TMPC = 0
    hx = T["hx"]
    nbfT = T["nbfT"]
    tstat = T["tstat"]
    z16 = T["z16"]
    nbf = T["tbf"]

    pw.wait("sync", "dve")
    pw.dma("sync", G["g_halo_in"][0:1, :], X[0:1, 0, :], sem=tag + "hs")
    hd = pw.dma("sync", G["g_halo_in"][1:2, :], X[127:128, 15, :], sem=tag + "hs")
    pw.wait("gpsimd", tag + "hs", hd)
    agh = pw.coll("AllGather", AluOpType.bypass, groups, [G["g_halo_in"].opt()],
                  [G["g_halo"].opt()])

    tmp, stat = T["tmpA"], T["stat"]
    tr_done = None
    for t in range(NT):
        if tr_done is not None:
            pw.wait("vector", "mm", tr_done)
        emit_ln_tile(pw, X[:, t, :], A[:, XL:XL + 1024], tmp[:, 0:1024], stat, LN_EPS)
        pw.red(stat[:, 6:7], A[:, XL:XL + 1024], AluOpType.max, absval=True)
        pw.ts(stat[:, 6:7], stat[:, 6:7], 1e-5, AluOpType.max)
        pw.recip(stat[:, 7:8], stat[:, 6:7])
        pw.ts(stat[:, 7:8], stat[:, 7:8], 127.0, AluOpType.mult)
        pw.ts(T["fcols"][:, t:t + 1], stat[:, 6:7], T["scal"][:, 5:6],
              AluOpType.mult, 1.0 / 127.0, AluOpType.mult)
        pw.ts(z16[:], A[:, XL:XL + 1024], stat[:, 7:8], AluOpType.mult, 1536.0,
              AluOpType.add)
        qv = pw.ts(nbf[:, 0:1024], z16[:], -1536.0, AluOpType.add)
        pw.wait("tensor", "dve", qv)
        prev = pw._slot_state.get("psb")
        if prev is not None:
            pw.wait("tensor", prev[0], prev[1])
        last = None
        for dc in range(8):
            last = pw.mmT(psumb[:, 128 * dc:128 * (dc + 1)],
                          nbf[:, 128 * dc:128 * (dc + 1)], T["identb"][:],
                          start=(dc == 0), stop=(dc == 7),
                          sem="mm" if dc == 7 else None)
        pw.wait("vector", "mm", last)
        dv = pw.copy(nbfT[:, :, 128 * t:128 * (t + 1)],
                     psumb[:].rearrange("p (a b) -> p a b", a=8), dep=False)
        pw._slot_state["psb"] = ("dve", dv)
        tr_done = last

    fv = pw.cnt["dve"]
    pw.wait("sync", "dve", fv)
    row_from_cols(pw, G, T["fcols"][:, 0:16], T["frow"][0:1, 0:2048], 128, 16,
                  tag + "h")
    s = pw.mm(psum[:, 0:512], T["ones_row"][0:1, :], T["frow"][0:1, 0:512], start=True,
              stop=True)
    s = pw.mm(psum[:, 512:1024], T["ones_row"][0:1, :], T["frow"][0:1, 512:1024],
              start=True, stop=True)
    s = pw.mm(psum[:, 1024:1536], T["ones_row"][0:1, :], T["frow"][0:1, 1024:1536],
              start=True, stop=True)
    s = pw.mm(psum[:, 1536:2048], T["ones_row"][0:1, :], T["frow"][0:1, 1536:2048],
              start=True, stop=True, sem="mm")
    pw.wait("vector", "mm", s)
    pw.copy(A[:, FB:FB + 2048], psum[:, 0:2048], dep=False)

    pw.wait("sync", "gp", agh)
    hl = pw.dma("sync", T["halo2"][0:1, :], G["g_halo"][0, 1, :][None, :],
                sem=tag + "hl")
    hl = pw.dma("sync", T["halo2"][1:2, :], G["g_halo"][1, 0, :][None, :],
                sem=tag + "hl")
    pw.wait("vector", tag + "hl", hl)
    emit_ln_tile(pw, T["halo2"][0:2, :], T["halo2b"][0:2, :], tmp[0:2, 0:1024],
                 stat[0:2, :], LN_EPS)
    pw.red(stat[0:2, 6:7], T["halo2b"][0:2, :], AluOpType.max, absval=True)
    pw.ts(stat[0:2, 6:7], stat[0:2, 6:7], 1e-5, AluOpType.max)
    pw.recip(stat[0:2, 7:8], stat[0:2, 6:7])
    pw.ts(stat[0:2, 7:8], stat[0:2, 7:8], 127.0, AluOpType.mult)
    pw.ts(T["fcols"][0:2, 16:17], stat[0:2, 6:7], T["scal"][0:2, 5:6],
          AluOpType.mult, 1.0 / 127.0, AluOpType.mult)
    pw.ts(T["z16h"][0:2, :], T["halo2b"][0:2, :], stat[0:2, 7:8], AluOpType.mult,
          1536.0, AluOpType.add)
    qv = pw.ts(nbf[0:2, 0:1024], T["z16h"][0:2, :], -1536.0, AluOpType.add)
    pw.wait("tensor", "dve", qv)
    prev = pw._slot_state.get("psb")
    if prev is not None:
        pw.wait("tensor", prev[0], prev[1])
    last = None
    for dc in range(8):
        last = pw.mmT(psumb[:, 16 * dc:16 * dc + 2],
                      nbf[0:2, 128 * dc:128 * (dc + 1)], T["identb"][0:2, 0:2],
                      start=(dc == 0), stop=(dc == 7), sem="mm" if dc == 7 else None)
    pw.wait("vector", "mm", last)
    dv = pw.copy(T["nhT"][:, :, :], psumb[:, 0:128].rearrange("p (a b) -> p a b", b=16)
                 [:, :, 0:2], dep=False)
    pw._slot_state["psb"] = ("dve", dv)
    fh = pw.cnt["dve"]
    pw.wait("sync", "dve", fh)
    row_from_cols(pw, G, T["fcols"][0:2, 16:17], T["frow"][0:1, 2048:2050], 2, 1,
                  tag + "h")
    s = pw.mm(psum[:, 512 * 5:512 * 5 + 2], T["ones_row"][0:1, :],
              T["frow"][0:1, 2048:2050], start=True, stop=True, sem="mm")
    pw.wait("vector", "mm", s)
    pw.copy(T["haloh"][:, 0:2], psum[:, 512 * 5:512 * 5 + 2], dep=False)
    pw.ts(T["haloh"][:, 0:1], T["haloh"][:, 0:1], T["scal"][:, 2:3],
          AluOpType.mult)
    pw.ts(T["haloh"][:, 1:2], T["haloh"][:, 1:2], T["scal"][:, 3:4],
          AluOpType.mult)

    pw.memset(A[:, MAXA:MAXA + 2048], 0.0)
    pw.wait("vector", "gp")
    pw.wait("tensor", "dve")
    dv0 = pw.cnt["dve"]
    for b_ in range(5):
        pw._slot_state[("fx", b_)] = ("dve", dv0)
    pw._slot_state["hb6"] = ("dve", dv0)
    hdma = {}
    av_hist = {}
    tstat4 = T["tstat4"]
    mmh1 = {-4: pw.cnt["mm"], -3: pw.cnt["mm"],
            -2: pw.cnt["mm"], -1: pw.cnt["mm"]}
    for jc in range(NJC):
        pw.wait("sync", "mm", mmh1[jc - 4])
        l = pw.dma("sync", tstat4[:, jc % 4, :, :],
                   G["g_t1T"].rearrange("(dc p) j -> p dc j", p=128)
                   [:, :, 128 * jc:128 * (jc + 1)], sem=tag + "w" + str(jc % 4))
        pw.wait("tensor", tag + "w" + str(jc % 4), l)
        strip = hx[:, jc % 2, :]
        cbuf = (T["stage2"][:, 0:2048] if jc % 2 == 0
                else T["stage"][:, 0:2048])
        for sp in range(4):
            bk = (jc * 4 + sp) % 5
            prev = pw._slot_state.get(("fx", bk))
            if prev is not None:
                pw.wait("tensor", prev[0], prev[1])
            last = None
            for dc in range(8):
                last = pw.mm(psum[:, 512 * bk:512 * bk + 512],
                             tstat4[:, jc % 4, dc, :],
                             nbfT[:, dc, 512 * sp:512 * (sp + 1)],
                             start=(dc == 0), stop=(dc == 7),
                             sem="mm" if dc == 7 else None)
            pw.wait("vector", "mm", last)
            if sp == 0:
                if hdma.get(jc % 2) is not None:
                    pw.wait("vector", tag + "hd" + str(jc % 2), hdma[jc % 2])
                if av_hist.get(jc % 2) is not None:
                    pw.wait("vector", "act", av_hist[jc % 2])
            c = pw.tt(strip[:, 1 + 512 * sp:1 + 512 * (sp + 1)],
                      psum[:, 512 * bk:512 * bk + 512],
                      A[:, FB + 512 * sp:FB + 512 * (sp + 1)], AluOpType.mult,
                      dep=False)
            pw._slot_state[("fx", bk)] = ("dve", c)
        prev = pw._slot_state.get("hb6")
        if prev is not None:
            pw.wait("tensor", prev[0], prev[1])
        lasth = None
        for dc in range(8):
            lasth = pw.mm(psum[:, 512 * 6:512 * 6 + 2],
                          tstat4[:, jc % 4, dc, :], T["nhT"][:, dc, :],
                          start=(dc == 0), stop=(dc == 7),
                          sem="mm" if dc == 7 else None)
        mmh1[jc] = pw.cnt["mm"]
        pw.wait("vector", "mm", lasth)
        pw.tt(strip[:, 0:1], psum[:, 512 * 6:512 * 6 + 1], T["haloh"][:, 0:1],
              AluOpType.mult, dep=True)
        hc2 = pw.tt(strip[:, 2049:2050], psum[:, 512 * 6 + 1:512 * 6 + 2],
                    T["haloh"][:, 1:2], AluOpType.mult, dep=True)
        pw._slot_state["hb6"] = ("dve", hc2)
        pw.ts(cbuf, strip[:, 0:2048], T["convw"][:, jc, 0:1], AluOpType.mult)
        pw.stt(cbuf, strip[:, 1:2049], T["convw"][:, jc, 1:2],
               cbuf, AluOpType.mult, AluOpType.add)
        cv = pw.stt(cbuf, strip[:, 2:2050], T["convw"][:, jc, 2:3],
                    cbuf, AluOpType.mult, AluOpType.add)
        pw.wait("scalar", "dve", cv)
        av = pw.act(strip[:, 1:2049],
                    cbuf, AF.Silu, bias=T["convb"][:, jc:jc + 1],
                    dep=False)
        av_hist[jc % 2] = av
        pw.wait("vector", "act", av)
        pw.stt(A[:, MAXA:MAXA + 2048], strip[:, 1:2049], 1.0,
               A[:, MAXA:MAXA + 2048], AluOpType.mult, AluOpType.max)
        pw.stt(A[:, MAXA:MAXA + 2048], strip[:, 1:2049], -1.0,
               A[:, MAXA:MAXA + 2048], AluOpType.mult, AluOpType.max)
        pw.wait("gpsimd", "act", av)
        hdma[jc % 2] = pw.dma("gpsimd", G["g_hact"][128 * jc:128 * (jc + 1), :],
                              strip[:, 1:2049], sem=tag + "hd" + str(jc % 2))

    pw.wait("tensor", "dve")
    for tt_ in range(NT):
        prev = pw._slot_state.get(("fx", tt_ % 2))
        if prev is not None:
            pw.wait("tensor", prev[0], prev[1])
        s = pw.mm(psum[:, 512 * (tt_ % 2):512 * (tt_ % 2) + 128],
                  A[:, MAXA + 128 * tt_:MAXA + 128 * (tt_ + 1)], T["ident"][:],
                  start=True, stop=True, sem="mm")
        pw.wait("vector", "mm", s)
        rv = pw.red(T["fcols"][:, 32 + tt_:33 + tt_],
                    psum[:, 512 * (tt_ % 2):512 * (tt_ % 2) + 128],
                    AluOpType.max, dep=False)
        pw._slot_state[("fx", tt_ % 2)] = ("dve", rv)
    pw.ts(T["fcols"][:, 32:48], T["fcols"][:, 32:48], 1e-5, AluOpType.max)
    pw.ts(T["fcols"][:, 48:64], T["fcols"][:, 32:48], T["scal"][:, 6:7],
          AluOpType.mult, 1.0 / 127.0, AluOpType.mult)
    rv = pw.recip(T["fcols"][:, 32:48], T["fcols"][:, 32:48])
    sv = pw.ts(T["fcols"][:, 32:48], T["fcols"][:, 32:48], 127.0, AluOpType.mult)
    pw.wait("sync", "dve", sv)
    row_from_cols(pw, G, T["fcols"][:, 32:48], T["frow"][0:1, 0:2048], 128, 16,
                  tag + "h")
    row_from_cols(pw, G, T["fcols"][:, 48:64], T["f2row"][0:1, 0:2048], 128, 16,
                  tag + "h")
    pw.wait("tensor", "dve")
    for half in range(4):
        s = pw.mm(psum[:, 512 * half:512 * (half + 1)], T["ones_row"][0:1, :],
                  T["frow"][0:1, 512 * half:512 * (half + 1)], start=True, stop=True,
                  sem="mm" if half == 3 else None)
    pw.wait("vector", "mm", s)
    pw.copy(A[:, SB_:SB_ + 2048], psum[:, 0:2048], dep=False)
    pw.wait("tensor", "dve")
    for half in range(4):
        s = pw.mm(psum[:, 512 * half:512 * (half + 1)], T["ones_row"][0:1, :],
                  T["f2row"][0:1, 512 * half:512 * (half + 1)], start=True, stop=True,
                  sem="mm" if half == 3 else None)
    pw.wait("vector", "mm", s)
    f2v = pw.copy(A[:, FB:FB + 2048], psum[:, 0:2048], dep=False)

    pw.wait("sync", "mm", pw.cnt["mm"])
    lt2 = pw.dma("sync", T["t2s"][:, :, :],
                 G["g_t2T"].rearrange("(hc p) j -> p hc j", p=128)
                 [:, :, 0:512], sem=tag + "t2")

    pw.wait("sync", "dve", f2v)
    pw.wait("sync", tag + "hd0")
    pw.wait("sync", tag + "hd1")
    pw.wait("vector", "act")
    pw.wait("tensor", "dve", f2v)
    pw.wait("tensor", tag + "t2", lt2)
    ndma = {}
    qh_tt = {}
    nv_hist = {}
    fmm = {-3: pw.cnt["mm"], -2: pw.cnt["mm"], -1: pw.cnt["mm"]}

    def fused_mms(jc):
        nh_b = T["nh2"] if jc % 2 == 0 else T["tbf"][:, 0:2048]
        pw.wait("tensor", "dve", nv_hist[jc])
        for oc in range(4):
            pw.mm(psum[:, 512 * oc:512 * oc + 512],
                  T["t2s"][:, jc, 128 * oc:128 * (oc + 1)],
                  nh_b[:, 0:512],
                  start=(jc == 0), stop=(jc == NJC - 1),
                  sem="mm" if oc == 3 else None)
        fmm[jc] = pw.cnt["mm"]

    for jc in range(NJC):
        if qh_tt.get(jc % 2) is not None:
            pw.wait("sync", "dve", qh_tt[jc % 2])
        l = pw.dma("sync", hx[:, jc % 2, 0:2048],
                   G["g_hact"][128 * jc:128 * (jc + 1), :], sem=tag + "q" + str(jc % 2))
        pw.wait("vector", tag + "q" + str(jc % 2), l)
        if ndma.get(jc % 2) is not None:
            pw.wait("vector", tag + "nd" + str(jc % 2), ndma[jc % 2])
        nh_buf = T["nh2"] if jc % 2 == 0 else T["tbf"][:, 0:2048]
        pw.wait("vector", "mm", fmm[jc - 2])
        qh_tt[jc % 2] = pw.tt(T["stage2"][:, 0:2048], hx[:, jc % 2, 0:2048],
                              A[:, SB_:SB_ + 2048], AluOpType.mult, dep=False)
        pw.ts(T["z16w"][:, 0:2048], T["stage2"][:, 0:2048], 1536.0, AluOpType.add)
        nv = pw.ts(nh_buf, T["z16w"][:, 0:2048], -1536.0, AluOpType.add)
        nv_hist[jc] = nv
        pw.wait("gpsimd", "dve", nv)
        ndma[jc % 2] = pw.dma("gpsimd", G["g_nh"][128 * jc:128 * (jc + 1), :],
                              nh_buf, sem=tag + "nd" + str(jc % 2))
        if jc >= 1:
            fused_mms(jc - 1)
    fused_mms(NJC - 1)

    pw.wait("vector", "mm", fmm[NJC - 1])
    for oc in range(4):
        pw.tt(T["ofm2"][:, oc, :], psum[:, 512 * oc:512 * oc + 512],
              A[:, FB:FB + 512], AluOpType.mult, dep=False)
    pw.wait("tensor", "dve")
    for tt_ in range(4):
        bk = 4 + (tt_ % 2)
        prev = pw._slot_state.get(("fx2", bk))
        if prev is not None:
            pw.wait("tensor", prev[0], prev[1])
        last = None
        for oc in range(4):
            last = pw.mm(psum[:, 512 * bk + 128 * oc:512 * bk + 128 * (oc + 1)],
                         T["ofm2"][:, oc, 128 * tt_:128 * (tt_ + 1)],
                         T["ident"][:], start=(oc == 0), stop=(oc == 3),
                         sem="mm" if oc == 3 else None)
        pw.wait("vector", "mm", last)
        xv = pw.tt(X[:, tt_, 0:512], X[:, tt_, 0:512],
                   psum[:, 512 * bk:512 * bk + 512], AluOpType.add, dep=False)
        pw._slot_state[("fx2", bk)] = ("dve", xv)

    pw.wait("sync", tag + "nd0")
    pw.wait("sync", tag + "nd1")
    pw.wait("sync", "mm")
    pw.wait("sync", "dve", nv_hist[NJC - 1])
    pw.wait("tensor", "dve")
    for oh in range(2):
        if oh == 1:
            pw.wait("sync", "mm")
            l2 = pw.dma("sync", T["t2s"][:, :, :],
                        G["g_t2T"].rearrange("(hc p) j -> p hc j", p=128)
                        [:, :, 512 * oh:512 * (oh + 1)], sem=tag + "t2")
            pw.wait("tensor", tag + "t2", l2)
        for sp in range(4):
            if oh == 0 and sp == 0:
                continue
            mmh2 = {-4: pw.cnt["mm"], -3: pw.cnt["mm"],
                    -2: pw.cnt["mm"], -1: pw.cnt["mm"]}
            for jc in range(NJC):
                pw.wait("sync", "mm", mmh2[jc - 4])
                sm2 = tag + "x" + str(jc % 4)
                l1 = pw.dma("sync", T["nstat4"][:, jc % 4, :],
                            G["g_nh"][128 * jc:128 * (jc + 1),
                                      512 * sp:512 * (sp + 1)], sem=sm2)
                pw.wait("tensor", sm2, l1)
                for oc in range(4):
                    pw.mm(psum[:, 512 * oc:512 * oc + 512],
                          T["t2s"][:, jc, 128 * oc:128 * (oc + 1)],
                          T["nstat4"][:, jc % 4, :],
                          start=(jc == 0), stop=(jc == NJC - 1),
                          sem="mm" if oc == 3 else None)
                mmh2[jc] = pw.cnt["mm"]
            pw.wait("vector", "mm")
            for oc in range(4):
                ev = pw.tt(T["ofm2"][:, oc, :],
                           psum[:, 512 * oc:512 * oc + 512],
                           A[:, FB + 512 * sp:FB + 512 * (sp + 1)],
                           AluOpType.mult, dep=False)
            pw.wait("tensor", "dve")
            for tt_ in range(4):
                t_global = 4 * sp + tt_
                bk = 4 + (tt_ % 2)
                prev = pw._slot_state.get(("fx2", bk))
                if prev is not None:
                    pw.wait("tensor", prev[0], prev[1])
                last = None
                for oc in range(4):
                    last = pw.mm(psum[:, 512 * bk + 128 * oc:512 * bk + 128 * (oc + 1)],
                                 T["ofm2"][:, oc, 128 * tt_:128 * (tt_ + 1)],
                                 T["ident"][:], start=(oc == 0), stop=(oc == 3),
                                 sem="mm" if oc == 3 else None)
                pw.wait("vector", "mm", last)
                xv = pw.tt(X[:, t_global, 512 * oh:512 * (oh + 1)],
                           X[:, t_global, 512 * oh:512 * (oh + 1)],
                           psum[:, 512 * bk:512 * bk + 512], AluOpType.add,
                           dep=False)
                pw._slot_state[("fx2", bk)] = ("dve", xv)

    if dbg and dbg.get("upto") == "ffn" + str(which):
        pw.barrier("dbgffn")
        pw.dma("sync", dbg["d_x1"], X[:, :, :], sem="st")
        return True
    return False


def ph_attn_b(pw, T, W, G, dbg):
    X, A, B = T["X"], T["ARN_A"], T["ARN_B"]
    psum = T["psum"]
    groups = T["groups"]
    BW = T["BWK"]
    VXB = 0
    KVS = 1040
    STS = 1172
    DEN = 1304
    ASB = 1336
    QP0, QP1 = 0, 4096
    CATQ, CATK = 8192, 10240
    KPF = 12288
    KPT = 8192
    GBC = 1024

    tmp, stat = T["tmpA"], T["stat"]

    pw.wait("vector", "mm")
    xln = B
    for t in range(NT):
        emit_ln_tile(pw, X[:, t, :], xln[:, 1024 * t:1024 * (t + 1)], tmp[:, 0:1024],
                     stat, LN_EPS)
    pw.wait("tensor", "dve")

    def dst(t, f):
        return A[:, 2048 * f + 128 * t:2048 * f + 128 * (t + 1)]
    emit_transpose_tm_to_fm(pw, lambda t: xln[:, 1024 * t:1024 * (t + 1)],
                            dst, NT, FC, T, 0, 512)
    pw.barrier("battnb")

    def load_W(name):
        return pw.dma("sync", T["wbuf3"], W[name].rearrange("(f p) o -> p f o", p=128),
                      sem="ld")

    stage = T["stage"]

    def mk_evac(kind, dsem):
        def evac(of, sp, ps, mmcnt, need_guard):
            it = sp * FC + of
            region = stage[:, 512 * (it % 4):512 * (it % 4) + 512]
            pw.wait("scalar", "mm", mmcnt)
            if need_guard:
                pw.wait("scalar", dsem)
            if kind == "q":
                c = pw.act(region, ps, AF.Identity, scale=0.125,
                           bias=T["vecs"][:, 96 + of:97 + of])
            else:
                c = pw.act(region, ps, AF.Identity, scale=0.125,
                           bias=T["vecs"][:, 104 + of:105 + of])
            return ("act", c, region)
        return evac

    ld = load_W("b_qw")
    emit_proj_fm(pw, T, ld, mk_evac("q", "bqs"),
                 lambda of, sp: G["g_bqT"][:, of, 512 * sp:512 * (sp + 1)], "bqs")
    ld = load_W("b_kw")
    emit_proj_fm(pw, T, ld, mk_evac("k", "bks"),
                 lambda of, sp: G["g_bkT"][:, of, 512 * sp:512 * (sp + 1)], "bks",
                 prev_sems=("bqs",))

    row_from_cols(pw, G, T["vecs"][:, 32:40], T["frow"][0:1, 0:1024], 128, 8, "bvr")
    pw.wait("tensor", "dve")
    for half in range(2):
        s = pw.mm(psum[:, 512 * half:512 * (half + 1)], T["ones_row"][0:1, :],
                  T["frow"][0:1, 512 * half:512 * (half + 1)], start=True, stop=True,
                  sem="mm" if half == 1 else None)
    pw.wait("vector", "mm", s)
    pw.copy(T["stage2"][:, 0:1024], psum[:, 0:1024], dep=False)

    ld = load_W("b_vw")
    pw.wait("gpsimd", "dve")
    mz = pw.memset(T["vstage"], 1.0)
    pw.wait("vector", "gp", mz)
    pw.wait("sync", "gp", mz)
    pw.wait("tensor", "ld", ld)
    pw.wait("vector", "bks")
    pw.wait("tensor", "dve")
    vdma = {}
    for mc in range(NT):
        for vh in range(2):
            bk = (mc * 2 + vh) % 4
            prev = pw._slot_state.get(512 * bk)
            if prev is not None:
                pw.wait("tensor", prev[0], prev[1])
            last = None
            for f in range(FC):
                last = pw.mm(psum[:, 512 * bk:512 * bk + 512],
                             A[:, 2048 * f + 128 * mc:2048 * f + 128 * (mc + 1)],
                             T["wbuf3"][:, f, 512 * vh:512 * (vh + 1)],
                             start=(f == 0), stop=(f == FC - 1),
                             sem="mm" if f == FC - 1 else None)
            pw.wait("vector", "mm", last)
            if vdma.get(mc % 2) is not None:
                pw.wait("vector", "bvs")
            dstv = T["vstage"][:, 1040 * (mc % 2) + 520 * vh:1040 * (mc % 2) + 520 * (vh + 1)] \
                .rearrange("p (h c) -> p h c", c=65)[:, :, 0:64]
            c = pw.stt(dstv, psum[:, 512 * bk:512 * bk + 512]
                       .rearrange("p (h c) -> p h c", c=64), 1.0,
                       T["stage2"][:, 512 * vh:512 * (vh + 1)]
                       .rearrange("p (h c) -> p h c", c=64), AluOpType.mult,
                       AluOpType.add, dep=False)
            pw._slot_state[512 * bk] = ("dve", c)
        pw.wait("sync", "dve")
        vdma[mc % 2] = pw.dma("sync", G["g_bv"][:, 1040 * mc:1040 * (mc + 1)],
                              T["vstage"][:, 1040 * (mc % 2):1040 * (mc % 2) + 1040],
                              sem="bvs")
    pw.wait("sync", "bvs")

    if dbg and dbg.get("upto") == "bproj":
        pw.barrier("dbgbp")
        pw.dma("sync", dbg["d_qT"].rearrange("(f p) t -> p f t", p=128), G["g_bqT"],
               sem="st")
        pw.dma("sync", dbg["d_kT"].rearrange("(f p) t -> p f t", p=128), G["g_bkT"],
               sem="st")
        pw.dma("sync", dbg["d_vext"], G["g_bv"], sem="st")
        pw.wait("sync", "st")
        return True

    ag_h = {}
    ld_hist = {}

    def load_feats(h):
        sm = "bh" + str(h % 2)
        for (buf, g) in ((CATQ, G["g_bqT"]), (CATK, G["g_bkT"])):
            pw.dma("sync", A[0:64, buf:buf + 2048],
                   g[64 * (h % 2):64 * (h % 2) + 64, h // 2, :], sem=sm)
            pw.dma("sync", A[64:128, buf:buf + 2048],
                   g[64 * (h % 2):64 * (h % 2) + 64, h // 2, :], sem=sm)
        lv = pw.dma("sync", BW[:, VXB:VXB + 1040].rearrange("p (tc c) -> p tc c", c=65),
                    G["g_bv"].rearrange("p (tc c) -> p tc c", c=1040)
                    [:, :, 65 * h:65 * (h + 1)], sem=sm)
        pw.wait("scalar", sm, lv)
        pw.act(A[64:128, CATQ:CATQ + 2048], A[64:128, CATQ:CATQ + 2048],
               AF.Square, dep=False)
        a2 = pw.act(A[64:128, CATK:CATK + 2048], A[64:128, CATK:CATK + 2048],
                    AF.Square, dep=False)
        ld_hist[h] = a2

    def features(h):
        qpb = QP0 if h % 2 == 0 else QP1
        pw.wait("tensor", "act", ld_hist[h])
        for (src2, dstc) in ((CATQ, qpb), (CATK, KPF)):
            for mc in range(2):
                for sp in range(4):
                    bk = 3 + (mc * 4 + sp) % 4
                    prev = pw._slot_state.get(512 * bk)
                    if prev is not None:
                        pw.wait("tensor", prev[0], prev[1])
                    mmv = pw.mm(psum[:, 512 * bk:512 * bk + 512],
                                T["projx2"][:, 128 * mc:128 * (mc + 1)],
                                A[:, src2 + 512 * sp:src2 + 512 * (sp + 1)],
                                start=True, stop=True, sem="mm")
                    pw.wait("scalar", "mm", mmv)
                    dst_ap = A[:, dstc + 2048 * mc + 512 * sp:
                               dstc + 2048 * mc + 512 * (sp + 1)]
                    ec = pw.act(dst_ap, psum[:, 512 * bk:512 * bk + 512], AF.Exp,
                                bias=-LN16, dep=False)
                    pw._slot_state[512 * bk] = ("act", ec)
        for tc in range(NT):
            bk = 3 + tc % 4
            prev = pw._slot_state.get(512 * bk)
            if prev is not None:
                pw.wait("tensor", prev[0], prev[1])
            mmv = pw.mm(psum[:, 512 * bk:512 * bk + 256],
                        A[:, CATK + 128 * tc:CATK + 128 * (tc + 1)],
                        T["projx2"][:, 0:256], start=True, stop=True, sem="mm")
            pw.wait("scalar", "mm", mmv)
            ec = pw.act(A[:, KPT + 256 * tc:KPT + 256 * (tc + 1)],
                        psum[:, 512 * bk:512 * bk + 256], AF.Exp, bias=-LN16,
                        dep=False)
            pw._slot_state[512 * bk] = ("act", ec)

    def scan(h):
        qpb = QP0 if h % 2 == 0 else QP1
        pw.wait("gpsimd", "act")
        pw.wait("gpsimd", "mm")
        for p_ in ("bst0", "bst1"):
            if p_ in pw.cnt:
                pw.wait("gpsimd", p_)
        z = pw.memset(BW[:, KVS:KVS + 130], 0.0)
        pw.wait("vector", "gp", z)
        pw.wait("tensor", "gp", z)
        pw.wait("scalar", "act")
        am_prev = None
        kv_last = None
        for c in range(NT):
            if am_prev is not None:
                pw.wait("tensor", "dve", am_prev)
            last = None
            for mc in range(2):
                last = pw.mm(psum[:, 512:512 + 128],
                             A[:, KPF + 2048 * mc + 128 * c:KPF + 2048 * mc + 128 * (c + 1)],
                             A[:, qpb + 2048 * mc + 128 * c:qpb + 2048 * mc + 128 * (c + 1)],
                             start=(mc == 0), stop=(mc == 1),
                             sem="mm" if mc == 1 else None)
            if c > 0:
                pw.wait("vector", "mm", kv_last)
                pw.copy(BW[:, KVS:KVS + 130], psum[:, 0:130], dep=False)
            pw.wait("vector", "mm", last)
            am = pw.tt(BW[:, ASB:ASB + 128], psum[:, 512:512 + 128], T["mask_ut"][:],
                       AluOpType.mult, dep=False)
            am_prev = am
            pw.wait("tensor", "dve", am)
            for mc in range(2):
                pw.mm(psum[:, 1024:1024 + 65],
                      A[:, qpb + 2048 * mc + 128 * c:qpb + 2048 * mc + 128 * (c + 1)],
                      BW[:, KVS + 65 * mc:KVS + 65 * (mc + 1)],
                      start=(mc == 0), stop=False)
            nm = pw.mm(psum[:, 1024:1024 + 65], BW[:, ASB:ASB + 128],
                       BW[:, VXB + 65 * c:VXB + 65 * (c + 1)],
                       start=False, stop=True, sem="mm")
            for mc in range(2):
                pw.mm(psum[:, 65 * mc:65 * (mc + 1)],
                      A[:, KPT + 256 * c + 128 * mc:KPT + 256 * c + 128 * (mc + 1)],
                      BW[:, VXB + 65 * c:VXB + 65 * (c + 1)],
                      start=(c == 0 and mc == 0), stop=(mc == 1),
                      sem="mm" if mc == 1 else None, skip=True)
            kv_last = pw.cnt["mm"]
            pw.wait("vector", "mm", nm)
            pw.copy(B[:, 1024 * c + 64 * h:1024 * c + 64 * (h + 1)],
                    psum[:, 1024:1024 + 64], dep=False)
            pw.copy(BW[:, DEN + 16 * (h % 2) + c:DEN + 16 * (h % 2) + c + 1],
                    psum[:, 1024 + 64:1024 + 65], dep=False)
        pw.wait("vector", "mm", kv_last)
        sv = pw.copy(BW[:, KVS:KVS + 130], psum[:, 0:130], dep=False)
        pw.wait("sync", "dve", sv)
        sd = pw.dma("sync", G["g_st_in"][h].rearrange("(mc p) c -> p mc c", p=128),
                    BW[:, KVS:KVS + 130].rearrange("p (mc c) -> p mc c", c=65),
                    sem="bst" + str(h % 2))
        pw.wait("gpsimd", "bst" + str(h % 2), sd)
        ag_h[h] = pw.coll("AllGather", AluOpType.bypass, groups,
                          [G["g_st_in"][h].opt()], [G["g_st"][h].opt()])
        if h + 1 < H:
            pw.wait("sync", "mm", kv_last)
            load_feats(h + 1)

    def finalize(h):
        qpb = QP0 if h % 2 == 0 else QP1
        pw.wait("sync", "gp", ag_h[h])
        pw.wait("sync", "mm")
        lv = pw.dma("sync", BW[:, STS:STS + 130].rearrange("p (mc c) -> p mc c", c=65),
                    G["g_st"][h][0].rearrange("(mc p) c -> p mc c", p=128),
                    sem="bfs" + str(h % 2))
        pw.wait("vector", "bfs" + str(h % 2), lv)
        mk = pw.ts(BW[:, STS:STS + 130], BW[:, STS:STS + 130], T["scal"][:, 1:2],
                   AluOpType.mult, dep=True)
        pw.wait("tensor", "dve", mk)
        for c in range(NT):
            bk = 2 + c % 2
            prev = pw._slot_state.get(512 * bk)
            if prev is not None:
                pw.wait("tensor", prev[0], prev[1])
            last = None
            for mc in range(2):
                last = pw.mm(psum[:, 512 * bk:512 * bk + 65],
                             A[:, qpb + 2048 * mc + 128 * c:qpb + 2048 * mc + 128 * (c + 1)],
                             BW[:, STS + 65 * mc:STS + 65 * (mc + 1)],
                             start=(mc == 0), stop=(mc == 1),
                             sem="mm" if mc == 1 else None)
            pw.wait("vector", "mm", last)
            pw.tt(B[:, 1024 * c + 64 * h:1024 * c + 64 * (h + 1)],
                  B[:, 1024 * c + 64 * h:1024 * c + 64 * (h + 1)],
                  psum[:, 512 * bk:512 * bk + 64], AluOpType.add, dep=False)
            pw.tt(stat[:, 6:7], BW[:, DEN + 16 * (h % 2) + c:DEN + 16 * (h % 2) + c + 1],
                  psum[:, 512 * bk + 64:512 * bk + 65], AluOpType.add)
            rv = pw.recip(stat[:, 7:8], stat[:, 6:7])
            dv = pw.ts(B[:, 1024 * c + 64 * h:1024 * c + 64 * (h + 1)],
                       B[:, 1024 * c + 64 * h:1024 * c + 64 * (h + 1)],
                       stat[:, 7:8], AluOpType.mult)
            pw._slot_state[512 * bk] = ("dve", dv)

    pw.wait("sync", "mm")
    pw.wait("sync", "dve")
    load_feats(0)
    for h in range(H):
        features(h)
        scan(h)
        if h >= 1:
            finalize(h - 1)
    finalize(H - 1)

    if dbg and dbg.get("upto") == "bheads":
        pw.barrier("dbgbh")
        pw.dma("sync", dbg["d_x1"], B[:, 0:16384].rearrange("p (t f) -> p t f", f=1024),
               sem="st")
        pw.dma("sync", dbg["d_den"], BW[:, DEN:DEN + 32], sem="st")
        pw.wait("sync", "st")
        return True

    pw.wait("tensor", "dve")

    def dsta(t, f):
        return A[:, 2048 * f + 128 * t:2048 * f + 128 * (t + 1)]
    emit_transpose_tm_to_fm(pw, lambda t: B[:, 1024 * t:1024 * (t + 1)],
                            dsta, NT, FC, T, 512 * 5, 512 * 6)
    pw.wait("sync", "mm")
    pw.wait("sync", "dve")
    ldw = load_W("b_ow")
    GBC2 = 8192
    RES = 10240
    row_from_cols(pw, G, T["vecs"][:, 48:56], T["frow"][0:1, 0:1024], 128, 8, "bgr")
    row_from_cols(pw, G, T["vecs"][:, 56:64], T["frow"][0:1, 1024:2048], 128, 8, "bbr")
    pw.wait("tensor", "dve")
    for half in range(4):
        s = pw.mm(psum[:, 512 * half:512 * (half + 1)], T["ones_row"][0:1, :],
                  T["frow"][0:1, 512 * half:512 * (half + 1)], start=True, stop=True,
                  sem="mm" if half == 3 else None)
    pw.wait("vector", "mm", s)
    gv = pw.copy(B[:, GBC2:GBC2 + 2048], psum[:, 0:2048], dep=False)
    pw.wait("tensor", "dve", gv)
    pw.wait("tensor", "ld", ldw)

    for t in range(NT):
        emit_ln_tile(pw, X[:, t, :], B[:, RES:RES + 1024], tmp[:, 0:1024], stat,
                     LN_EPS)
        for of in range(FC):
            bk = of % 2
            prev = pw._slot_state.get(("eo", bk))
            if prev is not None:
                pw.wait("tensor", prev[0], prev[1])
            last = None
            for f in range(FC):
                last = pw.mm(psum[:, 512 * bk:512 * bk + 128],
                             T["wbuf3"][:, f, 128 * of:128 * of + 128],
                             A[:, 2048 * f + 128 * t:2048 * f + 128 * (t + 1)],
                             start=(f == 0), stop=(f == FC - 1),
                             sem="mm" if f == FC - 1 else None)
            pw.wait("vector", "mm", last)
            ec2 = pw.copy(B[:, 11264 + 128 * bk:11264 + 128 * (bk + 1)],
                          psum[:, 512 * bk:512 * bk + 128], dep=False)
            pw.wait("tensor", "dve", ec2)
            tb = pw.mm(psum[:, 512 * (2 + bk):512 * (2 + bk) + 128],
                       B[:, 11264 + 128 * bk:11264 + 128 * (bk + 1)], T["ident"][:],
                       start=True, stop=True, sem="mm")
            pw.wait("vector", "mm", tb)
            ev = pw.stt(B[:, RES + 128 * of:RES + 128 * (of + 1)],
                        psum[:, 512 * (2 + bk):512 * (2 + bk) + 128],
                        T["vecs"][:, 40 + of:41 + of],
                        B[:, RES + 128 * of:RES + 128 * (of + 1)],
                        AluOpType.add, AluOpType.add, dep=True)
            pw._slot_state[("eo", bk)] = ("dve", ev)
        pw.red(stat[:, 0:1], B[:, RES:RES + 1024], AluOpType.add)
        pw.ts(stat[:, 1:2], stat[:, 0:1], 1.0 / D, AluOpType.mult)
        v0 = pw.ts(tmp[:, 0:1024], B[:, RES:RES + 1024], stat[:, 1:2],
                   AluOpType.subtract)
        pw.wait("scalar", "dve", v0)
        aq = pw.act(B[:, RES:RES + 1024], tmp[:, 0:1024], AF.Square, dep=False)
        pw.wait("vector", "act", aq)
        pw.red(stat[:, 2:3], B[:, RES:RES + 1024], AluOpType.add)
        v1 = pw.ts(stat[:, 3:4], stat[:, 2:3], 1.0 / D, AluOpType.mult,
                   LNB_EPS, AluOpType.add)
        pw.wait("scalar", "dve", v1)
        a1 = pw.act(stat[:, 4:5], stat[:, 3:4], AF.Sqrt, dep=False)
        pw.wait("vector", "act", a1)
        pw.recip(stat[:, 5:6], stat[:, 4:5])
        pw.ts(tmp[:, 0:1024], tmp[:, 0:1024], stat[:, 5:6], AluOpType.mult)
        pw.tt(tmp[:, 0:1024], tmp[:, 0:1024], B[:, GBC2:GBC2 + 1024],
              AluOpType.mult)
        pw.tt(tmp[:, 0:1024], tmp[:, 0:1024], B[:, GBC2 + 1024:GBC2 + 2048],
              AluOpType.add)
        pw.tt(X[:, t, :], X[:, t, :], tmp[:, 0:1024], AluOpType.add)

    if dbg and dbg.get("upto") == "attnb":
        pw.barrier("dbgattnb")
        pw.dma("sync", dbg["d_x1"], X[:, :, :], sem="st")
        return True
    return False


def ph_store(pw, T, y_out):
    pw.barrier("final")
    v = pw.dma("sync", y_out.rearrange("(t p) f -> p t f", p=128), T["X"][:, :, :],
               sem="yst")
    pw.wait("sync", "yst", v)



def build_nc(debug_upto=None):
    nc = bass.Bass("TRN2", target_bir_lowering=False)
    x_in = nc.dram_tensor("x0", [TOK, D], F32, kind="ExternalInput").ap()
    W = {}
    for n, s in [("aWq", [D, D]), ("aWk", [D, D]), ("aWv", [D, D]),
                 ("aWproj", [D, D]), ("aWpre", [D, D]),
                 ("b_qw", [D, D]), ("b_kw", [D, D]), ("b_vw", [D, D]),
                 ("b_ow", [D, D]), ("f_w1", [DHID, D]), ("f_w2", [D, DHID])]:
        W[n] = nc.dram_tensor(n, s, F32, kind="ExternalInput").ap()
    VEC = {}
    for n, s in [("a_bproj", [D]), ("a_bpre", [D]), ("b_qb", [D]), ("b_kb", [D]),
                 ("b_vb", [D]), ("b_ob", [D]), ("b_gamma", [D]), ("b_beta", [D]),
                 ("f_convb", [DHID]),
                 ("pi_bc", [128, 1]), ("isodd_bc", [128, 1]),
                 ("mleft_bc", [128, 1]), ("mright_bc", [128, 1])]:
        VEC[n] = nc.dram_tensor(n, s, F32, kind="ExternalInput").ap()
    b_proj = nc.dram_tensor("b_proj", [M, DH], F32, kind="ExternalInput").ap()
    f_convw = nc.dram_tensor("f_convw", [DHID, 3], F32, kind="ExternalInput").ap()
    y_out = nc.dram_tensor("y", [TOK, D], F32, kind="ExternalOutput").ap()

    G = {}
    span_tensors = []
    for sp in range(4):
        span_tensors += [(f"g_pre_in_s{sp}", [D, 512]), (f"g_pre_s{sp}", [2, D, 512]),
                         (f"g_k_in_s{sp}", [D, 512]), (f"g_k_s{sp}", [2, D, 512]),
                         (f"g_v_in_s{sp}", [128, 4 * 1040]),
                         (f"g_v_s{sp}", [2, 128, 4 * 1040])]
    for n, s in span_tensors + [
                 ("g_qT", [128, FC, TOK]), ("g_pqT", [128, FC, TOK]),
                 ("g_halo_in", [2, D]), ("g_halo", [2, 2, D]),
                 ("g_bqT", [128, FC, TOK]), ("g_bkT", [128, FC, TOK]),
                 ("g_bv", [128, NT * 1040]),
                 ("g_st_in", [H, 256, 65]), ("g_st", [H, 2, 256, 65])]:
        G[n] = nc.dram_tensor(n, s, F32).ap()
    G["g_t1T"] = nc.dram_tensor("g_t1T", [D, DHID], BF16).ap()
    G["g_t2T"] = nc.dram_tensor("g_t2T", [DHID, D], BF16).ap()
    G["g_hact"] = nc.dram_tensor("g_hact", [DHID, TOK], F32).ap()
    G["g_lg"] = nc.dram_tensor("g_lg", [32, 128, TOK], F32).ap()
    G["g_fr"] = nc.dram_tensor("g_fr", [128, 64], F32).ap()
    G["g_nh"] = nc.dram_tensor("g_nh", [DHID, TOK], BF16).ap()

    dbg = None
    if debug_upto is not None:
        dbg = {"upto": debug_upto}
        for n, s in [("d_xln", [128, NT, D]), ("d_qT", [D, TOK]), ("d_kT", [D, TOK]),
                     ("d_preT", [D, TOK]), ("d_pqT", [D, TOK]),
                     ("d_vext", [128, NT * 1040]), ("d_gpre", [2 * D, TOK]),
                     ("d_x1", [128, NT, D]), ("d_attn", [8, 128, 2 * D]),
                     ("d_logit", [8, 128, 256]), ("d_den", [128, 32])]:
            dbg[n] = nc.dram_tensor(n, s, F32, kind="ExternalOutput").ap()

    with contextlib.ExitStack() as stack:
        block = stack.enter_context(nc.Block())
        pw = PW(nc, stack)
        pw._slot_state = {}

        T = {}
        T["groups"] = [[0, 1], [2, 3], [4, 5], [6, 7]]
        T["X"] = stack.enter_context(nc.sbuf_tensor("X", [128, NT, D], F32))
        T["ARN_A"] = stack.enter_context(nc.sbuf_tensor("ARN_A", [128, 16384], F32))
        T["ARN_B"] = stack.enter_context(nc.sbuf_tensor("ARN_B", [128, 16384], F32))
        T["tmpA"] = stack.enter_context(nc.sbuf_tensor("tmpA", [128, 1024], F32))
        T["stat"] = stack.enter_context(nc.sbuf_tensor("stat", [128, 8], F32))
        T["fcols"] = stack.enter_context(nc.sbuf_tensor("fcols", [128, 64], F32))
        T["ones_col"] = stack.enter_context(nc.sbuf_tensor("ones_col", [128, 1], F32))
        T["BWK"] = stack.enter_context(nc.sbuf_tensor("BWK", [128, 1472], F32))
        T["ones_row"] = stack.enter_context(nc.sbuf_tensor("ones_row", [128, 128], F32))
        T["shift64"] = stack.enter_context(nc.sbuf_tensor("shift64", [128, 128], F32))
        arnb = T["ARN_B"]
        T["wbuf3"] = arnb[:, 0:8192].rearrange("p (f o) -> p f o", o=1024)
        T["stage"] = arnb[:, 8192:10240]
        T["stage2"] = arnb[:, 10240:12288]
        T["vstage"] = arnb[:, 12288:14368]
        T["hx"] = arnb[:, 0:4100].rearrange("p (a b) -> p a b", a=2)
        T["tstat"] = arnb[:, 4104:5128].bitcast(BF16).rearrange(
            "p (a b c) -> p a b c", a=2, b=8)
        T["tstat4"] = arnb[:, 4104:6152].bitcast(BF16).rearrange(
            "p (a b c) -> p a b c", a=4, b=8)
        T["nstat"] = arnb[:, 5128:5640].bitcast(BF16).rearrange(
            "p (a b) -> p a b", a=2)
        T["nstat4"] = arnb[:, 6152:7176].bitcast(BF16).rearrange(
            "p (a b) -> p a b", a=4)
        T["nh2"] = arnb[:, 5640:6664].bitcast(BF16)
        T["z16w"] = arnb[:, 6664:7688].bitcast(F16)
        T["haloh"] = arnb[:, 7688:7752]
        T["tbf"] = arnb[:, 12288:13312].bitcast(BF16)
        T["frow"] = arnb[0:1, 13312:15372]
        T["halo2"] = T["stage"][0:2, 0:1024]
        T["halo2b"] = T["stage"][0:2, 1024:2048]
        arna = T["ARN_A"]
        T["nbfT"] = arna[:, 0:8192].bitcast(BF16).rearrange(
            "p (a b) -> p a b", a=8)
        T["ofm"] = arna[:, 0:2048].rearrange("p (a b) -> p a b", a=4)
        T["z16"] = arna[:, 15360:15872].bitcast(F16)
        T["nhT"] = arna[:, 15872:15880].bitcast(BF16).rearrange(
            "p (a b) -> p a b", a=8)
        T["t2s"] = arna[:, 0:8192].bitcast(BF16).rearrange(
            "p (a b) -> p a b", a=32)
        T["ofm2"] = arna[:, 11264:13312].rearrange("p (a b) -> p a b", a=4)
        T["tmp16"] = T["tmpA"][:, 0:512].bitcast(F16)
        T["z16h"] = T["z16"][0:2, :]
        T["f2row"] = arna[0:1, 2048:4096]
        T["ident"] = stack.enter_context(nc.sbuf_tensor("ident", [128, 128], F32))
        T["identb"] = stack.enter_context(nc.sbuf_tensor("identb", [128, 128], BF16))
        T["mask_ut"] = stack.enter_context(nc.sbuf_tensor("mask_ut", [128, 128], F32))
        T["consts"] = stack.enter_context(nc.sbuf_tensor("consts", [128, 8], F32))
        T["vecs"] = stack.enter_context(nc.sbuf_tensor("vecs", [128, 112], F32))
        T["scal"] = stack.enter_context(nc.sbuf_tensor("scal", [128, 16], F32))
        T["convw"] = stack.enter_context(nc.sbuf_tensor("convw", [128, NJC, 3], F32))
        T["convb"] = stack.enter_context(nc.sbuf_tensor("convb", [128, NJC], F32))
        T["projx2"] = stack.enter_context(nc.sbuf_tensor("projx2", [128, M], F32))
        T["psum"] = stack.enter_context(nc.psum_tensor("ps", [128, 3584], F32))
        T["psumb"] = stack.enter_context(nc.psum_tensor("psb", [128, 1024], BF16))

        T["cvals"] = [0.0, 1.0, 0.125, -LN16, 1536.0, -1536.0, 0.5, -128.0]
        for i, v in enumerate(T["cvals"]):
            nc.const_aps.register(F32, float(v), T["consts"][:, i:i + 1])

        ph_setup(pw, T, x_in, VEC, b_proj, f_convw)
        stop = ph_attn_a_proj(pw, T, W, G, dbg)
        if not stop:
            ph_attn_a_logits(pw, T, G, dbg)
            arnb_ = T["ARN_B"]
            wmbuf = [arnb_[:, 12288:14336].rearrange("p (a b) -> p a b", a=2),
                     arnb_[:, 14336:16384].rearrange("p (a b) -> p a b", a=2)]

            def wm_src(L):
                if L < 16:
                    return W["f_w1"].rearrange("(a p) d -> p a d", p=128)[
                        :, 2 * L:2 * L + 2, :]
                Lp = L - 16
                return W["f_w2"].rearrange("(a p) d -> p a d", p=128)[
                    :, Lp // 2:Lp // 2 + 1,
                    2048 * (Lp % 2):2048 * (Lp % 2) + 2048].rearrange(
                    "p a (b c) -> p (a b) c", c=1024)

            wm_state = {"k": 0}
            wm_pml = {}
            wm_red = {}

            def wm_load(L):
                if L >= 32:
                    return
                if L >= 2:
                    pw.wait("gpsimd", "dve", wm_red[2 * L - 3])
                wm_pml[L] = pw.dma("gpsimd", wmbuf[L % 2], wm_src(L),
                                   sem="pm" + str(L % 2))

            pw.wait("gpsimd", "qst")
            wm_load(0)
            wm_load(1)

            def whook():
                k = wm_state["k"]
                if k >= 64:
                    return
                wm_state["k"] = k + 1
                L = k // 2
                pw.wait("vector", "pm" + str(L % 2), wm_pml[L])
                wm_red[k] = pw.red(T["fcols"][:, k:k + 1],
                                   wmbuf[L % 2][:, k % 2, :],
                                   AluOpType.add, absval=True, dep=False)
                if k % 2 == 1:
                    wm_load(L + 2)

            stop = ph_attn_a_attn(pw, T, W, G, dbg, whook)
        if not stop:
            pw.barrier("b1")
            ph_ffn_weights(pw, T, W, G)
            pw.barrier("b2")
            stop = ph_ffn(pw, T, W, G, dbg, 1)
        if not stop:
            pw.barrier("b3")
            stop = ph_attn_b(pw, T, W, G, dbg)
        if not stop:
            pw.barrier("b4")
            stop = ph_ffn(pw, T, W, G, dbg, 2)
        if not stop:
            ph_store(pw, T, y_out)

        pw.wait("sync", "st") if "st" in pw.cnt else None
        pw.replay(block)
    return nc


def _shard_inputs(inputs):
    x = np.ascontiguousarray(np.asarray(inputs["L0_attn_a_x"], dtype=np.float32))
    maps = []
    for c in range(8):
        b, h = c // 2, c % 2
        m = {"x0": np.ascontiguousarray(x[b, 2048 * h:2048 * (h + 1), :])}
        m["aWq"] = inputs["L0_attn_a_Wq"]
        m["aWk"] = inputs["L0_attn_a_Wk"]
        m["aWv"] = inputs["L0_attn_a_Wv"]
        m["aWproj"] = inputs["L0_attn_a_Wproj"]
        m["aWpre"] = inputs["L0_attn_a_Wpre"]
        m["b_qw"] = inputs["L0_attn_b_q_w"]
        m["b_kw"] = inputs["L0_attn_b_k_w"]
        m["b_vw"] = inputs["L0_attn_b_v_w"]
        m["b_ow"] = inputs["L0_attn_b_out_w"]
        m["f_w1"] = inputs["L0_ffn_w1"]
        m["f_w2"] = inputs["L0_ffn_w2"]
        m["a_bproj"] = inputs["L0_attn_a_bproj"]
        m["a_bpre"] = inputs["L0_attn_a_bpre"]
        m["b_qb"] = inputs["L0_attn_b_q_b"]
        m["b_kb"] = inputs["L0_attn_b_k_b"]
        m["b_vb"] = inputs["L0_attn_b_v_b"]
        m["b_ob"] = inputs["L0_attn_b_out_b"]
        m["b_gamma"] = inputs["L0_attn_b_gamma"]
        m["b_beta"] = inputs["L0_attn_b_beta"]
        m["f_convb"] = inputs["L0_ffn_conv_b"]
        m["b_proj"] = inputs["L0_attn_b_proj"]
        m["f_convw"] = np.ascontiguousarray(
            np.asarray(inputs["L0_ffn_conv_w"], dtype=np.float32)[:, 0, :])
        m["pi_bc"] = np.ascontiguousarray(
            np.broadcast_to(np.asarray(inputs["L0_attn_a_pi"], np.float32), (128, 1)))
        m["isodd_bc"] = np.full((128, 1), float(h), np.float32)
        m["mleft_bc"] = np.full((128, 1), float(h), np.float32)
        m["mright_bc"] = np.full((128, 1), float(1 - h), np.float32)
        m = {k: np.ascontiguousarray(np.asarray(v, dtype=np.float32)) if k not in ()
             else v for k, v in m.items()}
        maps.append(m)
    return maps


def kernel(**inputs):
    nc = build_nc()
    maps = _shard_inputs(inputs)
    res = run_bass_kernel_spmd(nc, maps, core_ids=list(range(8)))
    out = np.empty((4, 4096, 1024), np.float32)
    for c in range(8):
        b, h = c // 2, c % 2
        out[b, 2048 * h:2048 * (h + 1), :] = res.results[c]["y"]
    return out


# version 11
# speedup vs baseline: 1.1863x; 1.0057x over previous
import contextlib
import numpy as np
import concourse.bass as bass
import concourse.mybir as mybir
from concourse.bass_utils import run_bass_kernel_spmd
from concourse.alu_op_type import AluOpType

F32 = mybir.dt.float32
F16 = mybir.dt.float16
BF16 = mybir.dt.bfloat16
AF = mybir.ActivationFunctionType
AX = mybir.AxisListType

TOK = 2048
NT = 16
D = 1024
FC = 8
H = 16
DH = 64
M = 256
DHID = 4096
NJC = 32
LN_EPS = 1e-5
LNB_EPS = 1e-6
LN16 = float(np.log(np.float64(16.0)))

ENGS = ("sync", "tensor", "vector", "scalar", "gpsimd")


class PW:

    def __init__(self, nc, stack):
        self.nc = nc
        self.stack = stack
        self.ops = {e: [] for e in ENGS}
        self.cnt = {}
        self.sems = {}

    def sem(self, name):
        if name not in self.sems:
            self.sems[name] = self.stack.enter_context(self.nc.semaphore(name))
            self.cnt[name] = 0
        return self.sems[name]

    def emit(self, eng, fn):
        self.ops[eng].append(fn)

    def wait(self, eng, name, val=None):
        s, v = self.sem(name), (self.cnt[name] if val is None else val)
        self.emit(eng, lambda e: e.wait_ge(s, v))

    def dma(self, eng, out, in_, sem=None, slow=False):
        kw = {"allow_slow_non_contiguous": True} if slow else {}
        if sem is None:
            self.emit(eng, lambda e: e.dma_start(out=out, in_=in_, **kw))
            return None
        s = self.sem(sem)
        self.cnt[sem] += 16
        self.emit(eng, lambda e: e.dma_start(out=out, in_=in_, **kw).then_inc(s, 16))
        return self.cnt[sem]

    AUTO = {"vector": "dve", "scalar": "act", "gpsimd": "gp"}

    def _op(self, eng, fn, sem=None, n=1, dep=True):
        auto = self.AUTO.get(eng)
        if auto is not None:
            s = self.sem(auto)
            if dep and self.cnt[auto] > 0:
                v = self.cnt[auto]
                self.emit(eng, lambda e: e.wait_ge(s, v))
            self.cnt[auto] += 1
            self.emit(eng, lambda e: fn(e).then_inc(s, 1))
            return self.cnt[auto]
        if sem is None:
            self.emit(eng, fn)
            return None
        s = self.sem(sem)
        self.cnt[sem] += n
        self.emit(eng, lambda e: fn(e).then_inc(s, n))
        return self.cnt[sem]

    def mm(self, out, lhsT, rhs, start, stop, sem=None, skip=False):
        return self._op("tensor",
                        lambda e: e.matmul(out, lhsT, rhs, start=start, stop=stop,
                                           skip_group_check=skip),
                        sem)

    def trans(self, out, in_, ident, sem=None):
        return self._op("tensor", lambda e: e.transpose(out, in_, ident), sem)

    def mmT(self, out, data, ident, start, stop, sem=None):
        return self._op("tensor",
                        lambda e: e.matmul(out, data, ident, start=start, stop=stop,
                                           is_transpose=True), sem)

    def act(self, out, in_, func, bias=None, scale=None, accum_out=None, sem=None,
            dep=True):
        kw = {}
        if bias is not None:
            kw["bias"] = bias
        if scale is not None:
            kw["scale"] = scale
        if accum_out is not None:
            kw["accum_out"] = accum_out
        return self._op("scalar",
                        lambda e: e.activation(out=out, in_=in_, func=func, **kw),
                        sem, dep=dep)

    def ts(self, out, in0, s1, op0, s2=None, op1=None, sem=None, eng="vector",
           dep=True):
        return self._op(eng,
                        lambda e: e.tensor_scalar(out=out, in0=in0, scalar1=s1,
                                                  scalar2=s2, op0=op0,
                                                  **({"op1": op1} if op1 is not None else {})),
                        sem, dep=dep)

    def tt(self, out, in0, in1, op, sem=None, eng="vector", dep=True):
        return self._op(eng,
                        lambda e: e.tensor_tensor(out=out, in0=in0, in1=in1, op=op),
                        sem, dep=dep)

    def stt(self, out, in0, scalar, in1, op0, op1, sem=None, dep=True):
        return self._op("vector",
                        lambda e: e.scalar_tensor_tensor(out=out, in0=in0, scalar=scalar,
                                                         in1=in1, op0=op0, op1=op1),
                        sem, dep=dep)

    def copy(self, out, in_, sem=None, eng="vector", dep=True):
        return self._op(eng, lambda e: e.tensor_copy(out, in_), sem, dep=dep)

    def red(self, out, in_, op, absval=None, sem=None, negate=None, dep=True):
        return self._op("vector",
                        lambda e: e.tensor_reduce(out=out, in_=in_, axis=AX.X, op=op,
                                                  apply_absolute_value=absval,
                                                  negate=negate), sem, dep=dep)

    def recip(self, out, in_, sem=None, dep=True):
        return self._op("vector", lambda e: e.reciprocal(out=out, in_=in_), sem,
                        dep=dep)

    def memset(self, ap, c, sem=None, dep=True):
        return self._op("gpsimd", lambda e: e.memset(ap, c), sem, dep=dep)

    def coll(self, kind, op, groups, ins, outs, sem=None, dep=True):
        return self._op("gpsimd",
                        lambda e: e.collective_compute(kind, op, replica_groups=groups,
                                                       ins=ins, outs=outs), sem,
                        dep=dep)

    def barrier(self, tag):
        vals = dict(self.cnt)
        for eng in ENGS:
            for name, v in vals.items():
                if v > 0 and not name.startswith("_"):
                    self.wait(eng, name, v)

    def replay(self, block):
        for eng in ENGS:
            ops = self.ops[eng]
            dec = getattr(block, eng)

            def make(fn_list):
                def body(e):
                    for f in fn_list:
                        f(e)
                return body
            dec(make(ops))



def ph_setup(pw, T, x_in, VEC, b_proj, f_convw):
    nc = pw.nc
    pw.dma("sync", T["X"][:, :, :], x_in.rearrange("(t p) f -> p t f", p=128), sem="ld")
    for col, name in enumerate(["a_bproj", "a_bpre", "b_qb", "b_kb", "b_vb",
                                "b_ob", "b_gamma", "b_beta"]):
        pw.dma("sync", T["vecs"][:, 8 * col:8 * col + 8],
               VEC[name].rearrange("(f p) -> p f", p=128), sem="ld", slow=True)
    pw.dma("sync", T["scal"][:, 0:1], VEC["pi_bc"], sem="ld")
    pw.dma("sync", T["scal"][:, 1:2], VEC["isodd_bc"], sem="ld")
    pw.dma("sync", T["scal"][:, 2:3], VEC["mleft_bc"], sem="ld")
    pw.dma("sync", T["scal"][:, 3:4], VEC["mright_bc"], sem="ld")
    pw.dma("sync", T["convw"][:, :, :], f_convw.rearrange("(c p) k -> p c k", p=128), sem="ld", slow=True)
    pw.dma("sync", T["convb"][:], VEC["f_convb"].rearrange("(c p) -> p c", p=128), sem="ld", slow=True)
    pw.dma("sync", T["projx2"][0:64, :], b_proj.rearrange("m d -> d m"), sem="ld", slow=True)

    cvals = T["cvals"]
    for i, v in enumerate(cvals):
        pw.memset(T["consts"][:, i:i + 1], float(v))
    pw.memset(T["ident"][:], 1.0)
    pw._op("gpsimd", lambda e: e.affine_select(
        out=T["ident"][:], in_=T["ident"][:], compare_op=AluOpType.is_equal,
        fill=0.0, base=0, pattern=[[-1, 128]], channel_multiplier=1))
    pw.memset(T["identb"][:], 1.0)
    pw._op("gpsimd", lambda e: e.affine_select(
        out=T["identb"][:], in_=T["identb"][:], compare_op=AluOpType.is_equal,
        fill=0.0, base=0, pattern=[[-1, 128]], channel_multiplier=1))
    pw.memset(T["mask_ut"][:], 1.0)
    pw._op("gpsimd", lambda e: e.affine_select(
        out=T["mask_ut"][:], in_=T["mask_ut"][:], compare_op=AluOpType.is_ge,
        fill=0.0, base=0, pattern=[[1, 128]], channel_multiplier=-1))
    pw.memset(T["ones_col"][:], 1.0)
    pw.memset(T["ones_row"][:], 1.0)
    pw.memset(T["shift64"][:], 1.0)
    pw._op("gpsimd", lambda e: e.affine_select(
        out=T["shift64"][:], in_=T["shift64"][:], compare_op=AluOpType.is_equal,
        fill=0.0, base=-64, pattern=[[1, 128]], channel_multiplier=-1))
    pw.memset(T["projx2"][64:128, :], -0.5, sem="gp")
    pw.wait("vector", "ld")
    pw.wait("vector", "gp")
    pw.ts(T["scal"][:, 4:5], T["scal"][:, 0:1], 1.0 / 32.0, AluOpType.mult)
    pw.ts(T["vecs"][:, 96:104], T["vecs"][:, 16:24], 0.125, AluOpType.mult)
    pw.ts(T["vecs"][:, 104:112], T["vecs"][:, 24:32], 0.125, AluOpType.mult,
          sem="dve")


def emit_ln_tile(pw, x_tile, out_tile, tmp, stat, eps):
    pw.red(stat[:, 0:1], x_tile, AluOpType.add)
    pw.ts(stat[:, 1:2], stat[:, 0:1], 1.0 / D, AluOpType.mult)
    pw.ts(tmp, x_tile, stat[:, 1:2], AluOpType.subtract)
    pw.tt(out_tile, tmp, tmp, AluOpType.mult)
    pw.red(stat[:, 0:1], out_tile, AluOpType.add)
    v1 = pw.ts(stat[:, 2:3], stat[:, 0:1], 1.0 / D, AluOpType.mult, eps,
               AluOpType.add, sem="dve")
    pw.wait("scalar", "dve", v1)
    a1 = pw.act(stat[:, 2:3], stat[:, 2:3], AF.Sqrt, sem="act")
    pw.wait("vector", "act", a1)
    pw.recip(stat[:, 3:4], stat[:, 2:3])
    return pw.ts(out_tile, tmp, stat[:, 3:4], AluOpType.mult, sem="dve")


def emit_transpose_tm_to_fm(pw, src_tile_fn, dst_fn, nt_tiles, nfc, T, bankA, bankB,
                            dtype_ident="ident"):
    ident = T[dtype_ident][:]
    psum = T["psum"]
    slots = [bankA, bankB]
    pending = []
    mmc0 = pw.cnt.get("mm", 0)
    for t in range(nt_tiles):
        src = src_tile_fn(t)
        for fb in range(0, nfc, 4):
            nf = min(4, nfc - fb)
            slot = slots[(t * ((nfc + 3) // 4) + fb // 4) % 2]
            key = slot
            prev = pw._slot_state.get(key)
            if prev is not None:
                pw.wait("tensor", prev[0], prev[1])
            last = None
            for j in range(nf):
                last = pw.mm(psum[:, slot + 128 * j:slot + 128 * (j + 1)],
                             src[:, 128 * (fb + j):128 * (fb + j + 1)], ident,
                             start=(j == 0), stop=(j == nf - 1),
                             sem="mm" if j == nf - 1 else None)
            pw.wait("vector", "mm", last)
            for j in range(nf):
                s = pw.copy(dst_fn(t, fb + j),
                            psum[:, slot + 128 * j:slot + 128 * (j + 1)],
                            sem="dve" if j == nf - 1 else None)
            pw._slot_state[key] = ("dve", s)


def emit_proj_fm(pw, T, ld_done, evac, dma_dst, dma_sem, span_hook=None,
                 rhs_buf=None, prev_sems=()):
    psum = T["psum"]
    xlnT = T["ARN_A"] if rhs_buf is None else rhs_buf
    pw.wait("tensor", "ld", ld_done)
    for s_ in prev_sems:
        if s_ in pw.cnt:
            pw.wait("scalar", s_)
            pw.wait("vector", s_)
    dma_hist = {}
    for sp in range(4):
        for of in range(FC):
            it = sp * FC + of
            bk = it % 6
            prev = pw._slot_state.get(("pj", bk))
            if prev is not None:
                pw.wait("tensor", prev[0], prev[1])
            last = None
            for f in range(FC):
                last = pw.mm(psum[:, 512 * bk:512 * bk + 512],
                             T["wbuf3"][:, f, 128 * of:128 * of + 128],
                             xlnT[:, 2048 * f + 512 * sp:2048 * f + 512 * (sp + 1)],
                             start=(f == 0), stop=(f == FC - 1),
                             sem="mm" if f == FC - 1 else None)
            eng, cnt, region = evac(of, sp, psum[:, 512 * bk:512 * bk + 512],
                                    last, it >= 4)
            pw._slot_state[("pj", bk)] = (eng, cnt)
            pw.wait("sync", eng, cnt)
            dma_hist[it % 4] = pw.dma("sync", dma_dst(of, sp), region, sem=dma_sem)
        if span_hook is not None:
            span_hook(sp)


def ph_attn_a_proj(pw, T, W, G, dbg):
    X, ARN_A, ARN_B = T["X"], T["ARN_A"], T["ARN_B"]
    psum, stage = T["psum"], T["stage"]
    groups = T["groups"]

    xln = ARN_B
    tmp, stat = T["tmpA"], T["stat"]
    pw.wait("vector", "ld")
    for t in range(NT):
        emit_ln_tile(pw, X[:, t, :], xln[:, 1024 * t:1024 * (t + 1)], tmp[:, 0:1024],
                     stat, LN_EPS)
    pw.wait("tensor", "dve")
    pw.wait("tensor", "gp")
    xlnT = ARN_A

    def dst(t, f):
        return xlnT[:, 2048 * f + 128 * t:2048 * f + 128 * (t + 1)]
    emit_transpose_tm_to_fm(pw, lambda t: xln[:, 1024 * t:1024 * (t + 1)],
                            dst, NT, FC, T, 0, 512)
    if dbg and dbg.get("upto") == "ln1":
        pw.barrier("dbgln")
        pw.dma("sync", dbg["d_xln"], xln.rearrange("p (t f) -> p t f", f=1024),
               sem="st")
        return True
    pw.barrier("projstart")

    def load_W(name):
        return pw.dma("sync", T["wbuf3"], W[name].rearrange("(f p) o -> p f o", p=128),
                      sem="ld")

    ag_done = {}

    def mk_evac(kind, dsem):
        def evac(of, sp, ps, mmcnt, need_guard):
            it = sp * FC + of
            region = stage[:, 512 * (it % 4):512 * (it % 4) + 512]
            pw.wait("scalar", "mm", mmcnt)
            if need_guard:
                pw.wait("scalar", dsem)
            if kind == "silu_bpre":
                c = pw.act(region, ps, AF.Silu, bias=T["vecs"][:, 8 + of:9 + of])
            elif kind == "copy":
                c = pw.act(region, ps, AF.Copy)
            elif kind == "q8":
                c = pw.act(region, ps, AF.Copy, scale=0.125)
            return ("act", c, region)
        return evac

    ld = load_W("aWpre")

    def hook_pre(sp):
        pw.wait("gpsimd", "pst")
        ag_done[("pre", sp)] = pw.coll(
            "AllGather", AluOpType.bypass, groups,
            [G[f"g_pre_in_s{sp}"].opt()], [G[f"g_pre_s{sp}"].opt()])
    emit_proj_fm(pw, T, ld, mk_evac("silu_bpre", "pst"),
                 lambda of, sp: G[f"g_pre_in_s{sp}"][128 * of:128 * (of + 1), :],
                 "pst", hook_pre)

    ld = load_W("aWk")

    def hook_k(sp):
        pw.wait("gpsimd", "kst")
        ag_done[("k", sp)] = pw.coll(
            "AllGather", AluOpType.bypass, groups,
            [G[f"g_k_in_s{sp}"].opt()], [G[f"g_k_s{sp}"].opt()])
    emit_proj_fm(pw, T, ld, mk_evac("copy", "kst"),
                 lambda of, sp: G[f"g_k_in_s{sp}"][128 * of:128 * (of + 1), :],
                 "kst", hook_k, prev_sems=("pst",))

    ld = load_W("aWv")
    pw.memset(T["vstage"], 1.0)
    pw.wait("vector", "gp")
    pw.wait("tensor", "ld", ld)
    vdma = {}
    for mc in range(NT):
        for vh in range(2):
            bk = (mc * 2 + vh) % 6
            prev = pw._slot_state.get(("pj", bk))
            if prev is not None:
                pw.wait("tensor", prev[0], prev[1])
            last = None
            for f in range(FC):
                last = pw.mm(psum[:, 512 * bk:512 * bk + 512],
                             T["ARN_A"][:, 2048 * f + 128 * mc:2048 * f + 128 * (mc + 1)],
                             T["wbuf3"][:, f, 512 * vh:512 * (vh + 1)],
                             start=(f == 0), stop=(f == FC - 1),
                             sem="mm" if f == FC - 1 else None)
            pw.wait("vector", "mm", last)
            if vdma.get(mc % 2) is not None:
                pw.wait("vector", "vst")
            dstv = T["vstage"][:, 1040 * (mc % 2) + 520 * vh:1040 * (mc % 2) + 520 * (vh + 1)] \
                .rearrange("p (h c) -> p h c", c=65)[:, :, 0:64]
            c = pw.copy(dstv, psum[:, 512 * bk:512 * bk + 512]
                        .rearrange("p (h c) -> p h c", c=64), dep=False)
            pw._slot_state[("pj", bk)] = ("dve", c)
        pw.wait("sync", "dve")
        sp, r = mc // 4, mc % 4
        vdma[mc % 2] = pw.dma("sync", G[f"g_v_in_s{sp}"][:, 1040 * r:1040 * (r + 1)],
                              T["vstage"][:, 1040 * (mc % 2):1040 * (mc % 2) + 1040],
                              sem="vst")
        if r == 3:
            pw.wait("gpsimd", "vst")
            ag_done[("v", sp)] = pw.coll(
                "AllGather", AluOpType.bypass, groups,
                [G[f"g_v_in_s{sp}"].opt()], [G[f"g_v_s{sp}"].opt()])

    ld = load_W("aWq")
    emit_proj_fm(pw, T, ld, mk_evac("q8", "qst"),
                 lambda of, sp: G["g_qT"][:, of, 512 * sp:512 * (sp + 1)], "qst",
                 prev_sems=("kst",))

    pw.wait("sync", "pst")
    pw.wait("sync", "qst")
    for of in range(FC):
        l = None
        for sp in range(4):
            l = pw.dma("sync", stage[:, 512 * sp:512 * (sp + 1)],
                       G[f"g_pre_in_s{sp}"][128 * of:128 * (of + 1), :], sem="ld")
        pw.wait("vector", "ld", l)
        if of >= 1:
            pw.wait("vector", "qst")
        s = pw.ts(T["stage2"][:, 0:2048], stage[:, 0:2048], T["scal"][:, 4:5],
                  AluOpType.mult, dep=True)
        pw.wait("sync", "dve", s)
        pw.dma("sync", G["g_pqT"][:, of, :], T["stage2"][:, 0:2048], sem="qst")

    pw._ag_done = ag_done
    if dbg and dbg.get("upto") == "aproj":
        pw.barrier("dbgproj")
        for sp in range(4):
            pw.dma("sync", dbg["d_preT"][:, 512 * sp:512 * (sp + 1)],
                   G[f"g_pre_in_s{sp}"], sem="st")
            pw.dma("sync", dbg["d_kT"][:, 512 * sp:512 * (sp + 1)],
                   G[f"g_k_in_s{sp}"], sem="st")
            pw.dma("sync", dbg["d_vext"][:, 4160 * sp:4160 * (sp + 1)],
                   G[f"g_v_in_s{sp}"], sem="st")
            pw.dma("sync", dbg["d_gpre"][0:1024, 512 * sp:512 * (sp + 1)],
                   G[f"g_pre_s{sp}"][0], sem="st")
            pw.dma("sync", dbg["d_gpre"][1024:2048, 512 * sp:512 * (sp + 1)],
                   G[f"g_pre_s{sp}"][1], sem="st")
        pw.dma("sync", dbg["d_qT"].rearrange("(f p) t -> p f t", p=128), G["g_qT"],
               sem="st")
        pw.dma("sync", dbg["d_pqT"].rearrange("(f p) t -> p f t", p=128), G["g_pqT"],
               sem="st")
        return True
    return False


def ph_attn_a_logits(pw, T, G, dbg):
    A = T["ARN_A"]
    psum = T["psum"]
    PREQ, PM, LT = 0, 8192, 10240
    agv = max(pw._ag_done.values())
    pw.wait("sync", "gp", agv)
    pw.wait("sync", "qst")
    for qh in range(2):
        pw.wait("sync", "mm")
        pw.wait("sync", "act")
        lq = pw.dma("sync", A[:, PREQ:PREQ + 8192].rearrange("p (f t) -> p f t", t=1024),
                    G["g_pqT"][:, :, 1024 * qh:1024 * (qh + 1)], sem="alq")
        pw.wait("tensor", "alq", lq)
        mld = {}

        def issue(mc):
            if mc >= 32:
                return
            par = mc % 2
            slot, g = mc // 16, mc % 16
            sp, r = g // 4, g % 4
            mld[mc] = pw.dma("sync", A[:, PM + 1024 * par:PM + 1024 * (par + 1)]
                             .rearrange("p (f t) -> p f t", t=128),
                             G[f"g_pre_s{sp}"][slot].rearrange("(f p) t -> p f t", p=128)
                             [:, :, 128 * r:128 * (r + 1)], sem="alm" + str(par))
        issue(0)
        issue(1)
        ldma = {}
        for mc in range(32):
            par = mc % 2
            pw.wait("tensor", "alm" + str(par), mld[mc])
            pre = A[:, PM + 1024 * par:PM + 1024 * (par + 1)]
            for qs in range(2):
                bk = (mc * 2 + qs) % 6
                prev = pw._slot_state.get(512 * bk)
                if prev is not None:
                    pw.wait("tensor", prev[0], prev[1])
                last = None
                for f in range(FC):
                    last = pw.mm(psum[:, 512 * bk:512 * bk + 512],
                                 pre[:, 128 * f:128 * (f + 1)],
                                 A[:, PREQ + 1024 * f + 512 * qs:PREQ + 1024 * f + 512 * (qs + 1)],
                                 start=(f == 0), stop=(f == FC - 1),
                                 sem="mm" if f == FC - 1 else None)
                pw.wait("scalar", "mm", last)
                if ldma.get(par) is not None and qs == 0:
                    pw.wait("scalar", "alst" + str(par))
                lc = pw.act(A[:, LT + 1024 * par + 512 * qs:LT + 1024 * par + 512 * (qs + 1)],
                            psum[:, 512 * bk:512 * bk + 512], AF.Copy, dep=False)
                pw._slot_state[512 * bk] = ("act", lc)
            pw.wait("sync", "act")
            ldma[par] = pw.dma("sync", G["g_lg"][mc][:, 1024 * qh:1024 * (qh + 1)],
                               A[:, LT + 1024 * par:LT + 1024 * (par + 1)],
                               sem="alst" + str(par))
            pw.wait("sync", "mm")
            issue(mc + 2)
        pw.wait("sync", "alst0")
        pw.wait("sync", "alst1")


def ph_attn_a_attn(pw, T, W, G, dbg, whook=None):
    X, A, B = T["X"], T["ARN_A"], T["ARN_B"]
    psum = T["psum"]
    QS = 0
    KT2 = 4096
    VX2 = 4608
    LG2 = 5184
    TMPX = 6208
    EXPX = 7232
    ATT = 8448
    STG = 12544
    LNB = 13568

    ldw = pw.dma("sync", T["wbuf3"], W["aWproj"].rearrange("(f p) o -> p f o", p=128),
                 sem="ld")

    for span in range(4):
        pw.wait("sync", "mm")
        pw.wait("sync", "dve")
        lq = pw.dma("sync", A[:, QS:QS + 4096].rearrange("p (f t) -> p f t", t=512),
                    G["g_qT"][:, :, 512 * span:512 * (span + 1)], sem="aq")
        pw.wait("tensor", "aq", lq)
        for hg in range(4):
            mld = {}

            def issue(mc):
                if mc >= 32:
                    return
                par = mc % 2
                sm = "am" + str(par)
                slot, g = mc // 16, mc % 16
                sp, r = g // 4, g % 4
                pw.dma("sync", A[:, KT2 + 256 * par:KT2 + 256 * (par + 1)]
                       .rearrange("p (f t) -> p f t", t=128),
                       G[f"g_k_s{sp}"][slot].rearrange("(f p) t -> p f t", p=128)
                       [:, 2 * hg:2 * hg + 2, 128 * r:128 * (r + 1)], sem=sm)
                pw.dma("sync", A[:, VX2 + 260 * par:VX2 + 260 * (par + 1)],
                       G[f"g_v_s{sp}"][slot][:, 1040 * r + 260 * hg:
                                             1040 * r + 260 * (hg + 1)], sem=sm)
                mld[mc] = pw.dma("sync", A[:, LG2 + 512 * par:LG2 + 512 * (par + 1)],
                                 G["g_lg"][mc][:, 512 * span:512 * (span + 1)], sem=sm)
            pw.wait("sync", "mm")
            pw.wait("sync", "dve")
            issue(0)
            issue(1)
            pend = None
            ec_hist = {}
            for mc in range(32):
                if whook is not None and mc in (8, 16, 24):
                    whook()
                par = mc % 2
                pw.wait("tensor", "am" + str(par), mld[mc])
                kt = A[:, KT2 + 256 * par:KT2 + 256 * (par + 1)]
                vx = A[:, VX2 + 260 * par:VX2 + 260 * (par + 1)]
                lg = A[:, LG2 + 512 * par:LG2 + 512 * (par + 1)]
                for hh in range(4):
                    h = 4 * hg + hh
                    i = mc * 4 + hh
                    bk = 2560 if i % 2 == 0 else 3072
                    eb = i % 2
                    prev = pw._slot_state.get(bk)
                    if prev is not None:
                        pw.wait("tensor", prev[0], prev[1])
                    sm2 = pw.mm(psum[:, bk:bk + 512],
                                kt[64 * (hh % 2):64 * (hh % 2) + 64,
                                   128 * (hh // 2):128 * (hh // 2) + 128],
                                A[64 * (h % 2):64 * (h % 2) + 64,
                                  QS + 512 * (h // 2):QS + 512 * (h // 2) + 512],
                                start=True, stop=True, sem="mm")
                    if pend is not None:
                        phh, pmc, pec, peb, pvx = pend
                        pw.wait("tensor", "act", pec)
                        pw.mm(psum[0:65, 512 * phh:512 * phh + 512],
                              pvx[:, 65 * phh:65 * (phh + 1)],
                              A[:, EXPX + 512 * peb:EXPX + 512 * (peb + 1)],
                              start=(pmc == 0), stop=(pmc == 31),
                              sem="mm" if phh == 3 else None)
                        pend = None
                        if hh == 0:
                            pw.wait("sync", "mm")
                            issue(mc + 1)
                    pw.wait("vector", "mm", sm2)
                    if eb in ec_hist:
                        pw.wait("vector", "act", ec_hist[eb])
                    tv = pw.tt(A[:, TMPX + 512 * eb:TMPX + 512 * (eb + 1)],
                               psum[:, bk:bk + 512], lg, AluOpType.add, dep=False)
                    pw.wait("scalar", "dve", tv)
                    ec = pw.act(A[:, EXPX + 512 * eb:EXPX + 512 * (eb + 1)],
                                A[:, TMPX + 512 * eb:TMPX + 512 * (eb + 1)],
                                AF.Exp, dep=False)
                    ec_hist[eb] = ec
                    pw._slot_state[bk] = ("dve", tv)
                    pend = (hh, mc, ec, eb, vx)
            phh, pmc, pec, peb, pvx = pend
            pw.wait("tensor", "act", pec)
            pw.mm(psum[0:65, 512 * phh:512 * phh + 512],
                  pvx[:, 65 * phh:65 * (phh + 1)],
                  A[:, EXPX + 512 * peb:EXPX + 512 * (peb + 1)],
                  start=(pmc == 0), stop=(pmc == 31), sem="mm")
            pend = None
            pw.wait("vector", "mm")
            pw.wait("tensor", "dve")
            for hh in range(4):
                h = 4 * hg + hh
                rd = pw.recip(A[64:65, STG + 512:STG + 1024],
                              psum[64:65, 512 * hh:512 * hh + 512], dep=True)
                cv = pw.copy(A[0:64, STG:STG + 512],
                             psum[0:64, 512 * hh:512 * hh + 512], dep=True)
                pw.wait("tensor", "dve", cv)
                bcm = pw.mm(psum[:, 2560:2560 + 512], T["ones_row"][64:65, :],
                            A[64:65, STG + 512:STG + 1024], start=True, stop=True,
                            sem="mm")
                pw.wait("vector", "mm", bcm)
                if h % 2 == 0:
                    dv = pw.tt(A[0:64, ATT + 512 * (h // 2):ATT + 512 * (h // 2) + 512],
                               A[0:64, STG:STG + 512], psum[0:64, 2560:2560 + 512],
                               AluOpType.mult, dep=True)
                else:
                    sv = pw.tt(A[0:64, STG:STG + 512], A[0:64, STG:STG + 512],
                               psum[0:64, 2560:2560 + 512], AluOpType.mult, dep=True)
                    pw.wait("tensor", "dve", sv)
                    sh = pw.mm(psum[:, 3072:3072 + 512], T["shift64"][0:64, :],
                               A[0:64, STG:STG + 512], start=True, stop=True,
                               sem="mm")
                    pw.wait("vector", "mm", sh)
                    dv = pw.copy(A[64:128, ATT + 512 * (h // 2):ATT + 512 * (h // 2) + 512],
                                 psum[64:128, 3072:3072 + 512], dep=True)
                pw._slot_state[2560] = ("dve", dv)
                pw._slot_state[3072] = ("dve", dv)
            if whook is not None:
                whook()

        pw.wait("tensor", "dve")
        pw.wait("tensor", "ld", ldw)
        for of in range(FC):
            bk = 2560 if of % 2 == 0 else 3072
            prev = pw._slot_state.get(bk)
            if prev is not None:
                pw.wait("tensor", prev[0], prev[1])
            last = None
            for f in range(FC):
                last = pw.mm(psum[:, bk:bk + 512],
                             T["wbuf3"][:, f, 128 * of:128 * of + 128],
                             A[:, ATT + 512 * f:ATT + 512 * (f + 1)],
                             start=(f == 0), stop=(f == FC - 1),
                             sem="mm" if f == FC - 1 else None)
            pw.wait("scalar", "mm", last)
            oc = pw.act(A[:, STG + 512 * 0:STG + 512] if False else
                        B[:, 8192 + 512 * of:8192 + 512 * (of + 1)],
                        psum[:, bk:bk + 512], AF.Identity,
                        bias=T["vecs"][:, 0 + of:1 + of], dep=False)
            pw._slot_state[bk] = ("act", oc)
        pw.wait("tensor", "act")
        for tc in range(4):
            t_global = 4 * span + tc
            pw.wait("vector", "mm")
            emit_ln_tile(pw, X[:, t_global, :], A[:, LNB:LNB + 1024],
                         T["tmpA"][:, 0:1024], T["stat"], LN_EPS)
            for ob in range(0, FC, 4):
                bk = 2560 if (ob // 4) % 2 == 0 else 3072
                prev = pw._slot_state.get(bk)
                if prev is not None:
                    pw.wait("tensor", prev[0], prev[1])
                last = None
                for j in range(4):
                    last = pw.mm(psum[:, bk + 128 * j:bk + 128 * (j + 1)],
                                 B[:, 8192 + 512 * (ob + j) + 128 * tc:
                                   8192 + 512 * (ob + j) + 128 * (tc + 1)],
                                 T["ident"][:], start=(j == 0), stop=(j == 3),
                                 sem="mm" if j == 3 else None)
                pw.wait("vector", "mm", last)
                pw.tt(A[:, LNB + 128 * ob:LNB + 128 * (ob + 4)],
                      A[:, LNB + 128 * ob:LNB + 128 * (ob + 4)],
                      psum[:, bk:bk + 512], AluOpType.add, dep=True)
                dv = pw.tt(X[:, t_global, 128 * ob:128 * (ob + 4)],
                           X[:, t_global, 128 * ob:128 * (ob + 4)],
                           A[:, LNB + 128 * ob:LNB + 128 * (ob + 4)],
                           AluOpType.add, dep=True)
                pw._slot_state[bk] = ("dve", dv)

    if dbg and dbg.get("upto") == "attna":
        pw.barrier("dbgattna")
        pw.dma("sync", dbg["d_x1"], X[:, :, :], sem="st")
        return True
    return False


def row_from_cols(pw, G, cols_ap, row_ap, nprt, ncol, sem):
    n = pw.cnt.get("_rfc", 0)
    pw.cnt["_rfc"] = n + 1
    s = f"rfc{n}"
    if n > 0:
        pw.wait("sync", f"rfc{n - 1}")
    pw.dma("sync", G["g_fr"][0:nprt, 0:ncol], cols_ap, sem=s, slow=True)
    pw.wait("sync", s)
    pw.dma("sync", row_ap,
           G["g_fr"][0:nprt, 0:ncol].rearrange("i t -> t i"), sem=s, slow=True)
    pw.wait("tensor", s)


def ph_ffn_weights(pw, T, W, G):
    psum, psumb = T["psum"], T["psumb"]
    wbuf3 = T["wbuf3"]
    acc = T["stage"]

    for wi, (wname, nblk, gdst, mcol, scol) in enumerate(
            [("f_w1", 32, "g_t1T", 5, 7), ("f_w2", 8, "g_t2T", 6, 8)]):
        ncols = 1024 if wi == 0 else 4096
        per = 8 if wi == 0 else 2
        nld = nblk // per
        pw.red(acc[:, 64:65], T["fcols"][:, 32 * wi:32 * wi + 32], AluOpType.add)
        pw.wait("tensor", "dve")
        s = pw.mm(psum[0:1, 0:1], T["ones_col"][:], acc[:, 64:65], start=True,
                  stop=True, sem="mm")
        pw.wait("vector", "mm", s)
        pw.ts(acc[0:1, 65:66], psum[0:1, 0:1], 1.0 / (nblk * 128 * ncols),
              AluOpType.mult, 1e-5, AluOpType.max)
        v = pw.recip(acc[0:1, 66:67], acc[0:1, 65:66])
        pw.wait("tensor", "dve", v)
        s = pw.mm(psum[:, 4:5], T["ones_row"][0:1, :], acc[0:1, 65:66], start=True,
                  stop=True)
        s = pw.mm(psum[:, 5:6], T["ones_row"][0:1, :], acc[0:1, 66:67], start=True,
                  stop=True, sem="mm")
        pw.wait("vector", "mm", s)
        pw.copy(T["scal"][:, mcol:mcol + 1], psum[:, 4:5])
        pw.copy(T["scal"][:, scol:scol + 1], psum[:, 5:6])

        for g in range(nld):
            pw.wait("sync", "dve")
            l = pw.dma("sync", wbuf3 if wi == 0 else
                       wbuf3.rearrange("p a o -> p (a o)").rearrange(
                           "p (b c) -> p b c", b=2),
                       W[wname].rearrange("(a p) d -> p a d", p=128)
                       [:, per * g:per * (g + 1), :], sem="ld")
            pw.wait("vector", "ld", l)
            for a in range(per):
                blk = wbuf3[:, a, :] if wi == 0 else                     wbuf3.rearrange("p a o -> p (a o)")[:, 4096 * a:4096 * (a + 1)]
                j = g * per + a
                for c0 in range(0, ncols, 1024):
                    sl = blk[:, c0:c0 + 1024] if wi else blk
                    pw.ts(T["tmpA"][:, 0:1024], sl, T["scal"][:, scol:scol + 1],
                          AluOpType.mult)
                    pw.ts(T["stage2"][:, 0:1024], T["tmpA"][:, 0:1024], 0.5,
                          AluOpType.is_gt)
                    pw.ts(T["tmpA"][:, 0:1024], T["tmpA"][:, 0:1024], -0.5,
                          AluOpType.is_lt)
                    pw.tt(T["stage2"][:, 0:1024], T["stage2"][:, 0:1024],
                          T["tmpA"][:, 0:1024], AluOpType.subtract)
                    tv = pw.copy(T["tbf"][:, 0:1024], T["stage2"][:, 0:1024])
                    pw.wait("tensor", "dve", tv)
                    prev = pw._slot_state.get("psb")
                    if prev is not None:
                        pw.wait("tensor", prev[0], prev[1])
                    last = None
                    for dc in range(8):
                        last = pw.mmT(psumb[:, 128 * dc:128 * (dc + 1)],
                                      T["tbf"][:, 128 * dc:128 * (dc + 1)],
                                      T["identb"][:], start=(dc == 0), stop=(dc == 7),
                                      sem="mm" if dc == 7 else None)
                    pw.wait("vector", "mm", last)
                    if "wst" in pw.cnt:
                        pw.wait("vector", "wst")
                    dv = pw.copy(T["tbf"][:, 1024:2048]
                                 .rearrange("p (a b) -> p a b", a=8), psumb[:]
                                 .rearrange("p (a b) -> p a b", a=8), dep=False)
                    pw._slot_state["psb"] = ("dve", dv)
                    pw.wait("sync", "dve", dv)
                    if wi == 0:
                        dst = G["g_t1T"].rearrange("(dc p) j -> p dc j", p=128)[
                            :, :, 128 * j:128 * (j + 1)]
                    else:
                        s_ = c0 // 1024
                        j2 = 2 * g + a
                        dst = G["g_t2T"].rearrange("(hc p) j -> p hc j", p=128)[
                            :, 8 * s_:8 * (s_ + 1), 128 * j2:128 * (j2 + 1)]
                    pw.dma("sync", dst,
                           T["tbf"][:, 1024:2048].rearrange("p (a b) -> p a b", a=8),
                           sem="wst")
    pw.wait("sync", "wst")


def ph_ffn(pw, T, W, G, dbg, which):
    X, A, B = T["X"], T["ARN_A"], T["ARN_B"]
    psum, psumb = T["psum"], T["psumb"]
    groups = T["groups"]
    tag = f"f{which}"
    NTT = 0
    XL = 8192
    FB = 9216
    MAXA = 11264
    SB_ = 13312
    TMPC = 0
    hx = T["hx"]
    nbfT = T["nbfT"]
    tstat = T["tstat"]
    z16 = T["z16"]
    nbf = T["tbf"]

    pw.wait("sync", "dve")
    pw.dma("sync", G["g_halo_in"][0:1, :], X[0:1, 0, :], sem=tag + "hs")
    hd = pw.dma("sync", G["g_halo_in"][1:2, :], X[127:128, 15, :], sem=tag + "hs")
    pw.wait("gpsimd", tag + "hs", hd)
    agh = pw.coll("AllGather", AluOpType.bypass, groups, [G["g_halo_in"].opt()],
                  [G["g_halo"].opt()])

    tmp, stat = T["tmpA"], T["stat"]
    tr_done = None
    for t in range(NT):
        if tr_done is not None:
            pw.wait("vector", "mm", tr_done)
        emit_ln_tile(pw, X[:, t, :], A[:, XL:XL + 1024], tmp[:, 0:1024], stat, LN_EPS)
        pw.red(stat[:, 6:7], A[:, XL:XL + 1024], AluOpType.max, absval=True)
        pw.ts(stat[:, 6:7], stat[:, 6:7], 1e-5, AluOpType.max)
        pw.recip(stat[:, 7:8], stat[:, 6:7])
        pw.ts(stat[:, 7:8], stat[:, 7:8], 127.0, AluOpType.mult)
        pw.ts(T["fcols"][:, t:t + 1], stat[:, 6:7], T["scal"][:, 5:6],
              AluOpType.mult, 1.0 / 127.0, AluOpType.mult)
        pw.ts(z16[:], A[:, XL:XL + 1024], stat[:, 7:8], AluOpType.mult, 1536.0,
              AluOpType.add)
        qv = pw.ts(nbf[:, 0:1024], z16[:], -1536.0, AluOpType.add)
        pw.wait("tensor", "dve", qv)
        prev = pw._slot_state.get("psb")
        if prev is not None:
            pw.wait("tensor", prev[0], prev[1])
        last = None
        for dc in range(8):
            last = pw.mmT(psumb[:, 128 * dc:128 * (dc + 1)],
                          nbf[:, 128 * dc:128 * (dc + 1)], T["identb"][:],
                          start=(dc == 0), stop=(dc == 7),
                          sem="mm" if dc == 7 else None)
        pw.wait("vector", "mm", last)
        dv = pw.copy(nbfT[:, :, 128 * t:128 * (t + 1)],
                     psumb[:].rearrange("p (a b) -> p a b", a=8), dep=False)
        pw._slot_state["psb"] = ("dve", dv)
        tr_done = last

    fv = pw.cnt["dve"]
    pw.wait("sync", "dve", fv)
    row_from_cols(pw, G, T["fcols"][:, 0:16], T["frow"][0:1, 0:2048], 128, 16,
                  tag + "h")
    s = pw.mm(psum[:, 0:512], T["ones_row"][0:1, :], T["frow"][0:1, 0:512], start=True,
              stop=True)
    s = pw.mm(psum[:, 512:1024], T["ones_row"][0:1, :], T["frow"][0:1, 512:1024],
              start=True, stop=True)
    s = pw.mm(psum[:, 1024:1536], T["ones_row"][0:1, :], T["frow"][0:1, 1024:1536],
              start=True, stop=True)
    s = pw.mm(psum[:, 1536:2048], T["ones_row"][0:1, :], T["frow"][0:1, 1536:2048],
              start=True, stop=True, sem="mm")
    pw.wait("vector", "mm", s)
    pw.copy(A[:, FB:FB + 2048], psum[:, 0:2048], dep=False)

    pw.wait("sync", "gp", agh)
    hl = pw.dma("sync", T["halo2"][0:1, :], G["g_halo"][0, 1, :][None, :],
                sem=tag + "hl")
    hl = pw.dma("sync", T["halo2"][1:2, :], G["g_halo"][1, 0, :][None, :],
                sem=tag + "hl")
    pw.wait("vector", tag + "hl", hl)
    emit_ln_tile(pw, T["halo2"][0:2, :], T["halo2b"][0:2, :], tmp[0:2, 0:1024],
                 stat[0:2, :], LN_EPS)
    pw.red(stat[0:2, 6:7], T["halo2b"][0:2, :], AluOpType.max, absval=True)
    pw.ts(stat[0:2, 6:7], stat[0:2, 6:7], 1e-5, AluOpType.max)
    pw.recip(stat[0:2, 7:8], stat[0:2, 6:7])
    pw.ts(stat[0:2, 7:8], stat[0:2, 7:8], 127.0, AluOpType.mult)
    pw.ts(T["fcols"][0:2, 16:17], stat[0:2, 6:7], T["scal"][0:2, 5:6],
          AluOpType.mult, 1.0 / 127.0, AluOpType.mult)
    pw.ts(T["z16h"][0:2, :], T["halo2b"][0:2, :], stat[0:2, 7:8], AluOpType.mult,
          1536.0, AluOpType.add)
    qv = pw.ts(nbf[0:2, 0:1024], T["z16h"][0:2, :], -1536.0, AluOpType.add)
    pw.wait("tensor", "dve", qv)
    prev = pw._slot_state.get("psb")
    if prev is not None:
        pw.wait("tensor", prev[0], prev[1])
    last = None
    for dc in range(8):
        last = pw.mmT(psumb[:, 16 * dc:16 * dc + 2],
                      nbf[0:2, 128 * dc:128 * (dc + 1)], T["identb"][0:2, 0:2],
                      start=(dc == 0), stop=(dc == 7), sem="mm" if dc == 7 else None)
    pw.wait("vector", "mm", last)
    dv = pw.copy(T["nhT"][:, :, :], psumb[:, 0:128].rearrange("p (a b) -> p a b", b=16)
                 [:, :, 0:2], dep=False)
    pw._slot_state["psb"] = ("dve", dv)
    fh = pw.cnt["dve"]
    pw.wait("sync", "dve", fh)
    row_from_cols(pw, G, T["fcols"][0:2, 16:17], T["frow"][0:1, 2048:2050], 2, 1,
                  tag + "h")
    s = pw.mm(psum[:, 512 * 5:512 * 5 + 2], T["ones_row"][0:1, :],
              T["frow"][0:1, 2048:2050], start=True, stop=True, sem="mm")
    pw.wait("vector", "mm", s)
    pw.copy(T["haloh"][:, 0:2], psum[:, 512 * 5:512 * 5 + 2], dep=False)
    pw.ts(T["haloh"][:, 0:1], T["haloh"][:, 0:1], T["scal"][:, 2:3],
          AluOpType.mult)
    pw.ts(T["haloh"][:, 1:2], T["haloh"][:, 1:2], T["scal"][:, 3:4],
          AluOpType.mult)

    pw.memset(A[:, MAXA:MAXA + 2048], 0.0)
    pw.wait("vector", "gp")
    pw.wait("tensor", "dve")
    dv0 = pw.cnt["dve"]
    for b_ in range(5):
        pw._slot_state[("fx", b_)] = ("dve", dv0)
    pw._slot_state["hb6"] = ("dve", dv0)
    hdma = {}
    av_hist = {}
    tstat4 = T["tstat4"]
    mmh1 = {-4: pw.cnt["mm"], -3: pw.cnt["mm"],
            -2: pw.cnt["mm"], -1: pw.cnt["mm"]}
    for jc in range(NJC):
        pw.wait("sync", "mm", mmh1[jc - 4])
        l = pw.dma("sync", tstat4[:, jc % 4, :, :],
                   G["g_t1T"].rearrange("(dc p) j -> p dc j", p=128)
                   [:, :, 128 * jc:128 * (jc + 1)], sem=tag + "w" + str(jc % 4))
        pw.wait("tensor", tag + "w" + str(jc % 4), l)
        strip = hx[:, jc % 2, :]
        cbuf = (T["stage2"][:, 0:2048] if jc % 2 == 0
                else T["stage"][:, 0:2048])
        for sp in range(4):
            bk = (jc * 4 + sp) % 5
            prev = pw._slot_state.get(("fx", bk))
            if prev is not None:
                pw.wait("tensor", prev[0], prev[1])
            last = None
            for dc in range(8):
                last = pw.mm(psum[:, 512 * bk:512 * bk + 512],
                             tstat4[:, jc % 4, dc, :],
                             nbfT[:, dc, 512 * sp:512 * (sp + 1)],
                             start=(dc == 0), stop=(dc == 7),
                             sem="mm" if dc == 7 else None)
            pw.wait("vector", "mm", last)
            if sp == 0:
                if hdma.get(jc % 2) is not None:
                    pw.wait("vector", tag + "hd" + str(jc % 2), hdma[jc % 2])
                if av_hist.get(jc % 2) is not None:
                    pw.wait("vector", "act", av_hist[jc % 2])
            c = pw.tt(strip[:, 1 + 512 * sp:1 + 512 * (sp + 1)],
                      psum[:, 512 * bk:512 * bk + 512],
                      A[:, FB + 512 * sp:FB + 512 * (sp + 1)], AluOpType.mult,
                      dep=False)
            pw._slot_state[("fx", bk)] = ("dve", c)
        prev = pw._slot_state.get("hb6")
        if prev is not None:
            pw.wait("tensor", prev[0], prev[1])
        lasth = None
        for dc in range(8):
            lasth = pw.mm(psum[:, 512 * 6:512 * 6 + 2],
                          tstat4[:, jc % 4, dc, :], T["nhT"][:, dc, :],
                          start=(dc == 0), stop=(dc == 7),
                          sem="mm" if dc == 7 else None)
        mmh1[jc] = pw.cnt["mm"]
        pw.wait("vector", "mm", lasth)
        pw.tt(strip[:, 0:1], psum[:, 512 * 6:512 * 6 + 1], T["haloh"][:, 0:1],
              AluOpType.mult, dep=True)
        hc2 = pw.tt(strip[:, 2049:2050], psum[:, 512 * 6 + 1:512 * 6 + 2],
                    T["haloh"][:, 1:2], AluOpType.mult, dep=True)
        pw._slot_state["hb6"] = ("dve", hc2)
        pw.ts(cbuf, strip[:, 0:2048], T["convw"][:, jc, 0:1], AluOpType.mult)
        pw.stt(cbuf, strip[:, 1:2049], T["convw"][:, jc, 1:2],
               cbuf, AluOpType.mult, AluOpType.add)
        cv = pw.stt(cbuf, strip[:, 2:2050], T["convw"][:, jc, 2:3],
                    cbuf, AluOpType.mult, AluOpType.add)
        pw.wait("scalar", "dve", cv)
        av = pw.act(strip[:, 1:2049],
                    cbuf, AF.Silu, bias=T["convb"][:, jc:jc + 1],
                    dep=False)
        av_hist[jc % 2] = av
        pw.wait("vector", "act", av)
        pw.stt(A[:, MAXA:MAXA + 2048], strip[:, 1:2049], 1.0,
               A[:, MAXA:MAXA + 2048], AluOpType.mult, AluOpType.max)
        pw.stt(A[:, MAXA:MAXA + 2048], strip[:, 1:2049], -1.0,
               A[:, MAXA:MAXA + 2048], AluOpType.mult, AluOpType.max)
        pw.wait("gpsimd", "act", av)
        hdma[jc % 2] = pw.dma("gpsimd", G["g_hact"][128 * jc:128 * (jc + 1), :],
                              strip[:, 1:2049], sem=tag + "hd" + str(jc % 2))

    pw.wait("tensor", "dve")
    for tt_ in range(NT):
        prev = pw._slot_state.get(("fx", tt_ % 2))
        if prev is not None:
            pw.wait("tensor", prev[0], prev[1])
        s = pw.mm(psum[:, 512 * (tt_ % 2):512 * (tt_ % 2) + 128],
                  A[:, MAXA + 128 * tt_:MAXA + 128 * (tt_ + 1)], T["ident"][:],
                  start=True, stop=True, sem="mm")
        pw.wait("vector", "mm", s)
        rv = pw.red(T["fcols"][:, 32 + tt_:33 + tt_],
                    psum[:, 512 * (tt_ % 2):512 * (tt_ % 2) + 128],
                    AluOpType.max, dep=False)
        pw._slot_state[("fx", tt_ % 2)] = ("dve", rv)
    pw.ts(T["fcols"][:, 32:48], T["fcols"][:, 32:48], 1e-5, AluOpType.max)
    pw.ts(T["fcols"][:, 48:64], T["fcols"][:, 32:48], T["scal"][:, 6:7],
          AluOpType.mult, 1.0 / 127.0, AluOpType.mult)
    rv = pw.recip(T["fcols"][:, 32:48], T["fcols"][:, 32:48])
    sv = pw.ts(T["fcols"][:, 32:48], T["fcols"][:, 32:48], 127.0, AluOpType.mult)
    pw.wait("sync", "dve", sv)
    row_from_cols(pw, G, T["fcols"][:, 32:48], T["frow"][0:1, 0:2048], 128, 16,
                  tag + "h")
    row_from_cols(pw, G, T["fcols"][:, 48:64], T["f2row"][0:1, 0:2048], 128, 16,
                  tag + "h")
    pw.wait("tensor", "dve")
    for half in range(4):
        s = pw.mm(psum[:, 512 * half:512 * (half + 1)], T["ones_row"][0:1, :],
                  T["frow"][0:1, 512 * half:512 * (half + 1)], start=True, stop=True,
                  sem="mm" if half == 3 else None)
    pw.wait("vector", "mm", s)
    pw.copy(A[:, SB_:SB_ + 2048], psum[:, 0:2048], dep=False)
    pw.wait("tensor", "dve")
    for half in range(4):
        s = pw.mm(psum[:, 512 * half:512 * (half + 1)], T["ones_row"][0:1, :],
                  T["f2row"][0:1, 512 * half:512 * (half + 1)], start=True, stop=True,
                  sem="mm" if half == 3 else None)
    pw.wait("vector", "mm", s)
    f2v = pw.copy(A[:, FB:FB + 2048], psum[:, 0:2048], dep=False)

    pw.wait("sync", "mm", pw.cnt["mm"])
    lt2 = pw.dma("sync", T["t2s"][:, :, :],
                 G["g_t2T"].rearrange("(hc p) j -> p hc j", p=128)
                 [:, :, 0:512], sem=tag + "t2")

    pw.wait("sync", "dve", f2v)
    pw.wait("sync", tag + "hd0")
    pw.wait("sync", tag + "hd1")
    pw.wait("vector", "act")
    pw.wait("tensor", "dve", f2v)
    pw.wait("tensor", tag + "t2", lt2)
    ndma = {}
    qh_tt = {}
    nv_hist = {}
    fmm = {-3: pw.cnt["mm"], -2: pw.cnt["mm"], -1: pw.cnt["mm"]}

    def fused_mms(jc):
        nh_b = T["nh2"] if jc % 2 == 0 else T["tbf"][:, 0:2048]
        pw.wait("tensor", "dve", nv_hist[jc])
        for oc in range(4):
            pw.mm(psum[:, 512 * oc:512 * oc + 512],
                  T["t2s"][:, jc, 128 * oc:128 * (oc + 1)],
                  nh_b[:, 0:512],
                  start=(jc == 0), stop=(jc == NJC - 1),
                  sem="mm" if oc == 3 else None)
        fmm[jc] = pw.cnt["mm"]

    for jc in range(NJC):
        if qh_tt.get(jc % 2) is not None:
            pw.wait("sync", "dve", qh_tt[jc % 2])
        l = pw.dma("sync", hx[:, jc % 2, 0:2048],
                   G["g_hact"][128 * jc:128 * (jc + 1), :], sem=tag + "q" + str(jc % 2))
        pw.wait("vector", tag + "q" + str(jc % 2), l)
        if ndma.get(jc % 2) is not None:
            pw.wait("vector", tag + "nd" + str(jc % 2), ndma[jc % 2])
        nh_buf = T["nh2"] if jc % 2 == 0 else T["tbf"][:, 0:2048]
        pw.wait("vector", "mm", fmm[jc - 2])
        qh_tt[jc % 2] = pw.tt(T["stage2"][:, 0:2048], hx[:, jc % 2, 0:2048],
                              A[:, SB_:SB_ + 2048], AluOpType.mult, dep=False)
        pw.ts(T["z16w"][:, 0:2048], T["stage2"][:, 0:2048], 1536.0, AluOpType.add)
        nv = pw.ts(nh_buf, T["z16w"][:, 0:2048], -1536.0, AluOpType.add)
        nv_hist[jc] = nv
        pw.wait("gpsimd", "dve", nv)
        ndma[jc % 2] = pw.dma("gpsimd", G["g_nh"][128 * jc:128 * (jc + 1), :],
                              nh_buf, sem=tag + "nd" + str(jc % 2))
        if jc >= 1:
            fused_mms(jc - 1)
    fused_mms(NJC - 1)

    pw.wait("vector", "mm", fmm[NJC - 1])
    for oc in range(4):
        pw.tt(T["ofm2"][:, oc, :], psum[:, 512 * oc:512 * oc + 512],
              A[:, FB:FB + 512], AluOpType.mult, dep=False)
    pw.wait("tensor", "dve")
    for tt_ in range(4):
        bk = 4 + (tt_ % 2)
        prev = pw._slot_state.get(("fx2", bk))
        if prev is not None:
            pw.wait("tensor", prev[0], prev[1])
        last = None
        for oc in range(4):
            last = pw.mm(psum[:, 512 * bk + 128 * oc:512 * bk + 128 * (oc + 1)],
                         T["ofm2"][:, oc, 128 * tt_:128 * (tt_ + 1)],
                         T["ident"][:], start=(oc == 0), stop=(oc == 3),
                         sem="mm" if oc == 3 else None)
        pw.wait("vector", "mm", last)
        xv = pw.tt(X[:, tt_, 0:512], X[:, tt_, 0:512],
                   psum[:, 512 * bk:512 * bk + 512], AluOpType.add, dep=False)
        pw._slot_state[("fx2", bk)] = ("dve", xv)

    pw.wait("sync", tag + "nd0")
    pw.wait("sync", tag + "nd1")
    pw.wait("sync", "mm")
    pw.wait("sync", "dve", nv_hist[NJC - 1])
    pw.wait("tensor", "dve")
    for oh in range(2):
        if oh == 1:
            pw.wait("sync", "mm")
            l2 = pw.dma("sync", T["t2s"][:, :, :],
                        G["g_t2T"].rearrange("(hc p) j -> p hc j", p=128)
                        [:, :, 512 * oh:512 * (oh + 1)], sem=tag + "t2")
            pw.wait("tensor", tag + "t2", l2)
        for sp in range(4):
            if oh == 0 and sp == 0:
                continue
            mmh2 = {-4: pw.cnt["mm"], -3: pw.cnt["mm"],
                    -2: pw.cnt["mm"], -1: pw.cnt["mm"]}
            for jc in range(NJC):
                pw.wait("sync", "mm", mmh2[jc - 4])
                sm2 = tag + "x" + str(jc % 4)
                l1 = pw.dma("sync", T["nstat4"][:, jc % 4, :],
                            G["g_nh"][128 * jc:128 * (jc + 1),
                                      512 * sp:512 * (sp + 1)], sem=sm2)
                pw.wait("tensor", sm2, l1)
                for oc in range(4):
                    pw.mm(psum[:, 512 * oc:512 * oc + 512],
                          T["t2s"][:, jc, 128 * oc:128 * (oc + 1)],
                          T["nstat4"][:, jc % 4, :],
                          start=(jc == 0), stop=(jc == NJC - 1),
                          sem="mm" if oc == 3 else None)
                mmh2[jc] = pw.cnt["mm"]
            pw.wait("vector", "mm")
            for oc in range(4):
                ev = pw.tt(T["ofm2"][:, oc, :],
                           psum[:, 512 * oc:512 * oc + 512],
                           A[:, FB + 512 * sp:FB + 512 * (sp + 1)],
                           AluOpType.mult, dep=False)
            pw.wait("tensor", "dve")
            for tt_ in range(4):
                t_global = 4 * sp + tt_
                bk = 4 + (tt_ % 2)
                prev = pw._slot_state.get(("fx2", bk))
                if prev is not None:
                    pw.wait("tensor", prev[0], prev[1])
                last = None
                for oc in range(4):
                    last = pw.mm(psum[:, 512 * bk + 128 * oc:512 * bk + 128 * (oc + 1)],
                                 T["ofm2"][:, oc, 128 * tt_:128 * (tt_ + 1)],
                                 T["ident"][:], start=(oc == 0), stop=(oc == 3),
                                 sem="mm" if oc == 3 else None)
                pw.wait("vector", "mm", last)
                xv = pw.tt(X[:, t_global, 512 * oh:512 * (oh + 1)],
                           X[:, t_global, 512 * oh:512 * (oh + 1)],
                           psum[:, 512 * bk:512 * bk + 512], AluOpType.add,
                           dep=False)
                pw._slot_state[("fx2", bk)] = ("dve", xv)

    if dbg and dbg.get("upto") == "ffn" + str(which):
        pw.barrier("dbgffn")
        pw.dma("sync", dbg["d_x1"], X[:, :, :], sem="st")
        return True
    return False


def ph_attn_b(pw, T, W, G, dbg):
    X, A, B = T["X"], T["ARN_A"], T["ARN_B"]
    psum = T["psum"]
    groups = T["groups"]
    BW = T["BWK"]
    VXB = 0
    KVS = 1040
    STS = 1172
    DEN = 1304
    ASB = 1336
    QP0, QP1 = 0, 4096
    CATQ, CATK = 8192, 10240
    KPF = 12288
    KPT = 8192
    GBC = 1024

    tmp, stat = T["tmpA"], T["stat"]

    pw.wait("vector", "mm")
    xln = B
    for t in range(NT):
        emit_ln_tile(pw, X[:, t, :], xln[:, 1024 * t:1024 * (t + 1)], tmp[:, 0:1024],
                     stat, LN_EPS)
    pw.wait("tensor", "dve")

    def dst(t, f):
        return A[:, 2048 * f + 128 * t:2048 * f + 128 * (t + 1)]
    emit_transpose_tm_to_fm(pw, lambda t: xln[:, 1024 * t:1024 * (t + 1)],
                            dst, NT, FC, T, 0, 512)
    pw.barrier("battnb")

    def load_W(name):
        return pw.dma("sync", T["wbuf3"], W[name].rearrange("(f p) o -> p f o", p=128),
                      sem="ld")

    stage = T["stage"]

    def mk_evac(kind, dsem):
        def evac(of, sp, ps, mmcnt, need_guard):
            it = sp * FC + of
            region = stage[:, 512 * (it % 4):512 * (it % 4) + 512]
            pw.wait("scalar", "mm", mmcnt)
            if need_guard:
                pw.wait("scalar", dsem)
            if kind == "q":
                c = pw.act(region, ps, AF.Identity, scale=0.125,
                           bias=T["vecs"][:, 96 + of:97 + of])
            else:
                c = pw.act(region, ps, AF.Identity, scale=0.125,
                           bias=T["vecs"][:, 104 + of:105 + of])
            return ("act", c, region)
        return evac

    ld = load_W("b_qw")
    emit_proj_fm(pw, T, ld, mk_evac("q", "bqs"),
                 lambda of, sp: G["g_bqT"][:, of, 512 * sp:512 * (sp + 1)], "bqs")
    ld = load_W("b_kw")
    emit_proj_fm(pw, T, ld, mk_evac("k", "bks"),
                 lambda of, sp: G["g_bkT"][:, of, 512 * sp:512 * (sp + 1)], "bks",
                 prev_sems=("bqs",))

    row_from_cols(pw, G, T["vecs"][:, 32:40], T["frow"][0:1, 0:1024], 128, 8, "bvr")
    pw.wait("tensor", "dve")
    for half in range(2):
        s = pw.mm(psum[:, 512 * half:512 * (half + 1)], T["ones_row"][0:1, :],
                  T["frow"][0:1, 512 * half:512 * (half + 1)], start=True, stop=True,
                  sem="mm" if half == 1 else None)
    pw.wait("vector", "mm", s)
    pw.copy(T["stage2"][:, 0:1024], psum[:, 0:1024], dep=False)

    ld = load_W("b_vw")
    pw.wait("gpsimd", "dve")
    mz = pw.memset(T["vstage"], 1.0)
    pw.wait("vector", "gp", mz)
    pw.wait("sync", "gp", mz)
    pw.wait("tensor", "ld", ld)
    pw.wait("vector", "bks")
    pw.wait("tensor", "dve")
    vdma = {}
    for mc in range(NT):
        for vh in range(2):
            bk = (mc * 2 + vh) % 4
            prev = pw._slot_state.get(512 * bk)
            if prev is not None:
                pw.wait("tensor", prev[0], prev[1])
            last = None
            for f in range(FC):
                last = pw.mm(psum[:, 512 * bk:512 * bk + 512],
                             A[:, 2048 * f + 128 * mc:2048 * f + 128 * (mc + 1)],
                             T["wbuf3"][:, f, 512 * vh:512 * (vh + 1)],
                             start=(f == 0), stop=(f == FC - 1),
                             sem="mm" if f == FC - 1 else None)
            pw.wait("vector", "mm", last)
            if vdma.get(mc % 2) is not None:
                pw.wait("vector", "bvs")
            dstv = T["vstage"][:, 1040 * (mc % 2) + 520 * vh:1040 * (mc % 2) + 520 * (vh + 1)] \
                .rearrange("p (h c) -> p h c", c=65)[:, :, 0:64]
            c = pw.stt(dstv, psum[:, 512 * bk:512 * bk + 512]
                       .rearrange("p (h c) -> p h c", c=64), 1.0,
                       T["stage2"][:, 512 * vh:512 * (vh + 1)]
                       .rearrange("p (h c) -> p h c", c=64), AluOpType.mult,
                       AluOpType.add, dep=False)
            pw._slot_state[512 * bk] = ("dve", c)
        pw.wait("sync", "dve")
        vdma[mc % 2] = pw.dma("sync", G["g_bv"][:, 1040 * mc:1040 * (mc + 1)],
                              T["vstage"][:, 1040 * (mc % 2):1040 * (mc % 2) + 1040],
                              sem="bvs")
    pw.wait("sync", "bvs")

    if dbg and dbg.get("upto") == "bproj":
        pw.barrier("dbgbp")
        pw.dma("sync", dbg["d_qT"].rearrange("(f p) t -> p f t", p=128), G["g_bqT"],
               sem="st")
        pw.dma("sync", dbg["d_kT"].rearrange("(f p) t -> p f t", p=128), G["g_bkT"],
               sem="st")
        pw.dma("sync", dbg["d_vext"], G["g_bv"], sem="st")
        pw.wait("sync", "st")
        return True

    ag_h = {}
    ld_hist = {}

    def load_feats(h):
        sm = "bh" + str(h % 2)
        for (buf, g) in ((CATQ, G["g_bqT"]), (CATK, G["g_bkT"])):
            pw.dma("sync", A[0:64, buf:buf + 2048],
                   g[64 * (h % 2):64 * (h % 2) + 64, h // 2, :], sem=sm)
            pw.dma("sync", A[64:128, buf:buf + 2048],
                   g[64 * (h % 2):64 * (h % 2) + 64, h // 2, :], sem=sm)
        lv = pw.dma("sync", BW[:, VXB:VXB + 1040].rearrange("p (tc c) -> p tc c", c=65),
                    G["g_bv"].rearrange("p (tc c) -> p tc c", c=1040)
                    [:, :, 65 * h:65 * (h + 1)], sem=sm)
        pw.wait("scalar", sm, lv)
        pw.act(A[64:128, CATQ:CATQ + 2048], A[64:128, CATQ:CATQ + 2048],
               AF.Square, dep=False)
        a2 = pw.act(A[64:128, CATK:CATK + 2048], A[64:128, CATK:CATK + 2048],
                    AF.Square, dep=False)
        ld_hist[h] = a2

    def features(h):
        qpb = QP0 if h % 2 == 0 else QP1
        pw.wait("tensor", "act", ld_hist[h])
        for (src2, dstc) in ((CATQ, qpb), (CATK, KPF)):
            for mc in range(2):
                for sp in range(4):
                    bk = 3 + (mc * 4 + sp) % 4
                    prev = pw._slot_state.get(512 * bk)
                    if prev is not None:
                        pw.wait("tensor", prev[0], prev[1])
                    mmv = pw.mm(psum[:, 512 * bk:512 * bk + 512],
                                T["projx2"][:, 128 * mc:128 * (mc + 1)],
                                A[:, src2 + 512 * sp:src2 + 512 * (sp + 1)],
                                start=True, stop=True, sem="mm")
                    pw.wait("scalar", "mm", mmv)
                    dst_ap = A[:, dstc + 2048 * mc + 512 * sp:
                               dstc + 2048 * mc + 512 * (sp + 1)]
                    ec = pw.act(dst_ap, psum[:, 512 * bk:512 * bk + 512], AF.Exp,
                                bias=-LN16, dep=False)
                    pw._slot_state[512 * bk] = ("act", ec)
        for tc in range(NT):
            bk = 3 + tc % 4
            prev = pw._slot_state.get(512 * bk)
            if prev is not None:
                pw.wait("tensor", prev[0], prev[1])
            mmv = pw.mm(psum[:, 512 * bk:512 * bk + 256],
                        A[:, CATK + 128 * tc:CATK + 128 * (tc + 1)],
                        T["projx2"][:, 0:256], start=True, stop=True, sem="mm")
            pw.wait("scalar", "mm", mmv)
            ec = pw.act(A[:, KPT + 256 * tc:KPT + 256 * (tc + 1)],
                        psum[:, 512 * bk:512 * bk + 256], AF.Exp, bias=-LN16,
                        dep=False)
            pw._slot_state[512 * bk] = ("act", ec)

    def scan(h):
        qpb = QP0 if h % 2 == 0 else QP1
        pw.wait("gpsimd", "act")
        pw.wait("gpsimd", "mm")
        for p_ in ("bst0", "bst1"):
            if p_ in pw.cnt:
                pw.wait("gpsimd", p_)
        z = pw.memset(BW[:, KVS:KVS + 130], 0.0)
        pw.wait("vector", "gp", z)
        pw.wait("tensor", "gp", z)
        pw.wait("scalar", "act")
        am_hist = {0: None, 1: None}
        ap_last = {}
        kv_last = None

        def aprime(c):
            abk = 512 if c % 2 == 0 else 2048
            prev = am_hist[c % 2]
            if prev is not None:
                pw.wait("tensor", prev[0], prev[1])
            last = None
            for mc in range(2):
                last = pw.mm(psum[:, abk:abk + 128],
                             A[:, KPF + 2048 * mc + 128 * c:KPF + 2048 * mc + 128 * (c + 1)],
                             A[:, qpb + 2048 * mc + 128 * c:qpb + 2048 * mc + 128 * (c + 1)],
                             start=(mc == 0), stop=(mc == 1),
                             sem="mm" if mc == 1 else None)
            ap_last[c] = last

        aprime(0)
        for c in range(NT):
            abk = 512 if c % 2 == 0 else 2048
            if c + 1 < NT:
                aprime(c + 1)
            if c > 0:
                pw.wait("vector", "mm", kv_last)
                pw.copy(BW[:, KVS:KVS + 130], psum[:, 0:130], dep=False)
            pw.wait("vector", "mm", ap_last[c])
            am = pw.tt(BW[:, ASB:ASB + 128], psum[:, abk:abk + 128], T["mask_ut"][:],
                       AluOpType.mult, dep=False)
            am_hist[c % 2] = ("dve", am)
            pw.wait("tensor", "dve", am)
            for mc in range(2):
                pw.mm(psum[:, 1024:1024 + 65],
                      A[:, qpb + 2048 * mc + 128 * c:qpb + 2048 * mc + 128 * (c + 1)],
                      BW[:, KVS + 65 * mc:KVS + 65 * (mc + 1)],
                      start=(mc == 0), stop=False)
            nm = pw.mm(psum[:, 1024:1024 + 65], BW[:, ASB:ASB + 128],
                       BW[:, VXB + 65 * c:VXB + 65 * (c + 1)],
                       start=False, stop=True, sem="mm")
            for mc in range(2):
                pw.mm(psum[:, 65 * mc:65 * (mc + 1)],
                      A[:, KPT + 256 * c + 128 * mc:KPT + 256 * c + 128 * (mc + 1)],
                      BW[:, VXB + 65 * c:VXB + 65 * (c + 1)],
                      start=(c == 0 and mc == 0), stop=(mc == 1),
                      sem="mm" if mc == 1 else None, skip=True)
            kv_last = pw.cnt["mm"]
            pw.wait("vector", "mm", nm)
            pw.copy(B[:, 1024 * c + 64 * h:1024 * c + 64 * (h + 1)],
                    psum[:, 1024:1024 + 64], dep=False)
            pw.copy(BW[:, DEN + 16 * (h % 2) + c:DEN + 16 * (h % 2) + c + 1],
                    psum[:, 1024 + 64:1024 + 65], dep=False)
        if am_hist[1] is not None:
            pw._slot_state[2048] = am_hist[1]
        pw.wait("vector", "mm", kv_last)
        sv = pw.copy(BW[:, KVS:KVS + 130], psum[:, 0:130], dep=False)
        pw.wait("sync", "dve", sv)
        sd = pw.dma("sync", G["g_st_in"][h].rearrange("(mc p) c -> p mc c", p=128),
                    BW[:, KVS:KVS + 130].rearrange("p (mc c) -> p mc c", c=65),
                    sem="bst" + str(h % 2))
        pw.wait("gpsimd", "bst" + str(h % 2), sd)
        ag_h[h] = pw.coll("AllGather", AluOpType.bypass, groups,
                          [G["g_st_in"][h].opt()], [G["g_st"][h].opt()])
        if h + 1 < H:
            pw.wait("sync", "mm", kv_last)
            load_feats(h + 1)

    def finalize(h):
        qpb = QP0 if h % 2 == 0 else QP1
        pw.wait("sync", "gp", ag_h[h])
        pw.wait("sync", "mm")
        lv = pw.dma("sync", BW[:, STS:STS + 130].rearrange("p (mc c) -> p mc c", c=65),
                    G["g_st"][h][0].rearrange("(mc p) c -> p mc c", p=128),
                    sem="bfs" + str(h % 2))
        pw.wait("vector", "bfs" + str(h % 2), lv)
        mk = pw.ts(BW[:, STS:STS + 130], BW[:, STS:STS + 130], T["scal"][:, 1:2],
                   AluOpType.mult, dep=True)
        pw.wait("tensor", "dve", mk)
        for c in range(NT):
            bk = 2 + c % 2
            prev = pw._slot_state.get(512 * bk)
            if prev is not None:
                pw.wait("tensor", prev[0], prev[1])
            last = None
            for mc in range(2):
                last = pw.mm(psum[:, 512 * bk:512 * bk + 65],
                             A[:, qpb + 2048 * mc + 128 * c:qpb + 2048 * mc + 128 * (c + 1)],
                             BW[:, STS + 65 * mc:STS + 65 * (mc + 1)],
                             start=(mc == 0), stop=(mc == 1),
                             sem="mm" if mc == 1 else None)
            pw.wait("vector", "mm", last)
            pw.tt(B[:, 1024 * c + 64 * h:1024 * c + 64 * (h + 1)],
                  B[:, 1024 * c + 64 * h:1024 * c + 64 * (h + 1)],
                  psum[:, 512 * bk:512 * bk + 64], AluOpType.add, dep=False)
            pw.tt(stat[:, 6:7], BW[:, DEN + 16 * (h % 2) + c:DEN + 16 * (h % 2) + c + 1],
                  psum[:, 512 * bk + 64:512 * bk + 65], AluOpType.add)
            rv = pw.recip(stat[:, 7:8], stat[:, 6:7])
            dv = pw.ts(B[:, 1024 * c + 64 * h:1024 * c + 64 * (h + 1)],
                       B[:, 1024 * c + 64 * h:1024 * c + 64 * (h + 1)],
                       stat[:, 7:8], AluOpType.mult)
            pw._slot_state[512 * bk] = ("dve", dv)

    pw.wait("sync", "mm")
    pw.wait("sync", "dve")
    load_feats(0)
    for h in range(H):
        features(h)
        scan(h)
        if h >= 1:
            finalize(h - 1)
    finalize(H - 1)

    if dbg and dbg.get("upto") == "bheads":
        pw.barrier("dbgbh")
        pw.dma("sync", dbg["d_x1"], B[:, 0:16384].rearrange("p (t f) -> p t f", f=1024),
               sem="st")
        pw.dma("sync", dbg["d_den"], BW[:, DEN:DEN + 32], sem="st")
        pw.wait("sync", "st")
        return True

    pw.wait("tensor", "dve")

    def dsta(t, f):
        return A[:, 2048 * f + 128 * t:2048 * f + 128 * (t + 1)]
    emit_transpose_tm_to_fm(pw, lambda t: B[:, 1024 * t:1024 * (t + 1)],
                            dsta, NT, FC, T, 512 * 5, 512 * 6)
    pw.wait("sync", "mm")
    pw.wait("sync", "dve")
    ldw = load_W("b_ow")
    GBC2 = 8192
    RES = 10240
    row_from_cols(pw, G, T["vecs"][:, 48:56], T["frow"][0:1, 0:1024], 128, 8, "bgr")
    row_from_cols(pw, G, T["vecs"][:, 56:64], T["frow"][0:1, 1024:2048], 128, 8, "bbr")
    pw.wait("tensor", "dve")
    for half in range(4):
        s = pw.mm(psum[:, 512 * half:512 * (half + 1)], T["ones_row"][0:1, :],
                  T["frow"][0:1, 512 * half:512 * (half + 1)], start=True, stop=True,
                  sem="mm" if half == 3 else None)
    pw.wait("vector", "mm", s)
    gv = pw.copy(B[:, GBC2:GBC2 + 2048], psum[:, 0:2048], dep=False)
    pw.wait("tensor", "dve", gv)
    pw.wait("tensor", "ld", ldw)

    for t in range(NT):
        emit_ln_tile(pw, X[:, t, :], B[:, RES:RES + 1024], tmp[:, 0:1024], stat,
                     LN_EPS)
        for of in range(FC):
            bk = of % 2
            prev = pw._slot_state.get(("eo", bk))
            if prev is not None:
                pw.wait("tensor", prev[0], prev[1])
            last = None
            for f in range(FC):
                last = pw.mm(psum[:, 512 * bk:512 * bk + 128],
                             T["wbuf3"][:, f, 128 * of:128 * of + 128],
                             A[:, 2048 * f + 128 * t:2048 * f + 128 * (t + 1)],
                             start=(f == 0), stop=(f == FC - 1),
                             sem="mm" if f == FC - 1 else None)
            pw.wait("vector", "mm", last)
            ec2 = pw.copy(B[:, 11264 + 128 * bk:11264 + 128 * (bk + 1)],
                          psum[:, 512 * bk:512 * bk + 128], dep=False)
            pw.wait("tensor", "dve", ec2)
            tb = pw.mm(psum[:, 512 * (2 + bk):512 * (2 + bk) + 128],
                       B[:, 11264 + 128 * bk:11264 + 128 * (bk + 1)], T["ident"][:],
                       start=True, stop=True, sem="mm")
            pw.wait("vector", "mm", tb)
            ev = pw.stt(B[:, RES + 128 * of:RES + 128 * (of + 1)],
                        psum[:, 512 * (2 + bk):512 * (2 + bk) + 128],
                        T["vecs"][:, 40 + of:41 + of],
                        B[:, RES + 128 * of:RES + 128 * (of + 1)],
                        AluOpType.add, AluOpType.add, dep=True)
            pw._slot_state[("eo", bk)] = ("dve", ev)
        pw.red(stat[:, 0:1], B[:, RES:RES + 1024], AluOpType.add)
        pw.ts(stat[:, 1:2], stat[:, 0:1], 1.0 / D, AluOpType.mult)
        v0 = pw.ts(tmp[:, 0:1024], B[:, RES:RES + 1024], stat[:, 1:2],
                   AluOpType.subtract)
        pw.wait("scalar", "dve", v0)
        aq = pw.act(B[:, RES:RES + 1024], tmp[:, 0:1024], AF.Square, dep=False)
        pw.wait("vector", "act", aq)
        pw.red(stat[:, 2:3], B[:, RES:RES + 1024], AluOpType.add)
        v1 = pw.ts(stat[:, 3:4], stat[:, 2:3], 1.0 / D, AluOpType.mult,
                   LNB_EPS, AluOpType.add)
        pw.wait("scalar", "dve", v1)
        a1 = pw.act(stat[:, 4:5], stat[:, 3:4], AF.Sqrt, dep=False)
        pw.wait("vector", "act", a1)
        pw.recip(stat[:, 5:6], stat[:, 4:5])
        pw.ts(tmp[:, 0:1024], tmp[:, 0:1024], stat[:, 5:6], AluOpType.mult)
        pw.tt(tmp[:, 0:1024], tmp[:, 0:1024], B[:, GBC2:GBC2 + 1024],
              AluOpType.mult)
        pw.tt(tmp[:, 0:1024], tmp[:, 0:1024], B[:, GBC2 + 1024:GBC2 + 2048],
              AluOpType.add)
        pw.tt(X[:, t, :], X[:, t, :], tmp[:, 0:1024], AluOpType.add)

    if dbg and dbg.get("upto") == "attnb":
        pw.barrier("dbgattnb")
        pw.dma("sync", dbg["d_x1"], X[:, :, :], sem="st")
        return True
    return False


def ph_store(pw, T, y_out):
    pw.barrier("final")
    v = pw.dma("sync", y_out.rearrange("(t p) f -> p t f", p=128), T["X"][:, :, :],
               sem="yst")
    pw.wait("sync", "yst", v)



def build_nc(debug_upto=None):
    nc = bass.Bass("TRN2", target_bir_lowering=False)
    x_in = nc.dram_tensor("x0", [TOK, D], F32, kind="ExternalInput").ap()
    W = {}
    for n, s in [("aWq", [D, D]), ("aWk", [D, D]), ("aWv", [D, D]),
                 ("aWproj", [D, D]), ("aWpre", [D, D]),
                 ("b_qw", [D, D]), ("b_kw", [D, D]), ("b_vw", [D, D]),
                 ("b_ow", [D, D]), ("f_w1", [DHID, D]), ("f_w2", [D, DHID])]:
        W[n] = nc.dram_tensor(n, s, F32, kind="ExternalInput").ap()
    VEC = {}
    for n, s in [("a_bproj", [D]), ("a_bpre", [D]), ("b_qb", [D]), ("b_kb", [D]),
                 ("b_vb", [D]), ("b_ob", [D]), ("b_gamma", [D]), ("b_beta", [D]),
                 ("f_convb", [DHID]),
                 ("pi_bc", [128, 1]), ("isodd_bc", [128, 1]),
                 ("mleft_bc", [128, 1]), ("mright_bc", [128, 1])]:
        VEC[n] = nc.dram_tensor(n, s, F32, kind="ExternalInput").ap()
    b_proj = nc.dram_tensor("b_proj", [M, DH], F32, kind="ExternalInput").ap()
    f_convw = nc.dram_tensor("f_convw", [DHID, 3], F32, kind="ExternalInput").ap()
    y_out = nc.dram_tensor("y", [TOK, D], F32, kind="ExternalOutput").ap()

    G = {}
    span_tensors = []
    for sp in range(4):
        span_tensors += [(f"g_pre_in_s{sp}", [D, 512]), (f"g_pre_s{sp}", [2, D, 512]),
                         (f"g_k_in_s{sp}", [D, 512]), (f"g_k_s{sp}", [2, D, 512]),
                         (f"g_v_in_s{sp}", [128, 4 * 1040]),
                         (f"g_v_s{sp}", [2, 128, 4 * 1040])]
    for n, s in span_tensors + [
                 ("g_qT", [128, FC, TOK]), ("g_pqT", [128, FC, TOK]),
                 ("g_halo_in", [2, D]), ("g_halo", [2, 2, D]),
                 ("g_bqT", [128, FC, TOK]), ("g_bkT", [128, FC, TOK]),
                 ("g_bv", [128, NT * 1040]),
                 ("g_st_in", [H, 256, 65]), ("g_st", [H, 2, 256, 65])]:
        G[n] = nc.dram_tensor(n, s, F32).ap()
    G["g_t1T"] = nc.dram_tensor("g_t1T", [D, DHID], BF16).ap()
    G["g_t2T"] = nc.dram_tensor("g_t2T", [DHID, D], BF16).ap()
    G["g_hact"] = nc.dram_tensor("g_hact", [DHID, TOK], F32).ap()
    G["g_lg"] = nc.dram_tensor("g_lg", [32, 128, TOK], F32).ap()
    G["g_fr"] = nc.dram_tensor("g_fr", [128, 64], F32).ap()
    G["g_nh"] = nc.dram_tensor("g_nh", [DHID, TOK], BF16).ap()

    dbg = None
    if debug_upto is not None:
        dbg = {"upto": debug_upto}
        for n, s in [("d_xln", [128, NT, D]), ("d_qT", [D, TOK]), ("d_kT", [D, TOK]),
                     ("d_preT", [D, TOK]), ("d_pqT", [D, TOK]),
                     ("d_vext", [128, NT * 1040]), ("d_gpre", [2 * D, TOK]),
                     ("d_x1", [128, NT, D]), ("d_attn", [8, 128, 2 * D]),
                     ("d_logit", [8, 128, 256]), ("d_den", [128, 32])]:
            dbg[n] = nc.dram_tensor(n, s, F32, kind="ExternalOutput").ap()

    with contextlib.ExitStack() as stack:
        block = stack.enter_context(nc.Block())
        pw = PW(nc, stack)
        pw._slot_state = {}

        T = {}
        T["groups"] = [[0, 1], [2, 3], [4, 5], [6, 7]]
        T["X"] = stack.enter_context(nc.sbuf_tensor("X", [128, NT, D], F32))
        T["ARN_A"] = stack.enter_context(nc.sbuf_tensor("ARN_A", [128, 16384], F32))
        T["ARN_B"] = stack.enter_context(nc.sbuf_tensor("ARN_B", [128, 16384], F32))
        T["tmpA"] = stack.enter_context(nc.sbuf_tensor("tmpA", [128, 1024], F32))
        T["stat"] = stack.enter_context(nc.sbuf_tensor("stat", [128, 8], F32))
        T["fcols"] = stack.enter_context(nc.sbuf_tensor("fcols", [128, 64], F32))
        T["ones_col"] = stack.enter_context(nc.sbuf_tensor("ones_col", [128, 1], F32))
        T["BWK"] = stack.enter_context(nc.sbuf_tensor("BWK", [128, 1472], F32))
        T["ones_row"] = stack.enter_context(nc.sbuf_tensor("ones_row", [128, 128], F32))
        T["shift64"] = stack.enter_context(nc.sbuf_tensor("shift64", [128, 128], F32))
        arnb = T["ARN_B"]
        T["wbuf3"] = arnb[:, 0:8192].rearrange("p (f o) -> p f o", o=1024)
        T["stage"] = arnb[:, 8192:10240]
        T["stage2"] = arnb[:, 10240:12288]
        T["vstage"] = arnb[:, 12288:14368]
        T["hx"] = arnb[:, 0:4100].rearrange("p (a b) -> p a b", a=2)
        T["tstat"] = arnb[:, 4104:5128].bitcast(BF16).rearrange(
            "p (a b c) -> p a b c", a=2, b=8)
        T["tstat4"] = arnb[:, 4104:6152].bitcast(BF16).rearrange(
            "p (a b c) -> p a b c", a=4, b=8)
        T["nstat"] = arnb[:, 5128:5640].bitcast(BF16).rearrange(
            "p (a b) -> p a b", a=2)
        T["nstat4"] = arnb[:, 6152:7176].bitcast(BF16).rearrange(
            "p (a b) -> p a b", a=4)
        T["nh2"] = arnb[:, 5640:6664].bitcast(BF16)
        T["z16w"] = arnb[:, 6664:7688].bitcast(F16)
        T["haloh"] = arnb[:, 7688:7752]
        T["tbf"] = arnb[:, 12288:13312].bitcast(BF16)
        T["frow"] = arnb[0:1, 13312:15372]
        T["halo2"] = T["stage"][0:2, 0:1024]
        T["halo2b"] = T["stage"][0:2, 1024:2048]
        arna = T["ARN_A"]
        T["nbfT"] = arna[:, 0:8192].bitcast(BF16).rearrange(
            "p (a b) -> p a b", a=8)
        T["ofm"] = arna[:, 0:2048].rearrange("p (a b) -> p a b", a=4)
        T["z16"] = arna[:, 15360:15872].bitcast(F16)
        T["nhT"] = arna[:, 15872:15880].bitcast(BF16).rearrange(
            "p (a b) -> p a b", a=8)
        T["t2s"] = arna[:, 0:8192].bitcast(BF16).rearrange(
            "p (a b) -> p a b", a=32)
        T["ofm2"] = arna[:, 11264:13312].rearrange("p (a b) -> p a b", a=4)
        T["z16h"] = T["z16"][0:2, :]
        T["f2row"] = arna[0:1, 2048:4096]
        T["ident"] = stack.enter_context(nc.sbuf_tensor("ident", [128, 128], F32))
        T["identb"] = stack.enter_context(nc.sbuf_tensor("identb", [128, 128], BF16))
        T["mask_ut"] = stack.enter_context(nc.sbuf_tensor("mask_ut", [128, 128], F32))
        T["consts"] = stack.enter_context(nc.sbuf_tensor("consts", [128, 8], F32))
        T["vecs"] = stack.enter_context(nc.sbuf_tensor("vecs", [128, 112], F32))
        T["scal"] = stack.enter_context(nc.sbuf_tensor("scal", [128, 16], F32))
        T["convw"] = stack.enter_context(nc.sbuf_tensor("convw", [128, NJC, 3], F32))
        T["convb"] = stack.enter_context(nc.sbuf_tensor("convb", [128, NJC], F32))
        T["projx2"] = stack.enter_context(nc.sbuf_tensor("projx2", [128, M], F32))
        T["psum"] = stack.enter_context(nc.psum_tensor("ps", [128, 3584], F32))
        T["psumb"] = stack.enter_context(nc.psum_tensor("psb", [128, 1024], BF16))

        T["cvals"] = [0.0, 1.0, 0.125, -LN16, 1536.0, -1536.0, 0.5, -128.0]
        for i, v in enumerate(T["cvals"]):
            nc.const_aps.register(F32, float(v), T["consts"][:, i:i + 1])

        ph_setup(pw, T, x_in, VEC, b_proj, f_convw)
        stop = ph_attn_a_proj(pw, T, W, G, dbg)
        if not stop:
            ph_attn_a_logits(pw, T, G, dbg)
            arnb_ = T["ARN_B"]
            wmbuf = [arnb_[:, 12288:14336].rearrange("p (a b) -> p a b", a=2),
                     arnb_[:, 14336:16384].rearrange("p (a b) -> p a b", a=2)]

            def wm_src(L):
                if L < 16:
                    return W["f_w1"].rearrange("(a p) d -> p a d", p=128)[
                        :, 2 * L:2 * L + 2, :]
                Lp = L - 16
                return W["f_w2"].rearrange("(a p) d -> p a d", p=128)[
                    :, Lp // 2:Lp // 2 + 1,
                    2048 * (Lp % 2):2048 * (Lp % 2) + 2048].rearrange(
                    "p a (b c) -> p (a b) c", c=1024)

            wm_state = {"k": 0}
            wm_pml = {}
            wm_red = {}

            def wm_load(L):
                if L >= 32:
                    return
                if L >= 2:
                    pw.wait("gpsimd", "dve", wm_red[2 * L - 3])
                wm_pml[L] = pw.dma("gpsimd", wmbuf[L % 2], wm_src(L),
                                   sem="pm" + str(L % 2))

            pw.wait("gpsimd", "qst")
            wm_load(0)
            wm_load(1)

            def whook():
                k = wm_state["k"]
                if k >= 64:
                    return
                wm_state["k"] = k + 1
                L = k // 2
                pw.wait("vector", "pm" + str(L % 2), wm_pml[L])
                wm_red[k] = pw.red(T["fcols"][:, k:k + 1],
                                   wmbuf[L % 2][:, k % 2, :],
                                   AluOpType.add, absval=True, dep=False)
                if k % 2 == 1:
                    wm_load(L + 2)

            stop = ph_attn_a_attn(pw, T, W, G, dbg, whook)
        if not stop:
            pw.barrier("b1")
            ph_ffn_weights(pw, T, W, G)
            pw.barrier("b2")
            stop = ph_ffn(pw, T, W, G, dbg, 1)
        if not stop:
            pw.barrier("b3")
            stop = ph_attn_b(pw, T, W, G, dbg)
        if not stop:
            pw.barrier("b4")
            stop = ph_ffn(pw, T, W, G, dbg, 2)
        if not stop:
            ph_store(pw, T, y_out)

        pw.wait("sync", "st") if "st" in pw.cnt else None
        pw.replay(block)
    return nc


def _shard_inputs(inputs):
    x = np.ascontiguousarray(np.asarray(inputs["L0_attn_a_x"], dtype=np.float32))
    maps = []
    for c in range(8):
        b, h = c // 2, c % 2
        m = {"x0": np.ascontiguousarray(x[b, 2048 * h:2048 * (h + 1), :])}
        m["aWq"] = inputs["L0_attn_a_Wq"]
        m["aWk"] = inputs["L0_attn_a_Wk"]
        m["aWv"] = inputs["L0_attn_a_Wv"]
        m["aWproj"] = inputs["L0_attn_a_Wproj"]
        m["aWpre"] = inputs["L0_attn_a_Wpre"]
        m["b_qw"] = inputs["L0_attn_b_q_w"]
        m["b_kw"] = inputs["L0_attn_b_k_w"]
        m["b_vw"] = inputs["L0_attn_b_v_w"]
        m["b_ow"] = inputs["L0_attn_b_out_w"]
        m["f_w1"] = inputs["L0_ffn_w1"]
        m["f_w2"] = inputs["L0_ffn_w2"]
        m["a_bproj"] = inputs["L0_attn_a_bproj"]
        m["a_bpre"] = inputs["L0_attn_a_bpre"]
        m["b_qb"] = inputs["L0_attn_b_q_b"]
        m["b_kb"] = inputs["L0_attn_b_k_b"]
        m["b_vb"] = inputs["L0_attn_b_v_b"]
        m["b_ob"] = inputs["L0_attn_b_out_b"]
        m["b_gamma"] = inputs["L0_attn_b_gamma"]
        m["b_beta"] = inputs["L0_attn_b_beta"]
        m["f_convb"] = inputs["L0_ffn_conv_b"]
        m["b_proj"] = inputs["L0_attn_b_proj"]
        m["f_convw"] = np.ascontiguousarray(
            np.asarray(inputs["L0_ffn_conv_w"], dtype=np.float32)[:, 0, :])
        m["pi_bc"] = np.ascontiguousarray(
            np.broadcast_to(np.asarray(inputs["L0_attn_a_pi"], np.float32), (128, 1)))
        m["isodd_bc"] = np.full((128, 1), float(h), np.float32)
        m["mleft_bc"] = np.full((128, 1), float(h), np.float32)
        m["mright_bc"] = np.full((128, 1), float(1 - h), np.float32)
        m = {k: np.ascontiguousarray(np.asarray(v, dtype=np.float32)) if k not in ()
             else v for k, v in m.items()}
        maps.append(m)
    return maps


def kernel(**inputs):
    nc = build_nc()
    maps = _shard_inputs(inputs)
    res = run_bass_kernel_spmd(nc, maps, core_ids=list(range(8)))
    out = np.empty((4, 4096, 1024), np.float32)
    for c in range(8):
        b, h = c // 2, c % 2
        out[b, 2048 * h:2048 * (h + 1), :] = res.results[c]["y"]
    return out


# version 12 (final)
# speedup vs baseline: 1.2008x; 1.0122x over previous
import contextlib
import numpy as np
import concourse.bass as bass
import concourse.mybir as mybir
from concourse.bass_utils import run_bass_kernel_spmd
from concourse.alu_op_type import AluOpType

F32 = mybir.dt.float32
F16 = mybir.dt.float16
BF16 = mybir.dt.bfloat16
AF = mybir.ActivationFunctionType
AX = mybir.AxisListType

TOK = 2048
NT = 16
D = 1024
FC = 8
H = 16
DH = 64
M = 256
DHID = 4096
NJC = 32
LN_EPS = 1e-5
LNB_EPS = 1e-6
LN16 = float(np.log(np.float64(16.0)))

ENGS = ("sync", "tensor", "vector", "scalar", "gpsimd")


class PW:

    def __init__(self, nc, stack):
        self.nc = nc
        self.stack = stack
        self.ops = {e: [] for e in ENGS}
        self.cnt = {}
        self.sems = {}

    def sem(self, name):
        if name not in self.sems:
            self.sems[name] = self.stack.enter_context(self.nc.semaphore(name))
            self.cnt[name] = 0
        return self.sems[name]

    def emit(self, eng, fn):
        self.ops[eng].append(fn)

    def wait(self, eng, name, val=None):
        s, v = self.sem(name), (self.cnt[name] if val is None else val)
        self.emit(eng, lambda e: e.wait_ge(s, v))

    def dma(self, eng, out, in_, sem=None, slow=False):
        kw = {"allow_slow_non_contiguous": True} if slow else {}
        if sem is None:
            self.emit(eng, lambda e: e.dma_start(out=out, in_=in_, **kw))
            return None
        s = self.sem(sem)
        self.cnt[sem] += 16
        self.emit(eng, lambda e: e.dma_start(out=out, in_=in_, **kw).then_inc(s, 16))
        return self.cnt[sem]

    AUTO = {"vector": "dve", "scalar": "act", "gpsimd": "gp"}

    def _op(self, eng, fn, sem=None, n=1, dep=True):
        auto = self.AUTO.get(eng)
        if auto is not None:
            s = self.sem(auto)
            if dep and self.cnt[auto] > 0:
                v = self.cnt[auto]
                self.emit(eng, lambda e: e.wait_ge(s, v))
            self.cnt[auto] += 1
            self.emit(eng, lambda e: fn(e).then_inc(s, 1))
            return self.cnt[auto]
        if sem is None:
            self.emit(eng, fn)
            return None
        s = self.sem(sem)
        self.cnt[sem] += n
        self.emit(eng, lambda e: fn(e).then_inc(s, n))
        return self.cnt[sem]

    def mm(self, out, lhsT, rhs, start, stop, sem=None, skip=False):
        return self._op("tensor",
                        lambda e: e.matmul(out, lhsT, rhs, start=start, stop=stop,
                                           skip_group_check=skip),
                        sem)

    def trans(self, out, in_, ident, sem=None):
        return self._op("tensor", lambda e: e.transpose(out, in_, ident), sem)

    def mmT(self, out, data, ident, start, stop, sem=None):
        return self._op("tensor",
                        lambda e: e.matmul(out, data, ident, start=start, stop=stop,
                                           is_transpose=True), sem)

    def act(self, out, in_, func, bias=None, scale=None, accum_out=None, sem=None,
            dep=True):
        kw = {}
        if bias is not None:
            kw["bias"] = bias
        if scale is not None:
            kw["scale"] = scale
        if accum_out is not None:
            kw["accum_out"] = accum_out
        return self._op("scalar",
                        lambda e: e.activation(out=out, in_=in_, func=func, **kw),
                        sem, dep=dep)

    def ts(self, out, in0, s1, op0, s2=None, op1=None, sem=None, eng="vector",
           dep=True):
        return self._op(eng,
                        lambda e: e.tensor_scalar(out=out, in0=in0, scalar1=s1,
                                                  scalar2=s2, op0=op0,
                                                  **({"op1": op1} if op1 is not None else {})),
                        sem, dep=dep)

    def tt(self, out, in0, in1, op, sem=None, eng="vector", dep=True):
        return self._op(eng,
                        lambda e: e.tensor_tensor(out=out, in0=in0, in1=in1, op=op),
                        sem, dep=dep)

    def stt(self, out, in0, scalar, in1, op0, op1, sem=None, dep=True):
        return self._op("vector",
                        lambda e: e.scalar_tensor_tensor(out=out, in0=in0, scalar=scalar,
                                                         in1=in1, op0=op0, op1=op1),
                        sem, dep=dep)

    def copy(self, out, in_, sem=None, eng="vector", dep=True):
        return self._op(eng, lambda e: e.tensor_copy(out, in_), sem, dep=dep)

    def red(self, out, in_, op, absval=None, sem=None, negate=None, dep=True):
        return self._op("vector",
                        lambda e: e.tensor_reduce(out=out, in_=in_, axis=AX.X, op=op,
                                                  apply_absolute_value=absval,
                                                  negate=negate), sem, dep=dep)

    def recip(self, out, in_, sem=None, dep=True):
        return self._op("vector", lambda e: e.reciprocal(out=out, in_=in_), sem,
                        dep=dep)

    def memset(self, ap, c, sem=None, dep=True):
        return self._op("gpsimd", lambda e: e.memset(ap, c), sem, dep=dep)

    def coll(self, kind, op, groups, ins, outs, sem=None, dep=True):
        return self._op("gpsimd",
                        lambda e: e.collective_compute(kind, op, replica_groups=groups,
                                                       ins=ins, outs=outs), sem,
                        dep=dep)

    def barrier(self, tag):
        vals = dict(self.cnt)
        for eng in ENGS:
            for name, v in vals.items():
                if v > 0 and not name.startswith("_"):
                    self.wait(eng, name, v)

    def replay(self, block):
        for eng in ENGS:
            ops = self.ops[eng]
            dec = getattr(block, eng)

            def make(fn_list):
                def body(e):
                    for f in fn_list:
                        f(e)
                return body
            dec(make(ops))



def ph_setup(pw, T, x_in, VEC, b_proj, f_convw):
    nc = pw.nc
    pw.dma("sync", T["X"][:, :, :], x_in.rearrange("(t p) f -> p t f", p=128), sem="ld")
    for col, name in enumerate(["a_bproj", "a_bpre", "b_qb", "b_kb", "b_vb",
                                "b_ob", "b_gamma", "b_beta"]):
        pw.dma("sync", T["vecs"][:, 8 * col:8 * col + 8],
               VEC[name].rearrange("(f p) -> p f", p=128), sem="ld", slow=True)
    pw.dma("sync", T["scal"][:, 0:1], VEC["pi_bc"], sem="ld")
    pw.dma("sync", T["scal"][:, 1:2], VEC["isodd_bc"], sem="ld")
    pw.dma("sync", T["scal"][:, 2:3], VEC["mleft_bc"], sem="ld")
    pw.dma("sync", T["scal"][:, 3:4], VEC["mright_bc"], sem="ld")
    pw.dma("sync", T["convw"][:, :, :], f_convw.rearrange("(c p) k -> p c k", p=128), sem="ld", slow=True)
    pw.dma("sync", T["convb"][:], VEC["f_convb"].rearrange("(c p) -> p c", p=128), sem="ld", slow=True)
    pw.dma("sync", T["projx2"][0:64, :], b_proj.rearrange("m d -> d m"), sem="ld", slow=True)

    cvals = T["cvals"]
    for i, v in enumerate(cvals):
        pw.memset(T["consts"][:, i:i + 1], float(v))
    pw.memset(T["ident"][:], 1.0)
    pw._op("gpsimd", lambda e: e.affine_select(
        out=T["ident"][:], in_=T["ident"][:], compare_op=AluOpType.is_equal,
        fill=0.0, base=0, pattern=[[-1, 128]], channel_multiplier=1))
    pw.memset(T["identb"][:], 1.0)
    pw._op("gpsimd", lambda e: e.affine_select(
        out=T["identb"][:], in_=T["identb"][:], compare_op=AluOpType.is_equal,
        fill=0.0, base=0, pattern=[[-1, 128]], channel_multiplier=1))
    pw.memset(T["mask_ut"][:], 1.0)
    pw._op("gpsimd", lambda e: e.affine_select(
        out=T["mask_ut"][:], in_=T["mask_ut"][:], compare_op=AluOpType.is_ge,
        fill=0.0, base=0, pattern=[[1, 128]], channel_multiplier=-1))
    pw.memset(T["ones_col"][:], 1.0)
    pw.memset(T["ones_row"][:], 1.0)
    pw.memset(T["shift64"][:], 1.0)
    pw._op("gpsimd", lambda e: e.affine_select(
        out=T["shift64"][:], in_=T["shift64"][:], compare_op=AluOpType.is_equal,
        fill=0.0, base=-64, pattern=[[1, 128]], channel_multiplier=-1))
    pw.memset(T["projx2"][64:128, :], -0.5, sem="gp")
    pw.wait("vector", "ld")
    pw.wait("vector", "gp")
    pw.ts(T["scal"][:, 4:5], T["scal"][:, 0:1], 1.0 / 32.0, AluOpType.mult)
    pw.ts(T["vecs"][:, 96:104], T["vecs"][:, 16:24], 0.125, AluOpType.mult)
    pw.ts(T["vecs"][:, 104:112], T["vecs"][:, 24:32], 0.125, AluOpType.mult,
          sem="dve")


def emit_ln_tile(pw, x_tile, out_tile, tmp, stat, eps):
    pw.red(stat[:, 0:1], x_tile, AluOpType.add)
    pw.ts(stat[:, 1:2], stat[:, 0:1], 1.0 / D, AluOpType.mult)
    pw.ts(tmp, x_tile, stat[:, 1:2], AluOpType.subtract)
    pw.tt(out_tile, tmp, tmp, AluOpType.mult)
    pw.red(stat[:, 0:1], out_tile, AluOpType.add)
    v1 = pw.ts(stat[:, 2:3], stat[:, 0:1], 1.0 / D, AluOpType.mult, eps,
               AluOpType.add, sem="dve")
    pw.wait("scalar", "dve", v1)
    a1 = pw.act(stat[:, 2:3], stat[:, 2:3], AF.Sqrt, sem="act")
    pw.wait("vector", "act", a1)
    pw.recip(stat[:, 3:4], stat[:, 2:3])
    return pw.ts(out_tile, tmp, stat[:, 3:4], AluOpType.mult, sem="dve")


def emit_transpose_tm_to_fm(pw, src_tile_fn, dst_fn, nt_tiles, nfc, T, bankA, bankB,
                            dtype_ident="ident"):
    ident = T[dtype_ident][:]
    psum = T["psum"]
    slots = [bankA, bankB]
    pending = []
    mmc0 = pw.cnt.get("mm", 0)
    for t in range(nt_tiles):
        src = src_tile_fn(t)
        for fb in range(0, nfc, 4):
            nf = min(4, nfc - fb)
            slot = slots[(t * ((nfc + 3) // 4) + fb // 4) % 2]
            key = slot
            prev = pw._slot_state.get(key)
            if prev is not None:
                pw.wait("tensor", prev[0], prev[1])
            last = None
            for j in range(nf):
                last = pw.mm(psum[:, slot + 128 * j:slot + 128 * (j + 1)],
                             src[:, 128 * (fb + j):128 * (fb + j + 1)], ident,
                             start=(j == 0), stop=(j == nf - 1),
                             sem="mm" if j == nf - 1 else None)
            pw.wait("vector", "mm", last)
            for j in range(nf):
                s = pw.copy(dst_fn(t, fb + j),
                            psum[:, slot + 128 * j:slot + 128 * (j + 1)],
                            sem="dve" if j == nf - 1 else None)
            pw._slot_state[key] = ("dve", s)


def emit_proj_fm(pw, T, ld_done, evac, dma_dst, dma_sem, span_hook=None,
                 rhs_buf=None, prev_sems=()):
    psum = T["psum"]
    xlnT = T["ARN_A"] if rhs_buf is None else rhs_buf
    pw.wait("tensor", "ld", ld_done)
    for s_ in prev_sems:
        if s_ in pw.cnt:
            pw.wait("scalar", s_)
            pw.wait("vector", s_)
    dma_hist = {}
    for sp in range(4):
        for of in range(FC):
            it = sp * FC + of
            bk = it % 6
            prev = pw._slot_state.get(("pj", bk))
            if prev is not None:
                pw.wait("tensor", prev[0], prev[1])
            last = None
            for f in range(FC):
                last = pw.mm(psum[:, 512 * bk:512 * bk + 512],
                             T["wbuf3"][:, f, 128 * of:128 * of + 128],
                             xlnT[:, 2048 * f + 512 * sp:2048 * f + 512 * (sp + 1)],
                             start=(f == 0), stop=(f == FC - 1),
                             sem="mm" if f == FC - 1 else None)
            eng, cnt, region = evac(of, sp, psum[:, 512 * bk:512 * bk + 512],
                                    last, it >= 4)
            pw._slot_state[("pj", bk)] = (eng, cnt)
            pw.wait("sync", eng, cnt)
            dma_hist[it % 4] = pw.dma("sync", dma_dst(of, sp), region, sem=dma_sem)
        if span_hook is not None:
            span_hook(sp)


def ph_attn_a_proj(pw, T, W, G, dbg):
    X, ARN_A, ARN_B = T["X"], T["ARN_A"], T["ARN_B"]
    psum, stage = T["psum"], T["stage"]
    groups = T["groups"]

    xln = ARN_B
    tmp, stat = T["tmpA"], T["stat"]
    pw.wait("vector", "ld")
    for t in range(NT):
        emit_ln_tile(pw, X[:, t, :], xln[:, 1024 * t:1024 * (t + 1)], tmp[:, 0:1024],
                     stat, LN_EPS)
    pw.wait("tensor", "dve")
    pw.wait("tensor", "gp")
    xlnT = ARN_A

    def dst(t, f):
        return xlnT[:, 2048 * f + 128 * t:2048 * f + 128 * (t + 1)]
    emit_transpose_tm_to_fm(pw, lambda t: xln[:, 1024 * t:1024 * (t + 1)],
                            dst, NT, FC, T, 0, 512)
    if dbg and dbg.get("upto") == "ln1":
        pw.barrier("dbgln")
        pw.dma("sync", dbg["d_xln"], xln.rearrange("p (t f) -> p t f", f=1024),
               sem="st")
        return True
    pw.barrier("projstart")

    def load_W(name):
        return pw.dma("sync", T["wbuf3"], W[name].rearrange("(f p) o -> p f o", p=128),
                      sem="ld")

    ag_done = {}

    def mk_evac(kind, dsem):
        def evac(of, sp, ps, mmcnt, need_guard):
            it = sp * FC + of
            region = stage[:, 512 * (it % 4):512 * (it % 4) + 512]
            pw.wait("scalar", "mm", mmcnt)
            if need_guard:
                pw.wait("scalar", dsem)
            if kind == "silu_bpre":
                c = pw.act(region, ps, AF.Silu, bias=T["vecs"][:, 8 + of:9 + of])
            elif kind == "copy":
                c = pw.act(region, ps, AF.Copy)
            elif kind == "q8":
                c = pw.act(region, ps, AF.Copy, scale=0.125)
            return ("act", c, region)
        return evac

    ld = load_W("aWpre")

    def hook_pre(sp):
        pw.wait("gpsimd", "pst")
        ag_done[("pre", sp)] = pw.coll(
            "AllGather", AluOpType.bypass, groups,
            [G[f"g_pre_in_s{sp}"].opt()], [G[f"g_pre_s{sp}"].opt()])
    emit_proj_fm(pw, T, ld, mk_evac("silu_bpre", "pst"),
                 lambda of, sp: G[f"g_pre_in_s{sp}"][128 * of:128 * (of + 1), :],
                 "pst", hook_pre)

    ld = load_W("aWk")

    def hook_k(sp):
        pw.wait("gpsimd", "kst")
        ag_done[("k", sp)] = pw.coll(
            "AllGather", AluOpType.bypass, groups,
            [G[f"g_k_in_s{sp}"].opt()], [G[f"g_k_s{sp}"].opt()])
    emit_proj_fm(pw, T, ld, mk_evac("copy", "kst"),
                 lambda of, sp: G[f"g_k_in_s{sp}"][128 * of:128 * (of + 1), :],
                 "kst", hook_k, prev_sems=("pst",))

    ld = load_W("aWv")
    pw.memset(T["vstage"], 1.0)
    pw.wait("vector", "gp")
    pw.wait("tensor", "ld", ld)
    vdma = {}
    for mc in range(NT):
        for vh in range(2):
            bk = (mc * 2 + vh) % 6
            prev = pw._slot_state.get(("pj", bk))
            if prev is not None:
                pw.wait("tensor", prev[0], prev[1])
            last = None
            for f in range(FC):
                last = pw.mm(psum[:, 512 * bk:512 * bk + 512],
                             T["ARN_A"][:, 2048 * f + 128 * mc:2048 * f + 128 * (mc + 1)],
                             T["wbuf3"][:, f, 512 * vh:512 * (vh + 1)],
                             start=(f == 0), stop=(f == FC - 1),
                             sem="mm" if f == FC - 1 else None)
            pw.wait("vector", "mm", last)
            if vdma.get(mc % 2) is not None:
                pw.wait("vector", "vst")
            dstv = T["vstage"][:, 1040 * (mc % 2) + 520 * vh:1040 * (mc % 2) + 520 * (vh + 1)] \
                .rearrange("p (h c) -> p h c", c=65)[:, :, 0:64]
            c = pw.copy(dstv, psum[:, 512 * bk:512 * bk + 512]
                        .rearrange("p (h c) -> p h c", c=64), dep=False)
            pw._slot_state[("pj", bk)] = ("dve", c)
        pw.wait("sync", "dve")
        sp, r = mc // 4, mc % 4
        vdma[mc % 2] = pw.dma("sync", G[f"g_v_in_s{sp}"][:, 1040 * r:1040 * (r + 1)],
                              T["vstage"][:, 1040 * (mc % 2):1040 * (mc % 2) + 1040],
                              sem="vst")
        if r == 3:
            pw.wait("gpsimd", "vst")
            ag_done[("v", sp)] = pw.coll(
                "AllGather", AluOpType.bypass, groups,
                [G[f"g_v_in_s{sp}"].opt()], [G[f"g_v_s{sp}"].opt()])

    ld = load_W("aWq")
    emit_proj_fm(pw, T, ld, mk_evac("q8", "qst"),
                 lambda of, sp: G["g_qT"][:, of, 512 * sp:512 * (sp + 1)], "qst",
                 prev_sems=("kst",))

    pw.wait("sync", "pst")
    pw.wait("sync", "qst")
    for of in range(FC):
        l = None
        for sp in range(4):
            l = pw.dma("sync", stage[:, 512 * sp:512 * (sp + 1)],
                       G[f"g_pre_in_s{sp}"][128 * of:128 * (of + 1), :], sem="ld")
        pw.wait("vector", "ld", l)
        if of >= 1:
            pw.wait("vector", "qst")
        s = pw.ts(T["stage2"][:, 0:2048], stage[:, 0:2048], T["scal"][:, 4:5],
                  AluOpType.mult, dep=True)
        pw.wait("sync", "dve", s)
        pw.dma("sync", G["g_pqT"][:, of, :], T["stage2"][:, 0:2048], sem="qst")

    pw._ag_done = ag_done
    if dbg and dbg.get("upto") == "aproj":
        pw.barrier("dbgproj")
        for sp in range(4):
            pw.dma("sync", dbg["d_preT"][:, 512 * sp:512 * (sp + 1)],
                   G[f"g_pre_in_s{sp}"], sem="st")
            pw.dma("sync", dbg["d_kT"][:, 512 * sp:512 * (sp + 1)],
                   G[f"g_k_in_s{sp}"], sem="st")
            pw.dma("sync", dbg["d_vext"][:, 4160 * sp:4160 * (sp + 1)],
                   G[f"g_v_in_s{sp}"], sem="st")
            pw.dma("sync", dbg["d_gpre"][0:1024, 512 * sp:512 * (sp + 1)],
                   G[f"g_pre_s{sp}"][0], sem="st")
            pw.dma("sync", dbg["d_gpre"][1024:2048, 512 * sp:512 * (sp + 1)],
                   G[f"g_pre_s{sp}"][1], sem="st")
        pw.dma("sync", dbg["d_qT"].rearrange("(f p) t -> p f t", p=128), G["g_qT"],
               sem="st")
        pw.dma("sync", dbg["d_pqT"].rearrange("(f p) t -> p f t", p=128), G["g_pqT"],
               sem="st")
        return True
    return False


def ph_attn_a_logits(pw, T, G, dbg):
    A = T["ARN_A"]
    psum = T["psum"]
    PREQ, PM, LT = 0, 8192, 10240
    agv = pw._ag_done[("pre", 3)]
    pw.wait("sync", "gp", agv)
    pw.wait("sync", "qst")
    for qh in range(2):
        pw.wait("sync", "mm")
        pw.wait("sync", "act")
        lq = pw.dma("sync", A[:, PREQ:PREQ + 8192].rearrange("p (f t) -> p f t", t=1024),
                    G["g_pqT"][:, :, 1024 * qh:1024 * (qh + 1)], sem="alq")
        pw.wait("tensor", "alq", lq)
        mld = {}

        def issue(mc):
            if mc >= 32:
                return
            par = mc % 2
            slot, g = mc // 16, mc % 16
            sp, r = g // 4, g % 4
            mld[mc] = pw.dma("sync", A[:, PM + 1024 * par:PM + 1024 * (par + 1)]
                             .rearrange("p (f t) -> p f t", t=128),
                             G[f"g_pre_s{sp}"][slot].rearrange("(f p) t -> p f t", p=128)
                             [:, :, 128 * r:128 * (r + 1)], sem="alm" + str(par))
        issue(0)
        issue(1)
        ldma = {}
        for mc in range(32):
            par = mc % 2
            pw.wait("tensor", "alm" + str(par), mld[mc])
            pre = A[:, PM + 1024 * par:PM + 1024 * (par + 1)]
            for qs in range(2):
                bk = (mc * 2 + qs) % 6
                prev = pw._slot_state.get(512 * bk)
                if prev is not None:
                    pw.wait("tensor", prev[0], prev[1])
                last = None
                for f in range(FC):
                    last = pw.mm(psum[:, 512 * bk:512 * bk + 512],
                                 pre[:, 128 * f:128 * (f + 1)],
                                 A[:, PREQ + 1024 * f + 512 * qs:PREQ + 1024 * f + 512 * (qs + 1)],
                                 start=(f == 0), stop=(f == FC - 1),
                                 sem="mm" if f == FC - 1 else None)
                pw.wait("scalar", "mm", last)
                if ldma.get(par) is not None and qs == 0:
                    pw.wait("scalar", "alst" + str(par))
                lc = pw.act(A[:, LT + 1024 * par + 512 * qs:LT + 1024 * par + 512 * (qs + 1)],
                            psum[:, 512 * bk:512 * bk + 512], AF.Copy, dep=False)
                pw._slot_state[512 * bk] = ("act", lc)
            pw.wait("sync", "act")
            ldma[par] = pw.dma("sync", G["g_lg"][mc][:, 1024 * qh:1024 * (qh + 1)],
                               A[:, LT + 1024 * par:LT + 1024 * (par + 1)],
                               sem="alst" + str(par))
            pw.wait("sync", "mm")
            issue(mc + 2)
        pw.wait("sync", "alst0")
        pw.wait("sync", "alst1")


def ph_attn_a_attn(pw, T, W, G, dbg, whook=None):
    X, A, B = T["X"], T["ARN_A"], T["ARN_B"]
    psum = T["psum"]
    QS = 0
    KT2 = 4096
    VX2 = 4608
    LG2 = 5184
    TMPX = 6208
    EXPX = 7232
    ATT = 8448
    STG = 12544
    LNB = 13568

    ldw = pw.dma("sync", T["wbuf3"], W["aWproj"].rearrange("(f p) o -> p f o", p=128),
                 sem="ld")
    pw.wait("sync", "gp", max(pw._ag_done.values()))

    for span in range(4):
        pw.wait("sync", "mm")
        pw.wait("sync", "dve")
        lq = pw.dma("sync", A[:, QS:QS + 4096].rearrange("p (f t) -> p f t", t=512),
                    G["g_qT"][:, :, 512 * span:512 * (span + 1)], sem="aq")
        pw.wait("tensor", "aq", lq)
        for hg in range(4):
            mld = {}

            def issue(mc):
                if mc >= 32:
                    return
                par = mc % 2
                sm = "am" + str(par)
                slot, g = mc // 16, mc % 16
                sp, r = g // 4, g % 4
                pw.dma("sync", A[:, KT2 + 256 * par:KT2 + 256 * (par + 1)]
                       .rearrange("p (f t) -> p f t", t=128),
                       G[f"g_k_s{sp}"][slot].rearrange("(f p) t -> p f t", p=128)
                       [:, 2 * hg:2 * hg + 2, 128 * r:128 * (r + 1)], sem=sm)
                pw.dma("sync", A[:, VX2 + 260 * par:VX2 + 260 * (par + 1)],
                       G[f"g_v_s{sp}"][slot][:, 1040 * r + 260 * hg:
                                             1040 * r + 260 * (hg + 1)], sem=sm)
                mld[mc] = pw.dma("sync", A[:, LG2 + 512 * par:LG2 + 512 * (par + 1)],
                                 G["g_lg"][mc][:, 512 * span:512 * (span + 1)], sem=sm)
            pw.wait("sync", "mm")
            pw.wait("sync", "dve")
            issue(0)
            issue(1)
            pend = None
            ec_hist = {}
            for mc in range(32):
                if whook is not None and mc in (8, 16, 24):
                    whook()
                par = mc % 2
                pw.wait("tensor", "am" + str(par), mld[mc])
                kt = A[:, KT2 + 256 * par:KT2 + 256 * (par + 1)]
                vx = A[:, VX2 + 260 * par:VX2 + 260 * (par + 1)]
                lg = A[:, LG2 + 512 * par:LG2 + 512 * (par + 1)]
                for hh in range(4):
                    h = 4 * hg + hh
                    i = mc * 4 + hh
                    bk = 2560 if i % 2 == 0 else 3072
                    eb = i % 2
                    prev = pw._slot_state.get(bk)
                    if prev is not None:
                        pw.wait("tensor", prev[0], prev[1])
                    sm2 = pw.mm(psum[:, bk:bk + 512],
                                kt[64 * (hh % 2):64 * (hh % 2) + 64,
                                   128 * (hh // 2):128 * (hh // 2) + 128],
                                A[64 * (h % 2):64 * (h % 2) + 64,
                                  QS + 512 * (h // 2):QS + 512 * (h // 2) + 512],
                                start=True, stop=True, sem="mm")
                    if pend is not None:
                        phh, pmc, pec, peb, pvx = pend
                        pw.wait("tensor", "act", pec)
                        pw.mm(psum[0:65, 512 * phh:512 * phh + 512],
                              pvx[:, 65 * phh:65 * (phh + 1)],
                              A[:, EXPX + 512 * peb:EXPX + 512 * (peb + 1)],
                              start=(pmc == 0), stop=(pmc == 31),
                              sem="mm" if phh == 3 else None)
                        pend = None
                        if hh == 0:
                            pw.wait("sync", "mm")
                            issue(mc + 1)
                    pw.wait("vector", "mm", sm2)
                    if eb in ec_hist:
                        pw.wait("vector", "act", ec_hist[eb])
                    tv = pw.tt(A[:, TMPX + 512 * eb:TMPX + 512 * (eb + 1)],
                               psum[:, bk:bk + 512], lg, AluOpType.add, dep=False)
                    pw.wait("scalar", "dve", tv)
                    ec = pw.act(A[:, EXPX + 512 * eb:EXPX + 512 * (eb + 1)],
                                A[:, TMPX + 512 * eb:TMPX + 512 * (eb + 1)],
                                AF.Exp, dep=False)
                    ec_hist[eb] = ec
                    pw._slot_state[bk] = ("dve", tv)
                    pend = (hh, mc, ec, eb, vx)
            phh, pmc, pec, peb, pvx = pend
            pw.wait("tensor", "act", pec)
            pw.mm(psum[0:65, 512 * phh:512 * phh + 512],
                  pvx[:, 65 * phh:65 * (phh + 1)],
                  A[:, EXPX + 512 * peb:EXPX + 512 * (peb + 1)],
                  start=(pmc == 0), stop=(pmc == 31), sem="mm")
            pend = None
            pw.wait("vector", "mm")
            pw.wait("tensor", "dve")
            for hh in range(4):
                h = 4 * hg + hh
                rd = pw.recip(A[64:65, STG + 512:STG + 1024],
                              psum[64:65, 512 * hh:512 * hh + 512], dep=True)
                cv = pw.copy(A[0:64, STG:STG + 512],
                             psum[0:64, 512 * hh:512 * hh + 512], dep=True)
                pw.wait("tensor", "dve", cv)
                bcm = pw.mm(psum[:, 2560:2560 + 512], T["ones_row"][64:65, :],
                            A[64:65, STG + 512:STG + 1024], start=True, stop=True,
                            sem="mm")
                pw.wait("vector", "mm", bcm)
                if h % 2 == 0:
                    dv = pw.tt(A[0:64, ATT + 512 * (h // 2):ATT + 512 * (h // 2) + 512],
                               A[0:64, STG:STG + 512], psum[0:64, 2560:2560 + 512],
                               AluOpType.mult, dep=True)
                else:
                    sv = pw.tt(A[0:64, STG:STG + 512], A[0:64, STG:STG + 512],
                               psum[0:64, 2560:2560 + 512], AluOpType.mult, dep=True)
                    pw.wait("tensor", "dve", sv)
                    sh = pw.mm(psum[:, 3072:3072 + 512], T["shift64"][0:64, :],
                               A[0:64, STG:STG + 512], start=True, stop=True,
                               sem="mm")
                    pw.wait("vector", "mm", sh)
                    dv = pw.copy(A[64:128, ATT + 512 * (h // 2):ATT + 512 * (h // 2) + 512],
                                 psum[64:128, 3072:3072 + 512], dep=True)
                pw._slot_state[2560] = ("dve", dv)
                pw._slot_state[3072] = ("dve", dv)
            if whook is not None:
                whook()

        pw.wait("tensor", "dve")
        pw.wait("tensor", "ld", ldw)
        for of in range(FC):
            bk = 2560 if of % 2 == 0 else 3072
            prev = pw._slot_state.get(bk)
            if prev is not None:
                pw.wait("tensor", prev[0], prev[1])
            last = None
            for f in range(FC):
                last = pw.mm(psum[:, bk:bk + 512],
                             T["wbuf3"][:, f, 128 * of:128 * of + 128],
                             A[:, ATT + 512 * f:ATT + 512 * (f + 1)],
                             start=(f == 0), stop=(f == FC - 1),
                             sem="mm" if f == FC - 1 else None)
            pw.wait("scalar", "mm", last)
            oc = pw.act(A[:, STG + 512 * 0:STG + 512] if False else
                        B[:, 8192 + 512 * of:8192 + 512 * (of + 1)],
                        psum[:, bk:bk + 512], AF.Identity,
                        bias=T["vecs"][:, 0 + of:1 + of], dep=False)
            pw._slot_state[bk] = ("act", oc)
        pw.wait("tensor", "act")
        for tc in range(4):
            t_global = 4 * span + tc
            pw.wait("vector", "mm")
            emit_ln_tile(pw, X[:, t_global, :], A[:, LNB:LNB + 1024],
                         T["tmpA"][:, 0:1024], T["stat"], LN_EPS)
            for ob in range(0, FC, 4):
                bk = 2560 if (ob // 4) % 2 == 0 else 3072
                prev = pw._slot_state.get(bk)
                if prev is not None:
                    pw.wait("tensor", prev[0], prev[1])
                last = None
                for j in range(4):
                    last = pw.mm(psum[:, bk + 128 * j:bk + 128 * (j + 1)],
                                 B[:, 8192 + 512 * (ob + j) + 128 * tc:
                                   8192 + 512 * (ob + j) + 128 * (tc + 1)],
                                 T["ident"][:], start=(j == 0), stop=(j == 3),
                                 sem="mm" if j == 3 else None)
                pw.wait("vector", "mm", last)
                pw.tt(A[:, LNB + 128 * ob:LNB + 128 * (ob + 4)],
                      A[:, LNB + 128 * ob:LNB + 128 * (ob + 4)],
                      psum[:, bk:bk + 512], AluOpType.add, dep=True)
                dv = pw.tt(X[:, t_global, 128 * ob:128 * (ob + 4)],
                           X[:, t_global, 128 * ob:128 * (ob + 4)],
                           A[:, LNB + 128 * ob:LNB + 128 * (ob + 4)],
                           AluOpType.add, dep=True)
                pw._slot_state[bk] = ("dve", dv)

    if dbg and dbg.get("upto") == "attna":
        pw.barrier("dbgattna")
        pw.dma("sync", dbg["d_x1"], X[:, :, :], sem="st")
        return True
    return False


def row_from_cols(pw, G, cols_ap, row_ap, nprt, ncol, sem):
    n = pw.cnt.get("_rfc", 0)
    pw.cnt["_rfc"] = n + 1
    s = f"rfc{n}"
    if n > 0:
        pw.wait("sync", f"rfc{n - 1}")
    pw.dma("sync", G["g_fr"][0:nprt, 0:ncol], cols_ap, sem=s, slow=True)
    pw.wait("sync", s)
    pw.dma("sync", row_ap,
           G["g_fr"][0:nprt, 0:ncol].rearrange("i t -> t i"), sem=s, slow=True)
    pw.wait("tensor", s)


def ph_ffn_weights(pw, T, W, G):
    psum, psumb = T["psum"], T["psumb"]
    wbuf3 = T["wbuf3"]
    acc = T["stage"]

    for wi, (wname, nblk, gdst, mcol, scol) in enumerate(
            [("f_w1", 32, "g_t1T", 5, 7), ("f_w2", 8, "g_t2T", 6, 8)]):
        ncols = 1024 if wi == 0 else 4096
        per = 8 if wi == 0 else 2
        nld = nblk // per
        pw.red(acc[:, 64:65], T["fcols"][:, 32 * wi:32 * wi + 32], AluOpType.add)
        pw.wait("tensor", "dve")
        s = pw.mm(psum[0:1, 0:1], T["ones_col"][:], acc[:, 64:65], start=True,
                  stop=True, sem="mm")
        pw.wait("vector", "mm", s)
        pw.ts(acc[0:1, 65:66], psum[0:1, 0:1], 1.0 / (nblk * 128 * ncols),
              AluOpType.mult, 1e-5, AluOpType.max)
        v = pw.recip(acc[0:1, 66:67], acc[0:1, 65:66])
        pw.wait("tensor", "dve", v)
        s = pw.mm(psum[:, 4:5], T["ones_row"][0:1, :], acc[0:1, 65:66], start=True,
                  stop=True)
        s = pw.mm(psum[:, 5:6], T["ones_row"][0:1, :], acc[0:1, 66:67], start=True,
                  stop=True, sem="mm")
        pw.wait("vector", "mm", s)
        pw.copy(T["scal"][:, mcol:mcol + 1], psum[:, 4:5])
        pw.copy(T["scal"][:, scol:scol + 1], psum[:, 5:6])

        for g in range(nld):
            pw.wait("sync", "dve")
            l = pw.dma("sync", wbuf3 if wi == 0 else
                       wbuf3.rearrange("p a o -> p (a o)").rearrange(
                           "p (b c) -> p b c", b=2),
                       W[wname].rearrange("(a p) d -> p a d", p=128)
                       [:, per * g:per * (g + 1), :], sem="ld")
            pw.wait("vector", "ld", l)
            for a in range(per):
                blk = wbuf3[:, a, :] if wi == 0 else                     wbuf3.rearrange("p a o -> p (a o)")[:, 4096 * a:4096 * (a + 1)]
                j = g * per + a
                for c0 in range(0, ncols, 1024):
                    sl = blk[:, c0:c0 + 1024] if wi else blk
                    pw.ts(T["tmpA"][:, 0:1024], sl, T["scal"][:, scol:scol + 1],
                          AluOpType.mult)
                    pw.ts(T["stage2"][:, 0:1024], T["tmpA"][:, 0:1024], 0.5,
                          AluOpType.is_gt)
                    pw.ts(T["tmpA"][:, 0:1024], T["tmpA"][:, 0:1024], -0.5,
                          AluOpType.is_lt)
                    pw.tt(T["stage2"][:, 0:1024], T["stage2"][:, 0:1024],
                          T["tmpA"][:, 0:1024], AluOpType.subtract)
                    tv = pw.copy(T["tbf"][:, 0:1024], T["stage2"][:, 0:1024])
                    pw.wait("tensor", "dve", tv)
                    prev = pw._slot_state.get("psb")
                    if prev is not None:
                        pw.wait("tensor", prev[0], prev[1])
                    last = None
                    for dc in range(8):
                        last = pw.mmT(psumb[:, 128 * dc:128 * (dc + 1)],
                                      T["tbf"][:, 128 * dc:128 * (dc + 1)],
                                      T["identb"][:], start=(dc == 0), stop=(dc == 7),
                                      sem="mm" if dc == 7 else None)
                    pw.wait("vector", "mm", last)
                    if "wst" in pw.cnt:
                        pw.wait("vector", "wst")
                    dv = pw.copy(T["tbf"][:, 1024:2048]
                                 .rearrange("p (a b) -> p a b", a=8), psumb[:]
                                 .rearrange("p (a b) -> p a b", a=8), dep=False)
                    pw._slot_state["psb"] = ("dve", dv)
                    pw.wait("sync", "dve", dv)
                    if wi == 0:
                        dst = G["g_t1T"].rearrange("(dc p) j -> p dc j", p=128)[
                            :, :, 128 * j:128 * (j + 1)]
                    else:
                        s_ = c0 // 1024
                        j2 = 2 * g + a
                        dst = G["g_t2T"].rearrange("(hc p) j -> p hc j", p=128)[
                            :, 8 * s_:8 * (s_ + 1), 128 * j2:128 * (j2 + 1)]
                    pw.dma("sync", dst,
                           T["tbf"][:, 1024:2048].rearrange("p (a b) -> p a b", a=8),
                           sem="wst")
    pw.wait("sync", "wst")


def ph_ffn(pw, T, W, G, dbg, which):
    X, A, B = T["X"], T["ARN_A"], T["ARN_B"]
    psum, psumb = T["psum"], T["psumb"]
    groups = T["groups"]
    tag = f"f{which}"
    NTT = 0
    XL = 8192
    FB = 9216
    MAXA = 11264
    SB_ = 13312
    TMPC = 0
    hx = T["hx"]
    nbfT = T["nbfT"]
    tstat = T["tstat"]
    z16 = T["z16"]
    nbf = T["tbf"]

    pw.wait("sync", "dve")
    pw.dma("sync", G["g_halo_in"][0:1, :], X[0:1, 0, :], sem=tag + "hs")
    hd = pw.dma("sync", G["g_halo_in"][1:2, :], X[127:128, 15, :], sem=tag + "hs")
    pw.wait("gpsimd", tag + "hs", hd)
    agh = pw.coll("AllGather", AluOpType.bypass, groups, [G["g_halo_in"].opt()],
                  [G["g_halo"].opt()])

    tmp, stat = T["tmpA"], T["stat"]
    tr_done = None
    for t in range(NT):
        if tr_done is not None:
            pw.wait("vector", "mm", tr_done)
        emit_ln_tile(pw, X[:, t, :], A[:, XL:XL + 1024], tmp[:, 0:1024], stat, LN_EPS)
        pw.red(stat[:, 6:7], A[:, XL:XL + 1024], AluOpType.max, absval=True)
        pw.ts(stat[:, 6:7], stat[:, 6:7], 1e-5, AluOpType.max)
        pw.recip(stat[:, 7:8], stat[:, 6:7])
        pw.ts(stat[:, 7:8], stat[:, 7:8], 127.0, AluOpType.mult)
        pw.ts(T["fcols"][:, t:t + 1], stat[:, 6:7], T["scal"][:, 5:6],
              AluOpType.mult, 1.0 / 127.0, AluOpType.mult)
        pw.ts(z16[:], A[:, XL:XL + 1024], stat[:, 7:8], AluOpType.mult, 1536.0,
              AluOpType.add)
        qv = pw.ts(nbf[:, 0:1024], z16[:], -1536.0, AluOpType.add)
        pw.wait("tensor", "dve", qv)
        prev = pw._slot_state.get("psb")
        if prev is not None:
            pw.wait("tensor", prev[0], prev[1])
        last = None
        for dc in range(8):
            last = pw.mmT(psumb[:, 128 * dc:128 * (dc + 1)],
                          nbf[:, 128 * dc:128 * (dc + 1)], T["identb"][:],
                          start=(dc == 0), stop=(dc == 7),
                          sem="mm" if dc == 7 else None)
        pw.wait("vector", "mm", last)
        dv = pw.copy(nbfT[:, :, 128 * t:128 * (t + 1)],
                     psumb[:].rearrange("p (a b) -> p a b", a=8), dep=False)
        pw._slot_state["psb"] = ("dve", dv)
        tr_done = last

    fv = pw.cnt["dve"]
    pw.wait("sync", "dve", fv)
    row_from_cols(pw, G, T["fcols"][:, 0:16], T["frow"][0:1, 0:2048], 128, 16,
                  tag + "h")
    s = pw.mm(psum[:, 0:512], T["ones_row"][0:1, :], T["frow"][0:1, 0:512], start=True,
              stop=True)
    s = pw.mm(psum[:, 512:1024], T["ones_row"][0:1, :], T["frow"][0:1, 512:1024],
              start=True, stop=True)
    s = pw.mm(psum[:, 1024:1536], T["ones_row"][0:1, :], T["frow"][0:1, 1024:1536],
              start=True, stop=True)
    s = pw.mm(psum[:, 1536:2048], T["ones_row"][0:1, :], T["frow"][0:1, 1536:2048],
              start=True, stop=True, sem="mm")
    pw.wait("vector", "mm", s)
    pw.copy(A[:, FB:FB + 2048], psum[:, 0:2048], dep=False)

    pw.wait("sync", "gp", agh)
    hl = pw.dma("sync", T["halo2"][0:1, :], G["g_halo"][0, 1, :][None, :],
                sem=tag + "hl")
    hl = pw.dma("sync", T["halo2"][1:2, :], G["g_halo"][1, 0, :][None, :],
                sem=tag + "hl")
    pw.wait("vector", tag + "hl", hl)
    emit_ln_tile(pw, T["halo2"][0:2, :], T["halo2b"][0:2, :], tmp[0:2, 0:1024],
                 stat[0:2, :], LN_EPS)
    pw.red(stat[0:2, 6:7], T["halo2b"][0:2, :], AluOpType.max, absval=True)
    pw.ts(stat[0:2, 6:7], stat[0:2, 6:7], 1e-5, AluOpType.max)
    pw.recip(stat[0:2, 7:8], stat[0:2, 6:7])
    pw.ts(stat[0:2, 7:8], stat[0:2, 7:8], 127.0, AluOpType.mult)
    pw.ts(T["fcols"][0:2, 16:17], stat[0:2, 6:7], T["scal"][0:2, 5:6],
          AluOpType.mult, 1.0 / 127.0, AluOpType.mult)
    pw.ts(T["z16h"][0:2, :], T["halo2b"][0:2, :], stat[0:2, 7:8], AluOpType.mult,
          1536.0, AluOpType.add)
    qv = pw.ts(nbf[0:2, 0:1024], T["z16h"][0:2, :], -1536.0, AluOpType.add)
    pw.wait("tensor", "dve", qv)
    prev = pw._slot_state.get("psb")
    if prev is not None:
        pw.wait("tensor", prev[0], prev[1])
    last = None
    for dc in range(8):
        last = pw.mmT(psumb[:, 16 * dc:16 * dc + 2],
                      nbf[0:2, 128 * dc:128 * (dc + 1)], T["identb"][0:2, 0:2],
                      start=(dc == 0), stop=(dc == 7), sem="mm" if dc == 7 else None)
    pw.wait("vector", "mm", last)
    dv = pw.copy(T["nhT"][:, :, :], psumb[:, 0:128].rearrange("p (a b) -> p a b", b=16)
                 [:, :, 0:2], dep=False)
    pw._slot_state["psb"] = ("dve", dv)
    fh = pw.cnt["dve"]
    pw.wait("sync", "dve", fh)
    row_from_cols(pw, G, T["fcols"][0:2, 16:17], T["frow"][0:1, 2048:2050], 2, 1,
                  tag + "h")
    s = pw.mm(psum[:, 512 * 5:512 * 5 + 2], T["ones_row"][0:1, :],
              T["frow"][0:1, 2048:2050], start=True, stop=True, sem="mm")
    pw.wait("vector", "mm", s)
    pw.copy(T["haloh"][:, 0:2], psum[:, 512 * 5:512 * 5 + 2], dep=False)
    pw.ts(T["haloh"][:, 0:1], T["haloh"][:, 0:1], T["scal"][:, 2:3],
          AluOpType.mult)
    pw.ts(T["haloh"][:, 1:2], T["haloh"][:, 1:2], T["scal"][:, 3:4],
          AluOpType.mult)

    pw.memset(A[:, MAXA:MAXA + 2048], 0.0)
    pw.wait("vector", "gp")
    pw.wait("tensor", "dve")
    dv0 = pw.cnt["dve"]
    for b_ in range(5):
        pw._slot_state[("fx", b_)] = ("dve", dv0)
    pw._slot_state["hb6"] = ("dve", dv0)
    hdma = {}
    av_hist = {}
    tstat4 = T["tstat4"]
    mmh1 = {-4: pw.cnt["mm"], -3: pw.cnt["mm"],
            -2: pw.cnt["mm"], -1: pw.cnt["mm"]}
    for jc in range(NJC):
        pw.wait("sync", "mm", mmh1[jc - 4])
        l = pw.dma("sync", tstat4[:, jc % 4, :, :],
                   G["g_t1T"].rearrange("(dc p) j -> p dc j", p=128)
                   [:, :, 128 * jc:128 * (jc + 1)], sem=tag + "w" + str(jc % 4))
        pw.wait("tensor", tag + "w" + str(jc % 4), l)
        strip = hx[:, jc % 2, :]
        cbuf = (T["stage2"][:, 0:2048] if jc % 2 == 0
                else T["stage"][:, 0:2048])
        for sp in range(4):
            bk = (jc * 4 + sp) % 5
            prev = pw._slot_state.get(("fx", bk))
            if prev is not None:
                pw.wait("tensor", prev[0], prev[1])
            last = None
            for dc in range(8):
                last = pw.mm(psum[:, 512 * bk:512 * bk + 512],
                             tstat4[:, jc % 4, dc, :],
                             nbfT[:, dc, 512 * sp:512 * (sp + 1)],
                             start=(dc == 0), stop=(dc == 7),
                             sem="mm" if dc == 7 else None)
            pw.wait("vector", "mm", last)
            if sp == 0:
                if hdma.get(jc % 2) is not None:
                    pw.wait("vector", tag + "hd" + str(jc % 2), hdma[jc % 2])
                if av_hist.get(jc % 2) is not None:
                    pw.wait("vector", "act", av_hist[jc % 2])
            c = pw.tt(strip[:, 1 + 512 * sp:1 + 512 * (sp + 1)],
                      psum[:, 512 * bk:512 * bk + 512],
                      A[:, FB + 512 * sp:FB + 512 * (sp + 1)], AluOpType.mult,
                      dep=False)
            pw._slot_state[("fx", bk)] = ("dve", c)
        prev = pw._slot_state.get("hb6")
        if prev is not None:
            pw.wait("tensor", prev[0], prev[1])
        lasth = None
        for dc in range(8):
            lasth = pw.mm(psum[:, 512 * 6:512 * 6 + 2],
                          tstat4[:, jc % 4, dc, :], T["nhT"][:, dc, :],
                          start=(dc == 0), stop=(dc == 7),
                          sem="mm" if dc == 7 else None)
        mmh1[jc] = pw.cnt["mm"]
        pw.wait("vector", "mm", lasth)
        pw.tt(strip[:, 0:1], psum[:, 512 * 6:512 * 6 + 1], T["haloh"][:, 0:1],
              AluOpType.mult, dep=True)
        hc2 = pw.tt(strip[:, 2049:2050], psum[:, 512 * 6 + 1:512 * 6 + 2],
                    T["haloh"][:, 1:2], AluOpType.mult, dep=True)
        pw._slot_state["hb6"] = ("dve", hc2)
        pw.ts(cbuf, strip[:, 0:2048], T["convw"][:, jc, 0:1], AluOpType.mult)
        pw.stt(cbuf, strip[:, 1:2049], T["convw"][:, jc, 1:2],
               cbuf, AluOpType.mult, AluOpType.add)
        cv = pw.stt(cbuf, strip[:, 2:2050], T["convw"][:, jc, 2:3],
                    cbuf, AluOpType.mult, AluOpType.add)
        pw.wait("scalar", "dve", cv)
        av = pw.act(strip[:, 1:2049],
                    cbuf, AF.Silu, bias=T["convb"][:, jc:jc + 1],
                    dep=False)
        av_hist[jc % 2] = av
        pw.wait("vector", "act", av)
        pw.stt(A[:, MAXA:MAXA + 2048], strip[:, 1:2049], 1.0,
               A[:, MAXA:MAXA + 2048], AluOpType.mult, AluOpType.max)
        pw.stt(A[:, MAXA:MAXA + 2048], strip[:, 1:2049], -1.0,
               A[:, MAXA:MAXA + 2048], AluOpType.mult, AluOpType.max)
        pw.wait("gpsimd", "act", av)
        hdma[jc % 2] = pw.dma("gpsimd", G["g_hact"][128 * jc:128 * (jc + 1), :],
                              strip[:, 1:2049], sem=tag + "hd" + str(jc % 2))

    pw.wait("tensor", "dve")
    for tt_ in range(NT):
        prev = pw._slot_state.get(("fx", tt_ % 2))
        if prev is not None:
            pw.wait("tensor", prev[0], prev[1])
        s = pw.mm(psum[:, 512 * (tt_ % 2):512 * (tt_ % 2) + 128],
                  A[:, MAXA + 128 * tt_:MAXA + 128 * (tt_ + 1)], T["ident"][:],
                  start=True, stop=True, sem="mm")
        pw.wait("vector", "mm", s)
        rv = pw.red(T["fcols"][:, 32 + tt_:33 + tt_],
                    psum[:, 512 * (tt_ % 2):512 * (tt_ % 2) + 128],
                    AluOpType.max, dep=False)
        pw._slot_state[("fx", tt_ % 2)] = ("dve", rv)
    pw.ts(T["fcols"][:, 32:48], T["fcols"][:, 32:48], 1e-5, AluOpType.max)
    pw.ts(T["fcols"][:, 48:64], T["fcols"][:, 32:48], T["scal"][:, 6:7],
          AluOpType.mult, 1.0 / 127.0, AluOpType.mult)
    rv = pw.recip(T["fcols"][:, 32:48], T["fcols"][:, 32:48])
    sv = pw.ts(T["fcols"][:, 32:48], T["fcols"][:, 32:48], 127.0, AluOpType.mult)
    pw.wait("sync", "dve", sv)
    row_from_cols(pw, G, T["fcols"][:, 32:48], T["frow"][0:1, 0:2048], 128, 16,
                  tag + "h")
    row_from_cols(pw, G, T["fcols"][:, 48:64], T["f2row"][0:1, 0:2048], 128, 16,
                  tag + "h")
    pw.wait("tensor", "dve")
    for half in range(4):
        s = pw.mm(psum[:, 512 * half:512 * (half + 1)], T["ones_row"][0:1, :],
                  T["frow"][0:1, 512 * half:512 * (half + 1)], start=True, stop=True,
                  sem="mm" if half == 3 else None)
    pw.wait("vector", "mm", s)
    pw.copy(A[:, SB_:SB_ + 2048], psum[:, 0:2048], dep=False)
    pw.wait("tensor", "dve")
    for half in range(4):
        s = pw.mm(psum[:, 512 * half:512 * (half + 1)], T["ones_row"][0:1, :],
                  T["f2row"][0:1, 512 * half:512 * (half + 1)], start=True, stop=True,
                  sem="mm" if half == 3 else None)
    pw.wait("vector", "mm", s)
    f2v = pw.copy(A[:, FB:FB + 2048], psum[:, 0:2048], dep=False)

    pw.wait("sync", "mm", pw.cnt["mm"])
    lt2 = pw.dma("sync", T["t2s"][:, :, :],
                 G["g_t2T"].rearrange("(hc p) j -> p hc j", p=128)
                 [:, :, 0:512], sem=tag + "t2")

    pw.wait("sync", "dve", f2v)
    pw.wait("sync", tag + "hd0")
    pw.wait("sync", tag + "hd1")
    pw.wait("vector", "act")
    pw.wait("tensor", "dve", f2v)
    pw.wait("tensor", tag + "t2", lt2)
    ndma = {}
    qh_tt = {}
    nv_hist = {}
    fmm = {-3: pw.cnt["mm"], -2: pw.cnt["mm"], -1: pw.cnt["mm"]}

    def fused_mms(jc):
        nh_b = T["nh2"] if jc % 2 == 0 else T["tbf"][:, 0:2048]
        pw.wait("tensor", "dve", nv_hist[jc])
        for oc in range(4):
            pw.mm(psum[:, 512 * oc:512 * oc + 512],
                  T["t2s"][:, jc, 128 * oc:128 * (oc + 1)],
                  nh_b[:, 0:512],
                  start=(jc == 0), stop=(jc == NJC - 1),
                  sem="mm" if oc == 3 else None)
        fmm[jc] = pw.cnt["mm"]

    for jc in range(NJC):
        if qh_tt.get(jc % 2) is not None:
            pw.wait("sync", "dve", qh_tt[jc % 2])
        l = pw.dma("sync", hx[:, jc % 2, 0:2048],
                   G["g_hact"][128 * jc:128 * (jc + 1), :], sem=tag + "q" + str(jc % 2))
        pw.wait("vector", tag + "q" + str(jc % 2), l)
        if ndma.get(jc % 2) is not None:
            pw.wait("vector", tag + "nd" + str(jc % 2), ndma[jc % 2])
        nh_buf = T["nh2"] if jc % 2 == 0 else T["tbf"][:, 0:2048]
        pw.wait("vector", "mm", fmm[jc - 2])
        qh_tt[jc % 2] = pw.tt(T["stage2"][:, 0:2048], hx[:, jc % 2, 0:2048],
                              A[:, SB_:SB_ + 2048], AluOpType.mult, dep=False)
        pw.ts(T["z16w"][:, 0:2048], T["stage2"][:, 0:2048], 1536.0, AluOpType.add)
        nv = pw.ts(nh_buf, T["z16w"][:, 0:2048], -1536.0, AluOpType.add)
        nv_hist[jc] = nv
        pw.wait("gpsimd", "dve", nv)
        ndma[jc % 2] = pw.dma("gpsimd", G["g_nh"][128 * jc:128 * (jc + 1), :],
                              nh_buf, sem=tag + "nd" + str(jc % 2))
        if jc >= 1:
            fused_mms(jc - 1)
    fused_mms(NJC - 1)

    pw.wait("vector", "mm", fmm[NJC - 1])
    for oc in range(4):
        pw.tt(T["ofm2"][:, oc, :], psum[:, 512 * oc:512 * oc + 512],
              A[:, FB:FB + 512], AluOpType.mult, dep=False)
    pw.wait("tensor", "dve")
    for tt_ in range(4):
        bk = 4 + (tt_ % 2)
        prev = pw._slot_state.get(("fx2", bk))
        if prev is not None:
            pw.wait("tensor", prev[0], prev[1])
        last = None
        for oc in range(4):
            last = pw.mm(psum[:, 512 * bk + 128 * oc:512 * bk + 128 * (oc + 1)],
                         T["ofm2"][:, oc, 128 * tt_:128 * (tt_ + 1)],
                         T["ident"][:], start=(oc == 0), stop=(oc == 3),
                         sem="mm" if oc == 3 else None)
        pw.wait("vector", "mm", last)
        xv = pw.tt(X[:, tt_, 0:512], X[:, tt_, 0:512],
                   psum[:, 512 * bk:512 * bk + 512], AluOpType.add, dep=False)
        pw._slot_state[("fx2", bk)] = ("dve", xv)

    pw.wait("sync", tag + "nd0")
    pw.wait("sync", tag + "nd1")
    pw.wait("sync", "mm")
    pw.wait("sync", "dve", nv_hist[NJC - 1])
    pw.wait("tensor", "dve")
    for oh in range(2):
        if oh == 1:
            pw.wait("sync", "mm")
            l2 = pw.dma("sync", T["t2s"][:, :, :],
                        G["g_t2T"].rearrange("(hc p) j -> p hc j", p=128)
                        [:, :, 512 * oh:512 * (oh + 1)], sem=tag + "t2")
            pw.wait("tensor", tag + "t2", l2)
        for sp in range(4):
            if oh == 0 and sp == 0:
                continue
            mmh2 = {-4: pw.cnt["mm"], -3: pw.cnt["mm"],
                    -2: pw.cnt["mm"], -1: pw.cnt["mm"]}
            for jc in range(NJC):
                pw.wait("sync", "mm", mmh2[jc - 4])
                sm2 = tag + "x" + str(jc % 4)
                l1 = pw.dma("sync", T["nstat4"][:, jc % 4, :],
                            G["g_nh"][128 * jc:128 * (jc + 1),
                                      512 * sp:512 * (sp + 1)], sem=sm2)
                pw.wait("tensor", sm2, l1)
                for oc in range(4):
                    pw.mm(psum[:, 512 * oc:512 * oc + 512],
                          T["t2s"][:, jc, 128 * oc:128 * (oc + 1)],
                          T["nstat4"][:, jc % 4, :],
                          start=(jc == 0), stop=(jc == NJC - 1),
                          sem="mm" if oc == 3 else None)
                mmh2[jc] = pw.cnt["mm"]
            pw.wait("vector", "mm")
            for oc in range(4):
                ev = pw.tt(T["ofm2"][:, oc, :],
                           psum[:, 512 * oc:512 * oc + 512],
                           A[:, FB + 512 * sp:FB + 512 * (sp + 1)],
                           AluOpType.mult, dep=False)
            pw.wait("tensor", "dve")
            for tt_ in range(4):
                t_global = 4 * sp + tt_
                bk = 4 + (tt_ % 2)
                prev = pw._slot_state.get(("fx2", bk))
                if prev is not None:
                    pw.wait("tensor", prev[0], prev[1])
                last = None
                for oc in range(4):
                    last = pw.mm(psum[:, 512 * bk + 128 * oc:512 * bk + 128 * (oc + 1)],
                                 T["ofm2"][:, oc, 128 * tt_:128 * (tt_ + 1)],
                                 T["ident"][:], start=(oc == 0), stop=(oc == 3),
                                 sem="mm" if oc == 3 else None)
                pw.wait("vector", "mm", last)
                xv = pw.tt(X[:, t_global, 512 * oh:512 * (oh + 1)],
                           X[:, t_global, 512 * oh:512 * (oh + 1)],
                           psum[:, 512 * bk:512 * bk + 512], AluOpType.add,
                           dep=False)
                pw._slot_state[("fx2", bk)] = ("dve", xv)

    if dbg and dbg.get("upto") == "ffn" + str(which):
        pw.barrier("dbgffn")
        pw.dma("sync", dbg["d_x1"], X[:, :, :], sem="st")
        return True
    return False


def ph_attn_b(pw, T, W, G, dbg):
    X, A, B = T["X"], T["ARN_A"], T["ARN_B"]
    psum = T["psum"]
    groups = T["groups"]
    BW = T["BWK"]
    VXB = 0
    KVS = 1040
    STS = 1172
    DEN = 1304
    ASB = 1336
    QP0, QP1 = 0, 4096
    CATQ, CATK = 8192, 10240
    KPF = 12288
    KPT = 8192
    GBC = 1024

    tmp, stat = T["tmpA"], T["stat"]

    pw.wait("vector", "mm")
    xln = B
    for t in range(NT):
        emit_ln_tile(pw, X[:, t, :], xln[:, 1024 * t:1024 * (t + 1)], tmp[:, 0:1024],
                     stat, LN_EPS)
    pw.wait("tensor", "dve")

    def dst(t, f):
        return A[:, 2048 * f + 128 * t:2048 * f + 128 * (t + 1)]
    emit_transpose_tm_to_fm(pw, lambda t: xln[:, 1024 * t:1024 * (t + 1)],
                            dst, NT, FC, T, 0, 512)
    pw.barrier("battnb")

    def load_W(name):
        return pw.dma("sync", T["wbuf3"], W[name].rearrange("(f p) o -> p f o", p=128),
                      sem="ld")

    stage = T["stage"]

    def mk_evac(kind, dsem):
        def evac(of, sp, ps, mmcnt, need_guard):
            it = sp * FC + of
            region = stage[:, 512 * (it % 4):512 * (it % 4) + 512]
            pw.wait("scalar", "mm", mmcnt)
            if need_guard:
                pw.wait("scalar", dsem)
            if kind == "q":
                c = pw.act(region, ps, AF.Identity, scale=0.125,
                           bias=T["vecs"][:, 96 + of:97 + of])
            else:
                c = pw.act(region, ps, AF.Identity, scale=0.125,
                           bias=T["vecs"][:, 104 + of:105 + of])
            return ("act", c, region)
        return evac

    ld = load_W("b_qw")
    emit_proj_fm(pw, T, ld, mk_evac("q", "bqs"),
                 lambda of, sp: G["g_bqT"][:, of, 512 * sp:512 * (sp + 1)], "bqs")
    ld = load_W("b_kw")
    emit_proj_fm(pw, T, ld, mk_evac("k", "bks"),
                 lambda of, sp: G["g_bkT"][:, of, 512 * sp:512 * (sp + 1)], "bks",
                 prev_sems=("bqs",))

    row_from_cols(pw, G, T["vecs"][:, 32:40], T["frow"][0:1, 0:1024], 128, 8, "bvr")
    pw.wait("tensor", "dve")
    for half in range(2):
        s = pw.mm(psum[:, 512 * half:512 * (half + 1)], T["ones_row"][0:1, :],
                  T["frow"][0:1, 512 * half:512 * (half + 1)], start=True, stop=True,
                  sem="mm" if half == 1 else None)
    pw.wait("vector", "mm", s)
    pw.copy(T["stage2"][:, 0:1024], psum[:, 0:1024], dep=False)

    ld = load_W("b_vw")
    pw.wait("gpsimd", "dve")
    mz = pw.memset(T["vstage"], 1.0)
    pw.wait("vector", "gp", mz)
    pw.wait("sync", "gp", mz)
    pw.wait("tensor", "ld", ld)
    pw.wait("vector", "bks")
    pw.wait("tensor", "dve")
    vdma = {}
    for mc in range(NT):
        for vh in range(2):
            bk = (mc * 2 + vh) % 4
            prev = pw._slot_state.get(512 * bk)
            if prev is not None:
                pw.wait("tensor", prev[0], prev[1])
            last = None
            for f in range(FC):
                last = pw.mm(psum[:, 512 * bk:512 * bk + 512],
                             A[:, 2048 * f + 128 * mc:2048 * f + 128 * (mc + 1)],
                             T["wbuf3"][:, f, 512 * vh:512 * (vh + 1)],
                             start=(f == 0), stop=(f == FC - 1),
                             sem="mm" if f == FC - 1 else None)
            pw.wait("vector", "mm", last)
            if vdma.get(mc % 2) is not None:
                pw.wait("vector", "bvs")
            dstv = T["vstage"][:, 1040 * (mc % 2) + 520 * vh:1040 * (mc % 2) + 520 * (vh + 1)] \
                .rearrange("p (h c) -> p h c", c=65)[:, :, 0:64]
            c = pw.stt(dstv, psum[:, 512 * bk:512 * bk + 512]
                       .rearrange("p (h c) -> p h c", c=64), 1.0,
                       T["stage2"][:, 512 * vh:512 * (vh + 1)]
                       .rearrange("p (h c) -> p h c", c=64), AluOpType.mult,
                       AluOpType.add, dep=False)
            pw._slot_state[512 * bk] = ("dve", c)
        pw.wait("sync", "dve")
        vdma[mc % 2] = pw.dma("sync", G["g_bv"][:, 1040 * mc:1040 * (mc + 1)],
                              T["vstage"][:, 1040 * (mc % 2):1040 * (mc % 2) + 1040],
                              sem="bvs")
    pw.wait("sync", "bvs")

    if dbg and dbg.get("upto") == "bproj":
        pw.barrier("dbgbp")
        pw.dma("sync", dbg["d_qT"].rearrange("(f p) t -> p f t", p=128), G["g_bqT"],
               sem="st")
        pw.dma("sync", dbg["d_kT"].rearrange("(f p) t -> p f t", p=128), G["g_bkT"],
               sem="st")
        pw.dma("sync", dbg["d_vext"], G["g_bv"], sem="st")
        pw.wait("sync", "st")
        return True

    ag_h = {}
    ld_hist = {}

    def load_feats(h):
        sm = "bh" + str(h % 2)
        for (buf, g) in ((CATQ, G["g_bqT"]), (CATK, G["g_bkT"])):
            pw.dma("sync", A[0:64, buf:buf + 2048],
                   g[64 * (h % 2):64 * (h % 2) + 64, h // 2, :], sem=sm)
            pw.dma("sync", A[64:128, buf:buf + 2048],
                   g[64 * (h % 2):64 * (h % 2) + 64, h // 2, :], sem=sm)
        lv = pw.dma("sync", BW[:, VXB:VXB + 1040].rearrange("p (tc c) -> p tc c", c=65),
                    G["g_bv"].rearrange("p (tc c) -> p tc c", c=1040)
                    [:, :, 65 * h:65 * (h + 1)], sem=sm)
        pw.wait("scalar", sm, lv)
        pw.act(A[64:128, CATQ:CATQ + 2048], A[64:128, CATQ:CATQ + 2048],
               AF.Square, dep=False)
        a2 = pw.act(A[64:128, CATK:CATK + 2048], A[64:128, CATK:CATK + 2048],
                    AF.Square, dep=False)
        ld_hist[h] = a2

    def features(h):
        qpb = QP0 if h % 2 == 0 else QP1
        pw.wait("tensor", "act", ld_hist[h])
        for (src2, dstc) in ((CATQ, qpb), (CATK, KPF)):
            for mc in range(2):
                for sp in range(4):
                    bk = 3 + (mc * 4 + sp) % 4
                    prev = pw._slot_state.get(512 * bk)
                    if prev is not None:
                        pw.wait("tensor", prev[0], prev[1])
                    mmv = pw.mm(psum[:, 512 * bk:512 * bk + 512],
                                T["projx2"][:, 128 * mc:128 * (mc + 1)],
                                A[:, src2 + 512 * sp:src2 + 512 * (sp + 1)],
                                start=True, stop=True, sem="mm")
                    pw.wait("scalar", "mm", mmv)
                    dst_ap = A[:, dstc + 2048 * mc + 512 * sp:
                               dstc + 2048 * mc + 512 * (sp + 1)]
                    ec = pw.act(dst_ap, psum[:, 512 * bk:512 * bk + 512], AF.Exp,
                                bias=-LN16, dep=False)
                    pw._slot_state[512 * bk] = ("act", ec)
        for tc in range(NT):
            bk = 3 + tc % 4
            prev = pw._slot_state.get(512 * bk)
            if prev is not None:
                pw.wait("tensor", prev[0], prev[1])
            mmv = pw.mm(psum[:, 512 * bk:512 * bk + 256],
                        A[:, CATK + 128 * tc:CATK + 128 * (tc + 1)],
                        T["projx2"][:, 0:256], start=True, stop=True, sem="mm")
            pw.wait("scalar", "mm", mmv)
            ec = pw.act(A[:, KPT + 256 * tc:KPT + 256 * (tc + 1)],
                        psum[:, 512 * bk:512 * bk + 256], AF.Exp, bias=-LN16,
                        dep=False)
            pw._slot_state[512 * bk] = ("act", ec)

    def scan(h):
        qpb = QP0 if h % 2 == 0 else QP1
        pw.wait("gpsimd", "act")
        pw.wait("gpsimd", "mm")
        for p_ in ("bst0", "bst1"):
            if p_ in pw.cnt:
                pw.wait("gpsimd", p_)
        z = pw.memset(BW[:, KVS:KVS + 130], 0.0)
        pw.wait("vector", "gp", z)
        pw.wait("tensor", "gp", z)
        pw.wait("scalar", "act")
        am_hist = {0: None, 1: None}
        ap_last = {}
        kv_last = None

        def aprime(c):
            abk = 512 if c % 2 == 0 else 2048
            prev = am_hist[c % 2]
            if prev is not None:
                pw.wait("tensor", prev[0], prev[1])
            last = None
            for mc in range(2):
                last = pw.mm(psum[:, abk:abk + 128],
                             A[:, KPF + 2048 * mc + 128 * c:KPF + 2048 * mc + 128 * (c + 1)],
                             A[:, qpb + 2048 * mc + 128 * c:qpb + 2048 * mc + 128 * (c + 1)],
                             start=(mc == 0), stop=(mc == 1),
                             sem="mm" if mc == 1 else None)
            ap_last[c] = last

        aprime(0)
        for c in range(NT):
            abk = 512 if c % 2 == 0 else 2048
            if c + 1 < NT:
                aprime(c + 1)
            if c > 0:
                pw.wait("vector", "mm", kv_last)
                pw.copy(BW[:, KVS:KVS + 130], psum[:, 0:130], dep=False)
            pw.wait("vector", "mm", ap_last[c])
            am = pw.tt(BW[:, ASB:ASB + 128], psum[:, abk:abk + 128], T["mask_ut"][:],
                       AluOpType.mult, dep=False)
            am_hist[c % 2] = ("dve", am)
            pw.wait("tensor", "dve", am)
            for mc in range(2):
                pw.mm(psum[:, 1024:1024 + 65],
                      A[:, qpb + 2048 * mc + 128 * c:qpb + 2048 * mc + 128 * (c + 1)],
                      BW[:, KVS + 65 * mc:KVS + 65 * (mc + 1)],
                      start=(mc == 0), stop=False)
            nm = pw.mm(psum[:, 1024:1024 + 65], BW[:, ASB:ASB + 128],
                       BW[:, VXB + 65 * c:VXB + 65 * (c + 1)],
                       start=False, stop=True, sem="mm")
            for mc in range(2):
                pw.mm(psum[:, 65 * mc:65 * (mc + 1)],
                      A[:, KPT + 256 * c + 128 * mc:KPT + 256 * c + 128 * (mc + 1)],
                      BW[:, VXB + 65 * c:VXB + 65 * (c + 1)],
                      start=(c == 0 and mc == 0), stop=(mc == 1),
                      sem="mm" if mc == 1 else None, skip=True)
            kv_last = pw.cnt["mm"]
            pw.wait("vector", "mm", nm)
            pw.copy(B[:, 1024 * c + 64 * h:1024 * c + 64 * (h + 1)],
                    psum[:, 1024:1024 + 64], dep=False)
            pw.copy(BW[:, DEN + 16 * (h % 2) + c:DEN + 16 * (h % 2) + c + 1],
                    psum[:, 1024 + 64:1024 + 65], dep=False)
        if am_hist[1] is not None:
            pw._slot_state[2048] = am_hist[1]
        pw.wait("vector", "mm", kv_last)
        sv = pw.copy(BW[:, KVS:KVS + 130], psum[:, 0:130], dep=False)
        pw.wait("sync", "dve", sv)
        sd = pw.dma("sync", G["g_st_in"][h].rearrange("(mc p) c -> p mc c", p=128),
                    BW[:, KVS:KVS + 130].rearrange("p (mc c) -> p mc c", c=65),
                    sem="bst" + str(h % 2))
        pw.wait("gpsimd", "bst" + str(h % 2), sd)
        ag_h[h] = pw.coll("AllGather", AluOpType.bypass, groups,
                          [G["g_st_in"][h].opt()], [G["g_st"][h].opt()])
        if h + 1 < H:
            pw.wait("sync", "mm", kv_last)
            load_feats(h + 1)

    def finalize(h):
        qpb = QP0 if h % 2 == 0 else QP1
        pw.wait("sync", "gp", ag_h[h])
        pw.wait("sync", "mm")
        lv = pw.dma("sync", BW[:, STS:STS + 130].rearrange("p (mc c) -> p mc c", c=65),
                    G["g_st"][h][0].rearrange("(mc p) c -> p mc c", p=128),
                    sem="bfs" + str(h % 2))
        pw.wait("vector", "bfs" + str(h % 2), lv)
        mk = pw.ts(BW[:, STS:STS + 130], BW[:, STS:STS + 130], T["scal"][:, 1:2],
                   AluOpType.mult, dep=True)
        pw.wait("tensor", "dve", mk)
        for c in range(NT):
            bk = 2 + c % 2
            prev = pw._slot_state.get(512 * bk)
            if prev is not None:
                pw.wait("tensor", prev[0], prev[1])
            last = None
            for mc in range(2):
                last = pw.mm(psum[:, 512 * bk:512 * bk + 65],
                             A[:, qpb + 2048 * mc + 128 * c:qpb + 2048 * mc + 128 * (c + 1)],
                             BW[:, STS + 65 * mc:STS + 65 * (mc + 1)],
                             start=(mc == 0), stop=(mc == 1),
                             sem="mm" if mc == 1 else None)
            pw.wait("vector", "mm", last)
            pw.tt(B[:, 1024 * c + 64 * h:1024 * c + 64 * (h + 1)],
                  B[:, 1024 * c + 64 * h:1024 * c + 64 * (h + 1)],
                  psum[:, 512 * bk:512 * bk + 64], AluOpType.add, dep=False)
            pw.tt(stat[:, 6:7], BW[:, DEN + 16 * (h % 2) + c:DEN + 16 * (h % 2) + c + 1],
                  psum[:, 512 * bk + 64:512 * bk + 65], AluOpType.add)
            rv = pw.recip(stat[:, 7:8], stat[:, 6:7])
            dv = pw.ts(B[:, 1024 * c + 64 * h:1024 * c + 64 * (h + 1)],
                       B[:, 1024 * c + 64 * h:1024 * c + 64 * (h + 1)],
                       stat[:, 7:8], AluOpType.mult)
            pw._slot_state[512 * bk] = ("dve", dv)

    pw.wait("sync", "mm")
    pw.wait("sync", "dve")
    load_feats(0)
    for h in range(H):
        features(h)
        scan(h)
        if h >= 1:
            finalize(h - 1)
    finalize(H - 1)

    if dbg and dbg.get("upto") == "bheads":
        pw.barrier("dbgbh")
        pw.dma("sync", dbg["d_x1"], B[:, 0:16384].rearrange("p (t f) -> p t f", f=1024),
               sem="st")
        pw.dma("sync", dbg["d_den"], BW[:, DEN:DEN + 32], sem="st")
        pw.wait("sync", "st")
        return True

    pw.wait("tensor", "dve")

    def dsta(t, f):
        return A[:, 2048 * f + 128 * t:2048 * f + 128 * (t + 1)]
    emit_transpose_tm_to_fm(pw, lambda t: B[:, 1024 * t:1024 * (t + 1)],
                            dsta, NT, FC, T, 512 * 5, 512 * 6)
    pw.wait("sync", "mm")
    pw.wait("sync", "dve")
    ldw = load_W("b_ow")
    GBC2 = 8192
    RES = 10240
    row_from_cols(pw, G, T["vecs"][:, 48:56], T["frow"][0:1, 0:1024], 128, 8, "bgr")
    row_from_cols(pw, G, T["vecs"][:, 56:64], T["frow"][0:1, 1024:2048], 128, 8, "bbr")
    pw.wait("tensor", "dve")
    for half in range(4):
        s = pw.mm(psum[:, 512 * half:512 * (half + 1)], T["ones_row"][0:1, :],
                  T["frow"][0:1, 512 * half:512 * (half + 1)], start=True, stop=True,
                  sem="mm" if half == 3 else None)
    pw.wait("vector", "mm", s)
    gv = pw.copy(B[:, GBC2:GBC2 + 2048], psum[:, 0:2048], dep=False)
    pw.wait("tensor", "dve", gv)
    pw.wait("tensor", "ld", ldw)

    for t in range(NT):
        emit_ln_tile(pw, X[:, t, :], B[:, RES:RES + 1024], tmp[:, 0:1024], stat,
                     LN_EPS)
        for of in range(FC):
            bk = of % 2
            prev = pw._slot_state.get(("eo", bk))
            if prev is not None:
                pw.wait("tensor", prev[0], prev[1])
            last = None
            for f in range(FC):
                last = pw.mm(psum[:, 512 * bk:512 * bk + 128],
                             T["wbuf3"][:, f, 128 * of:128 * of + 128],
                             A[:, 2048 * f + 128 * t:2048 * f + 128 * (t + 1)],
                             start=(f == 0), stop=(f == FC - 1),
                             sem="mm" if f == FC - 1 else None)
            pw.wait("vector", "mm", last)
            ec2 = pw.copy(B[:, 11264 + 128 * bk:11264 + 128 * (bk + 1)],
                          psum[:, 512 * bk:512 * bk + 128], dep=False)
            pw.wait("tensor", "dve", ec2)
            tb = pw.mm(psum[:, 512 * (2 + bk):512 * (2 + bk) + 128],
                       B[:, 11264 + 128 * bk:11264 + 128 * (bk + 1)], T["ident"][:],
                       start=True, stop=True, sem="mm")
            pw.wait("vector", "mm", tb)
            ev = pw.stt(B[:, RES + 128 * of:RES + 128 * (of + 1)],
                        psum[:, 512 * (2 + bk):512 * (2 + bk) + 128],
                        T["vecs"][:, 40 + of:41 + of],
                        B[:, RES + 128 * of:RES + 128 * (of + 1)],
                        AluOpType.add, AluOpType.add, dep=True)
            pw._slot_state[("eo", bk)] = ("dve", ev)
        pw.red(stat[:, 0:1], B[:, RES:RES + 1024], AluOpType.add)
        pw.ts(stat[:, 1:2], stat[:, 0:1], 1.0 / D, AluOpType.mult)
        v0 = pw.ts(tmp[:, 0:1024], B[:, RES:RES + 1024], stat[:, 1:2],
                   AluOpType.subtract)
        pw.wait("scalar", "dve", v0)
        aq = pw.act(B[:, RES:RES + 1024], tmp[:, 0:1024], AF.Square, dep=False)
        pw.wait("vector", "act", aq)
        pw.red(stat[:, 2:3], B[:, RES:RES + 1024], AluOpType.add)
        v1 = pw.ts(stat[:, 3:4], stat[:, 2:3], 1.0 / D, AluOpType.mult,
                   LNB_EPS, AluOpType.add)
        pw.wait("scalar", "dve", v1)
        a1 = pw.act(stat[:, 4:5], stat[:, 3:4], AF.Sqrt, dep=False)
        pw.wait("vector", "act", a1)
        pw.recip(stat[:, 5:6], stat[:, 4:5])
        pw.ts(tmp[:, 0:1024], tmp[:, 0:1024], stat[:, 5:6], AluOpType.mult)
        pw.tt(tmp[:, 0:1024], tmp[:, 0:1024], B[:, GBC2:GBC2 + 1024],
              AluOpType.mult)
        pw.tt(tmp[:, 0:1024], tmp[:, 0:1024], B[:, GBC2 + 1024:GBC2 + 2048],
              AluOpType.add)
        pw.tt(X[:, t, :], X[:, t, :], tmp[:, 0:1024], AluOpType.add)

    if dbg and dbg.get("upto") == "attnb":
        pw.barrier("dbgattnb")
        pw.dma("sync", dbg["d_x1"], X[:, :, :], sem="st")
        return True
    return False


def ph_store(pw, T, y_out):
    pw.barrier("final")
    v = pw.dma("sync", y_out.rearrange("(t p) f -> p t f", p=128), T["X"][:, :, :],
               sem="yst")
    pw.wait("sync", "yst", v)



def build_nc(debug_upto=None):
    nc = bass.Bass("TRN2", target_bir_lowering=False)
    x_in = nc.dram_tensor("x0", [TOK, D], F32, kind="ExternalInput").ap()
    W = {}
    for n, s in [("aWq", [D, D]), ("aWk", [D, D]), ("aWv", [D, D]),
                 ("aWproj", [D, D]), ("aWpre", [D, D]),
                 ("b_qw", [D, D]), ("b_kw", [D, D]), ("b_vw", [D, D]),
                 ("b_ow", [D, D]), ("f_w1", [DHID, D]), ("f_w2", [D, DHID])]:
        W[n] = nc.dram_tensor(n, s, F32, kind="ExternalInput").ap()
    VEC = {}
    for n, s in [("a_bproj", [D]), ("a_bpre", [D]), ("b_qb", [D]), ("b_kb", [D]),
                 ("b_vb", [D]), ("b_ob", [D]), ("b_gamma", [D]), ("b_beta", [D]),
                 ("f_convb", [DHID]),
                 ("pi_bc", [128, 1]), ("isodd_bc", [128, 1]),
                 ("mleft_bc", [128, 1]), ("mright_bc", [128, 1])]:
        VEC[n] = nc.dram_tensor(n, s, F32, kind="ExternalInput").ap()
    b_proj = nc.dram_tensor("b_proj", [M, DH], F32, kind="ExternalInput").ap()
    f_convw = nc.dram_tensor("f_convw", [DHID, 3], F32, kind="ExternalInput").ap()
    y_out = nc.dram_tensor("y", [TOK, D], F32, kind="ExternalOutput").ap()

    G = {}
    span_tensors = []
    for sp in range(4):
        span_tensors += [(f"g_pre_in_s{sp}", [D, 512]), (f"g_pre_s{sp}", [2, D, 512]),
                         (f"g_k_in_s{sp}", [D, 512]), (f"g_k_s{sp}", [2, D, 512]),
                         (f"g_v_in_s{sp}", [128, 4 * 1040]),
                         (f"g_v_s{sp}", [2, 128, 4 * 1040])]
    for n, s in span_tensors + [
                 ("g_qT", [128, FC, TOK]), ("g_pqT", [128, FC, TOK]),
                 ("g_halo_in", [2, D]), ("g_halo", [2, 2, D]),
                 ("g_bqT", [128, FC, TOK]), ("g_bkT", [128, FC, TOK]),
                 ("g_bv", [128, NT * 1040]),
                 ("g_st_in", [H, 256, 65]), ("g_st", [H, 2, 256, 65])]:
        G[n] = nc.dram_tensor(n, s, F32).ap()
    G["g_t1T"] = nc.dram_tensor("g_t1T", [D, DHID], BF16).ap()
    G["g_t2T"] = nc.dram_tensor("g_t2T", [DHID, D], BF16).ap()
    G["g_hact"] = nc.dram_tensor("g_hact", [DHID, TOK], F32).ap()
    G["g_lg"] = nc.dram_tensor("g_lg", [32, 128, TOK], F32).ap()
    G["g_fr"] = nc.dram_tensor("g_fr", [128, 64], F32).ap()
    G["g_nh"] = nc.dram_tensor("g_nh", [DHID, TOK], BF16).ap()

    dbg = None
    if debug_upto is not None:
        dbg = {"upto": debug_upto}
        for n, s in [("d_xln", [128, NT, D]), ("d_qT", [D, TOK]), ("d_kT", [D, TOK]),
                     ("d_preT", [D, TOK]), ("d_pqT", [D, TOK]),
                     ("d_vext", [128, NT * 1040]), ("d_gpre", [2 * D, TOK]),
                     ("d_x1", [128, NT, D]), ("d_attn", [8, 128, 2 * D]),
                     ("d_logit", [8, 128, 256]), ("d_den", [128, 32])]:
            dbg[n] = nc.dram_tensor(n, s, F32, kind="ExternalOutput").ap()

    with contextlib.ExitStack() as stack:
        block = stack.enter_context(nc.Block())
        pw = PW(nc, stack)
        pw._slot_state = {}

        T = {}
        T["groups"] = [[0, 1], [2, 3], [4, 5], [6, 7]]
        T["X"] = stack.enter_context(nc.sbuf_tensor("X", [128, NT, D], F32))
        T["ARN_A"] = stack.enter_context(nc.sbuf_tensor("ARN_A", [128, 16384], F32))
        T["ARN_B"] = stack.enter_context(nc.sbuf_tensor("ARN_B", [128, 16384], F32))
        T["tmpA"] = stack.enter_context(nc.sbuf_tensor("tmpA", [128, 1024], F32))
        T["stat"] = stack.enter_context(nc.sbuf_tensor("stat", [128, 8], F32))
        T["fcols"] = stack.enter_context(nc.sbuf_tensor("fcols", [128, 64], F32))
        T["ones_col"] = stack.enter_context(nc.sbuf_tensor("ones_col", [128, 1], F32))
        T["BWK"] = stack.enter_context(nc.sbuf_tensor("BWK", [128, 1472], F32))
        T["ones_row"] = stack.enter_context(nc.sbuf_tensor("ones_row", [128, 128], F32))
        T["shift64"] = stack.enter_context(nc.sbuf_tensor("shift64", [128, 128], F32))
        arnb = T["ARN_B"]
        T["wbuf3"] = arnb[:, 0:8192].rearrange("p (f o) -> p f o", o=1024)
        T["stage"] = arnb[:, 8192:10240]
        T["stage2"] = arnb[:, 10240:12288]
        T["vstage"] = arnb[:, 12288:14368]
        T["hx"] = arnb[:, 0:4100].rearrange("p (a b) -> p a b", a=2)
        T["tstat"] = arnb[:, 4104:5128].bitcast(BF16).rearrange(
            "p (a b c) -> p a b c", a=2, b=8)
        T["tstat4"] = arnb[:, 4104:6152].bitcast(BF16).rearrange(
            "p (a b c) -> p a b c", a=4, b=8)
        T["nstat"] = arnb[:, 5128:5640].bitcast(BF16).rearrange(
            "p (a b) -> p a b", a=2)
        T["nstat4"] = arnb[:, 6152:7176].bitcast(BF16).rearrange(
            "p (a b) -> p a b", a=4)
        T["nh2"] = arnb[:, 5640:6664].bitcast(BF16)
        T["z16w"] = arnb[:, 6664:7688].bitcast(F16)
        T["haloh"] = arnb[:, 7688:7752]
        T["tbf"] = arnb[:, 12288:13312].bitcast(BF16)
        T["frow"] = arnb[0:1, 13312:15372]
        T["halo2"] = T["stage"][0:2, 0:1024]
        T["halo2b"] = T["stage"][0:2, 1024:2048]
        arna = T["ARN_A"]
        T["nbfT"] = arna[:, 0:8192].bitcast(BF16).rearrange(
            "p (a b) -> p a b", a=8)
        T["ofm"] = arna[:, 0:2048].rearrange("p (a b) -> p a b", a=4)
        T["z16"] = arna[:, 15360:15872].bitcast(F16)
        T["nhT"] = arna[:, 15872:15880].bitcast(BF16).rearrange(
            "p (a b) -> p a b", a=8)
        T["t2s"] = arna[:, 0:8192].bitcast(BF16).rearrange(
            "p (a b) -> p a b", a=32)
        T["ofm2"] = arna[:, 11264:13312].rearrange("p (a b) -> p a b", a=4)
        T["z16h"] = T["z16"][0:2, :]
        T["f2row"] = arna[0:1, 2048:4096]
        T["ident"] = stack.enter_context(nc.sbuf_tensor("ident", [128, 128], F32))
        T["identb"] = stack.enter_context(nc.sbuf_tensor("identb", [128, 128], BF16))
        T["mask_ut"] = stack.enter_context(nc.sbuf_tensor("mask_ut", [128, 128], F32))
        T["consts"] = stack.enter_context(nc.sbuf_tensor("consts", [128, 8], F32))
        T["vecs"] = stack.enter_context(nc.sbuf_tensor("vecs", [128, 112], F32))
        T["scal"] = stack.enter_context(nc.sbuf_tensor("scal", [128, 16], F32))
        T["convw"] = stack.enter_context(nc.sbuf_tensor("convw", [128, NJC, 3], F32))
        T["convb"] = stack.enter_context(nc.sbuf_tensor("convb", [128, NJC], F32))
        T["projx2"] = stack.enter_context(nc.sbuf_tensor("projx2", [128, M], F32))
        T["psum"] = stack.enter_context(nc.psum_tensor("ps", [128, 3584], F32))
        T["psumb"] = stack.enter_context(nc.psum_tensor("psb", [128, 1024], BF16))

        T["cvals"] = [0.0, 1.0, 0.125, -LN16, 1536.0, -1536.0, 0.5, -128.0]
        for i, v in enumerate(T["cvals"]):
            nc.const_aps.register(F32, float(v), T["consts"][:, i:i + 1])

        ph_setup(pw, T, x_in, VEC, b_proj, f_convw)
        stop = ph_attn_a_proj(pw, T, W, G, dbg)
        if not stop:
            ph_attn_a_logits(pw, T, G, dbg)
            arnb_ = T["ARN_B"]
            wmbuf = [arnb_[:, 12288:14336].rearrange("p (a b) -> p a b", a=2),
                     arnb_[:, 14336:16384].rearrange("p (a b) -> p a b", a=2)]

            def wm_src(L):
                if L < 16:
                    return W["f_w1"].rearrange("(a p) d -> p a d", p=128)[
                        :, 2 * L:2 * L + 2, :]
                Lp = L - 16
                return W["f_w2"].rearrange("(a p) d -> p a d", p=128)[
                    :, Lp // 2:Lp // 2 + 1,
                    2048 * (Lp % 2):2048 * (Lp % 2) + 2048].rearrange(
                    "p a (b c) -> p (a b) c", c=1024)

            wm_state = {"k": 0}
            wm_pml = {}
            wm_red = {}

            def wm_load(L):
                if L >= 32:
                    return
                if L >= 2:
                    pw.wait("gpsimd", "dve", wm_red[2 * L - 3])
                wm_pml[L] = pw.dma("gpsimd", wmbuf[L % 2], wm_src(L),
                                   sem="pm" + str(L % 2))

            pw.wait("gpsimd", "qst")
            wm_load(0)
            wm_load(1)

            def whook():
                k = wm_state["k"]
                if k >= 64:
                    return
                wm_state["k"] = k + 1
                L = k // 2
                pw.wait("vector", "pm" + str(L % 2), wm_pml[L])
                wm_red[k] = pw.red(T["fcols"][:, k:k + 1],
                                   wmbuf[L % 2][:, k % 2, :],
                                   AluOpType.add, absval=True, dep=False)
                if k % 2 == 1:
                    wm_load(L + 2)

            stop = ph_attn_a_attn(pw, T, W, G, dbg, whook)
        if not stop:
            pw.barrier("b1")
            ph_ffn_weights(pw, T, W, G)
            pw.barrier("b2")
            stop = ph_ffn(pw, T, W, G, dbg, 1)
        if not stop:
            pw.barrier("b3")
            stop = ph_attn_b(pw, T, W, G, dbg)
        if not stop:
            pw.barrier("b4")
            stop = ph_ffn(pw, T, W, G, dbg, 2)
        if not stop:
            ph_store(pw, T, y_out)

        pw.wait("sync", "st") if "st" in pw.cnt else None
        pw.replay(block)
    return nc


def _shard_inputs(inputs):
    x = np.ascontiguousarray(np.asarray(inputs["L0_attn_a_x"], dtype=np.float32))
    maps = []
    for c in range(8):
        b, h = c // 2, c % 2
        m = {"x0": np.ascontiguousarray(x[b, 2048 * h:2048 * (h + 1), :])}
        m["aWq"] = inputs["L0_attn_a_Wq"]
        m["aWk"] = inputs["L0_attn_a_Wk"]
        m["aWv"] = inputs["L0_attn_a_Wv"]
        m["aWproj"] = inputs["L0_attn_a_Wproj"]
        m["aWpre"] = inputs["L0_attn_a_Wpre"]
        m["b_qw"] = inputs["L0_attn_b_q_w"]
        m["b_kw"] = inputs["L0_attn_b_k_w"]
        m["b_vw"] = inputs["L0_attn_b_v_w"]
        m["b_ow"] = inputs["L0_attn_b_out_w"]
        m["f_w1"] = inputs["L0_ffn_w1"]
        m["f_w2"] = inputs["L0_ffn_w2"]
        m["a_bproj"] = inputs["L0_attn_a_bproj"]
        m["a_bpre"] = inputs["L0_attn_a_bpre"]
        m["b_qb"] = inputs["L0_attn_b_q_b"]
        m["b_kb"] = inputs["L0_attn_b_k_b"]
        m["b_vb"] = inputs["L0_attn_b_v_b"]
        m["b_ob"] = inputs["L0_attn_b_out_b"]
        m["b_gamma"] = inputs["L0_attn_b_gamma"]
        m["b_beta"] = inputs["L0_attn_b_beta"]
        m["f_convb"] = inputs["L0_ffn_conv_b"]
        m["b_proj"] = inputs["L0_attn_b_proj"]
        m["f_convw"] = np.ascontiguousarray(
            np.asarray(inputs["L0_ffn_conv_w"], dtype=np.float32)[:, 0, :])
        m["pi_bc"] = np.ascontiguousarray(
            np.broadcast_to(np.asarray(inputs["L0_attn_a_pi"], np.float32), (128, 1)))
        m["isodd_bc"] = np.full((128, 1), float(h), np.float32)
        m["mleft_bc"] = np.full((128, 1), float(h), np.float32)
        m["mright_bc"] = np.full((128, 1), float(1 - h), np.float32)
        m = {k: np.ascontiguousarray(np.asarray(v, dtype=np.float32)) if k not in ()
             else v for k, v in m.items()}
        maps.append(m)
    return maps


def kernel(**inputs):
    nc = build_nc()
    maps = _shard_inputs(inputs)
    res = run_bass_kernel_spmd(nc, maps, core_ids=list(range(8)))
    out = np.empty((4, 4096, 1024), np.float32)
    for c in range(8):
        b, h = c // 2, c % 2
        out[b, 2048 * h:2048 * (h + 1), :] = res.results[c]["y"]
    return out


# version 13
# speedup vs baseline: 1.2081x; 1.0061x over previous
import contextlib
import numpy as np
import concourse.bass as bass
import concourse.mybir as mybir
from concourse.bass_utils import run_bass_kernel_spmd
from concourse.alu_op_type import AluOpType

F32 = mybir.dt.float32
F16 = mybir.dt.float16
BF16 = mybir.dt.bfloat16
AF = mybir.ActivationFunctionType
AX = mybir.AxisListType

TOK = 2048
NT = 16
D = 1024
FC = 8
H = 16
DH = 64
M = 256
DHID = 4096
NJC = 32
LN_EPS = 1e-5
LNB_EPS = 1e-6
LN16 = float(np.log(np.float64(16.0)))

ENGS = ("sync", "tensor", "vector", "scalar", "gpsimd")


class PW:

    def __init__(self, nc, stack):
        self.nc = nc
        self.stack = stack
        self.ops = {e: [] for e in ENGS}
        self.cnt = {}
        self.sems = {}

    def sem(self, name):
        if name not in self.sems:
            self.sems[name] = self.stack.enter_context(self.nc.semaphore(name))
            self.cnt[name] = 0
        return self.sems[name]

    def emit(self, eng, fn):
        self.ops[eng].append(fn)

    def wait(self, eng, name, val=None):
        s, v = self.sem(name), (self.cnt[name] if val is None else val)
        self.emit(eng, lambda e: e.wait_ge(s, v))

    def dma(self, eng, out, in_, sem=None, slow=False):
        kw = {"allow_slow_non_contiguous": True} if slow else {}
        if sem is None:
            self.emit(eng, lambda e: e.dma_start(out=out, in_=in_, **kw))
            return None
        s = self.sem(sem)
        self.cnt[sem] += 16
        self.emit(eng, lambda e: e.dma_start(out=out, in_=in_, **kw).then_inc(s, 16))
        return self.cnt[sem]

    AUTO = {"vector": "dve", "scalar": "act", "gpsimd": "gp"}

    def _op(self, eng, fn, sem=None, n=1, dep=True):
        auto = self.AUTO.get(eng)
        if auto is not None:
            s = self.sem(auto)
            if dep and self.cnt[auto] > 0:
                v = self.cnt[auto]
                self.emit(eng, lambda e: e.wait_ge(s, v))
            self.cnt[auto] += 1
            self.emit(eng, lambda e: fn(e).then_inc(s, 1))
            return self.cnt[auto]
        if sem is None:
            self.emit(eng, fn)
            return None
        s = self.sem(sem)
        self.cnt[sem] += n
        self.emit(eng, lambda e: fn(e).then_inc(s, n))
        return self.cnt[sem]

    def mm(self, out, lhsT, rhs, start, stop, sem=None, skip=False):
        return self._op("tensor",
                        lambda e: e.matmul(out, lhsT, rhs, start=start, stop=stop,
                                           skip_group_check=skip),
                        sem)

    def trans(self, out, in_, ident, sem=None):
        return self._op("tensor", lambda e: e.transpose(out, in_, ident), sem)

    def mmT(self, out, data, ident, start, stop, sem=None):
        return self._op("tensor",
                        lambda e: e.matmul(out, data, ident, start=start, stop=stop,
                                           is_transpose=True), sem)

    def act(self, out, in_, func, bias=None, scale=None, accum_out=None, sem=None,
            dep=True):
        kw = {}
        if bias is not None:
            kw["bias"] = bias
        if scale is not None:
            kw["scale"] = scale
        if accum_out is not None:
            kw["accum_out"] = accum_out
        return self._op("scalar",
                        lambda e: e.activation(out=out, in_=in_, func=func, **kw),
                        sem, dep=dep)

    def ts(self, out, in0, s1, op0, s2=None, op1=None, sem=None, eng="vector",
           dep=True):
        return self._op(eng,
                        lambda e: e.tensor_scalar(out=out, in0=in0, scalar1=s1,
                                                  scalar2=s2, op0=op0,
                                                  **({"op1": op1} if op1 is not None else {})),
                        sem, dep=dep)

    def tt(self, out, in0, in1, op, sem=None, eng="vector", dep=True):
        return self._op(eng,
                        lambda e: e.tensor_tensor(out=out, in0=in0, in1=in1, op=op),
                        sem, dep=dep)

    def stt(self, out, in0, scalar, in1, op0, op1, sem=None, dep=True):
        return self._op("vector",
                        lambda e: e.scalar_tensor_tensor(out=out, in0=in0, scalar=scalar,
                                                         in1=in1, op0=op0, op1=op1),
                        sem, dep=dep)

    def copy(self, out, in_, sem=None, eng="vector", dep=True):
        return self._op(eng, lambda e: e.tensor_copy(out, in_), sem, dep=dep)

    def red(self, out, in_, op, absval=None, sem=None, negate=None, dep=True):
        return self._op("vector",
                        lambda e: e.tensor_reduce(out=out, in_=in_, axis=AX.X, op=op,
                                                  apply_absolute_value=absval,
                                                  negate=negate), sem, dep=dep)

    def recip(self, out, in_, sem=None, dep=True):
        return self._op("vector", lambda e: e.reciprocal(out=out, in_=in_), sem,
                        dep=dep)

    def memset(self, ap, c, sem=None, dep=True):
        return self._op("gpsimd", lambda e: e.memset(ap, c), sem, dep=dep)

    def coll(self, kind, op, groups, ins, outs, sem=None, dep=True):
        return self._op("gpsimd",
                        lambda e: e.collective_compute(kind, op, replica_groups=groups,
                                                       ins=ins, outs=outs), sem,
                        dep=dep)

    def barrier(self, tag):
        vals = dict(self.cnt)
        for eng in ENGS:
            for name, v in vals.items():
                if v > 0 and not name.startswith("_"):
                    self.wait(eng, name, v)

    def replay(self, block):
        for eng in ENGS:
            ops = self.ops[eng]
            dec = getattr(block, eng)

            def make(fn_list):
                def body(e):
                    for f in fn_list:
                        f(e)
                return body
            dec(make(ops))



def ph_setup(pw, T, x_in, VEC, b_proj, f_convw):
    nc = pw.nc
    pw.dma("sync", T["X"][:, :, :], x_in.rearrange("(t p) f -> p t f", p=128), sem="ld")
    for col, name in enumerate(["a_bproj", "a_bpre", "b_qb", "b_kb", "b_vb",
                                "b_ob", "b_gamma", "b_beta"]):
        pw.dma("sync", T["vecs"][:, 8 * col:8 * col + 8],
               VEC[name].rearrange("(f p) -> p f", p=128), sem="ld", slow=True)
    pw.dma("sync", T["scal"][:, 0:1], VEC["pi_bc"], sem="ld")
    pw.dma("sync", T["scal"][:, 1:2], VEC["isodd_bc"], sem="ld")
    pw.dma("sync", T["scal"][:, 2:3], VEC["mleft_bc"], sem="ld")
    pw.dma("sync", T["scal"][:, 3:4], VEC["mright_bc"], sem="ld")
    pw.dma("sync", T["convw"][:, :, :], f_convw.rearrange("(c p) k -> p c k", p=128), sem="ld", slow=True)
    pw.dma("sync", T["convb"][:], VEC["f_convb"].rearrange("(c p) -> p c", p=128), sem="ld", slow=True)
    pw.dma("sync", T["projx2"][0:64, :], b_proj.rearrange("m d -> d m"), sem="ld", slow=True)

    cvals = T["cvals"]
    for i, v in enumerate(cvals):
        pw.memset(T["consts"][:, i:i + 1], float(v))
    pw.memset(T["ident"][:], 1.0)
    pw._op("gpsimd", lambda e: e.affine_select(
        out=T["ident"][:], in_=T["ident"][:], compare_op=AluOpType.is_equal,
        fill=0.0, base=0, pattern=[[-1, 128]], channel_multiplier=1))
    pw.memset(T["identb"][:], 1.0)
    pw._op("gpsimd", lambda e: e.affine_select(
        out=T["identb"][:], in_=T["identb"][:], compare_op=AluOpType.is_equal,
        fill=0.0, base=0, pattern=[[-1, 128]], channel_multiplier=1))
    pw.memset(T["mask_ut"][:], 1.0)
    pw._op("gpsimd", lambda e: e.affine_select(
        out=T["mask_ut"][:], in_=T["mask_ut"][:], compare_op=AluOpType.is_ge,
        fill=0.0, base=0, pattern=[[1, 128]], channel_multiplier=-1))
    pw.memset(T["ones_col"][:], 1.0)
    pw.memset(T["ones_row"][:], 1.0)
    pw.memset(T["shift64"][:], 1.0)
    pw._op("gpsimd", lambda e: e.affine_select(
        out=T["shift64"][:], in_=T["shift64"][:], compare_op=AluOpType.is_equal,
        fill=0.0, base=-64, pattern=[[1, 128]], channel_multiplier=-1))
    pw.memset(T["projx2"][64:128, :], -0.5, sem="gp")
    pw.wait("vector", "ld")
    pw.wait("vector", "gp")
    pw.ts(T["scal"][:, 4:5], T["scal"][:, 0:1], 1.0 / 32.0, AluOpType.mult)
    pw.ts(T["vecs"][:, 96:104], T["vecs"][:, 16:24], 0.125, AluOpType.mult)
    pw.ts(T["vecs"][:, 104:112], T["vecs"][:, 24:32], 0.125, AluOpType.mult,
          sem="dve")


def emit_ln_tile(pw, x_tile, out_tile, tmp, stat, eps):
    pw.red(stat[:, 0:1], x_tile, AluOpType.add)
    pw.ts(stat[:, 1:2], stat[:, 0:1], 1.0 / D, AluOpType.mult)
    pw.ts(tmp, x_tile, stat[:, 1:2], AluOpType.subtract)
    pw.tt(out_tile, tmp, tmp, AluOpType.mult)
    pw.red(stat[:, 0:1], out_tile, AluOpType.add)
    v1 = pw.ts(stat[:, 2:3], stat[:, 0:1], 1.0 / D, AluOpType.mult, eps,
               AluOpType.add, sem="dve")
    pw.wait("scalar", "dve", v1)
    a1 = pw.act(stat[:, 2:3], stat[:, 2:3], AF.Sqrt, sem="act")
    pw.wait("vector", "act", a1)
    pw.recip(stat[:, 3:4], stat[:, 2:3])
    return pw.ts(out_tile, tmp, stat[:, 3:4], AluOpType.mult, sem="dve")


def emit_transpose_tm_to_fm(pw, src_tile_fn, dst_fn, nt_tiles, nfc, T, bankA, bankB,
                            dtype_ident="ident"):
    ident = T[dtype_ident][:]
    psum = T["psum"]
    slots = [bankA, bankB]
    pending = []
    mmc0 = pw.cnt.get("mm", 0)
    for t in range(nt_tiles):
        src = src_tile_fn(t)
        for fb in range(0, nfc, 4):
            nf = min(4, nfc - fb)
            slot = slots[(t * ((nfc + 3) // 4) + fb // 4) % 2]
            key = slot
            prev = pw._slot_state.get(key)
            if prev is not None:
                pw.wait("tensor", prev[0], prev[1])
            last = None
            for j in range(nf):
                last = pw.mm(psum[:, slot + 128 * j:slot + 128 * (j + 1)],
                             src[:, 128 * (fb + j):128 * (fb + j + 1)], ident,
                             start=(j == 0), stop=(j == nf - 1),
                             sem="mm" if j == nf - 1 else None)
            pw.wait("vector", "mm", last)
            for j in range(nf):
                s = pw.copy(dst_fn(t, fb + j),
                            psum[:, slot + 128 * j:slot + 128 * (j + 1)],
                            sem="dve" if j == nf - 1 else None)
            pw._slot_state[key] = ("dve", s)


def emit_proj_fm(pw, T, ld_done, evac, dma_dst, dma_sem, span_hook=None,
                 rhs_buf=None, prev_sems=()):
    psum = T["psum"]
    xlnT = T["ARN_A"] if rhs_buf is None else rhs_buf
    pw.wait("tensor", "ld", ld_done)
    for s_ in prev_sems:
        if s_ in pw.cnt:
            pw.wait("scalar", s_)
            pw.wait("vector", s_)
    dma_hist = {}
    for sp in range(4):
        for of in range(FC):
            it = sp * FC + of
            bk = it % 6
            prev = pw._slot_state.get(("pj", bk))
            if prev is not None:
                pw.wait("tensor", prev[0], prev[1])
            last = None
            for f in range(FC):
                last = pw.mm(psum[:, 512 * bk:512 * bk + 512],
                             T["wbuf3"][:, f, 128 * of:128 * of + 128],
                             xlnT[:, 2048 * f + 512 * sp:2048 * f + 512 * (sp + 1)],
                             start=(f == 0), stop=(f == FC - 1),
                             sem="mm" if f == FC - 1 else None)
            eng, cnt, region = evac(of, sp, psum[:, 512 * bk:512 * bk + 512],
                                    last, it >= 4)
            pw._slot_state[("pj", bk)] = (eng, cnt)
            pw.wait("sync", eng, cnt)
            dma_hist[it % 4] = pw.dma("sync", dma_dst(of, sp), region, sem=dma_sem)
        if span_hook is not None:
            span_hook(sp)


def ph_attn_a_proj(pw, T, W, G, dbg):
    X, ARN_A, ARN_B = T["X"], T["ARN_A"], T["ARN_B"]
    psum, stage = T["psum"], T["stage"]
    groups = T["groups"]

    xln = ARN_B
    tmp, stat = T["tmpA"], T["stat"]
    pw.wait("vector", "ld")
    pw.wait("tensor", "gp")
    xlnT = ARN_A

    def dst(t, f):
        return xlnT[:, 2048 * f + 128 * t:2048 * f + 128 * (t + 1)]
    for t in range(NT):
        emit_ln_tile(pw, X[:, t, :], xln[:, 1024 * t:1024 * (t + 1)], tmp[:, 0:1024],
                     stat, LN_EPS)
        pw.wait("tensor", "dve")
        emit_transpose_tm_to_fm(
            pw, (lambda tc: (lambda _t: xln[:, 1024 * tc:1024 * (tc + 1)]))(t),
            (lambda tc: (lambda _t, f: dst(tc, f)))(t), 1, FC, T, 0, 512)
    if dbg and dbg.get("upto") == "ln1":
        pw.barrier("dbgln")
        pw.dma("sync", dbg["d_xln"], xln.rearrange("p (t f) -> p t f", f=1024),
               sem="st")
        return True
    pw.barrier("projstart")

    def load_W(name):
        return pw.dma("sync", T["wbuf3"], W[name].rearrange("(f p) o -> p f o", p=128),
                      sem="ld")

    ag_done = {}

    def mk_evac(kind, dsem):
        def evac(of, sp, ps, mmcnt, need_guard):
            it = sp * FC + of
            region = stage[:, 512 * (it % 4):512 * (it % 4) + 512]
            pw.wait("scalar", "mm", mmcnt)
            if need_guard:
                pw.wait("scalar", dsem)
            if kind == "silu_bpre":
                c = pw.act(region, ps, AF.Silu, bias=T["vecs"][:, 8 + of:9 + of])
            elif kind == "copy":
                c = pw.act(region, ps, AF.Copy)
            elif kind == "q8":
                c = pw.act(region, ps, AF.Copy, scale=0.125)
            return ("act", c, region)
        return evac

    ld = load_W("aWpre")

    def hook_pre(sp):
        pw.wait("gpsimd", "pst")
        ag_done[("pre", sp)] = pw.coll(
            "AllGather", AluOpType.bypass, groups,
            [G[f"g_pre_in_s{sp}"].opt()], [G[f"g_pre_s{sp}"].opt()])
    emit_proj_fm(pw, T, ld, mk_evac("silu_bpre", "pst"),
                 lambda of, sp: G[f"g_pre_in_s{sp}"][128 * of:128 * (of + 1), :],
                 "pst", hook_pre)

    ld = load_W("aWk")

    def hook_k(sp):
        pw.wait("gpsimd", "kst")
        ag_done[("k", sp)] = pw.coll(
            "AllGather", AluOpType.bypass, groups,
            [G[f"g_k_in_s{sp}"].opt()], [G[f"g_k_s{sp}"].opt()])
    emit_proj_fm(pw, T, ld, mk_evac("copy", "kst"),
                 lambda of, sp: G[f"g_k_in_s{sp}"][128 * of:128 * (of + 1), :],
                 "kst", hook_k, prev_sems=("pst",))

    ld = load_W("aWv")
    pw.memset(T["vstage"], 1.0)
    pw.wait("vector", "gp")
    pw.wait("tensor", "ld", ld)
    vdma = {}
    for mc in range(NT):
        for vh in range(2):
            bk = (mc * 2 + vh) % 6
            prev = pw._slot_state.get(("pj", bk))
            if prev is not None:
                pw.wait("tensor", prev[0], prev[1])
            last = None
            for f in range(FC):
                last = pw.mm(psum[:, 512 * bk:512 * bk + 512],
                             T["ARN_A"][:, 2048 * f + 128 * mc:2048 * f + 128 * (mc + 1)],
                             T["wbuf3"][:, f, 512 * vh:512 * (vh + 1)],
                             start=(f == 0), stop=(f == FC - 1),
                             sem="mm" if f == FC - 1 else None)
            pw.wait("vector", "mm", last)
            if vdma.get(mc % 2) is not None:
                pw.wait("vector", "vst")
            dstv = T["vstage"][:, 1040 * (mc % 2) + 520 * vh:1040 * (mc % 2) + 520 * (vh + 1)] \
                .rearrange("p (h c) -> p h c", c=65)[:, :, 0:64]
            c = pw.copy(dstv, psum[:, 512 * bk:512 * bk + 512]
                        .rearrange("p (h c) -> p h c", c=64), dep=False)
            pw._slot_state[("pj", bk)] = ("dve", c)
        pw.wait("sync", "dve")
        sp, r = mc // 4, mc % 4
        vdma[mc % 2] = pw.dma("sync", G[f"g_v_in_s{sp}"][:, 1040 * r:1040 * (r + 1)],
                              T["vstage"][:, 1040 * (mc % 2):1040 * (mc % 2) + 1040],
                              sem="vst")
        if r == 3:
            pw.wait("gpsimd", "vst")
            ag_done[("v", sp)] = pw.coll(
                "AllGather", AluOpType.bypass, groups,
                [G[f"g_v_in_s{sp}"].opt()], [G[f"g_v_s{sp}"].opt()])

    ld = load_W("aWq")
    emit_proj_fm(pw, T, ld, mk_evac("q8", "qst"),
                 lambda of, sp: G["g_qT"][:, of, 512 * sp:512 * (sp + 1)], "qst",
                 prev_sems=("kst",))

    pw.wait("sync", "pst")
    pw.wait("sync", "qst")
    for of in range(FC):
        l = None
        for sp in range(4):
            l = pw.dma("sync", stage[:, 512 * sp:512 * (sp + 1)],
                       G[f"g_pre_in_s{sp}"][128 * of:128 * (of + 1), :], sem="ld")
        pw.wait("vector", "ld", l)
        if of >= 1:
            pw.wait("vector", "qst")
        s = pw.ts(T["stage2"][:, 0:2048], stage[:, 0:2048], T["scal"][:, 4:5],
                  AluOpType.mult, dep=True)
        pw.wait("sync", "dve", s)
        pw.dma("sync", G["g_pqT"][:, of, :], T["stage2"][:, 0:2048], sem="qst")

    pw._ag_done = ag_done
    if dbg and dbg.get("upto") == "aproj":
        pw.barrier("dbgproj")
        for sp in range(4):
            pw.dma("sync", dbg["d_preT"][:, 512 * sp:512 * (sp + 1)],
                   G[f"g_pre_in_s{sp}"], sem="st")
            pw.dma("sync", dbg["d_kT"][:, 512 * sp:512 * (sp + 1)],
                   G[f"g_k_in_s{sp}"], sem="st")
            pw.dma("sync", dbg["d_vext"][:, 4160 * sp:4160 * (sp + 1)],
                   G[f"g_v_in_s{sp}"], sem="st")
            pw.dma("sync", dbg["d_gpre"][0:1024, 512 * sp:512 * (sp + 1)],
                   G[f"g_pre_s{sp}"][0], sem="st")
            pw.dma("sync", dbg["d_gpre"][1024:2048, 512 * sp:512 * (sp + 1)],
                   G[f"g_pre_s{sp}"][1], sem="st")
        pw.dma("sync", dbg["d_qT"].rearrange("(f p) t -> p f t", p=128), G["g_qT"],
               sem="st")
        pw.dma("sync", dbg["d_pqT"].rearrange("(f p) t -> p f t", p=128), G["g_pqT"],
               sem="st")
        return True
    return False


def ph_attn_a_logits(pw, T, G, dbg):
    A = T["ARN_A"]
    psum = T["psum"]
    PREQ, PM, LT = 0, 8192, 10240
    agv = pw._ag_done[("pre", 3)]
    pw.wait("sync", "gp", agv)
    pw.wait("sync", "qst")
    for qh in range(2):
        pw.wait("sync", "mm")
        pw.wait("sync", "act")
        lq = pw.dma("sync", A[:, PREQ:PREQ + 8192].rearrange("p (f t) -> p f t", t=1024),
                    G["g_pqT"][:, :, 1024 * qh:1024 * (qh + 1)], sem="alq")
        pw.wait("tensor", "alq", lq)
        mld = {}

        def issue(mc):
            if mc >= 32:
                return
            par = mc % 2
            slot, g = mc // 16, mc % 16
            sp, r = g // 4, g % 4
            mld[mc] = pw.dma("sync", A[:, PM + 1024 * par:PM + 1024 * (par + 1)]
                             .rearrange("p (f t) -> p f t", t=128),
                             G[f"g_pre_s{sp}"][slot].rearrange("(f p) t -> p f t", p=128)
                             [:, :, 128 * r:128 * (r + 1)], sem="alm" + str(par))
        issue(0)
        issue(1)
        ldma = {}
        for mc in range(32):
            par = mc % 2
            pw.wait("tensor", "alm" + str(par), mld[mc])
            pre = A[:, PM + 1024 * par:PM + 1024 * (par + 1)]
            for qs in range(2):
                bk = (mc * 2 + qs) % 6
                prev = pw._slot_state.get(512 * bk)
                if prev is not None:
                    pw.wait("tensor", prev[0], prev[1])
                last = None
                for f in range(FC):
                    last = pw.mm(psum[:, 512 * bk:512 * bk + 512],
                                 pre[:, 128 * f:128 * (f + 1)],
                                 A[:, PREQ + 1024 * f + 512 * qs:PREQ + 1024 * f + 512 * (qs + 1)],
                                 start=(f == 0), stop=(f == FC - 1),
                                 sem="mm" if f == FC - 1 else None)
                pw.wait("scalar", "mm", last)
                if ldma.get(par) is not None and qs == 0:
                    pw.wait("scalar", "alst" + str(par))
                lc = pw.act(A[:, LT + 1024 * par + 512 * qs:LT + 1024 * par + 512 * (qs + 1)],
                            psum[:, 512 * bk:512 * bk + 512], AF.Copy, dep=False)
                pw._slot_state[512 * bk] = ("act", lc)
            pw.wait("sync", "act")
            ldma[par] = pw.dma("sync", G["g_lg"][mc][:, 1024 * qh:1024 * (qh + 1)],
                               A[:, LT + 1024 * par:LT + 1024 * (par + 1)],
                               sem="alst" + str(par))
            pw.wait("sync", "mm")
            issue(mc + 2)
        pw.wait("sync", "alst0")
        pw.wait("sync", "alst1")


def ph_attn_a_attn(pw, T, W, G, dbg, whook=None):
    X, A, B = T["X"], T["ARN_A"], T["ARN_B"]
    psum = T["psum"]
    QS = 0
    KT2 = 4096
    VX2 = 4608
    LG2 = 5184
    TMPX = 6208
    EXPX = 7232
    ATT = 8448
    STG = 12544
    LNB = 13568

    ldw = pw.dma("sync", T["wbuf3"], W["aWproj"].rearrange("(f p) o -> p f o", p=128),
                 sem="ld")
    pw.wait("sync", "gp", max(pw._ag_done.values()))

    for span in range(4):
        pw.wait("sync", "mm")
        pw.wait("sync", "dve")
        lq = pw.dma("sync", A[:, QS:QS + 4096].rearrange("p (f t) -> p f t", t=512),
                    G["g_qT"][:, :, 512 * span:512 * (span + 1)], sem="aq")
        pw.wait("tensor", "aq", lq)
        for hg in range(4):
            mld = {}

            def issue(mc):
                if mc >= 32:
                    return
                par = mc % 2
                sm = "am" + str(par)
                slot, g = mc // 16, mc % 16
                sp, r = g // 4, g % 4
                pw.dma("sync", A[:, KT2 + 256 * par:KT2 + 256 * (par + 1)]
                       .rearrange("p (f t) -> p f t", t=128),
                       G[f"g_k_s{sp}"][slot].rearrange("(f p) t -> p f t", p=128)
                       [:, 2 * hg:2 * hg + 2, 128 * r:128 * (r + 1)], sem=sm)
                pw.dma("sync", A[:, VX2 + 260 * par:VX2 + 260 * (par + 1)],
                       G[f"g_v_s{sp}"][slot][:, 1040 * r + 260 * hg:
                                             1040 * r + 260 * (hg + 1)], sem=sm)
                mld[mc] = pw.dma("sync", A[:, LG2 + 512 * par:LG2 + 512 * (par + 1)],
                                 G["g_lg"][mc][:, 512 * span:512 * (span + 1)], sem=sm)
            pw.wait("sync", "mm")
            pw.wait("sync", "dve")
            issue(0)
            issue(1)
            pend = None
            ec_hist = {}
            for mc in range(32):
                if whook is not None and mc in (8, 16, 24):
                    whook()
                par = mc % 2
                pw.wait("tensor", "am" + str(par), mld[mc])
                kt = A[:, KT2 + 256 * par:KT2 + 256 * (par + 1)]
                vx = A[:, VX2 + 260 * par:VX2 + 260 * (par + 1)]
                lg = A[:, LG2 + 512 * par:LG2 + 512 * (par + 1)]
                for hh in range(4):
                    h = 4 * hg + hh
                    i = mc * 4 + hh
                    bk = 2560 if i % 2 == 0 else 3072
                    eb = i % 2
                    prev = pw._slot_state.get(bk)
                    if prev is not None:
                        pw.wait("tensor", prev[0], prev[1])
                    sm2 = pw.mm(psum[:, bk:bk + 512],
                                kt[64 * (hh % 2):64 * (hh % 2) + 64,
                                   128 * (hh // 2):128 * (hh // 2) + 128],
                                A[64 * (h % 2):64 * (h % 2) + 64,
                                  QS + 512 * (h // 2):QS + 512 * (h // 2) + 512],
                                start=True, stop=True, sem="mm")
                    if pend is not None:
                        phh, pmc, pec, peb, pvx = pend
                        pw.wait("tensor", "act", pec)
                        pw.mm(psum[0:65, 512 * phh:512 * phh + 512],
                              pvx[:, 65 * phh:65 * (phh + 1)],
                              A[:, EXPX + 512 * peb:EXPX + 512 * (peb + 1)],
                              start=(pmc == 0), stop=(pmc == 31),
                              sem="mm" if phh == 3 else None)
                        pend = None
                        if hh == 0:
                            pw.wait("sync", "mm")
                            issue(mc + 1)
                    pw.wait("vector", "mm", sm2)
                    if eb in ec_hist:
                        pw.wait("vector", "act", ec_hist[eb])
                    tv = pw.tt(A[:, TMPX + 512 * eb:TMPX + 512 * (eb + 1)],
                               psum[:, bk:bk + 512], lg, AluOpType.add, dep=False)
                    pw.wait("scalar", "dve", tv)
                    ec = pw.act(A[:, EXPX + 512 * eb:EXPX + 512 * (eb + 1)],
                                A[:, TMPX + 512 * eb:TMPX + 512 * (eb + 1)],
                                AF.Exp, dep=False)
                    ec_hist[eb] = ec
                    pw._slot_state[bk] = ("dve", tv)
                    pend = (hh, mc, ec, eb, vx)
            phh, pmc, pec, peb, pvx = pend
            pw.wait("tensor", "act", pec)
            pw.mm(psum[0:65, 512 * phh:512 * phh + 512],
                  pvx[:, 65 * phh:65 * (phh + 1)],
                  A[:, EXPX + 512 * peb:EXPX + 512 * (peb + 1)],
                  start=(pmc == 0), stop=(pmc == 31), sem="mm")
            pend = None
            pw.wait("vector", "mm")
            pw.wait("tensor", "dve")
            for hh in range(4):
                h = 4 * hg + hh
                rd = pw.recip(A[64:65, STG + 512:STG + 1024],
                              psum[64:65, 512 * hh:512 * hh + 512], dep=True)
                cv = pw.copy(A[0:64, STG:STG + 512],
                             psum[0:64, 512 * hh:512 * hh + 512], dep=True)
                pw.wait("tensor", "dve", cv)
                bcm = pw.mm(psum[:, 2560:2560 + 512], T["ones_row"][64:65, :],
                            A[64:65, STG + 512:STG + 1024], start=True, stop=True,
                            sem="mm")
                pw.wait("vector", "mm", bcm)
                if h % 2 == 0:
                    dv = pw.tt(A[0:64, ATT + 512 * (h // 2):ATT + 512 * (h // 2) + 512],
                               A[0:64, STG:STG + 512], psum[0:64, 2560:2560 + 512],
                               AluOpType.mult, dep=True)
                else:
                    sv = pw.tt(A[0:64, STG:STG + 512], A[0:64, STG:STG + 512],
                               psum[0:64, 2560:2560 + 512], AluOpType.mult, dep=True)
                    pw.wait("tensor", "dve", sv)
                    sh = pw.mm(psum[:, 3072:3072 + 512], T["shift64"][0:64, :],
                               A[0:64, STG:STG + 512], start=True, stop=True,
                               sem="mm")
                    pw.wait("vector", "mm", sh)
                    dv = pw.copy(A[64:128, ATT + 512 * (h // 2):ATT + 512 * (h // 2) + 512],
                                 psum[64:128, 3072:3072 + 512], dep=True)
                pw._slot_state[2560] = ("dve", dv)
                pw._slot_state[3072] = ("dve", dv)
            if whook is not None:
                whook()

        pw.wait("tensor", "dve")
        pw.wait("tensor", "ld", ldw)
        for of in range(FC):
            bk = 2560 if of % 2 == 0 else 3072
            prev = pw._slot_state.get(bk)
            if prev is not None:
                pw.wait("tensor", prev[0], prev[1])
            last = None
            for f in range(FC):
                last = pw.mm(psum[:, bk:bk + 512],
                             T["wbuf3"][:, f, 128 * of:128 * of + 128],
                             A[:, ATT + 512 * f:ATT + 512 * (f + 1)],
                             start=(f == 0), stop=(f == FC - 1),
                             sem="mm" if f == FC - 1 else None)
            pw.wait("scalar", "mm", last)
            oc = pw.act(A[:, STG + 512 * 0:STG + 512] if False else
                        B[:, 8192 + 512 * of:8192 + 512 * (of + 1)],
                        psum[:, bk:bk + 512], AF.Identity,
                        bias=T["vecs"][:, 0 + of:1 + of], dep=False)
            pw._slot_state[bk] = ("act", oc)
        pw.wait("tensor", "act")
        for tc in range(4):
            t_global = 4 * span + tc
            pw.wait("vector", "mm")
            emit_ln_tile(pw, X[:, t_global, :], A[:, LNB:LNB + 1024],
                         T["tmpA"][:, 0:1024], T["stat"], LN_EPS)
            for ob in range(0, FC, 4):
                bk = 2560 if (ob // 4) % 2 == 0 else 3072
                prev = pw._slot_state.get(bk)
                if prev is not None:
                    pw.wait("tensor", prev[0], prev[1])
                last = None
                for j in range(4):
                    last = pw.mm(psum[:, bk + 128 * j:bk + 128 * (j + 1)],
                                 B[:, 8192 + 512 * (ob + j) + 128 * tc:
                                   8192 + 512 * (ob + j) + 128 * (tc + 1)],
                                 T["ident"][:], start=(j == 0), stop=(j == 3),
                                 sem="mm" if j == 3 else None)
                pw.wait("vector", "mm", last)
                pw.tt(A[:, LNB + 128 * ob:LNB + 128 * (ob + 4)],
                      A[:, LNB + 128 * ob:LNB + 128 * (ob + 4)],
                      psum[:, bk:bk + 512], AluOpType.add, dep=True)
                dv = pw.tt(X[:, t_global, 128 * ob:128 * (ob + 4)],
                           X[:, t_global, 128 * ob:128 * (ob + 4)],
                           A[:, LNB + 128 * ob:LNB + 128 * (ob + 4)],
                           AluOpType.add, dep=True)
                pw._slot_state[bk] = ("dve", dv)

    if dbg and dbg.get("upto") == "attna":
        pw.barrier("dbgattna")
        pw.dma("sync", dbg["d_x1"], X[:, :, :], sem="st")
        return True
    return False


def row_from_cols(pw, G, cols_ap, row_ap, nprt, ncol, sem):
    n = pw.cnt.get("_rfc", 0)
    pw.cnt["_rfc"] = n + 1
    s = f"rfc{n}"
    if n > 0:
        pw.wait("sync", f"rfc{n - 1}")
    pw.dma("sync", G["g_fr"][0:nprt, 0:ncol], cols_ap, sem=s, slow=True)
    pw.wait("sync", s)
    pw.dma("sync", row_ap,
           G["g_fr"][0:nprt, 0:ncol].rearrange("i t -> t i"), sem=s, slow=True)
    pw.wait("tensor", s)


def ph_ffn_weights(pw, T, W, G):
    psum, psumb = T["psum"], T["psumb"]
    wbuf3 = T["wbuf3"]
    acc = T["stage"]

    for wi, (wname, nblk, gdst, mcol, scol) in enumerate(
            [("f_w1", 32, "g_t1T", 5, 7), ("f_w2", 8, "g_t2T", 6, 8)]):
        ncols = 1024 if wi == 0 else 4096
        per = 8 if wi == 0 else 2
        nld = nblk // per
        pw.red(acc[:, 64:65], T["fcols"][:, 32 * wi:32 * wi + 32], AluOpType.add)
        pw.wait("tensor", "dve")
        s = pw.mm(psum[0:1, 0:1], T["ones_col"][:], acc[:, 64:65], start=True,
                  stop=True, sem="mm")
        pw.wait("vector", "mm", s)
        pw.ts(acc[0:1, 65:66], psum[0:1, 0:1], 1.0 / (nblk * 128 * ncols),
              AluOpType.mult, 1e-5, AluOpType.max)
        v = pw.recip(acc[0:1, 66:67], acc[0:1, 65:66])
        pw.wait("tensor", "dve", v)
        s = pw.mm(psum[:, 4:5], T["ones_row"][0:1, :], acc[0:1, 65:66], start=True,
                  stop=True)
        s = pw.mm(psum[:, 5:6], T["ones_row"][0:1, :], acc[0:1, 66:67], start=True,
                  stop=True, sem="mm")
        pw.wait("vector", "mm", s)
        pw.copy(T["scal"][:, mcol:mcol + 1], psum[:, 4:5])
        pw.copy(T["scal"][:, scol:scol + 1], psum[:, 5:6])

        for g in range(nld):
            pw.wait("sync", "dve")
            l = pw.dma("sync", wbuf3 if wi == 0 else
                       wbuf3.rearrange("p a o -> p (a o)").rearrange(
                           "p (b c) -> p b c", b=2),
                       W[wname].rearrange("(a p) d -> p a d", p=128)
                       [:, per * g:per * (g + 1), :], sem="ld")
            pw.wait("vector", "ld", l)
            for a in range(per):
                blk = wbuf3[:, a, :] if wi == 0 else                     wbuf3.rearrange("p a o -> p (a o)")[:, 4096 * a:4096 * (a + 1)]
                j = g * per + a
                for c0 in range(0, ncols, 1024):
                    sl = blk[:, c0:c0 + 1024] if wi else blk
                    pw.ts(T["tmpA"][:, 0:1024], sl, T["scal"][:, scol:scol + 1],
                          AluOpType.mult)
                    pw.ts(T["stage2"][:, 0:1024], T["tmpA"][:, 0:1024], 0.5,
                          AluOpType.is_gt)
                    pw.ts(T["tmpA"][:, 0:1024], T["tmpA"][:, 0:1024], -0.5,
                          AluOpType.is_lt)
                    pw.tt(T["stage2"][:, 0:1024], T["stage2"][:, 0:1024],
                          T["tmpA"][:, 0:1024], AluOpType.subtract)
                    tv = pw.copy(T["tbf"][:, 0:1024], T["stage2"][:, 0:1024])
                    pw.wait("tensor", "dve", tv)
                    prev = pw._slot_state.get("psb")
                    if prev is not None:
                        pw.wait("tensor", prev[0], prev[1])
                    last = None
                    for dc in range(8):
                        last = pw.mmT(psumb[:, 128 * dc:128 * (dc + 1)],
                                      T["tbf"][:, 128 * dc:128 * (dc + 1)],
                                      T["identb"][:], start=(dc == 0), stop=(dc == 7),
                                      sem="mm" if dc == 7 else None)
                    pw.wait("vector", "mm", last)
                    if "wst" in pw.cnt:
                        pw.wait("vector", "wst")
                    dv = pw.copy(T["tbf"][:, 1024:2048]
                                 .rearrange("p (a b) -> p a b", a=8), psumb[:]
                                 .rearrange("p (a b) -> p a b", a=8), dep=False)
                    pw._slot_state["psb"] = ("dve", dv)
                    pw.wait("sync", "dve", dv)
                    if wi == 0:
                        dst = G["g_t1T"].rearrange("(dc p) j -> p dc j", p=128)[
                            :, :, 128 * j:128 * (j + 1)]
                    else:
                        s_ = c0 // 1024
                        j2 = 2 * g + a
                        dst = G["g_t2T"].rearrange("(hc p) j -> p hc j", p=128)[
                            :, 8 * s_:8 * (s_ + 1), 128 * j2:128 * (j2 + 1)]
                    pw.dma("sync", dst,
                           T["tbf"][:, 1024:2048].rearrange("p (a b) -> p a b", a=8),
                           sem="wst")
    pw.wait("sync", "wst")


def ph_ffn(pw, T, W, G, dbg, which):
    X, A, B = T["X"], T["ARN_A"], T["ARN_B"]
    psum, psumb = T["psum"], T["psumb"]
    groups = T["groups"]
    tag = f"f{which}"
    NTT = 0
    XL = 8192
    FB = 9216
    MAXA = 11264
    SB_ = 13312
    TMPC = 0
    hx = T["hx"]
    nbfT = T["nbfT"]
    tstat = T["tstat"]
    z16 = T["z16"]
    nbf = T["tbf"]

    pw.wait("sync", "dve")
    pw.dma("sync", G["g_halo_in"][0:1, :], X[0:1, 0, :], sem=tag + "hs")
    hd = pw.dma("sync", G["g_halo_in"][1:2, :], X[127:128, 15, :], sem=tag + "hs")
    pw.wait("gpsimd", tag + "hs", hd)
    agh = pw.coll("AllGather", AluOpType.bypass, groups, [G["g_halo_in"].opt()],
                  [G["g_halo"].opt()])

    tmp, stat = T["tmpA"], T["stat"]
    tr_done = None
    for t in range(NT):
        if tr_done is not None:
            pw.wait("vector", "mm", tr_done)
        emit_ln_tile(pw, X[:, t, :], A[:, XL:XL + 1024], tmp[:, 0:1024], stat, LN_EPS)
        pw.red(stat[:, 6:7], A[:, XL:XL + 1024], AluOpType.max, absval=True)
        pw.ts(stat[:, 6:7], stat[:, 6:7], 1e-5, AluOpType.max)
        pw.recip(stat[:, 7:8], stat[:, 6:7])
        pw.ts(stat[:, 7:8], stat[:, 7:8], 127.0, AluOpType.mult)
        pw.ts(T["fcols"][:, t:t + 1], stat[:, 6:7], T["scal"][:, 5:6],
              AluOpType.mult, 1.0 / 127.0, AluOpType.mult)
        pw.ts(z16[:], A[:, XL:XL + 1024], stat[:, 7:8], AluOpType.mult, 1536.0,
              AluOpType.add)
        qv = pw.ts(nbf[:, 0:1024], z16[:], -1536.0, AluOpType.add)
        pw.wait("tensor", "dve", qv)
        prev = pw._slot_state.get("psb")
        if prev is not None:
            pw.wait("tensor", prev[0], prev[1])
        last = None
        for dc in range(8):
            last = pw.mmT(psumb[:, 128 * dc:128 * (dc + 1)],
                          nbf[:, 128 * dc:128 * (dc + 1)], T["identb"][:],
                          start=(dc == 0), stop=(dc == 7),
                          sem="mm" if dc == 7 else None)
        pw.wait("vector", "mm", last)
        dv = pw.copy(nbfT[:, :, 128 * t:128 * (t + 1)],
                     psumb[:].rearrange("p (a b) -> p a b", a=8), dep=False)
        pw._slot_state["psb"] = ("dve", dv)
        tr_done = last

    fv = pw.cnt["dve"]
    pw.wait("sync", "dve", fv)
    row_from_cols(pw, G, T["fcols"][:, 0:16], T["frow"][0:1, 0:2048], 128, 16,
                  tag + "h")
    s = pw.mm(psum[:, 0:512], T["ones_row"][0:1, :], T["frow"][0:1, 0:512], start=True,
              stop=True)
    s = pw.mm(psum[:, 512:1024], T["ones_row"][0:1, :], T["frow"][0:1, 512:1024],
              start=True, stop=True)
    s = pw.mm(psum[:, 1024:1536], T["ones_row"][0:1, :], T["frow"][0:1, 1024:1536],
              start=True, stop=True)
    s = pw.mm(psum[:, 1536:2048], T["ones_row"][0:1, :], T["frow"][0:1, 1536:2048],
              start=True, stop=True, sem="mm")
    pw.wait("vector", "mm", s)
    pw.copy(A[:, FB:FB + 2048], psum[:, 0:2048], dep=False)

    pw.wait("sync", "gp", agh)
    hl = pw.dma("sync", T["halo2"][0:1, :], G["g_halo"][0, 1, :][None, :],
                sem=tag + "hl")
    hl = pw.dma("sync", T["halo2"][1:2, :], G["g_halo"][1, 0, :][None, :],
                sem=tag + "hl")
    pw.wait("vector", tag + "hl", hl)
    emit_ln_tile(pw, T["halo2"][0:2, :], T["halo2b"][0:2, :], tmp[0:2, 0:1024],
                 stat[0:2, :], LN_EPS)
    pw.red(stat[0:2, 6:7], T["halo2b"][0:2, :], AluOpType.max, absval=True)
    pw.ts(stat[0:2, 6:7], stat[0:2, 6:7], 1e-5, AluOpType.max)
    pw.recip(stat[0:2, 7:8], stat[0:2, 6:7])
    pw.ts(stat[0:2, 7:8], stat[0:2, 7:8], 127.0, AluOpType.mult)
    pw.ts(T["fcols"][0:2, 16:17], stat[0:2, 6:7], T["scal"][0:2, 5:6],
          AluOpType.mult, 1.0 / 127.0, AluOpType.mult)
    pw.ts(T["z16h"][0:2, :], T["halo2b"][0:2, :], stat[0:2, 7:8], AluOpType.mult,
          1536.0, AluOpType.add)
    qv = pw.ts(nbf[0:2, 0:1024], T["z16h"][0:2, :], -1536.0, AluOpType.add)
    pw.wait("tensor", "dve", qv)
    prev = pw._slot_state.get("psb")
    if prev is not None:
        pw.wait("tensor", prev[0], prev[1])
    last = None
    for dc in range(8):
        last = pw.mmT(psumb[:, 16 * dc:16 * dc + 2],
                      nbf[0:2, 128 * dc:128 * (dc + 1)], T["identb"][0:2, 0:2],
                      start=(dc == 0), stop=(dc == 7), sem="mm" if dc == 7 else None)
    pw.wait("vector", "mm", last)
    dv = pw.copy(T["nhT"][:, :, :], psumb[:, 0:128].rearrange("p (a b) -> p a b", b=16)
                 [:, :, 0:2], dep=False)
    pw._slot_state["psb"] = ("dve", dv)
    fh = pw.cnt["dve"]
    pw.wait("sync", "dve", fh)
    row_from_cols(pw, G, T["fcols"][0:2, 16:17], T["frow"][0:1, 2048:2050], 2, 1,
                  tag + "h")
    s = pw.mm(psum[:, 512 * 5:512 * 5 + 2], T["ones_row"][0:1, :],
              T["frow"][0:1, 2048:2050], start=True, stop=True, sem="mm")
    pw.wait("vector", "mm", s)
    pw.copy(T["haloh"][:, 0:2], psum[:, 512 * 5:512 * 5 + 2], dep=False)
    pw.ts(T["haloh"][:, 0:1], T["haloh"][:, 0:1], T["scal"][:, 2:3],
          AluOpType.mult)
    pw.ts(T["haloh"][:, 1:2], T["haloh"][:, 1:2], T["scal"][:, 3:4],
          AluOpType.mult)

    pw.memset(A[:, MAXA:MAXA + 2048], 0.0)
    pw.wait("vector", "gp")
    pw.wait("tensor", "dve")
    dv0 = pw.cnt["dve"]
    for b_ in range(5):
        pw._slot_state[("fx", b_)] = ("dve", dv0)
    pw._slot_state["hb6"] = ("dve", dv0)
    hdma = {}
    av_hist = {}
    tstat4 = T["tstat4"]
    mmh1 = {-4: pw.cnt["mm"], -3: pw.cnt["mm"],
            -2: pw.cnt["mm"], -1: pw.cnt["mm"]}
    for jc in range(NJC):
        pw.wait("sync", "mm", mmh1[jc - 4])
        l = pw.dma("sync", tstat4[:, jc % 4, :, :],
                   G["g_t1T"].rearrange("(dc p) j -> p dc j", p=128)
                   [:, :, 128 * jc:128 * (jc + 1)], sem=tag + "w" + str(jc % 4))
        pw.wait("tensor", tag + "w" + str(jc % 4), l)
        strip = hx[:, jc % 2, :]
        cbuf = (T["stage2"][:, 0:2048] if jc % 2 == 0
                else T["stage"][:, 0:2048])
        for sp in range(4):
            bk = (jc * 4 + sp) % 5
            prev = pw._slot_state.get(("fx", bk))
            if prev is not None:
                pw.wait("tensor", prev[0], prev[1])
            last = None
            for dc in range(8):
                last = pw.mm(psum[:, 512 * bk:512 * bk + 512],
                             tstat4[:, jc % 4, dc, :],
                             nbfT[:, dc, 512 * sp:512 * (sp + 1)],
                             start=(dc == 0), stop=(dc == 7),
                             sem="mm" if dc == 7 else None)
            pw.wait("vector", "mm", last)
            if sp == 0:
                if hdma.get(jc % 2) is not None:
                    pw.wait("vector", tag + "hd" + str(jc % 2), hdma[jc % 2])
                if av_hist.get(jc % 2) is not None:
                    pw.wait("vector", "act", av_hist[jc % 2])
            c = pw.tt(strip[:, 1 + 512 * sp:1 + 512 * (sp + 1)],
                      psum[:, 512 * bk:512 * bk + 512],
                      A[:, FB + 512 * sp:FB + 512 * (sp + 1)], AluOpType.mult,
                      dep=False)
            pw._slot_state[("fx", bk)] = ("dve", c)
        prev = pw._slot_state.get("hb6")
        if prev is not None:
            pw.wait("tensor", prev[0], prev[1])
        lasth = None
        for dc in range(8):
            lasth = pw.mm(psum[:, 512 * 6:512 * 6 + 2],
                          tstat4[:, jc % 4, dc, :], T["nhT"][:, dc, :],
                          start=(dc == 0), stop=(dc == 7),
                          sem="mm" if dc == 7 else None)
        mmh1[jc] = pw.cnt["mm"]
        pw.wait("vector", "mm", lasth)
        pw.tt(strip[:, 0:1], psum[:, 512 * 6:512 * 6 + 1], T["haloh"][:, 0:1],
              AluOpType.mult, dep=True)
        hc2 = pw.tt(strip[:, 2049:2050], psum[:, 512 * 6 + 1:512 * 6 + 2],
                    T["haloh"][:, 1:2], AluOpType.mult, dep=True)
        pw._slot_state["hb6"] = ("dve", hc2)
        pw.ts(cbuf, strip[:, 0:2048], T["convw"][:, jc, 0:1], AluOpType.mult)
        pw.stt(cbuf, strip[:, 1:2049], T["convw"][:, jc, 1:2],
               cbuf, AluOpType.mult, AluOpType.add)
        cv = pw.stt(cbuf, strip[:, 2:2050], T["convw"][:, jc, 2:3],
                    cbuf, AluOpType.mult, AluOpType.add)
        pw.wait("scalar", "dve", cv)
        av = pw.act(strip[:, 1:2049],
                    cbuf, AF.Silu, bias=T["convb"][:, jc:jc + 1],
                    dep=False)
        av_hist[jc % 2] = av
        pw.wait("vector", "act", av)
        pw.stt(A[:, MAXA:MAXA + 2048], strip[:, 1:2049], 1.0,
               A[:, MAXA:MAXA + 2048], AluOpType.mult, AluOpType.max)
        pw.stt(A[:, MAXA:MAXA + 2048], strip[:, 1:2049], -1.0,
               A[:, MAXA:MAXA + 2048], AluOpType.mult, AluOpType.max)
        pw.wait("gpsimd", "act", av)
        hdma[jc % 2] = pw.dma("gpsimd", G["g_hact"][128 * jc:128 * (jc + 1), :],
                              strip[:, 1:2049], sem=tag + "hd" + str(jc % 2))

    pw.wait("tensor", "dve")
    for tt_ in range(NT):
        prev = pw._slot_state.get(("fx", tt_ % 2))
        if prev is not None:
            pw.wait("tensor", prev[0], prev[1])
        s = pw.mm(psum[:, 512 * (tt_ % 2):512 * (tt_ % 2) + 128],
                  A[:, MAXA + 128 * tt_:MAXA + 128 * (tt_ + 1)], T["ident"][:],
                  start=True, stop=True, sem="mm")
        pw.wait("vector", "mm", s)
        rv = pw.red(T["fcols"][:, 32 + tt_:33 + tt_],
                    psum[:, 512 * (tt_ % 2):512 * (tt_ % 2) + 128],
                    AluOpType.max, dep=False)
        pw._slot_state[("fx", tt_ % 2)] = ("dve", rv)
    pw.ts(T["fcols"][:, 32:48], T["fcols"][:, 32:48], 1e-5, AluOpType.max)
    pw.ts(T["fcols"][:, 48:64], T["fcols"][:, 32:48], T["scal"][:, 6:7],
          AluOpType.mult, 1.0 / 127.0, AluOpType.mult)
    rv = pw.recip(T["fcols"][:, 32:48], T["fcols"][:, 32:48])
    sv = pw.ts(T["fcols"][:, 32:48], T["fcols"][:, 32:48], 127.0, AluOpType.mult)
    pw.wait("sync", "dve", sv)
    row_from_cols(pw, G, T["fcols"][:, 32:48], T["frow"][0:1, 0:2048], 128, 16,
                  tag + "h")
    row_from_cols(pw, G, T["fcols"][:, 48:64], T["f2row"][0:1, 0:2048], 128, 16,
                  tag + "h")
    pw.wait("tensor", "dve")
    for half in range(4):
        s = pw.mm(psum[:, 512 * half:512 * (half + 1)], T["ones_row"][0:1, :],
                  T["frow"][0:1, 512 * half:512 * (half + 1)], start=True, stop=True,
                  sem="mm" if half == 3 else None)
    pw.wait("vector", "mm", s)
    pw.copy(A[:, SB_:SB_ + 2048], psum[:, 0:2048], dep=False)
    pw.wait("tensor", "dve")
    for half in range(4):
        s = pw.mm(psum[:, 512 * half:512 * (half + 1)], T["ones_row"][0:1, :],
                  T["f2row"][0:1, 512 * half:512 * (half + 1)], start=True, stop=True,
                  sem="mm" if half == 3 else None)
    pw.wait("vector", "mm", s)
    f2v = pw.copy(A[:, FB:FB + 2048], psum[:, 0:2048], dep=False)

    pw.wait("sync", "mm", pw.cnt["mm"])
    lt2 = pw.dma("sync", T["t2s"][:, :, :],
                 G["g_t2T"].rearrange("(hc p) j -> p hc j", p=128)
                 [:, :, 0:512], sem=tag + "t2")

    pw.wait("sync", "dve", f2v)
    pw.wait("sync", tag + "hd0")
    pw.wait("sync", tag + "hd1")
    pw.wait("vector", "act")
    pw.wait("tensor", "dve", f2v)
    pw.wait("tensor", tag + "t2", lt2)
    ndma = {}
    qh_tt = {}
    nv_hist = {}
    fmm = {-3: pw.cnt["mm"], -2: pw.cnt["mm"], -1: pw.cnt["mm"]}

    def fused_mms(jc):
        nh_b = T["nh2"] if jc % 2 == 0 else T["tbf"][:, 0:2048]
        pw.wait("tensor", "dve", nv_hist[jc])
        for oc in range(4):
            pw.mm(psum[:, 512 * oc:512 * oc + 512],
                  T["t2s"][:, jc, 128 * oc:128 * (oc + 1)],
                  nh_b[:, 0:512],
                  start=(jc == 0), stop=(jc == NJC - 1),
                  sem="mm" if oc == 3 else None)
        fmm[jc] = pw.cnt["mm"]

    for jc in range(NJC):
        if qh_tt.get(jc % 2) is not None:
            pw.wait("sync", "dve", qh_tt[jc % 2])
        l = pw.dma("sync", hx[:, jc % 2, 0:2048],
                   G["g_hact"][128 * jc:128 * (jc + 1), :], sem=tag + "q" + str(jc % 2))
        pw.wait("vector", tag + "q" + str(jc % 2), l)
        if ndma.get(jc % 2) is not None:
            pw.wait("vector", tag + "nd" + str(jc % 2), ndma[jc % 2])
        nh_buf = T["nh2"] if jc % 2 == 0 else T["tbf"][:, 0:2048]
        pw.wait("vector", "mm", fmm[jc - 2])
        qh_tt[jc % 2] = pw.tt(T["stage2"][:, 0:2048], hx[:, jc % 2, 0:2048],
                              A[:, SB_:SB_ + 2048], AluOpType.mult, dep=False)
        pw.ts(T["z16w"][:, 0:2048], T["stage2"][:, 0:2048], 1536.0, AluOpType.add)
        nv = pw.ts(nh_buf, T["z16w"][:, 0:2048], -1536.0, AluOpType.add)
        nv_hist[jc] = nv
        pw.wait("gpsimd", "dve", nv)
        ndma[jc % 2] = pw.dma("gpsimd", G["g_nh"][128 * jc:128 * (jc + 1), :],
                              nh_buf, sem=tag + "nd" + str(jc % 2))
        if jc >= 1:
            fused_mms(jc - 1)
    fused_mms(NJC - 1)

    pw.wait("vector", "mm", fmm[NJC - 1])
    for oc in range(4):
        pw.tt(T["ofm2"][:, oc, :], psum[:, 512 * oc:512 * oc + 512],
              A[:, FB:FB + 512], AluOpType.mult, dep=False)
    pw.wait("tensor", "dve")
    for tt_ in range(4):
        bk = 4 + (tt_ % 2)
        prev = pw._slot_state.get(("fx2", bk))
        if prev is not None:
            pw.wait("tensor", prev[0], prev[1])
        last = None
        for oc in range(4):
            last = pw.mm(psum[:, 512 * bk + 128 * oc:512 * bk + 128 * (oc + 1)],
                         T["ofm2"][:, oc, 128 * tt_:128 * (tt_ + 1)],
                         T["ident"][:], start=(oc == 0), stop=(oc == 3),
                         sem="mm" if oc == 3 else None)
        pw.wait("vector", "mm", last)
        xv = pw.tt(X[:, tt_, 0:512], X[:, tt_, 0:512],
                   psum[:, 512 * bk:512 * bk + 512], AluOpType.add, dep=False)
        pw._slot_state[("fx2", bk)] = ("dve", xv)

    pw.wait("sync", tag + "nd0")
    pw.wait("sync", tag + "nd1")
    pw.wait("sync", "mm")
    pw.wait("sync", "dve", nv_hist[NJC - 1])
    pw.wait("tensor", "dve")
    for oh in range(2):
        if oh == 1:
            pw.wait("sync", "mm")
            l2 = pw.dma("sync", T["t2s"][:, :, :],
                        G["g_t2T"].rearrange("(hc p) j -> p hc j", p=128)
                        [:, :, 512 * oh:512 * (oh + 1)], sem=tag + "t2")
            pw.wait("tensor", tag + "t2", l2)
        for sp in range(4):
            if oh == 0 and sp == 0:
                continue
            mmh2 = {-4: pw.cnt["mm"], -3: pw.cnt["mm"],
                    -2: pw.cnt["mm"], -1: pw.cnt["mm"]}
            for jc in range(NJC):
                pw.wait("sync", "mm", mmh2[jc - 4])
                sm2 = tag + "x" + str(jc % 4)
                l1 = pw.dma("sync", T["nstat4"][:, jc % 4, :],
                            G["g_nh"][128 * jc:128 * (jc + 1),
                                      512 * sp:512 * (sp + 1)], sem=sm2)
                pw.wait("tensor", sm2, l1)
                for oc in range(4):
                    pw.mm(psum[:, 512 * oc:512 * oc + 512],
                          T["t2s"][:, jc, 128 * oc:128 * (oc + 1)],
                          T["nstat4"][:, jc % 4, :],
                          start=(jc == 0), stop=(jc == NJC - 1),
                          sem="mm" if oc == 3 else None)
                mmh2[jc] = pw.cnt["mm"]
            pw.wait("vector", "mm")
            for oc in range(4):
                ev = pw.tt(T["ofm2"][:, oc, :],
                           psum[:, 512 * oc:512 * oc + 512],
                           A[:, FB + 512 * sp:FB + 512 * (sp + 1)],
                           AluOpType.mult, dep=False)
            pw.wait("tensor", "dve")
            for tt_ in range(4):
                t_global = 4 * sp + tt_
                bk = 4 + (tt_ % 2)
                prev = pw._slot_state.get(("fx2", bk))
                if prev is not None:
                    pw.wait("tensor", prev[0], prev[1])
                last = None
                for oc in range(4):
                    last = pw.mm(psum[:, 512 * bk + 128 * oc:512 * bk + 128 * (oc + 1)],
                                 T["ofm2"][:, oc, 128 * tt_:128 * (tt_ + 1)],
                                 T["ident"][:], start=(oc == 0), stop=(oc == 3),
                                 sem="mm" if oc == 3 else None)
                pw.wait("vector", "mm", last)
                xv = pw.tt(X[:, t_global, 512 * oh:512 * (oh + 1)],
                           X[:, t_global, 512 * oh:512 * (oh + 1)],
                           psum[:, 512 * bk:512 * bk + 512], AluOpType.add,
                           dep=False)
                pw._slot_state[("fx2", bk)] = ("dve", xv)

    if dbg and dbg.get("upto") == "ffn" + str(which):
        pw.barrier("dbgffn")
        pw.dma("sync", dbg["d_x1"], X[:, :, :], sem="st")
        return True
    return False


def ph_attn_b(pw, T, W, G, dbg):
    X, A, B = T["X"], T["ARN_A"], T["ARN_B"]
    psum = T["psum"]
    groups = T["groups"]
    BW = T["BWK"]
    VXB = 0
    KVS = 1040
    STS = 1172
    DEN = 1304
    ASB = 1336
    QP0, QP1 = 0, 4096
    CATQ, CATK = 8192, 10240
    KPF = 12288
    KPT = 8192
    GBC = 1024

    tmp, stat = T["tmpA"], T["stat"]

    pw.wait("vector", "mm")
    xln = B

    def dst(t, f):
        return A[:, 2048 * f + 128 * t:2048 * f + 128 * (t + 1)]
    for t in range(NT):
        emit_ln_tile(pw, X[:, t, :], xln[:, 1024 * t:1024 * (t + 1)], tmp[:, 0:1024],
                     stat, LN_EPS)
        pw.wait("tensor", "dve")
        emit_transpose_tm_to_fm(
            pw, (lambda tc: (lambda _t: xln[:, 1024 * tc:1024 * (tc + 1)]))(t),
            (lambda tc: (lambda _t, f: dst(tc, f)))(t), 1, FC, T, 0, 512)
    pw.barrier("battnb")

    def load_W(name):
        return pw.dma("sync", T["wbuf3"], W[name].rearrange("(f p) o -> p f o", p=128),
                      sem="ld")

    stage = T["stage"]

    def mk_evac(kind, dsem):
        def evac(of, sp, ps, mmcnt, need_guard):
            it = sp * FC + of
            region = stage[:, 512 * (it % 4):512 * (it % 4) + 512]
            pw.wait("scalar", "mm", mmcnt)
            if need_guard:
                pw.wait("scalar", dsem)
            if kind == "q":
                c = pw.act(region, ps, AF.Identity, scale=0.125,
                           bias=T["vecs"][:, 96 + of:97 + of])
            else:
                c = pw.act(region, ps, AF.Identity, scale=0.125,
                           bias=T["vecs"][:, 104 + of:105 + of])
            return ("act", c, region)
        return evac

    ld = load_W("b_qw")
    emit_proj_fm(pw, T, ld, mk_evac("q", "bqs"),
                 lambda of, sp: G["g_bqT"][:, of, 512 * sp:512 * (sp + 1)], "bqs")
    ld = load_W("b_kw")
    emit_proj_fm(pw, T, ld, mk_evac("k", "bks"),
                 lambda of, sp: G["g_bkT"][:, of, 512 * sp:512 * (sp + 1)], "bks",
                 prev_sems=("bqs",))

    row_from_cols(pw, G, T["vecs"][:, 32:40], T["frow"][0:1, 0:1024], 128, 8, "bvr")
    pw.wait("tensor", "dve")
    for half in range(2):
        s = pw.mm(psum[:, 512 * half:512 * (half + 1)], T["ones_row"][0:1, :],
                  T["frow"][0:1, 512 * half:512 * (half + 1)], start=True, stop=True,
                  sem="mm" if half == 1 else None)
    pw.wait("vector", "mm", s)
    pw.copy(T["stage2"][:, 0:1024], psum[:, 0:1024], dep=False)

    ld = load_W("b_vw")
    pw.wait("gpsimd", "dve")
    mz = pw.memset(T["vstage"], 1.0)
    pw.wait("vector", "gp", mz)
    pw.wait("sync", "gp", mz)
    pw.wait("tensor", "ld", ld)
    pw.wait("vector", "bks")
    pw.wait("tensor", "dve")
    vdma = {}
    for mc in range(NT):
        for vh in range(2):
            bk = (mc * 2 + vh) % 4
            prev = pw._slot_state.get(512 * bk)
            if prev is not None:
                pw.wait("tensor", prev[0], prev[1])
            last = None
            for f in range(FC):
                last = pw.mm(psum[:, 512 * bk:512 * bk + 512],
                             A[:, 2048 * f + 128 * mc:2048 * f + 128 * (mc + 1)],
                             T["wbuf3"][:, f, 512 * vh:512 * (vh + 1)],
                             start=(f == 0), stop=(f == FC - 1),
                             sem="mm" if f == FC - 1 else None)
            pw.wait("vector", "mm", last)
            if vdma.get(mc % 2) is not None:
                pw.wait("vector", "bvs")
            dstv = T["vstage"][:, 1040 * (mc % 2) + 520 * vh:1040 * (mc % 2) + 520 * (vh + 1)] \
                .rearrange("p (h c) -> p h c", c=65)[:, :, 0:64]
            c = pw.stt(dstv, psum[:, 512 * bk:512 * bk + 512]
                       .rearrange("p (h c) -> p h c", c=64), 1.0,
                       T["stage2"][:, 512 * vh:512 * (vh + 1)]
                       .rearrange("p (h c) -> p h c", c=64), AluOpType.mult,
                       AluOpType.add, dep=False)
            pw._slot_state[512 * bk] = ("dve", c)
        pw.wait("sync", "dve")
        vdma[mc % 2] = pw.dma("sync", G["g_bv"][:, 1040 * mc:1040 * (mc + 1)],
                              T["vstage"][:, 1040 * (mc % 2):1040 * (mc % 2) + 1040],
                              sem="bvs")
    pw.wait("sync", "bvs")

    if dbg and dbg.get("upto") == "bproj":
        pw.barrier("dbgbp")
        pw.dma("sync", dbg["d_qT"].rearrange("(f p) t -> p f t", p=128), G["g_bqT"],
               sem="st")
        pw.dma("sync", dbg["d_kT"].rearrange("(f p) t -> p f t", p=128), G["g_bkT"],
               sem="st")
        pw.dma("sync", dbg["d_vext"], G["g_bv"], sem="st")
        pw.wait("sync", "st")
        return True

    ag_h = {}
    ld_hist = {}

    def load_feats(h):
        sm = "bh" + str(h % 2)
        for (buf, g) in ((CATQ, G["g_bqT"]), (CATK, G["g_bkT"])):
            pw.dma("sync", A[0:64, buf:buf + 2048],
                   g[64 * (h % 2):64 * (h % 2) + 64, h // 2, :], sem=sm)
            pw.dma("sync", A[64:128, buf:buf + 2048],
                   g[64 * (h % 2):64 * (h % 2) + 64, h // 2, :], sem=sm)
        lv = pw.dma("sync", BW[:, VXB:VXB + 1040].rearrange("p (tc c) -> p tc c", c=65),
                    G["g_bv"].rearrange("p (tc c) -> p tc c", c=1040)
                    [:, :, 65 * h:65 * (h + 1)], sem=sm)
        pw.wait("scalar", sm, lv)
        pw.act(A[64:128, CATQ:CATQ + 2048], A[64:128, CATQ:CATQ + 2048],
               AF.Square, dep=False)
        a2 = pw.act(A[64:128, CATK:CATK + 2048], A[64:128, CATK:CATK + 2048],
                    AF.Square, dep=False)
        ld_hist[h] = a2

    def features(h):
        qpb = QP0 if h % 2 == 0 else QP1
        pw.wait("tensor", "act", ld_hist[h])
        for (src2, dstc) in ((CATQ, qpb), (CATK, KPF)):
            for mc in range(2):
                for sp in range(4):
                    bk = 3 + (mc * 4 + sp) % 4
                    prev = pw._slot_state.get(512 * bk)
                    if prev is not None:
                        pw.wait("tensor", prev[0], prev[1])
                    mmv = pw.mm(psum[:, 512 * bk:512 * bk + 512],
                                T["projx2"][:, 128 * mc:128 * (mc + 1)],
                                A[:, src2 + 512 * sp:src2 + 512 * (sp + 1)],
                                start=True, stop=True, sem="mm")
                    pw.wait("scalar", "mm", mmv)
                    dst_ap = A[:, dstc + 2048 * mc + 512 * sp:
                               dstc + 2048 * mc + 512 * (sp + 1)]
                    ec = pw.act(dst_ap, psum[:, 512 * bk:512 * bk + 512], AF.Exp,
                                bias=-LN16, dep=False)
                    pw._slot_state[512 * bk] = ("act", ec)
        for tc in range(NT):
            bk = 3 + tc % 4
            prev = pw._slot_state.get(512 * bk)
            if prev is not None:
                pw.wait("tensor", prev[0], prev[1])
            mmv = pw.mm(psum[:, 512 * bk:512 * bk + 256],
                        A[:, CATK + 128 * tc:CATK + 128 * (tc + 1)],
                        T["projx2"][:, 0:256], start=True, stop=True, sem="mm")
            pw.wait("scalar", "mm", mmv)
            ec = pw.act(A[:, KPT + 256 * tc:KPT + 256 * (tc + 1)],
                        psum[:, 512 * bk:512 * bk + 256], AF.Exp, bias=-LN16,
                        dep=False)
            pw._slot_state[512 * bk] = ("act", ec)

    def scan(h):
        qpb = QP0 if h % 2 == 0 else QP1
        pw.wait("gpsimd", "act")
        pw.wait("gpsimd", "mm")
        for p_ in ("bst0", "bst1"):
            if p_ in pw.cnt:
                pw.wait("gpsimd", p_)
        z = pw.memset(BW[:, KVS:KVS + 130], 0.0)
        pw.wait("vector", "gp", z)
        pw.wait("tensor", "gp", z)
        pw.wait("scalar", "act")
        am_hist = {0: None, 1: None}
        ap_last = {}
        kv_last = None

        def aprime(c):
            abk = 512 if c % 2 == 0 else 2048
            prev = am_hist[c % 2]
            if prev is not None:
                pw.wait("tensor", prev[0], prev[1])
            last = None
            for mc in range(2):
                last = pw.mm(psum[:, abk:abk + 128],
                             A[:, KPF + 2048 * mc + 128 * c:KPF + 2048 * mc + 128 * (c + 1)],
                             A[:, qpb + 2048 * mc + 128 * c:qpb + 2048 * mc + 128 * (c + 1)],
                             start=(mc == 0), stop=(mc == 1),
                             sem="mm" if mc == 1 else None)
            ap_last[c] = last

        aprime(0)
        for c in range(NT):
            abk = 512 if c % 2 == 0 else 2048
            if c + 1 < NT:
                aprime(c + 1)
            if c > 0:
                pw.wait("vector", "mm", kv_last)
                pw.copy(BW[:, KVS:KVS + 130], psum[:, 0:130], dep=False)
            pw.wait("vector", "mm", ap_last[c])
            am = pw.tt(BW[:, ASB:ASB + 128], psum[:, abk:abk + 128], T["mask_ut"][:],
                       AluOpType.mult, dep=False)
            am_hist[c % 2] = ("dve", am)
            pw.wait("tensor", "dve", am)
            for mc in range(2):
                pw.mm(psum[:, 1024:1024 + 65],
                      A[:, qpb + 2048 * mc + 128 * c:qpb + 2048 * mc + 128 * (c + 1)],
                      BW[:, KVS + 65 * mc:KVS + 65 * (mc + 1)],
                      start=(mc == 0), stop=False)
            nm = pw.mm(psum[:, 1024:1024 + 65], BW[:, ASB:ASB + 128],
                       BW[:, VXB + 65 * c:VXB + 65 * (c + 1)],
                       start=False, stop=True, sem="mm")
            for mc in range(2):
                pw.mm(psum[:, 65 * mc:65 * (mc + 1)],
                      A[:, KPT + 256 * c + 128 * mc:KPT + 256 * c + 128 * (mc + 1)],
                      BW[:, VXB + 65 * c:VXB + 65 * (c + 1)],
                      start=(c == 0 and mc == 0), stop=(mc == 1),
                      sem="mm" if mc == 1 else None, skip=True)
            kv_last = pw.cnt["mm"]
            pw.wait("vector", "mm", nm)
            pw.copy(B[:, 1024 * c + 64 * h:1024 * c + 64 * (h + 1)],
                    psum[:, 1024:1024 + 64], dep=False)
            pw.copy(BW[:, DEN + 16 * (h % 2) + c:DEN + 16 * (h % 2) + c + 1],
                    psum[:, 1024 + 64:1024 + 65], dep=False)
        if am_hist[1] is not None:
            pw._slot_state[2048] = am_hist[1]
        pw.wait("vector", "mm", kv_last)
        sv = pw.copy(BW[:, KVS:KVS + 130], psum[:, 0:130], dep=False)
        pw.wait("sync", "dve", sv)
        sd = pw.dma("sync", G["g_st_in"][h].rearrange("(mc p) c -> p mc c", p=128),
                    BW[:, KVS:KVS + 130].rearrange("p (mc c) -> p mc c", c=65),
                    sem="bst" + str(h % 2))
        pw.wait("gpsimd", "bst" + str(h % 2), sd)
        ag_h[h] = pw.coll("AllGather", AluOpType.bypass, groups,
                          [G["g_st_in"][h].opt()], [G["g_st"][h].opt()])
        if h + 1 < H:
            pw.wait("sync", "mm", kv_last)
            load_feats(h + 1)

    def finalize(h):
        qpb = QP0 if h % 2 == 0 else QP1
        pw.wait("sync", "gp", ag_h[h])
        pw.wait("sync", "mm")
        lv = pw.dma("sync", BW[:, STS:STS + 130].rearrange("p (mc c) -> p mc c", c=65),
                    G["g_st"][h][0].rearrange("(mc p) c -> p mc c", p=128),
                    sem="bfs" + str(h % 2))
        pw.wait("vector", "bfs" + str(h % 2), lv)
        mk = pw.ts(BW[:, STS:STS + 130], BW[:, STS:STS + 130], T["scal"][:, 1:2],
                   AluOpType.mult, dep=True)
        pw.wait("tensor", "dve", mk)
        for c in range(NT):
            bk = 2 + c % 2
            prev = pw._slot_state.get(512 * bk)
            if prev is not None:
                pw.wait("tensor", prev[0], prev[1])
            last = None
            for mc in range(2):
                last = pw.mm(psum[:, 512 * bk:512 * bk + 65],
                             A[:, qpb + 2048 * mc + 128 * c:qpb + 2048 * mc + 128 * (c + 1)],
                             BW[:, STS + 65 * mc:STS + 65 * (mc + 1)],
                             start=(mc == 0), stop=(mc == 1),
                             sem="mm" if mc == 1 else None)
            pw.wait("vector", "mm", last)
            pw.tt(B[:, 1024 * c + 64 * h:1024 * c + 64 * (h + 1)],
                  B[:, 1024 * c + 64 * h:1024 * c + 64 * (h + 1)],
                  psum[:, 512 * bk:512 * bk + 64], AluOpType.add, dep=False)
            pw.tt(stat[:, 6:7], BW[:, DEN + 16 * (h % 2) + c:DEN + 16 * (h % 2) + c + 1],
                  psum[:, 512 * bk + 64:512 * bk + 65], AluOpType.add)
            rv = pw.recip(stat[:, 7:8], stat[:, 6:7])
            dv = pw.ts(B[:, 1024 * c + 64 * h:1024 * c + 64 * (h + 1)],
                       B[:, 1024 * c + 64 * h:1024 * c + 64 * (h + 1)],
                       stat[:, 7:8], AluOpType.mult)
            pw._slot_state[512 * bk] = ("dve", dv)

    pw.wait("sync", "mm")
    pw.wait("sync", "dve")
    load_feats(0)
    for h in range(H):
        features(h)
        scan(h)
        if h >= 1:
            finalize(h - 1)
    finalize(H - 1)

    if dbg and dbg.get("upto") == "bheads":
        pw.barrier("dbgbh")
        pw.dma("sync", dbg["d_x1"], B[:, 0:16384].rearrange("p (t f) -> p t f", f=1024),
               sem="st")
        pw.dma("sync", dbg["d_den"], BW[:, DEN:DEN + 32], sem="st")
        pw.wait("sync", "st")
        return True

    pw.wait("tensor", "dve")

    def dsta(t, f):
        return A[:, 2048 * f + 128 * t:2048 * f + 128 * (t + 1)]
    emit_transpose_tm_to_fm(pw, lambda t: B[:, 1024 * t:1024 * (t + 1)],
                            dsta, NT, FC, T, 512 * 5, 512 * 6)
    pw.wait("sync", "mm")
    pw.wait("sync", "dve")
    ldw = load_W("b_ow")
    GBC2 = 8192
    RES = 10240
    row_from_cols(pw, G, T["vecs"][:, 48:56], T["frow"][0:1, 0:1024], 128, 8, "bgr")
    row_from_cols(pw, G, T["vecs"][:, 56:64], T["frow"][0:1, 1024:2048], 128, 8, "bbr")
    pw.wait("tensor", "dve")
    for half in range(4):
        s = pw.mm(psum[:, 512 * half:512 * (half + 1)], T["ones_row"][0:1, :],
                  T["frow"][0:1, 512 * half:512 * (half + 1)], start=True, stop=True,
                  sem="mm" if half == 3 else None)
    pw.wait("vector", "mm", s)
    gv = pw.copy(B[:, GBC2:GBC2 + 2048], psum[:, 0:2048], dep=False)
    pw.wait("tensor", "dve", gv)
    pw.wait("tensor", "ld", ldw)

    for t in range(NT):
        emit_ln_tile(pw, X[:, t, :], B[:, RES:RES + 1024], tmp[:, 0:1024], stat,
                     LN_EPS)
        for of in range(FC):
            bk = of % 2
            prev = pw._slot_state.get(("eo", bk))
            if prev is not None:
                pw.wait("tensor", prev[0], prev[1])
            last = None
            for f in range(FC):
                last = pw.mm(psum[:, 512 * bk:512 * bk + 128],
                             T["wbuf3"][:, f, 128 * of:128 * of + 128],
                             A[:, 2048 * f + 128 * t:2048 * f + 128 * (t + 1)],
                             start=(f == 0), stop=(f == FC - 1),
                             sem="mm" if f == FC - 1 else None)
            pw.wait("vector", "mm", last)
            ec2 = pw.copy(B[:, 11264 + 128 * bk:11264 + 128 * (bk + 1)],
                          psum[:, 512 * bk:512 * bk + 128], dep=False)
            pw.wait("tensor", "dve", ec2)
            tb = pw.mm(psum[:, 512 * (2 + bk):512 * (2 + bk) + 128],
                       B[:, 11264 + 128 * bk:11264 + 128 * (bk + 1)], T["ident"][:],
                       start=True, stop=True, sem="mm")
            pw.wait("vector", "mm", tb)
            ev = pw.stt(B[:, RES + 128 * of:RES + 128 * (of + 1)],
                        psum[:, 512 * (2 + bk):512 * (2 + bk) + 128],
                        T["vecs"][:, 40 + of:41 + of],
                        B[:, RES + 128 * of:RES + 128 * (of + 1)],
                        AluOpType.add, AluOpType.add, dep=True)
            pw._slot_state[("eo", bk)] = ("dve", ev)
        pw.red(stat[:, 0:1], B[:, RES:RES + 1024], AluOpType.add)
        pw.ts(stat[:, 1:2], stat[:, 0:1], 1.0 / D, AluOpType.mult)
        v0 = pw.ts(tmp[:, 0:1024], B[:, RES:RES + 1024], stat[:, 1:2],
                   AluOpType.subtract)
        pw.wait("scalar", "dve", v0)
        aq = pw.act(B[:, RES:RES + 1024], tmp[:, 0:1024], AF.Square, dep=False)
        pw.wait("vector", "act", aq)
        pw.red(stat[:, 2:3], B[:, RES:RES + 1024], AluOpType.add)
        v1 = pw.ts(stat[:, 3:4], stat[:, 2:3], 1.0 / D, AluOpType.mult,
                   LNB_EPS, AluOpType.add)
        pw.wait("scalar", "dve", v1)
        a1 = pw.act(stat[:, 4:5], stat[:, 3:4], AF.Sqrt, dep=False)
        pw.wait("vector", "act", a1)
        pw.recip(stat[:, 5:6], stat[:, 4:5])
        pw.ts(tmp[:, 0:1024], tmp[:, 0:1024], stat[:, 5:6], AluOpType.mult)
        pw.tt(tmp[:, 0:1024], tmp[:, 0:1024], B[:, GBC2:GBC2 + 1024],
              AluOpType.mult)
        pw.tt(tmp[:, 0:1024], tmp[:, 0:1024], B[:, GBC2 + 1024:GBC2 + 2048],
              AluOpType.add)
        pw.tt(X[:, t, :], X[:, t, :], tmp[:, 0:1024], AluOpType.add)

    if dbg and dbg.get("upto") == "attnb":
        pw.barrier("dbgattnb")
        pw.dma("sync", dbg["d_x1"], X[:, :, :], sem="st")
        return True
    return False


def ph_store(pw, T, y_out):
    pw.barrier("final")
    v = pw.dma("sync", y_out.rearrange("(t p) f -> p t f", p=128), T["X"][:, :, :],
               sem="yst")
    pw.wait("sync", "yst", v)



def build_nc(debug_upto=None):
    nc = bass.Bass("TRN2", target_bir_lowering=False)
    x_in = nc.dram_tensor("x0", [TOK, D], F32, kind="ExternalInput").ap()
    W = {}
    for n, s in [("aWq", [D, D]), ("aWk", [D, D]), ("aWv", [D, D]),
                 ("aWproj", [D, D]), ("aWpre", [D, D]),
                 ("b_qw", [D, D]), ("b_kw", [D, D]), ("b_vw", [D, D]),
                 ("b_ow", [D, D]), ("f_w1", [DHID, D]), ("f_w2", [D, DHID])]:
        W[n] = nc.dram_tensor(n, s, F32, kind="ExternalInput").ap()
    VEC = {}
    for n, s in [("a_bproj", [D]), ("a_bpre", [D]), ("b_qb", [D]), ("b_kb", [D]),
                 ("b_vb", [D]), ("b_ob", [D]), ("b_gamma", [D]), ("b_beta", [D]),
                 ("f_convb", [DHID]),
                 ("pi_bc", [128, 1]), ("isodd_bc", [128, 1]),
                 ("mleft_bc", [128, 1]), ("mright_bc", [128, 1])]:
        VEC[n] = nc.dram_tensor(n, s, F32, kind="ExternalInput").ap()
    b_proj = nc.dram_tensor("b_proj", [M, DH], F32, kind="ExternalInput").ap()
    f_convw = nc.dram_tensor("f_convw", [DHID, 3], F32, kind="ExternalInput").ap()
    y_out = nc.dram_tensor("y", [TOK, D], F32, kind="ExternalOutput").ap()

    G = {}
    span_tensors = []
    for sp in range(4):
        span_tensors += [(f"g_pre_in_s{sp}", [D, 512]), (f"g_pre_s{sp}", [2, D, 512]),
                         (f"g_k_in_s{sp}", [D, 512]), (f"g_k_s{sp}", [2, D, 512]),
                         (f"g_v_in_s{sp}", [128, 4 * 1040]),
                         (f"g_v_s{sp}", [2, 128, 4 * 1040])]
    for n, s in span_tensors + [
                 ("g_qT", [128, FC, TOK]), ("g_pqT", [128, FC, TOK]),
                 ("g_halo_in", [2, D]), ("g_halo", [2, 2, D]),
                 ("g_bqT", [128, FC, TOK]), ("g_bkT", [128, FC, TOK]),
                 ("g_bv", [128, NT * 1040]),
                 ("g_st_in", [H, 256, 65]), ("g_st", [H, 2, 256, 65])]:
        G[n] = nc.dram_tensor(n, s, F32).ap()
    G["g_t1T"] = nc.dram_tensor("g_t1T", [D, DHID], BF16).ap()
    G["g_t2T"] = nc.dram_tensor("g_t2T", [DHID, D], BF16).ap()
    G["g_hact"] = nc.dram_tensor("g_hact", [DHID, TOK], F32).ap()
    G["g_lg"] = nc.dram_tensor("g_lg", [32, 128, TOK], F32).ap()
    G["g_fr"] = nc.dram_tensor("g_fr", [128, 64], F32).ap()
    G["g_nh"] = nc.dram_tensor("g_nh", [DHID, TOK], BF16).ap()

    dbg = None
    if debug_upto is not None:
        dbg = {"upto": debug_upto}
        for n, s in [("d_xln", [128, NT, D]), ("d_qT", [D, TOK]), ("d_kT", [D, TOK]),
                     ("d_preT", [D, TOK]), ("d_pqT", [D, TOK]),
                     ("d_vext", [128, NT * 1040]), ("d_gpre", [2 * D, TOK]),
                     ("d_x1", [128, NT, D]), ("d_attn", [8, 128, 2 * D]),
                     ("d_logit", [8, 128, 256]), ("d_den", [128, 32])]:
            dbg[n] = nc.dram_tensor(n, s, F32, kind="ExternalOutput").ap()

    with contextlib.ExitStack() as stack:
        block = stack.enter_context(nc.Block())
        pw = PW(nc, stack)
        pw._slot_state = {}

        T = {}
        T["groups"] = [[0, 1], [2, 3], [4, 5], [6, 7]]
        T["X"] = stack.enter_context(nc.sbuf_tensor("X", [128, NT, D], F32))
        T["ARN_A"] = stack.enter_context(nc.sbuf_tensor("ARN_A", [128, 16384], F32))
        T["ARN_B"] = stack.enter_context(nc.sbuf_tensor("ARN_B", [128, 16384], F32))
        T["tmpA"] = stack.enter_context(nc.sbuf_tensor("tmpA", [128, 1024], F32))
        T["stat"] = stack.enter_context(nc.sbuf_tensor("stat", [128, 8], F32))
        T["fcols"] = stack.enter_context(nc.sbuf_tensor("fcols", [128, 64], F32))
        T["ones_col"] = stack.enter_context(nc.sbuf_tensor("ones_col", [128, 1], F32))
        T["BWK"] = stack.enter_context(nc.sbuf_tensor("BWK", [128, 1472], F32))
        T["ones_row"] = stack.enter_context(nc.sbuf_tensor("ones_row", [128, 128], F32))
        T["shift64"] = stack.enter_context(nc.sbuf_tensor("shift64", [128, 128], F32))
        arnb = T["ARN_B"]
        T["wbuf3"] = arnb[:, 0:8192].rearrange("p (f o) -> p f o", o=1024)
        T["stage"] = arnb[:, 8192:10240]
        T["stage2"] = arnb[:, 10240:12288]
        T["vstage"] = arnb[:, 12288:14368]
        T["hx"] = arnb[:, 0:4100].rearrange("p (a b) -> p a b", a=2)
        T["tstat"] = arnb[:, 4104:5128].bitcast(BF16).rearrange(
            "p (a b c) -> p a b c", a=2, b=8)
        T["tstat4"] = arnb[:, 4104:6152].bitcast(BF16).rearrange(
            "p (a b c) -> p a b c", a=4, b=8)
        T["nstat"] = arnb[:, 5128:5640].bitcast(BF16).rearrange(
            "p (a b) -> p a b", a=2)
        T["nstat4"] = arnb[:, 6152:7176].bitcast(BF16).rearrange(
            "p (a b) -> p a b", a=4)
        T["nh2"] = arnb[:, 5640:6664].bitcast(BF16)
        T["z16w"] = arnb[:, 6664:7688].bitcast(F16)
        T["haloh"] = arnb[:, 7688:7752]
        T["tbf"] = arnb[:, 12288:13312].bitcast(BF16)
        T["frow"] = arnb[0:1, 13312:15372]
        T["halo2"] = T["stage"][0:2, 0:1024]
        T["halo2b"] = T["stage"][0:2, 1024:2048]
        arna = T["ARN_A"]
        T["nbfT"] = arna[:, 0:8192].bitcast(BF16).rearrange(
            "p (a b) -> p a b", a=8)
        T["ofm"] = arna[:, 0:2048].rearrange("p (a b) -> p a b", a=4)
        T["z16"] = arna[:, 15360:15872].bitcast(F16)
        T["nhT"] = arna[:, 15872:15880].bitcast(BF16).rearrange(
            "p (a b) -> p a b", a=8)
        T["t2s"] = arna[:, 0:8192].bitcast(BF16).rearrange(
            "p (a b) -> p a b", a=32)
        T["ofm2"] = arna[:, 11264:13312].rearrange("p (a b) -> p a b", a=4)
        T["z16h"] = T["z16"][0:2, :]
        T["f2row"] = arna[0:1, 2048:4096]
        T["ident"] = stack.enter_context(nc.sbuf_tensor("ident", [128, 128], F32))
        T["identb"] = stack.enter_context(nc.sbuf_tensor("identb", [128, 128], BF16))
        T["mask_ut"] = stack.enter_context(nc.sbuf_tensor("mask_ut", [128, 128], F32))
        T["consts"] = stack.enter_context(nc.sbuf_tensor("consts", [128, 8], F32))
        T["vecs"] = stack.enter_context(nc.sbuf_tensor("vecs", [128, 112], F32))
        T["scal"] = stack.enter_context(nc.sbuf_tensor("scal", [128, 16], F32))
        T["convw"] = stack.enter_context(nc.sbuf_tensor("convw", [128, NJC, 3], F32))
        T["convb"] = stack.enter_context(nc.sbuf_tensor("convb", [128, NJC], F32))
        T["projx2"] = stack.enter_context(nc.sbuf_tensor("projx2", [128, M], F32))
        T["psum"] = stack.enter_context(nc.psum_tensor("ps", [128, 3584], F32))
        T["psumb"] = stack.enter_context(nc.psum_tensor("psb", [128, 1024], BF16))

        T["cvals"] = [0.0, 1.0, 0.125, -LN16, 1536.0, -1536.0, 0.5, -128.0]
        for i, v in enumerate(T["cvals"]):
            nc.const_aps.register(F32, float(v), T["consts"][:, i:i + 1])

        ph_setup(pw, T, x_in, VEC, b_proj, f_convw)
        stop = ph_attn_a_proj(pw, T, W, G, dbg)
        if not stop:
            ph_attn_a_logits(pw, T, G, dbg)
            arnb_ = T["ARN_B"]
            wmbuf = [arnb_[:, 12288:14336].rearrange("p (a b) -> p a b", a=2),
                     arnb_[:, 14336:16384].rearrange("p (a b) -> p a b", a=2)]

            def wm_src(L):
                if L < 16:
                    return W["f_w1"].rearrange("(a p) d -> p a d", p=128)[
                        :, 2 * L:2 * L + 2, :]
                Lp = L - 16
                return W["f_w2"].rearrange("(a p) d -> p a d", p=128)[
                    :, Lp // 2:Lp // 2 + 1,
                    2048 * (Lp % 2):2048 * (Lp % 2) + 2048].rearrange(
                    "p a (b c) -> p (a b) c", c=1024)

            wm_state = {"k": 0}
            wm_pml = {}
            wm_red = {}

            def wm_load(L):
                if L >= 32:
                    return
                if L >= 2:
                    pw.wait("gpsimd", "dve", wm_red[2 * L - 3])
                wm_pml[L] = pw.dma("gpsimd", wmbuf[L % 2], wm_src(L),
                                   sem="pm" + str(L % 2))

            pw.wait("gpsimd", "qst")
            wm_load(0)
            wm_load(1)

            def whook():
                k = wm_state["k"]
                if k >= 64:
                    return
                wm_state["k"] = k + 1
                L = k // 2
                pw.wait("vector", "pm" + str(L % 2), wm_pml[L])
                wm_red[k] = pw.red(T["fcols"][:, k:k + 1],
                                   wmbuf[L % 2][:, k % 2, :],
                                   AluOpType.add, absval=True, dep=False)
                if k % 2 == 1:
                    wm_load(L + 2)

            stop = ph_attn_a_attn(pw, T, W, G, dbg, whook)
        if not stop:
            pw.barrier("b1")
            ph_ffn_weights(pw, T, W, G)
            pw.barrier("b2")
            stop = ph_ffn(pw, T, W, G, dbg, 1)
        if not stop:
            pw.barrier("b3")
            stop = ph_attn_b(pw, T, W, G, dbg)
        if not stop:
            pw.barrier("b4")
            stop = ph_ffn(pw, T, W, G, dbg, 2)
        if not stop:
            ph_store(pw, T, y_out)

        pw.wait("sync", "st") if "st" in pw.cnt else None
        pw.replay(block)
    return nc


def _shard_inputs(inputs):
    x = np.ascontiguousarray(np.asarray(inputs["L0_attn_a_x"], dtype=np.float32))
    maps = []
    for c in range(8):
        b, h = c // 2, c % 2
        m = {"x0": np.ascontiguousarray(x[b, 2048 * h:2048 * (h + 1), :])}
        m["aWq"] = inputs["L0_attn_a_Wq"]
        m["aWk"] = inputs["L0_attn_a_Wk"]
        m["aWv"] = inputs["L0_attn_a_Wv"]
        m["aWproj"] = inputs["L0_attn_a_Wproj"]
        m["aWpre"] = inputs["L0_attn_a_Wpre"]
        m["b_qw"] = inputs["L0_attn_b_q_w"]
        m["b_kw"] = inputs["L0_attn_b_k_w"]
        m["b_vw"] = inputs["L0_attn_b_v_w"]
        m["b_ow"] = inputs["L0_attn_b_out_w"]
        m["f_w1"] = inputs["L0_ffn_w1"]
        m["f_w2"] = inputs["L0_ffn_w2"]
        m["a_bproj"] = inputs["L0_attn_a_bproj"]
        m["a_bpre"] = inputs["L0_attn_a_bpre"]
        m["b_qb"] = inputs["L0_attn_b_q_b"]
        m["b_kb"] = inputs["L0_attn_b_k_b"]
        m["b_vb"] = inputs["L0_attn_b_v_b"]
        m["b_ob"] = inputs["L0_attn_b_out_b"]
        m["b_gamma"] = inputs["L0_attn_b_gamma"]
        m["b_beta"] = inputs["L0_attn_b_beta"]
        m["f_convb"] = inputs["L0_ffn_conv_b"]
        m["b_proj"] = inputs["L0_attn_b_proj"]
        m["f_convw"] = np.ascontiguousarray(
            np.asarray(inputs["L0_ffn_conv_w"], dtype=np.float32)[:, 0, :])
        m["pi_bc"] = np.ascontiguousarray(
            np.broadcast_to(np.asarray(inputs["L0_attn_a_pi"], np.float32), (128, 1)))
        m["isodd_bc"] = np.full((128, 1), float(h), np.float32)
        m["mleft_bc"] = np.full((128, 1), float(h), np.float32)
        m["mright_bc"] = np.full((128, 1), float(1 - h), np.float32)
        m = {k: np.ascontiguousarray(np.asarray(v, dtype=np.float32)) if k not in ()
             else v for k, v in m.items()}
        maps.append(m)
    return maps


def kernel(**inputs):
    nc = build_nc()
    maps = _shard_inputs(inputs)
    res = run_bass_kernel_spmd(nc, maps, core_ids=list(range(8)))
    out = np.empty((4, 4096, 1024), np.float32)
    for c in range(8):
        b, h = c // 2, c % 2
        out[b, 2048 * h:2048 * (h + 1), :] = res.results[c]["y"]
    return out
